# version 06
# baseline (speedup 1.0000x reference)
import contextlib
import numpy as np
import ml_dtypes
import concourse.bass as bass
import concourse.mybir as mybir
from concourse.bass_utils import run_bass_kernel_spmd

F32 = mybir.dt.float32
BF16 = mybir.dt.bfloat16
I32 = mybir.dt.int32
ALU = mybir.AluOpType
AF = mybir.ActivationFunctionType
AXX = mybir.AxisListType.X

NCORES = 8
T = 1024
D = 2048
FF = 8192
NH = 16
HD = 128
KC = D // 128
FC = FF // 128
EPS = 1e-5


class Call:
    __slots__ = ("name", "args", "kwargs", "chain")

    def __init__(self, name, args, kwargs):
        self.name, self.args, self.kwargs, self.chain = name, args, kwargs, []

    def then_inc(self, sem, k=1):
        sem.val += k
        self.chain.append((sem, k))
        return self


_NOCHAIN = {"wait_ge", "dma_start", "dma_start_transpose", "nop", "collective_compute"}


class EngRec:

    def __init__(self, prog, key):
        self._prog, self._key = prog, key

    def __getattr__(self, name):
        def call(*a, **kw):
            chain = self._prog.chains.get(self._key)
            if chain is not None and name not in _NOCHAIN:
                if chain.val > 0:
                    self._prog.streams[self._key].append(Call("wait_ge", (chain, chain.val), {}))
                c = Call(name, a, kw)
                self._prog.streams[self._key].append(c)
                c.then_inc(chain, 1)
                return c
            c = Call(name, a, kw)
            self._prog.streams[self._key].append(c)
            return c
        return call


class Prog:
    ENGS = ("tensor", "vector", "scalar", "gpsimd", "sync")

    def __init__(self):
        self.streams = {e: [] for e in self.ENGS}
        self.chains = {}
        for e in self.ENGS:
            setattr(self, e, EngRec(self, e))

    def set_chains(self, sems):
        self.chains.update(sems)

    def mark_last(self, key, sem, k=1):
        self.streams[key][-1].then_inc(sem, k)

    def emit(self, block):
        def mk(key):
            def f(eng):
                for c in self.streams[key]:
                    args = [a.h if isinstance(a, Sem) else a for a in c.args]
                    r = getattr(eng, c.name)(*args, **c.kwargs)
                    for (sem, k) in c.chain:
                        r = r.then_inc(sem.h, k)
            return f
        block.tensor(mk("tensor"))
        block.vector(mk("vector"))
        block.scalar(mk("scalar"))
        block.gpsimd(mk("gpsimd"))
        block.sync(mk("sync"))


class Sem:

    def __init__(self, name):
        self.name, self.val, self.h = name, 0, None

    def inc(self, k=1):
        self.val += k
        return self.val


def prep_inputs(inp):
    x = np.ascontiguousarray(inp["L0_attn_a_x"].reshape(8192, 2048))
    wupT_q = np.ascontiguousarray(inp["L0_ffn_w_up_q"].T)
    wdnT_q = np.ascontiguousarray(inp["L0_ffn_w_down_q"].T)
    wup_sc = np.ascontiguousarray(np.repeat(inp["L0_ffn_w_up_scale"], 64, axis=1).T)
    wdn_sc = np.ascontiguousarray(np.repeat(inp["L0_ffn_w_down_scale"], 64, axis=1).T)
    I128 = np.eye(128, dtype=np.float32)
    base = {
        "I128": I128,
        "wupT_q": wupT_q, "wdnT_q": wdnT_q, "wup_sc": wup_sc, "wdn_sc": wdn_sc,
        "b_up": np.ascontiguousarray(inp["L0_ffn_b_up"].reshape(FC, 128).T),
        "b_down": np.ascontiguousarray(inp["L0_ffn_b_down"].reshape(KC, 128).T),
        "lua": inp["L0_ffn_w_up_lora_a"],
        "lub": inp["L0_ffn_w_up_lora_b"],
        "lda": inp["L0_ffn_w_down_lora_a"],
        "ldb": inp["L0_ffn_w_down_lora_b"],
    }
    maps = []
    for i in range(NCORES):
        m = dict(base)
        m["xrows"] = np.ascontiguousarray(x[T * i:T * (i + 1)])
        maps.append(m)
    return maps


class KB:

    def __init__(self):
        self.nc = bass.Bass("TRN2", target_bir_lowering=False)
        self.stack = contextlib.ExitStack()
        self.prog = Prog()
        self.sems = {}
        self._psum_used = []

    def sem(self, name):
        s = Sem(name)
        self.sems[name] = s
        return s

    def din(self, name, shape, dtype=F32):
        return self.nc.dram_tensor(name, list(shape), dtype, kind="ExternalInput").ap()

    def dout(self, name, shape, dtype=F32):
        return self.nc.dram_tensor(name, list(shape), dtype, kind="ExternalOutput").ap()

    def dscratch(self, name, shape, dtype=F32):
        return self.nc.dram_tensor(name, list(shape), dtype)

    def sb(self, name, shape, dtype=F32):
        return self.stack.enter_context(self.nc.sbuf_tensor(name, list(shape), dtype))

    def ps(self, name, shape, dtype=F32):
        return self.stack.enter_context(self.nc.psum_tensor(name, list(shape), dtype))

    def finalize(self):
        from concourse.aot_env import allow_ici_all_to_all
        with contextlib.ExitStack() as es:
            es.enter_context(allow_ici_all_to_all())
            es.enter_context(self.nc.allow_low_precision(
                reason="bf16 working tiles by design; matmul/reduce accumulate fp32 internally"))
            for s in self.sems.values():
                s.h = es.enter_context(self.nc.semaphore(s.name))
            block = es.enter_context(self.nc.Block())
            self.prog.emit(block)
        self.stack.close()
        return self.nc


def view(t, dtype, shape):
    ap = t[:, :] if not isinstance(t, bass.AP) else t
    ap = ap.bitcast(dtype)
    if ap.ndim == 3:
        ap = ap.rearrange("p a b -> p (a b)")
    if len(shape) == 2:
        return ap.rearrange("p (a b) -> p a b", a=shape[0], b=shape[1])
    if len(shape) == 1:
        return ap
    raise ValueError(shape)


class Common:

    def __init__(self, kb: KB):
        self.kb = kb
        self.PS = [kb.ps(f"PS{i}", [128, 512], F32) for i in range(8)]
        self.AR0 = kb.sb("AR0", [128, 32768], F32)
        ar0b = self.AR0[:, :].bitcast(BF16)
        self.XH = ar0b.rearrange("p (a b) -> p a b", a=FC, b=T)
        self.AB0 = [ar0b[:, 16384 * i:16384 * (i + 1)].rearrange("p (a b) -> p a b", a=KC, b=T) for i in range(4)]
        self.XT2 = kb.sb("XT2", [128, 2048], F32)
        self.XT = [self.XT2[:, 0:1024], self.XT2[:, 1024:2048]]
        self.XB = kb.sb("XB", [128, KC, 1028], BF16)
        self.SA = self.XT2[0:1, 0:1024]
        self.SB = self.XT2[0:1, 1024:2048]
        self.MB = kb.sb("MB", [128, 1024], BF16)
        self.RB = kb.sb("RB", [128, 1024], BF16)
        self.SQ = self.MB
        self.WQF = [kb.sb(f"WQF{i}", [128, 4, 128], F32) for i in range(2)]
        self.SCF = [kb.sb(f"SCF{i}", [128, 4, 128], F32) for i in range(2)]
        self.WUB = [kb.sb(f"WUB{i}", [128, KC, 128], BF16) for i in range(2)]
        self.STB = kb.sb("STB", [1, 2048], BF16)
        self.SMALL = kb.sb("SMALL", [128, 1024], F32)
        self.LUA = kb.sb("LUA", [128, KC, 16], BF16)
        self.LUBd = [kb.sb(f"LUBd{i}", [16, 128], BF16) for i in range(2)]
        self.LDA = kb.sb("LDA", [128, FC, 16], BF16)
        self.LDBd = [kb.sb(f"LDBd{i}", [16, 128], BF16) for i in range(2)]
        self.LT1 = kb.sb("LT1", [16, 1024], BF16)
        self.LT2 = kb.sb("LT2", [16, 1024], BF16)
        self.SM0 = self.LT1[0:1, :]
        self.SR0 = self.LT2[0:1, :]

        self.BUP = kb.sb("BUP", [128, FC], F32)
        self.BDN = kb.sb("BDN", [128, KC], F32)
        self.ones_col_bf = kb.sb("ones_col_bf", [128, 1], BF16)
        self.ones_row_bf = kb.sb("ones_row_bf", [1, 128], BF16)
        self.invd_col_bf = kb.sb("invd_col_bf", [128, 1], BF16)
        self.sqrthd_row_bf = kb.sb("sqrthd_row_bf", [1, 128], BF16)
        self.eps_l2 = kb.sb("eps_l2", [1, 1], F32)
        self.eps_ln = kb.sb("eps_ln", [1, 1], F32)
        self.m75 = kb.sb("m75", [128, 1], F32)
        self.wqf_guard = [None, None]
        self.qS = kb.sem("qS")
        self.qL = kb.sem("qL")
        self.qW = [kb.sem("qW0"), kb.sem("qW1")]
        self.qB = [kb.sem("qB0"), kb.sem("qB1")]
        self.qE = [kb.sem("qE0"), kb.sem("qE1")]
        self.qC = [kb.sem("qC0"), kb.sem("qC1")]
        self.qD = [kb.sem("qD0"), kb.sem("qD1")]
        self.qR = [kb.sem("qR0"), kb.sem("qR1")]
        self.qG = kb.sem("qG")
        self.qH = kb.sem("qH")
        self.cT = kb.sem("cT")
        self.cc = kb.sem("cc")
        self.cV = kb.sem("cV")
        self.cA = kb.sem("cA")
        self.cP = kb.sem("cP")
        kb.prog.set_chains({"vector": self.cV, "scalar": self.cA, "gpsimd": self.cP})
        self.ones_col_f32 = kb.sb("ones_col_f32", [128, 1], F32)
        self.ones_row_f32 = kb.sb("ones_row_f32", [1, 128], F32)
        self.WG = kb.sb("WG", [128, KC, 16], BF16)

    def xb(self, k, n0=0, n1=T):
        return self.XB[:, k, 4 + n0:4 + n1]

    def preload(self, din):
        P, C = self.kb.prog, self
        P.vector.memset(C.ones_col_bf[:, :], 1.0)
        P.vector.memset(C.ones_row_bf[:, :], 1.0)
        P.vector.memset(C.invd_col_bf[:, :], 1.0 / 128.0)
        P.vector.memset(C.sqrthd_row_bf[:, :], float(np.sqrt(128.0)))
        P.vector.memset(C.eps_l2[:, :], 1e-12)
        P.vector.memset(C.eps_ln[:, :], float(EPS * HD))
        P.vector.memset(C.m75[:, :], -7.5)
        P.vector.memset(C.ones_col_f32[:, :], 1.0)
        P.vector.memset(C.ones_row_f32[:, :], 1.0)
        P.gpsimd.dma_start(out=C.LUA[:, :, :], in_=din["lua"].rearrange("(kc p) r -> p kc r", p=128)).then_inc(C.qG, 16)
        P.gpsimd.dma_start(out=C.LDA[:, :, :], in_=din["lda"].rearrange("(h p) r -> p h r", p=128)).then_inc(C.qG, 16)
        P.sync.dma_start(out=C.BUP[:, :], in_=din["b_up"]).then_inc(C.qS, 16)
        P.sync.dma_start(out=C.BDN[:, :], in_=din["b_down"]).then_inc(C.qS, 16)


def emit_ln(kb, C, src_dram, *, fused_w=None, fused_b=None, fused_rho=False,
            out_bf=None, keep_rho=None):
    P = kb.prog
    out_bf = out_bf if out_bf is not None else C.XB

    def OBF(k, n0=0, n1=T):
        if out_bf is C.XB:
            return out_bf[:, k, 4 + n0:4 + n1]
        return out_bf[:, k, n0:n1]
    P.gpsimd.wait_ge(C.cT, C.cT.val)
    P.gpsimd.wait_ge(C.cV, C.cV.val)
    P.gpsimd.wait_ge(C.qG, C.qG.val)
    P.gpsimd.wait_ge(C.qD[0], C.qD[0].val)
    P.gpsimd.wait_ge(C.qD[1], C.qD[1].val)
    P.sync.wait_ge(C.qG, C.qG.val)
    P.tensor.wait_ge(C.cV, C.cV.val)
    for k in range(KC):
        P.gpsimd.dma_start(out=OBF(k), in_=src_dram[k]).then_inc(C.qL, 16)
    P.scalar.wait_ge(C.qL, C.qL.val)
    for k in range(KC):
        if k > 0:
            P.scalar.wait_ge(C.cT, C.cT.val)
        P.scalar.activation(C.SQ[:, :], OBF(k), AF.Square)
        P.tensor.wait_ge(C.cA, C.cA.val)
        P.tensor.matmul(C.PS[0][0:1, :], C.ones_col_bf[:, :], OBF(k, 0, 512),
                        start=(k == 0), stop=(k == KC - 1))
        P.tensor.matmul(C.PS[1][0:1, :], C.ones_col_bf[:, :], OBF(k, 512, 1024),
                        start=(k == 0), stop=(k == KC - 1))
        P.tensor.matmul(C.PS[2][0:1, :], C.ones_col_bf[:, :], C.SQ[:, 0:512],
                        start=(k == 0), stop=(k == KC - 1))
        P.tensor.matmul(C.PS[3][0:1, :], C.ones_col_bf[:, :], C.SQ[:, 512:1024],
                        start=(k == 0), stop=(k == KC - 1)).then_inc(C.cT, 1)
    P.vector.wait_ge(C.cT, C.cT.val)
    P.vector.tensor_scalar(C.SA[:, 0:512], C.PS[0][0:1, :], 1.0 / D, None, ALU.mult)
    P.vector.tensor_scalar(C.SA[:, 512:1024], C.PS[1][0:1, :], 1.0 / D, None, ALU.mult)
    P.vector.tensor_copy(C.STB[0:1, 0:1024], C.SA)
    P.vector.tensor_scalar(C.SB[:, 0:512], C.PS[2][0:1, :], 1.0 / D, None, ALU.mult)
    P.vector.tensor_scalar(C.SB[:, 512:1024], C.PS[3][0:1, :], 1.0 / D, None, ALU.mult)
    P.vector.tensor_tensor(C.STB[0:1, 1024:2048], C.SA, C.SA, ALU.mult)
    P.vector.tensor_tensor(C.SB, C.SB, C.STB[0:1, 1024:2048], ALU.subtract)
    P.vector.tensor_scalar(C.SA, C.SB, EPS, None, ALU.add)
    P.vector.reciprocal(C.SA, C.SA)
    if fused_rho or keep_rho is not None:
        P.vector.tensor_tensor(C.SB, C.SB, C.SA, ALU.mult)
        P.vector.tensor_scalar(C.SB, C.SB, EPS, None, ALU.add)
        P.scalar.wait_ge(C.cV, C.cV.val)
        P.scalar.activation(C.SB, C.SB, AF.Sqrt)
        P.scalar.activation(C.SA, C.SA, AF.Sqrt)
        P.vector.wait_ge(C.cA, C.cA.val)
        P.vector.reciprocal(C.SB, C.SB)
        if keep_rho is not None:
            P.vector.tensor_copy(C.SM0, C.STB[0:1, 0:1024])
            P.vector.tensor_copy(C.SR0, C.SA)
        if fused_rho:
            P.vector.tensor_tensor(C.SA, C.SA, C.SB, ALU.mult)
    else:
        P.vector.wait_ge(C.cV, C.cV.val)
        P.scalar.wait_ge(C.cV, C.cV.val)
        P.scalar.activation(C.SA, C.SA, AF.Sqrt)
        P.vector.wait_ge(C.cA, C.cA.val)
    P.vector.tensor_copy(C.STB[0:1, 1024:2048], C.SA)
    P.tensor.wait_ge(C.cV, C.cV.val)
    for n in range(2):
        P.tensor.matmul(C.PS[0 + n][:, :], C.ones_row_bf[:, :], C.STB[0:1, 512 * n:512 * (n + 1)], start=True, stop=True)
        P.tensor.matmul(C.PS[2 + n][:, :], C.ones_row_bf[:, :], C.STB[0:1, 1024 + 512 * n:1024 + 512 * (n + 1)], start=True, stop=True)
    P.mark_last('tensor', C.cT)
    P.vector.wait_ge(C.cT, C.cT.val)
    P.vector.tensor_copy(C.MB[:, 0:512], C.PS[0][:, :])
    P.vector.tensor_copy(C.MB[:, 512:1024], C.PS[1][:, :])
    P.vector.tensor_copy(C.RB[:, 0:512], C.PS[2][:, :])
    P.vector.tensor_copy(C.RB[:, 512:1024], C.PS[3][:, :])
    for k in range(KC):
        P.vector.tensor_tensor(OBF(k), OBF(k), C.MB[:, :], ALU.subtract)
        P.vector.tensor_tensor(OBF(k), OBF(k), C.RB[:, :], ALU.mult)
        if fused_w is not None:
            P.vector.tensor_scalar(OBF(k), OBF(k), fused_w[:, k:k + 1], fused_b[:, k:k + 1], ALU.mult, ALU.add)
    P.vector.nop()


def _stream_lhsT(kb, C, par, src, cache, mcols, rows0, dequant, mm_wait=None):
    P = kb.prog
    if not dequant:
        for w in (mm_wait or []):
            P.gpsimd.wait_ge(*w)
        P.gpsimd.dma_start(out=C.WUB[par][:, :, :], in_=src).then_inc(C.qW[par], 16)
        return [(C.qW[par], C.qW[par].val)]
    qsrc, ssrc = src
    done = []
    for q in range(4):
        b = q % 2
        r = rows0 + 512 * q
        g = C.wqf_guard[b]
        if g is not None:
            P.gpsimd.wait_ge(*g)
        if q == 0:
            for w in (mm_wait or []):
                P.gpsimd.wait_ge(*w)
        P.gpsimd.dma_start(out=C.WQF[b][:, :, :],
                           in_=qsrc[r:r + 512, mcols].rearrange("(kc p) n -> p kc n", p=128)).then_inc(C.qW[b], 16)
        P.gpsimd.dma_start(out=C.SCF[b][:, :, :],
                           in_=ssrc[r:r + 512, mcols].rearrange("(kc p) n -> p kc n", p=128)).then_inc(C.qW[b], 16)
        eng, cm = (P.vector, C.cV) if b == 0 else (P.gpsimd, C.cP)
        eng.wait_ge(C.qW[b], C.qW[b].val)
        for w in (mm_wait or []):
            eng.wait_ge(*w)
        if b == 0:
            eng.scalar_tensor_tensor(
                C.WUB[par][:, 4 * q:4 * (q + 1), :].rearrange("p a b -> p (a b)"),
                C.WQF[b][:, :, :].rearrange("p a b -> p (a b)"), -7.5,
                C.SCF[b][:, :, :].rearrange("p a b -> p (a b)"),
                ALU.add, ALU.mult)
        else:
            eng.tensor_tensor(C.WQF[b][:, :, :].rearrange("p a b -> p (a b)"),
                              C.WQF[b][:, :, :].rearrange("p a b -> p (a b)"),
                              C.m75[:, 0:1].to_broadcast([128, 512]), ALU.add)
            eng.tensor_tensor(
                C.WUB[par][:, 4 * q:4 * (q + 1), :].rearrange("p a b -> p (a b)"),
                C.WQF[b][:, :, :].rearrange("p a b -> p (a b)"),
                C.SCF[b][:, :, :].rearrange("p a b -> p (a b)"),
                ALU.mult)
        C.wqf_guard[b] = (cm, cm.val)
        done.append((cm, cm.val))
    if cache is not None:
        for (sm, vv) in done[-2:]:
            P.gpsimd.wait_ge(sm, vv)
        P.gpsimd.dma_start(out=cache, in_=C.WUB[par][:, :, :]).then_inc(C.qC[par], 16)
    return done


def emit_predequant(kb, C, wup_src, wdn_src, wup_cache, wdn_cache):
    P = kb.prog
    blocks = []
    for m in range(FC):
        blocks.append((wup_src, wup_cache[m], slice(128 * m, 128 * (m + 1)), 0))
    for d in range(KC):
        for r4 in range(4):
            blocks.append((wdn_src, wdn_cache[4 * d + r4], slice(128 * d, 128 * (d + 1)), 2048 * r4))
    wr = {}
    guard = {0: None, 1: None}
    P.gpsimd.wait_ge(C.cV, C.cV.val)
    for bi, ((qsrc, ssrc), cache, mcols, rows0) in enumerate(blocks):
        par = bi % 2
        if bi >= 2:
            P.gpsimd.wait_ge(C.qC[par], wr[bi - 2])
        for q in range(4):
            b = q % 2
            r = rows0 + 512 * q
            if guard[b] is not None:
                P.gpsimd.wait_ge(*guard[b])
            P.gpsimd.dma_start(out=C.WQF[b][:, :, :],
                               in_=qsrc[r:r + 512, mcols].rearrange("(kc p) n -> p kc n", p=128)).then_inc(C.qW[b], 16)
            P.gpsimd.dma_start(out=C.SCF[b][:, :, :],
                               in_=ssrc[r:r + 512, mcols].rearrange("(kc p) n -> p kc n", p=128)).then_inc(C.qW[b], 16)
            P.gpsimd.wait_ge(C.qW[b], C.qW[b].val)
            P.gpsimd.tensor_tensor(C.WQF[b][:, :, :].rearrange("p a b -> p (a b)"),
                                   C.WQF[b][:, :, :].rearrange("p a b -> p (a b)"),
                                   C.m75[:, 0:1].to_broadcast([128, 512]), ALU.add)
            P.gpsimd.tensor_tensor(C.WUB[par][:, 4 * q:4 * (q + 1), :].rearrange("p a b -> p (a b)"),
                                   C.WQF[b][:, :, :].rearrange("p a b -> p (a b)"),
                                   C.SCF[b][:, :, :].rearrange("p a b -> p (a b)"), ALU.mult)
            guard[b] = (C.cP, C.cP.val)
        P.gpsimd.wait_ge(C.cP, C.cP.val)
        P.gpsimd.dma_start(out=cache, in_=C.WUB[par][:, :, :]).then_inc(C.qC[par], 16)
        wr[bi] = C.qC[par].val


def emit_ffn(kb, C, src_dram, dst_dram, *, dequant, wup_src, wdn_src, lub_dram, ldb_dram,
             wup_cache=None, wdn_cache=None, halo_out=None):
    P = kb.prog
    emit_ln(kb, C, src_dram)
    P.tensor.wait_ge(C.cV, C.cV.val)
    for n in range(2):
        if n == 1:
            P.tensor.wait_ge(C.cV, C.cV.val)
        for k in range(KC):
            P.tensor.matmul(C.PS[4][0:16, :], C.LUA[:, k, :], C.xb(k, 512 * n, 512 * (n + 1)),
                            start=(k == 0), stop=(k == KC - 1))
        P.mark_last('tensor', C.cT)
        P.vector.wait_ge(C.cT, C.cT.val)
        P.vector.tensor_copy(C.LT1[:, 512 * n:512 * (n + 1)], C.PS[4][0:16, :])
    lt1_ready = C.cV.val

    mm_done = {}
    evac_done = {}
    cache_done = {}
    for m in range(FC):
        par = m % 2
        mm_wait = []
        if m >= 2:
            mm_wait.append((C.cT, mm_done[m - 2]))
            if m - 2 in cache_done:
                mm_wait.append((C.qC[par], cache_done[m - 2]))
        if dequant:
            src = wup_src
        else:
            src = wup_src[m]
        rdy = _stream_lhsT(kb, C, par, src, wup_cache[m] if wup_cache is not None else None,
                           slice(128 * m, 128 * (m + 1)), 0, dequant, mm_wait)
        if wup_cache is not None:
            cache_done[m] = C.qC[par].val
        P.gpsimd.wait_ge(C.cT, mm_done[m - 2] if m >= 2 else 0)
        P.gpsimd.dma_start(out=C.LUBd[par][:, :], in_=lub_dram[0:16, 128 * m:128 * (m + 1)]).then_inc(C.qB[par], 16)
        pu, pu2 = C.PS[4 + 2 * par], C.PS[5 + 2 * par]
        for (sm, vv) in rdy:
            P.tensor.wait_ge(sm, vv)
        P.tensor.wait_ge(C.qB[par], C.qB[par].val)
        if m == 0:
            P.tensor.wait_ge(C.cV, lt1_ready)
        if m >= 2:
            P.tensor.wait_ge(C.cA, evac_done[m - 2])
        for k in range(KC):
            P.tensor.matmul(pu[:, :], C.WUB[par][:, k, :], C.xb(k, 0, 512), start=(k == 0), stop=False)
            P.tensor.matmul(pu2[:, :], C.WUB[par][:, k, :], C.xb(k, 512, 1024), start=(k == 0), stop=False)
        P.tensor.matmul(pu[:, :], C.LUBd[par][:, :], C.LT1[:, 0:512], start=False, stop=True)
        P.tensor.matmul(pu2[:, :], C.LUBd[par][:, :], C.LT1[:, 512:1024], start=False, stop=True).then_inc(C.cT, 1)
        mm_done[m] = C.cT.val
        P.scalar.wait_ge(C.cT, C.cT.val)
        if m == 0:
            P.scalar.wait_ge(C.qS, C.qS.val)
        P.scalar.activation(C.XH[:, m, 0:512], pu[:, :], AF.Relu, bias=C.BUP[:, m:m + 1], scale=1.0)
        P.scalar.activation(C.XH[:, m, 512:1024], pu2[:, :], AF.Relu, bias=C.BUP[:, m:m + 1], scale=1.0)
        evac_done[m] = C.cA.val
    up_done = C.cA.val

    P.tensor.wait_ge(C.cA, up_done)
    for n in range(2):
        if n == 1:
            P.tensor.wait_ge(C.cV, C.cV.val)
        for h in range(FC):
            P.tensor.matmul(C.PS[0][0:16, :], C.LDA[:, h, :], C.XH[:, h, 512 * n:512 * (n + 1)],
                            start=(h == 0), stop=(h == FC - 1))
        P.mark_last('tensor', C.cT)
        P.vector.wait_ge(C.cT, C.cT.val)
        P.vector.tensor_copy(C.LT2[:, 512 * n:512 * (n + 1)], C.PS[0][0:16, :])
    lt2_ready = C.cV.val

    mmd_done = {}
    ep_done = {}
    out_done = {}
    cached_done = {}
    for d in range(KC):
        pd, pd2 = C.PS[4 + 2 * (d % 2)], C.PS[5 + 2 * (d % 2)]
        P.gpsimd.wait_ge(C.cT, mmd_done[4 * (d - 2) + 3] if d >= 2 else C.cT.val)
        P.gpsimd.dma_start(out=C.LDBd[d % 2][:, :], in_=ldb_dram[0:16, 128 * d:128 * (d + 1)]).then_inc(C.qB[d % 2], 16)
        for r4 in range(4):
            par = r4 % 2
            step = 4 * d + r4
            mm_wait = []
            if step >= 2:
                mm_wait.append((C.cT, mmd_done[step - 2]))
                if step - 2 in cached_done:
                    mm_wait.append((C.qC[par], cached_done[step - 2]))
            else:
                mm_wait.append((C.cT, C.cT.val))
                mm_wait.append((C.qC[par], C.qC[par].val))
            if dequant:
                src = wdn_src
            else:
                src = wdn_src[step]
            rdy = _stream_lhsT(kb, C, par, src, wdn_cache[step] if wdn_cache is not None else None,
                               slice(128 * d, 128 * (d + 1)), 2048 * r4, dequant, mm_wait)
            if wdn_cache is not None:
                cached_done[step] = C.qC[par].val
            for (sm, vv) in rdy:
                P.tensor.wait_ge(sm, vv)
            if step == 0:
                P.tensor.wait_ge(C.cV, lt2_ready)
            if r4 == 0 and d >= 2:
                P.tensor.wait_ge(C.cV, ep_done[d - 2])
            for kk in range(KC):
                h = KC * r4 + kk
                P.tensor.matmul(pd[:, :], C.WUB[par][:, kk, :], C.XH[:, h, 0:512], start=(step % 4 == 0 and kk == 0), stop=False)
                P.tensor.matmul(pd2[:, :], C.WUB[par][:, kk, :], C.XH[:, h, 512:1024], start=(step % 4 == 0 and kk == 0), stop=False)
            if r4 == 3:
                P.tensor.wait_ge(C.qB[d % 2], C.qB[d % 2].val)
                P.tensor.matmul(pd[:, :], C.LDBd[d % 2][:, :], C.LT2[:, 0:512], start=False, stop=True)
                P.tensor.matmul(pd2[:, :], C.LDBd[d % 2][:, :], C.LT2[:, 512:1024], start=False, stop=True)
            P.mark_last('tensor', C.cT)
            mmd_done[step] = C.cT.val
        buf = C.XT[d % 2]
        if d == 0:
            P.sync.wait_ge(C.cV, C.cV.val)
            P.sync.wait_ge(C.cA, C.cA.val)
        if d >= 2:
            P.sync.wait_ge(C.cV, ep_done[d - 2])
            P.sync.wait_ge(C.qD[d % 2], out_done[d - 2])
        P.sync.dma_start(out=buf, in_=src_dram[d]).then_inc(C.qE[d % 2], 16)
        P.vector.wait_ge(C.qE[d % 2], C.qE[d % 2].val)
        P.vector.wait_ge(C.cT, C.cT.val)
        if d == 0:
            P.vector.wait_ge(C.qS, C.qS.val)
        P.vector.scalar_tensor_tensor(buf[:, 0:512], pd[:, :], C.BDN[:, d:d + 1], buf[:, 0:512], ALU.add, ALU.add)
        P.vector.scalar_tensor_tensor(buf[:, 512:1024], pd2[:, :], C.BDN[:, d:d + 1], buf[:, 512:1024], ALU.add, ALU.add)
        ep_done[d] = C.cV.val
        P.gpsimd.wait_ge(C.cV, C.cV.val)
        P.gpsimd.dma_start(out=dst_dram[d], in_=buf).then_inc(C.qD[d % 2], 16)
        if halo_out is not None:
            P.gpsimd.dma_start(out=halo_out[:, d, :], in_=buf[:, 1021:1024]).then_inc(C.qD[d % 2], 16)
        out_done[d] = C.qD[d % 2].val


PP = 512
PP_CONVW = PP + 0
PP_CONVB = PP + 64
PP_VNW = PP + 80
PP_VNB = PP + 81
PP_MNW = PP + 82
PP_MNB = PP + 83
PP_GNW = PP + 84
PP_GNB = PP + 100
PP_BIG = PP + 116
PP_BOG = PP + 132
PP_LN1W = PP + 148
PP_LN1B = PP + 164
PP_LN2W = PP + 180
PP_LN2B = PP + 196
PP_BG = PP + 212
PP_CMS = PP + 213
PP_HM8 = PP + 221
PPN = 229
SM_SD = 741
SM_TMP = 757
SM_CARRY = 773
SM_HALO = 789
SM_HBT = 837


def emit_attn_b(kb, C, src_dram, dst_dram, hm_dram, aux):
    P = kb.prog
    SM = C.SMALL
    xcb, MEM, QB, OUTB = C.AB0
    T0 = C.XT2[:, 1024:1536].bitcast(BF16)
    T1 = C.XT2[:, 1536:2048].bitcast(BF16)
    S0 = SM[:, 0:512].bitcast(BF16)
    GAMS = C.XT2[0:16, 0:1024]

    def wslab(segs, wsrc, col, sem):
        for (dst, r0, r1) in segs:
            P.gpsimd.dma_start(
                out=dst,
                in_=wsrc[r0:r1, 128 * col:128 * (col + 1)].rearrange("(kc p) n -> p kc n", p=128),
            ).then_inc(sem, 16)

    WUB0q = [(C.WUB[0][:, 0:16, :], 0, 2048)]
    WUB1q = [(C.WUB[1][:, 0:16, :], 0, 2048)]
    WQFv = [view(C.WQF[0][:, :, :], BF16, (8, 128)), view(C.WQF[1][:, :, :], BF16, (8, 128))]
    SCFv = [view(C.SCF[0][:, :, :], BF16, (8, 128)), view(C.SCF[1][:, :, :], BF16, (8, 128))]
    WQFvq = [(WQFv[0][:, 0:8, :], 0, 1024), (WQFv[1][:, 0:8, :], 1024, 2048)]
    SCFvq = [(SCFv[0][:, 0:8, :], 0, 1024), (SCFv[1][:, 0:8, :], 1024, 2048)]

    def vtile(kc):
        return WQFv[kc // 8][:, kc % 8, :]

    def itile(kc):
        return SCFv[kc // 8][:, kc % 8, :]

    P.gpsimd.dma_start(out=SM[:, PP:PP + PPN], in_=aux["parpack"]).then_inc(C.qG, 16)
    P.gpsimd.dma_start(out=C.WG[:, :, :], in_=aux["Wg"].rearrange("(kc p) n -> p kc n", p=128)).then_inc(C.qG, 16)

    emit_ln(kb, C, src_dram, fused_w=SM[:, PP_LN1W:PP_LN1W + 16], fused_b=SM[:, PP_LN1B:PP_LN1B + 16],
            fused_rho=True, keep_rho=True)
    ln_done = C.cV.val

    P.gpsimd.wait_ge(C.qD[0], C.qD[0].val)
    P.gpsimd.wait_ge(C.qD[1], C.qD[1].val)
    P.gpsimd.collective_compute(
        "AllGather", ALU.bypass, replica_groups=[list(range(NCORES))],
        ins=[aux["halo_in"].ap().opt()], outs=[aux["halo_all"].ap().opt()],
    ).then_inc(C.cc, 1)
    P.gpsimd.wait_ge(C.cc, C.cc.val)
    HLD = C.XT2[:, 1024:1408]
    P.gpsimd.wait_ge(C.cV, C.cV.val)
    P.gpsimd.wait_ge(C.cA, C.cA.val)
    P.gpsimd.wait_ge(C.cT, C.cT.val)
    P.gpsimd.dma_start(out=HLD.rearrange("p (j k t) -> p j k t", j=8, k=16, t=3),
                       in_=aux["halo_all"].ap().rearrange("j p k t -> p j k t")).then_inc(C.qG, 16)
    P.vector.wait_ge(C.qG, C.qG.val)
    P.vector.memset(SM[:, SM_HALO:SM_HALO + 48], 0.0)
    for j in range(NCORES):
        P.vector.scalar_tensor_tensor(SM[:, SM_HALO:SM_HALO + 48], HLD[:, 48 * j:48 * (j + 1)],
                                      SM[:, PP_HM8 + j:PP_HM8 + j + 1], SM[:, SM_HALO:SM_HALO + 48],
                                      ALU.mult, ALU.add)
    P.tensor.wait_ge(C.cV, C.cV.val)
    P.tensor.matmul(C.PS[0][0:1, 0:48], C.ones_col_f32[:, :], SM[:, SM_HALO:SM_HALO + 48], start=True, stop=True)
    P.mark_last('tensor', C.cT)
    P.vector.wait_ge(C.cT, C.cT.val)
    SQH = C.XT2[:, 1408:1456]
    P.vector.tensor_tensor(SQH, SM[:, SM_HALO:SM_HALO + 48], SM[:, SM_HALO:SM_HALO + 48], ALU.mult)
    P.tensor.wait_ge(C.cV, C.cV.val)
    P.tensor.matmul(C.PS[1][0:1, 0:48], C.ones_col_f32[:, :], SQH, start=True, stop=True)
    P.mark_last('tensor', C.cT)
    P.vector.wait_ge(C.cT, C.cT.val)
    HS = C.XT2[0:1, 1456:1459]
    HQ = C.XT2[0:1, 1459:1462]
    P.vector.tensor_reduce(HS, C.PS[0][0:1, 0:48].rearrange("p (k t) -> p t k", k=16), axis=AXX, op=ALU.add)
    P.vector.tensor_reduce(HQ, C.PS[1][0:1, 0:48].rearrange("p (k t) -> p t k", k=16), axis=AXX, op=ALU.add)
    HM_ = C.XT2[0:1, 1462:1465]
    HR = C.XT2[0:1, 1465:1468]
    P.vector.tensor_scalar(HM_, HS, 1.0 / D, None, ALU.mult)
    P.vector.tensor_scalar(HQ, HQ, 1.0 / D, None, ALU.mult)
    P.vector.tensor_tensor(HS, HM_, HM_, ALU.mult)
    P.vector.tensor_tensor(HQ, HQ, HS, ALU.subtract)
    P.vector.tensor_scalar(HS, HQ, EPS, None, ALU.add)
    P.vector.reciprocal(HS, HS)
    P.vector.tensor_tensor(HQ, HQ, HS, ALU.mult)
    P.vector.tensor_scalar(HQ, HQ, EPS, None, ALU.add)
    P.scalar.wait_ge(C.cV, C.cV.val)
    P.scalar.activation(HQ, HQ, AF.Sqrt)
    P.scalar.activation(HR, HS, AF.Sqrt)
    P.vector.wait_ge(C.cA, C.cA.val)
    P.vector.reciprocal(HQ, HQ)
    P.vector.tensor_tensor(HR, HR, HQ, ALU.mult)
    P.tensor.wait_ge(C.cV, C.cV.val)
    P.tensor.matmul(C.PS[2][:, 0:3], C.ones_row_f32[:, :], HM_, start=True, stop=True)
    P.tensor.matmul(C.PS[3][:, 0:3], C.ones_row_f32[:, :], HR, start=True, stop=True)
    P.mark_last('tensor', C.cT)
    P.vector.wait_ge(C.cT, C.cT.val)
    HBTb = SM[:, SM_HBT:SM_HBT + 24].bitcast(BF16)
    for k in range(KC):
        hk = SM[:, SM_HALO + 3 * k:SM_HALO + 3 * (k + 1)]
        P.vector.tensor_tensor(hk, hk, C.PS[2][:, 0:3], ALU.subtract)
        P.vector.tensor_tensor(hk, hk, C.PS[3][:, 0:3], ALU.mult)
        P.vector.tensor_scalar(HBTb[:, 3 * k:3 * (k + 1)], hk, SM[:, PP_LN1W + k:PP_LN1W + k + 1],
                               SM[:, PP_LN1B + k:PP_LN1B + k + 1], ALU.mult, ALU.add)
    for k in range(KC):
        P.vector.tensor_copy(C.XB[:, k, 1:4], HBTb[:, 3 * k:3 * (k + 1)])
    P.vector.memset(C.XB[:, 0, 0:1], 0.0)

    for k in range(KC):
        acc = T0
        if k > 0:
            P.vector.wait_ge(C.cA, C.cA.val)
        P.vector.tensor_scalar(acc[:, :], C.XB[:, k, 1:1025], SM[:, PP_CONVW + 4 * k:PP_CONVW + 4 * k + 1],
                               None, ALU.mult)
        for j in (1, 2):
            P.vector.scalar_tensor_tensor(acc[:, :], C.XB[:, k, 1 + j:1025 + j],
                                          SM[:, PP_CONVW + 4 * k + j:PP_CONVW + 4 * k + j + 1], acc[:, :],
                                          ALU.mult, ALU.add)
        P.vector.scalar_tensor_tensor(acc[:, :], C.xb(k), SM[:, PP_CONVW + 4 * k + 3:PP_CONVW + 4 * k + 4],
                                      acc[:, :], ALU.mult, ALU.add)
        P.vector.tensor_scalar(acc[:, :], acc[:, :], SM[:, PP_CONVB + k:PP_CONVB + k + 1], None, ALU.add)
        P.scalar.wait_ge(C.cV, C.cV.val)
        P.scalar.activation(xcb[:, k, :], acc[:, :], AF.Silu)
    conv_done = C.cA.val

    P.tensor.wait_ge(C.cA, conv_done)
    P.tensor.wait_ge(C.qG, C.qG.val)
    for n in range(2):
        for k in range(KC):
            P.tensor.matmul(C.PS[4 + n][0:16, :], C.WG[:, k, :], xcb[:, k, 512 * n:512 * (n + 1)],
                            start=(k == 0), stop=(k == KC - 1))
    P.mark_last('tensor', C.cT)
    P.scalar.wait_ge(C.cT, C.cT.val)
    P.scalar.activation(GAMS[:, 0:512], C.PS[4][0:16, :], AF.Sigmoid, bias=SM[0:16, PP_BG:PP_BG + 1], scale=1.0)
    P.scalar.activation(GAMS[:, 512:1024], C.PS[5][0:16, :], AF.Sigmoid, bias=SM[0:16, PP_BG:PP_BG + 1], scale=1.0)
    P.gpsimd.wait_ge(C.cA, C.cA.val)
    P.gpsimd.dma_start(out=aux["gam"].ap(), in_=GAMS).then_inc(C.qG, 16)
    ONES16 = C.ones_col_f32[0:16, 0:1].to_broadcast([16, 512])
    P.vector.wait_ge(C.cA, C.cA.val)
    P.vector.tensor_tensor_scan(C.PS[6][0:16, :], GAMS[:, 0:512], ONES16, 1.0, ALU.mult, ALU.mult)
    P.vector.tensor_tensor_scan(C.PS[7][0:16, :], GAMS[:, 512:1024], ONES16, C.PS[6][0:16, 511:512], ALU.mult, ALU.mult)
    P.vector.wait_ge(C.qG, C.qG.val)
    P.vector.wait_ge(C.cP, C.cP.val)
    P.vector.tensor_copy(GAMS[:, 0:512], C.PS[6][0:16, :])
    P.vector.tensor_copy(GAMS[:, 512:1024], C.PS[7][0:16, :])
    P.gpsimd.wait_ge(C.cV, C.cV.val)
    P.gpsimd.dma_start(out=aux["gamc"].ap(), in_=GAMS).then_inc(C.qG, 16)
    P.gpsimd.dma_start(out=aux["abounce"].ap(), in_=GAMS[:, 1023:1024]).then_inc(C.qG, 16)

    def phead_norm(ps_a, ps_b, raw_tile, mmv, l2, w_ap, b_ap, out_tile, from_psum=True, war_cv=None,
                   war_stats_cv=None, war_stats_ca=None, stb_off=0, sq_tile=None):
        STBa = C.STB[0:1, stb_off:stb_off + 512]
        STBb = C.STB[0:1, stb_off + 512:stb_off + 1024]
        sq = C.MB if sq_tile is None else sq_tile
        if from_psum:
            P.scalar.wait_ge(C.cT, mmv)
            P.scalar.wait_ge(C.cV, C.cV.val if war_cv is None else war_cv)
            P.scalar.activation(raw_tile[:, 0:512], ps_a[:, :], AF.Identity)
            P.scalar.activation(raw_tile[:, 512:1024], ps_b[:, :], AF.Identity)
        if not l2:
            P.tensor.wait_ge(C.cA, C.cA.val if war_stats_ca is None else war_stats_ca)
            P.tensor.wait_ge(C.cV, C.cV.val if war_stats_cv is None else war_stats_cv)
            P.tensor.matmul(ps_a[0:1, :], C.invd_col_bf[:, :], raw_tile[:, 0:512], start=True, stop=True)
            P.tensor.matmul(ps_b[0:1, :], C.invd_col_bf[:, :], raw_tile[:, 512:1024], start=True, stop=True)
            P.mark_last('tensor', C.cT)
            P.scalar.wait_ge(C.cT, C.cT.val)
            P.scalar.activation(STBa, ps_a[0:1, :], AF.Identity)
            P.scalar.activation(STBb, ps_b[0:1, :], AF.Identity)
            P.tensor.wait_ge(C.cA, C.cA.val)
            P.tensor.matmul(ps_a[:, :], C.ones_row_bf[:, :], STBa, start=True, stop=True)
            P.tensor.matmul(ps_b[:, :], C.ones_row_bf[:, :], STBb, start=True, stop=True)
            P.mark_last('tensor', C.cT)
            P.vector.wait_ge(C.cT, C.cT.val)
            P.vector.tensor_tensor(raw_tile[:, 0:512], raw_tile[:, 0:512], ps_a[:, :], ALU.subtract)
            P.vector.tensor_tensor(raw_tile[:, 512:1024], raw_tile[:, 512:1024], ps_b[:, :], ALU.subtract)
        P.scalar.wait_ge(C.cV, C.cV.val)
        P.scalar.wait_ge(C.cT, C.cT.val)
        P.scalar.activation(sq[:, :], raw_tile[:, :], AF.Square)
        if l2:
            P.tensor.wait_ge(C.cV, C.cV.val if war_stats_cv is None else war_stats_cv)
        P.tensor.wait_ge(C.cA, C.cA.val)
        P.tensor.matmul(ps_a[0:1, :], C.ones_col_bf[:, :], sq[:, 0:512], start=True, stop=True)
        P.tensor.matmul(ps_b[0:1, :], C.ones_col_bf[:, :], sq[:, 512:1024], start=True, stop=True)
        P.mark_last('tensor', C.cT)
        epsap = C.eps_l2[0:1, 0:1] if l2 else C.eps_ln[0:1, 0:1]
        P.scalar.wait_ge(C.cT, C.cT.val)
        P.scalar.activation(ps_a[0:1, :], ps_a[0:1, :], AF.Ln, bias=epsap, scale=1.0)
        P.scalar.activation(ps_b[0:1, :], ps_b[0:1, :], AF.Ln, bias=epsap, scale=1.0)
        P.scalar.activation(STBa, ps_a[0:1, :], AF.Exp, bias=0.0, scale=-0.5)
        P.scalar.activation(STBb, ps_b[0:1, :], AF.Exp, bias=0.0, scale=-0.5)
        row = C.ones_row_bf if l2 else C.sqrthd_row_bf
        P.tensor.wait_ge(C.cA, C.cA.val)
        P.tensor.matmul(ps_a[:, :], row[:, :], STBa, start=True, stop=True)
        P.tensor.matmul(ps_b[:, :], row[:, :], STBb, start=True, stop=True)
        P.mark_last('tensor', C.cT)
        P.vector.wait_ge(C.cT, C.cT.val)
        P.vector.tensor_tensor(raw_tile[:, 0:512], raw_tile[:, 0:512], ps_a[:, :], ALU.mult)
        P.vector.tensor_tensor(raw_tile[:, 512:1024], raw_tile[:, 512:1024], ps_b[:, :], ALU.mult)
        if w_ap is not None:
            P.vector.tensor_scalar(out_tile[:, :], raw_tile[:, :], w_ap, b_ap, ALU.mult, ALU.add)
        elif out_tile is not raw_tile:
            P.vector.tensor_copy(out_tile[:, :], raw_tile[:, :])

    q_mm = {}
    k_mm = {}
    v_mm = {}
    i_mm = {}
    qn_done = {}
    kn_done = {}
    vn_done = {}
    ig_ev = {}
    gb_ev = {}
    u_done = {}
    scan_done = {}
    GBT = C.XT2[:, 0:1024]
    for h in range(NH):
        if h == 0:
            P.gpsimd.wait_ge(C.cT, C.cT.val)
            P.gpsimd.wait_ge(C.qC[0], C.qC[0].val)
            P.gpsimd.wait_ge(C.qC[1], C.qC[1].val)
        P.gpsimd.wait_ge(C.cT, q_mm.get(h - 1, 0))
        wslab(WUB0q, aux["Wq"], h, C.qW[0])
        ld_q = C.qW[0].val
        P.gpsimd.wait_ge(C.cT, k_mm.get(h - 1, 0))
        wslab(WUB1q, aux["Wk"], h, C.qW[1])
        ld_k = C.qW[1].val
        P.gpsimd.wait_ge(C.cT, v_mm.get(h - 1, 0))
        wslab(WQFvq, aux["Wv"], h, C.qB[0])
        ld_v = C.qB[0].val
        P.gpsimd.wait_ge(C.cT, i_mm.get(h - 1, 0))
        wslab(SCFvq, aux["Wig"], h, C.qB[1])
        ld_i = C.qB[1].val
        P.tensor.wait_ge(C.qB[1], ld_i)
        if h == 0:
            P.tensor.wait_ge(C.cV, C.cV.val)
        else:
            P.tensor.wait_ge(C.cA, gb_ev[h - 1])
        for k in range(KC):
            P.tensor.matmul(C.PS[6][:, :], itile(k), C.xb(k, 0, 512), start=(k == 0), stop=(k == KC - 1))
            P.tensor.matmul(C.PS[7][:, :], itile(k), C.xb(k, 512, 1024), start=(k == 0), stop=(k == KC - 1))
        P.mark_last('tensor', C.cT)
        i_mm[h] = C.cT.val
        P.tensor.wait_ge(C.qW[0], ld_q)
        if h > 0:
            P.tensor.wait_ge(C.cV, qn_done[h - 1])
        for k in range(KC):
            P.tensor.matmul(C.PS[0][:, :], C.WUB[0][:, k, :], C.xb(k, 0, 512), start=(k == 0), stop=(k == KC - 1))
            P.tensor.matmul(C.PS[1][:, :], C.WUB[0][:, k, :], C.xb(k, 512, 1024), start=(k == 0), stop=(k == KC - 1))
        P.mark_last('tensor', C.cT)
        q_mm[h] = C.cT.val
        P.tensor.wait_ge(C.qW[1], ld_k)
        if h > 0:
            P.tensor.wait_ge(C.cV, kn_done[h - 1])
        for k in range(KC):
            P.tensor.matmul(C.PS[2][:, :], C.WUB[1][:, k, :], C.xb(k, 0, 512), start=(k == 0), stop=(k == KC - 1))
            P.tensor.matmul(C.PS[3][:, :], C.WUB[1][:, k, :], C.xb(k, 512, 1024), start=(k == 0), stop=(k == KC - 1))
        P.mark_last('tensor', C.cT)
        k_mm[h] = C.cT.val
        P.tensor.wait_ge(C.qB[0], ld_v)
        if h > 0:
            P.tensor.wait_ge(C.cV, vn_done[h - 1])
        for k in range(KC):
            P.tensor.matmul(C.PS[4][:, :], vtile(k), C.xb(k, 0, 512), start=(k == 0), stop=(k == KC - 1))
            P.tensor.matmul(C.PS[5][:, :], vtile(k), C.xb(k, 512, 1024), start=(k == 0), stop=(k == KC - 1))
        P.mark_last('tensor', C.cT)
        v_mm[h] = C.cT.val

        P.scalar.wait_ge(C.cT, i_mm[h])
        if h > 0:
            P.scalar.wait_ge(C.cV, u_done[h - 1])
        P.scalar.activation(S0[:, 0:512], C.PS[6][:, :], AF.Sigmoid, bias=SM[:, PP_BIG + h:PP_BIG + h + 1], scale=1.0)
        P.scalar.activation(S0[:, 512:1024], C.PS[7][:, :], AF.Sigmoid, bias=SM[:, PP_BIG + h:PP_BIG + h + 1], scale=1.0)
        ig_ev[h] = C.cA.val
        P.gpsimd.wait_ge(C.cA, ig_ev[h])
        P.gpsimd.wait_ge(C.cT, C.cT.val)
        P.gpsimd.wait_ge(C.qG, C.qG.val)
        P.gpsimd.wait_ge(C.cV, C.cV.val)
        P.gpsimd.dma_start(out=C.XT2[0:1, 0:1024], in_=aux["gam"].ap()[h:h + 1, :]).then_inc(C.qR[0], 16)
        P.tensor.wait_ge(C.qR[0], C.qR[0].val)
        P.tensor.matmul(C.PS[6][:, :], C.ones_row_f32[:, :], C.XT2[0:1, 0:512], start=True, stop=True)
        P.tensor.matmul(C.PS[7][:, :], C.ones_row_f32[:, :], C.XT2[0:1, 512:1024], start=True, stop=True)
        P.mark_last('tensor', C.cT)
        P.scalar.wait_ge(C.cT, C.cT.val)
        P.scalar.activation(GBT[:, 0:512], C.PS[6][:, :], AF.Identity)
        P.scalar.activation(GBT[:, 512:1024], C.PS[7][:, :], AF.Identity)
        gb_ev[h] = C.cA.val

        phead_norm(C.PS[0], C.PS[1], QB[:, h, :], q_mm[h], True, None, None, QB[:, h, :], war_cv=0)
        qn_done[h] = C.cV.val
        phead_norm(C.PS[2], C.PS[3], T0, k_mm[h], True, None, None, T0,
                   war_cv=u_done.get(h - 1, 0))
        kn_done[h] = C.cV.val
        phead_norm(C.PS[4], C.PS[5], T1, v_mm[h], False,
                   SM[:, PP_VNW:PP_VNW + 1], SM[:, PP_VNB:PP_VNB + 1], T1,
                   war_cv=scan_done.get(h - 1, 0))
        vn_done[h] = C.cV.val
        P.vector.wait_ge(C.cA, ig_ev[h])
        P.vector.tensor_tensor(T1[:, :], T1[:, :], T0[:, :], ALU.mult)
        P.vector.tensor_tensor(T1[:, :], T1[:, :], S0[:, :], ALU.mult)
        u_done[h] = C.cV.val
        P.vector.wait_ge(C.cA, gb_ev[h])
        P.vector.tensor_tensor_scan(MEM[:, h, 0:512], GBT[:, 0:512], T1[:, 0:512], 0.0, ALU.mult, ALU.add)
        P.vector.tensor_tensor_scan(MEM[:, h, 512:1024], GBT[:, 512:1024], T1[:, 512:1024],
                                    MEM[:, h, 511:512], ALU.mult, ALU.add)
        scan_done[h] = C.cV.val

    P.vector.tensor_copy(S0[:, 0:16], MEM[:, :, 1023:1024].rearrange("p a b -> p (a b)"))
    P.gpsimd.wait_ge(C.cV, C.cV.val)
    P.gpsimd.dma_start(out=aux["sbounce"].ap(), in_=S0[:, 0:16]).then_inc(C.qG, 16)
    P.gpsimd.wait_ge(C.qG, C.qG.val)
    P.gpsimd.collective_compute(
        "AllGather", ALU.bypass, replica_groups=[list(range(NCORES))],
        ins=[aux["sbounce"].ap().opt()], outs=[aux["sa2_all"].ap().opt()],
    ).then_inc(C.cc, 1)
    P.gpsimd.collective_compute(
        "AllGather", ALU.bypass, replica_groups=[list(range(NCORES))],
        ins=[aux["abounce"].ap().opt()], outs=[aux["aa_all"].ap().opt()],
    ).then_inc(C.cc, 1)
    P.gpsimd.wait_ge(C.cc, C.cc.val)
    P.gpsimd.dma_start(out=C.XT2[0:1, 0:128], in_=aux["aa_all"].ap().rearrange("a b -> (a b)").unsqueeze(0)).then_inc(C.qG, 16)
    P.gpsimd.dma_start(out=C.XT2[0:1, 128:256], in_=aux["cmrow"][0:1, :]).then_inc(C.qG, 16)
    P.gpsimd.dma_start(out=C.XT2[0:1, 256:384], in_=aux["cmrow"][1:2, :]).then_inc(C.qG, 16)
    P.vector.wait_ge(C.qG, C.qG.val)
    P.vector.tensor_tensor(C.XT2[0:1, 0:128], C.XT2[0:1, 0:128], C.XT2[0:1, 128:256], ALU.mult)
    P.vector.tensor_tensor(C.XT2[0:1, 0:128], C.XT2[0:1, 0:128], C.XT2[0:1, 256:384], ALU.add)
    P.vector.tensor_copy(C.STB[0:1, 0:128], C.XT2[0:1, 0:128])
    P.tensor.wait_ge(C.cV, C.cV.val)
    P.tensor.matmul(C.PS[0][:, 0:128], C.ones_row_bf[:, :], C.STB[0:1, 0:128], start=True, stop=True)
    P.mark_last('tensor', C.cT)
    P.vector.wait_ge(C.cT, C.cT.val)
    P.vector.memset(SM[:, SM_CARRY:SM_CARRY + 16], 0.0)
    for j in range(NCORES):
        P.gpsimd.wait_ge(C.cV, C.cV.val)
        P.gpsimd.dma_start(out=SM[:, SM_SD:SM_SD + 16], in_=aux["sa2_all"].ap()[128 * j:128 * (j + 1), :]).then_inc(C.qR[1], 16)
        P.vector.wait_ge(C.qR[1], C.qR[1].val)
        P.vector.tensor_scalar(SM[:, SM_SD:SM_SD + 16], SM[:, SM_SD:SM_SD + 16],
                               SM[:, PP_CMS + j:PP_CMS + j + 1], None, ALU.mult)
        P.vector.tensor_tensor(SM[:, SM_CARRY:SM_CARRY + 16], SM[:, SM_CARRY:SM_CARRY + 16],
                               C.PS[0][:, 16 * j:16 * (j + 1)], ALU.mult)
        P.vector.tensor_tensor(SM[:, SM_CARRY:SM_CARRY + 16], SM[:, SM_CARRY:SM_CARRY + 16],
                               SM[:, SM_SD:SM_SD + 16], ALU.add)

    og_mm = {}
    og_sig = {}
    outb_done = {}
    fix_done = {}
    cv0 = C.cV.val
    ca0 = C.cA.val
    pair_cv = {0: cv0, 1: cv0}
    pair_ca = {0: ca0, 1: ca0}
    for h in range(NH):
        p = h % 2
        sa, sb = (C.PS[0], C.PS[1]) if p == 0 else (C.PS[4], C.PS[5])
        P.gpsimd.wait_ge(C.cT, C.cT.val)
        P.gpsimd.wait_ge(C.qG, C.qG.val)
        P.gpsimd.dma_start(out=C.XT2[0:1, 0:1024], in_=aux["gamc"].ap()[h:h + 1, :]).then_inc(C.qR[0], 16)
        P.tensor.wait_ge(C.qR[0], C.qR[0].val)
        P.tensor.wait_ge(C.cV, fix_done.get(h - 1, cv0))
        P.tensor.matmul(C.PS[6][:, :], C.ones_row_f32[:, :], C.XT2[0:1, 0:512], start=True, stop=True)
        P.tensor.matmul(C.PS[7][:, :], C.ones_row_f32[:, :], C.XT2[0:1, 512:1024], start=True, stop=True)
        P.mark_last('tensor', C.cT)
        gamc_mm = C.cT.val
        P.gpsimd.wait_ge(C.cT, og_mm.get(h - 1, 0))
        wslab(WUB0q, aux["Wog"], h, C.qW[0])
        P.tensor.wait_ge(C.qW[0], C.qW[0].val)
        P.tensor.wait_ge(C.cA, og_sig.get(h - 1, ca0))
        if h == 0:
            P.tensor.wait_ge(C.cV, cv0)
        for k in range(KC):
            P.tensor.matmul(C.PS[2][:, :], C.WUB[0][:, k, :], xcb[:, k, 0:512], start=(k == 0), stop=(k == KC - 1))
            P.tensor.matmul(C.PS[3][:, :], C.WUB[0][:, k, :], xcb[:, k, 512:1024], start=(k == 0), stop=(k == KC - 1))
        P.mark_last('tensor', C.cT)
        og_mm[h] = C.cT.val
        memv = MEM[:, h, :]
        P.vector.wait_ge(C.cT, gamc_mm)
        P.vector.scalar_tensor_tensor(memv[:, 0:512], C.PS[6][:, :], SM[:, SM_CARRY + h:SM_CARRY + h + 1],
                                      memv[:, 0:512], ALU.mult, ALU.add)
        P.vector.scalar_tensor_tensor(memv[:, 512:1024], C.PS[7][:, :], SM[:, SM_CARRY + h:SM_CARRY + h + 1],
                                      memv[:, 512:1024], ALU.mult, ALU.add)
        fix_done[h] = C.cV.val
        phead_norm(sa, sb, memv, 0, False,
                   SM[:, PP_MNW:PP_MNW + 1], SM[:, PP_MNB:PP_MNB + 1], T0, from_psum=False,
                   war_stats_cv=max(fix_done[h], pair_cv[p]), war_stats_ca=pair_ca[p], stb_off=1024 * p)
        P.vector.tensor_tensor(T0[:, :], T0[:, :], QB[:, h, :], ALU.mult)
        phead_norm(sa, sb, T0, 0, False,
                   SM[:, PP_GNW + h:PP_GNW + h + 1], SM[:, PP_GNB + h:PP_GNB + h + 1], T0,
                   from_psum=False, stb_off=1024 * p)
        pair_cv[p] = C.cV.val
        pair_ca[p] = C.cA.val
        P.scalar.wait_ge(C.cT, og_mm[h])
        P.scalar.wait_ge(C.cV, outb_done.get(h - 1, cv0))
        P.scalar.activation(S0[:, 0:512], C.PS[2][:, :], AF.Sigmoid, bias=SM[:, PP_BOG + h:PP_BOG + h + 1], scale=1.0)
        P.scalar.activation(S0[:, 512:1024], C.PS[3][:, :], AF.Sigmoid, bias=SM[:, PP_BOG + h:PP_BOG + h + 1], scale=1.0)
        og_sig[h] = C.cA.val
        P.vector.wait_ge(C.cA, og_sig[h])
        P.vector.tensor_tensor(OUTB[:, h, :], T0[:, :], S0[:, :], ALU.mult)
        outb_done[h] = C.cV.val
    p2_done = C.cV.val

    M0B = C.AR0[:, 0:1024]
    R0B = C.AR0[:, 1024:2048]
    Q0B = C.AR0[:, 2048:3072]
    P.tensor.wait_ge(C.cV, p2_done)
    P.tensor.matmul(C.PS[4][:, :], C.ones_row_bf[:, :], C.SM0[:, 0:512], start=True, stop=True)
    P.tensor.matmul(C.PS[5][:, :], C.ones_row_bf[:, :], C.SM0[:, 512:1024], start=True, stop=True)
    P.tensor.matmul(C.PS[6][:, :], C.ones_row_bf[:, :], C.SR0[:, 0:512], start=True, stop=True)
    P.tensor.matmul(C.PS[7][:, :], C.ones_row_bf[:, :], C.SR0[:, 512:1024], start=True, stop=True)
    P.mark_last('tensor', C.cT)
    P.vector.wait_ge(C.cT, C.cT.val)
    P.vector.tensor_copy(M0B[:, 0:512], C.PS[4][:, :])
    P.vector.tensor_copy(M0B[:, 512:1024], C.PS[5][:, :])
    P.vector.tensor_copy(R0B[:, 0:512], C.PS[6][:, :])
    P.vector.tensor_copy(R0B[:, 512:1024], C.PS[7][:, :])
    P.vector.tensor_tensor(Q0B, M0B, R0B, ALU.mult)
    wo_mm = {}
    ep = {}
    outw = {}
    for m in range(KC):
        par = m % 2
        P.gpsimd.wait_ge(C.cT, wo_mm.get(m - 2, C.cT.val if m < 2 else 0))
        wslab(WUB0q if par == 0 else WUB1q, aux["Wo"], m, C.qW[par])
        pd, pd2 = C.PS[0 + 2 * par], C.PS[1 + 2 * par]
        P.tensor.wait_ge(C.qW[par], C.qW[par].val)
        if m < 2:
            P.tensor.wait_ge(C.cV, C.cV.val)
        else:
            P.tensor.wait_ge(C.cV, ep.get(m - 2, 0))
        for k in range(NH):
            P.tensor.matmul(pd[:, :], (C.WUB[0] if par == 0 else C.WUB[1])[:, k, :], OUTB[:, k, 0:512],
                            start=(k == 0), stop=(k == NH - 1))
            P.tensor.matmul(pd2[:, :], (C.WUB[0] if par == 0 else C.WUB[1])[:, k, :], OUTB[:, k, 512:1024],
                            start=(k == 0), stop=(k == NH - 1))
        P.mark_last('tensor', C.cT)
        wo_mm[m] = C.cT.val
        buf = C.XT[par]
        if m >= 2:
            P.sync.wait_ge(C.cV, ep[m - 2])
            P.sync.wait_ge(C.qD[par], outw[m - 2])
        else:
            P.sync.wait_ge(C.cV, C.cV.val)
        P.sync.dma_start(out=buf, in_=src_dram[m]).then_inc(C.qE[par], 16)
        P.vector.wait_ge(C.qE[par], C.qE[par].val)
        P.vector.wait_ge(C.cT, C.cT.val)
        P.vector.tensor_tensor(buf, buf, R0B, ALU.mult)
        P.vector.tensor_tensor(buf, buf, Q0B, ALU.subtract)
        P.vector.tensor_tensor(buf[:, 0:512], buf[:, 0:512], pd[:, :], ALU.add)
        P.vector.tensor_tensor(buf[:, 512:1024], buf[:, 512:1024], pd2[:, :], ALU.add)
        ep[m] = C.cV.val
        P.gpsimd.wait_ge(C.cV, C.cV.val)
        P.gpsimd.dma_start(out=hm_dram[m], in_=buf).then_inc(C.qD[par], 16)
        outw[m] = C.qD[par].val

    emit_ln(kb, C, hm_dram, fused_w=SM[:, PP_LN2W:PP_LN2W + 16], fused_b=SM[:, PP_LN2B:PP_LN2B + 16])
    g_mm = {}
    ev2 = {}
    for m in range(FC):
        par = m % 2
        P.gpsimd.wait_ge(C.cT, g_mm.get(m - 1, C.cT.val if m < 1 else 0))
        wslab(WUB0q, aux["Wgate"], m, C.qW[0])
        wslab(WUB1q, aux["Wval"], m, C.qW[1])
        pg, pg2 = C.PS[0 + 2 * par], C.PS[1 + 2 * par]
        pv, pv2 = C.PS[4 + 2 * par], C.PS[5 + 2 * par]
        P.tensor.wait_ge(C.qW[0], C.qW[0].val)
        P.tensor.wait_ge(C.qW[1], C.qW[1].val)
        if m < 2:
            P.tensor.wait_ge(C.cV, C.cV.val)
        else:
            P.tensor.wait_ge(C.cV, ev2[m - 2])
        for k in range(KC):
            P.tensor.matmul(pg[:, :], C.WUB[0][:, k, :], C.xb(k, 0, 512), start=(k == 0), stop=(k == KC - 1))
            P.tensor.matmul(pg2[:, :], C.WUB[0][:, k, :], C.xb(k, 512, 1024), start=(k == 0), stop=(k == KC - 1))
            P.tensor.matmul(pv[:, :], C.WUB[1][:, k, :], C.xb(k, 0, 512), start=(k == 0), stop=(k == KC - 1))
            P.tensor.matmul(pv2[:, :], C.WUB[1][:, k, :], C.xb(k, 512, 1024), start=(k == 0), stop=(k == KC - 1))
        P.mark_last('tensor', C.cT)
        g_mm[m] = C.cT.val
        P.scalar.wait_ge(C.cT, C.cT.val)
        if m > 0:
            P.scalar.wait_ge(C.cV, ev2[m - 1])
        P.scalar.activation(S0[:, 0:512], pg[:, :], AF.Silu)
        P.scalar.activation(S0[:, 512:1024], pg2[:, :], AF.Silu)
        P.vector.wait_ge(C.cA, C.cA.val)
        P.vector.tensor_tensor(C.XH[:, m, 0:512], S0[:, 0:512], pv[:, :], ALU.mult)
        P.vector.tensor_tensor(C.XH[:, m, 512:1024], S0[:, 512:1024], pv2[:, :], ALU.mult)
        ev2[m] = C.cV.val
    mlp_up_done = C.cV.val

    wom = {}
    epo = {}
    outd = {}
    for d in range(KC):
        pd, pd2 = C.PS[0 + 2 * (d % 2)], C.PS[1 + 2 * (d % 2)]
        for r4 in range(4):
            par = r4 % 2
            step = 4 * d + r4
            P.gpsimd.wait_ge(C.cT, wom.get(step - 2, C.cT.val if step < 2 else 0))
            src_ap = aux["Wout"][2048 * r4:2048 * (r4 + 1), 128 * d:128 * (d + 1)]
            for q in range(4):
                P.gpsimd.dma_start(
                    out=C.WUB[par][:, 4 * q:4 * (q + 1), :],
                    in_=src_ap[512 * q:512 * (q + 1), :].rearrange("(kc p) n -> p kc n", p=128),
                ).then_inc(C.qW[par], 16)
            P.tensor.wait_ge(C.qW[par], C.qW[par].val)
            if step < 2:
                P.tensor.wait_ge(C.cV, mlp_up_done)
            if r4 == 0 and d >= 2:
                P.tensor.wait_ge(C.cV, epo[d - 2])
            for kk in range(KC):
                h = KC * r4 + kk
                st_ = (r4 == 0 and kk == 0)
                sp_ = (r4 == 3 and kk == KC - 1)
                P.tensor.matmul(pd[:, :], C.WUB[par][:, kk, :], C.XH[:, h, 0:512], start=st_, stop=sp_)
                P.tensor.matmul(pd2[:, :], C.WUB[par][:, kk, :], C.XH[:, h, 512:1024], start=st_, stop=sp_)
            P.mark_last('tensor', C.cT)
            wom[step] = C.cT.val
        if d >= 1:
            P.sync.wait_ge(C.cV, epo[d - 1])
            P.sync.wait_ge(C.qD[0], outd[d - 1])
        else:
            P.sync.wait_ge(C.cV, C.cV.val)
        P.sync.dma_start(out=C.XT[0], in_=src_dram[d]).then_inc(C.qE[0], 16)
        P.sync.dma_start(out=C.XT[1], in_=hm_dram[d]).then_inc(C.qE[1], 16)
        P.vector.wait_ge(C.qE[0], C.qE[0].val)
        P.vector.wait_ge(C.qE[1], C.qE[1].val)
        P.vector.wait_ge(C.cT, C.cT.val)
        P.vector.tensor_tensor(C.XT[0], C.XT[0], C.XT[1], ALU.add)
        P.vector.tensor_tensor(C.XT[0][:, 0:512], C.XT[0][:, 0:512], pd[:, :], ALU.add)
        P.vector.tensor_tensor(C.XT[0][:, 512:1024], C.XT[0][:, 512:1024], pd2[:, :], ALU.add)
        epo[d] = C.cV.val
        P.gpsimd.wait_ge(C.cV, C.cV.val)
        P.gpsimd.dma_start(out=dst_dram[d], in_=C.XT[0]).then_inc(C.qD[0], 16)
        outd[d] = C.qD[0].val


def prep_attnb_pack(inp, core):
    pp = np.zeros((128, PPN), dtype=np.float32)
    def col(c):
        return c - PP
    cw = inp["L0_attn_b_conv_w"].reshape(D, 4)
    for k in range(KC):
        for j in range(4):
            pp[:, col(PP_CONVW) + 4 * k + j] = cw[128 * k:128 * (k + 1), j]
        pp[:, col(PP_CONVB) + k] = inp["L0_attn_b_conv_b"][128 * k:128 * (k + 1)]
        pp[:, col(PP_LN1W) + k] = inp["L0_attn_b_ln1w"][128 * k:128 * (k + 1)]
        pp[:, col(PP_LN1B) + k] = inp["L0_attn_b_ln1b"][128 * k:128 * (k + 1)]
        pp[:, col(PP_LN2W) + k] = inp["L0_attn_b_ln2w"][128 * k:128 * (k + 1)]
        pp[:, col(PP_LN2B) + k] = inp["L0_attn_b_ln2b"][128 * k:128 * (k + 1)]
    pp[:, col(PP_VNW)] = inp["L0_attn_b_vnw"]
    pp[:, col(PP_VNB)] = inp["L0_attn_b_vnb"]
    pp[:, col(PP_MNW)] = inp["L0_attn_b_mnw"]
    pp[:, col(PP_MNB)] = inp["L0_attn_b_mnb"]
    for h in range(NH):
        pp[:, col(PP_GNW) + h] = inp["L0_attn_b_gnw"][128 * h:128 * (h + 1)]
        pp[:, col(PP_GNB) + h] = inp["L0_attn_b_gnb"][128 * h:128 * (h + 1)]
        pp[:, col(PP_BIG) + h] = inp["L0_attn_b_big"][128 * h:128 * (h + 1)]
        pp[:, col(PP_BOG) + h] = inp["L0_attn_b_bog"][128 * h:128 * (h + 1)]
    pp[0:16, col(PP_BG)] = inp["L0_attn_b_bg"]
    for j in range(NCORES):
        pp[:, col(PP_CMS) + j] = 1.0 if j < core else 0.0
        pp[:, col(PP_HM8) + j] = 1.0 if j == core - 1 else 0.0
    cmrow = np.zeros((2, 128), dtype=np.float32)
    for j in range(NCORES):
        for h in range(NH):
            cmrow[0, 16 * j + h] = 1.0 if j < core else 0.0
    cmrow[1] = 1.0 - cmrow[0]
    return np.ascontiguousarray(pp), cmrow


def declare_attnb_aux(kb, with_inputs=True):
    aux = {}
    for nm, shp in [("Wq", [D, D]), ("Wk", [D, D]), ("Wv", [D, D]), ("Wig", [D, D]),
                    ("Wog", [D, D]), ("Wo", [D, D]), ("Wg", [D, NH]),
                    ("Wgate", [D, FF]), ("Wval", [D, FF]), ("Wout", [FF, D])]:
        aux[nm] = kb.din("b_" + nm, shp)
    aux["parpack"] = kb.din("parpack", [128, PPN])
    aux["cmrow"] = kb.din("cmrow", [2, 128])
    aux["gam"] = kb.dscratch("gam", [NH, T])
    aux["gamc"] = kb.dscratch("gamc", [NH, T])
    aux["halo_in"] = kb.dscratch("halo_in", [128, KC, 3])
    aux["halo_all"] = kb.dscratch("halo_all", [NCORES, 128, KC, 3])
    aux["sbounce"] = kb.dscratch("sbounce", [128, NH])
    aux["sa2_all"] = kb.dscratch("sa2_all", [NCORES * 128, NH])
    aux["abounce"] = kb.dscratch("abounce", [NH, 1])
    aux["aa_all"] = kb.dscratch("aa_all", [NCORES * NH, 1])
    return aux


A_XNB, A_TT, A_AOUT, A_COUT = 0, 1024, 2048, 3072
A_DOUT, A_D2, A_EOUT, A_E2 = 4096, 5120, 6144, 7168
A_CQK, A_CV = 8192, 9216
A_O1B = 10240
A_O2T1, A_O2T2, A_LEP2, A_Y = 12288, 13312, 14336, 15360
A_PR1, A_S1F, A_P1B = 16384, 17408, 17920
A_S2F, A_S2B, A_P2T, A_RST = 18432, 18944, 19200, 19456
A_IBF, A_BD, A_BDP = 19472, 19536, 19920
A_W0B, A_W1, A_ALNW, A_ALNB, A_PB = 20176, 20272, 20275, 20403, 20531
A_M2, A_SLAB, A_YC, A_ST16, A_SQ16 = 20532, 21044, 22068, 22580, 22624
A_IF32 = 22884
A_PR1F, A_O1F, A_L2F = 23040, 25088, 27136
A_B0, A_B1 = 29184, 29248
A_SQF = 2048
A_YTR, A_SCR, A_X1T, A_STG = 0, 8192, 9216, 10240


def prep_attna_consts(inp):
    qw = inp["L0_attn_a_qkv_w"]
    pw = inp["L0_attn_a_proj_w"]
    cw0 = inp["L0_attn_a_conv_w0"]
    cw1 = inp["L0_attn_a_conv_w1"]
    def bd(colmap, umatch):
        m = np.zeros((128, 128), dtype=np.float32)
        for u in range(8):
            for c in range(16):
                for oc, (uu, col) in colmap(u):
                    pass
        return m
    bdA = np.zeros((128, 128), np.float32)
    bdC = np.zeros((128, 128), np.float32)
    bdD = np.zeros((128, 128), np.float32)
    bdD2 = np.zeros((128, 128), np.float32)
    bdE = np.zeros((128, 128), np.float32)
    bdE2 = np.zeros((128, 128), np.float32)
    for u in range(8):
        for c in range(16):
            r = 16 * u + c
            for oc in range(8):
                bdA[r, 16 * u + oc] = qw[c, 0 + oc]
                bdA[r, 16 * u + 8 + oc] = qw[c, 16 + oc]
            for oc in range(16):
                bdC[r, 16 * u + oc] = qw[c, 32 + oc]
            if u < 4:
                for oc in range(8):
                    bdD[r, 32 * u + oc] = qw[c, 8 + oc]
                    bdE[r, 32 * u + oc] = qw[c, 24 + oc]
            else:
                for oc in range(8):
                    bdD2[r, 32 * (u - 4) + oc] = qw[c, 8 + oc]
                    bdE2[r, 32 * (u - 4) + oc] = qw[c, 24 + oc]
    bdP1 = np.zeros((128, 128), np.float32)
    bdPL = np.zeros((128, 128), np.float32)
    for u in range(8):
        for c in range(8):
            for oc in range(16):
                bdP1[16 * u + c, 16 * u + oc] = pw[c, oc]
                bdPL[16 * u + 8 + c, 16 * u + oc] = pw[8 + c, oc]
    bdPG = np.zeros((128, 128), np.float32)
    bdPG2 = np.zeros((128, 128), np.float32)
    for r in range(4):
        for c in range(8):
            for oc in range(16):
                bdPG[32 * r + c, 16 * r + oc] = pw[8 + c, oc]
                bdPG2[32 * r + c, 16 * (r + 4) + oc] = pw[8 + c, oc]
    W0 = np.zeros((128, 192), np.float32)
    for du in range(3):
        for i in range(8):
            for c in range(8):
                W0[:, 64 * du + 8 * i + c] = cw0[c, 0, 1, du]
    W1 = np.zeros((128, 3), np.float32)
    for dt in range(3):
        for u in range(8):
            for c in range(8):
                W1[16 * u + 8 + c, dt] = cw1[c, 0, dt, 1]
    m2 = np.zeros((32, 32), np.float32)
    for a in range(32):
        for b in range(32):
            if a % 4 != b % 4:
                m2[a, b] = -1e9
    M2 = np.tile(m2, (4, 16))
    alnw = np.zeros((128, 128), np.float32)
    alnb = np.zeros((128, 128), np.float32)
    for u in range(8):
        for c in range(16):
            alnw[:, 16 * u + c] = inp["L0_attn_a_ln_w"][c]
            alnb[:, 16 * u + c] = inp["L0_attn_a_ln_b"][c]
    pb = np.zeros((128, 1), np.float32)
    for u in range(8):
        for c in range(16):
            pb[16 * u + c, 0] = inp["L0_attn_a_proj_b"][c]
    b0 = np.zeros((128, 64), np.float32)
    for i in range(8):
        for c in range(8):
            b0[:, 8 * i + c] = inp["L0_attn_a_conv_b0"][c]
    b1 = np.zeros((128, 1), np.float32)
    for u in range(8):
        for c in range(8):
            b1[16 * u + 8 + c, 0] = inp["L0_attn_a_conv_b1"][c]
    bds = np.stack([bdA, bdC, bdD, bdD2, bdE, bdE2], 0)
    bdp = np.stack([bdP1, bdPG, bdPG2, bdPL], 0)
    return {"a_bds": bds, "a_bdp": bdp, "a_w0": W0, "a_w1": W1, "a_m2": M2,
            "a_alnw": alnw, "a_alnb": alnb, "a_pb": pb, "a_b0": b0, "a_b1": b1}


def declare_attna_aux(kb):
    aux = {}
    aux["a_bds"] = kb.din("a_bds", [6, 128, 128])
    aux["a_bdp"] = kb.din("a_bdp", [4, 128, 128])
    aux["a_w0"] = kb.din("a_w0", [128, 192])
    aux["a_w1"] = kb.din("a_w1", [128, 3])
    aux["a_m2"] = kb.din("a_m2", [128, 512])
    aux["a_alnw"] = kb.din("a_alnw", [128, 128])
    aux["a_alnb"] = kb.din("a_alnb", [128, 128])
    aux["a_pb"] = kb.din("a_pb", [128, 1])
    aux["a_b0"] = kb.din("a_b0", [128, 64])
    aux["a_b1"] = kb.din("a_b1", [128, 1])
    aux["I128"] = kb.din("I128", [128, 128])
    aux["xrows"] = kb.din("xrows", [T, D])
    aux["a2a_in"] = kb.dscratch("a2a_in", [8, 8, 128, 256], BF16)
    aux["a2a_out"] = kb.dscratch("a2a_out", [8, 8, 128, 256], BF16)
    return aux


def emit_attn_a(kb, C, aux, x1_dram):
    P = kb.prog
    A = C.AR0
    Ab = A[:, :].bitcast(BF16)

    def bft(c0, n):
        return Ab[:, 2 * c0:2 * c0 + n]

    def f32t(c0, n):
        return A[:, c0:c0 + n]

    XNB = bft(A_XNB, 2048)
    TT_ = bft(A_TT, 2048)
    AOUT = bft(A_AOUT, 2048)
    COUT = bft(A_COUT, 2048)
    DOUT, D2OUT = bft(A_DOUT, 2048), bft(A_D2, 2048)
    EOUT, E2OUT = bft(A_EOUT, 2048), bft(A_E2, 2048)
    CQK, CVt = bft(A_CQK, 2048), bft(A_CV, 2048)
    O1B = bft(A_O1B, 2048)
    O2T1, O2T2 = bft(A_O2T1, 2048), bft(A_O2T2, 2048)
    LEP2, Yt = bft(A_LEP2, 2048), bft(A_Y, 2048)
    PR1 = bft(A_PR1, 2048)
    S1F = f32t(A_S1F, 512)
    P1B = bft(A_P1B, 1024)
    S2F = f32t(A_S2F, 512)
    S2B = bft(A_S2B, 1024)
    P2T = bft(A_P2T, 512)
    RST = f32t(A_RST, 16)
    IBF = bft(A_IBF, 128)
    BD = [bft(A_BD + 64 * i, 128) for i in range(6)]
    BDP = [bft(A_BDP + 64 * i, 128) for i in range(4)]
    W0B = bft(A_W0B, 192)
    W1T = f32t(A_W1, 3)
    ALNW, ALNB = f32t(A_ALNW, 128), f32t(A_ALNB, 128)
    PBT = f32t(A_PB, 1)
    M2T = f32t(A_M2, 512)
    SLAB = bft(A_SLAB, 2048)
    YC = f32t(A_YC, 512)
    ST32 = f32t(A_ST16, 32)
    SQ16 = bft(A_SQ16, 512)
    IF32 = f32t(A_IF32, 128)
    SQF = f32t(A_SQF, 2048)
    PR1F = f32t(A_PR1F, 2048)
    O1F = f32t(A_O1F, 2048)
    L2F = f32t(A_L2F, 2048)
    B0T = f32t(A_B0, 64)
    B1T = f32t(A_B1, 1)

    for i in range(6):
        P.gpsimd.dma_start(out=BD[i], in_=aux["a_bds"][i]).then_inc(C.qG, 16)
    for i in range(4):
        P.gpsimd.dma_start(out=BDP[i], in_=aux["a_bdp"][i]).then_inc(C.qG, 16)
    P.gpsimd.dma_start(out=IBF, in_=aux["I128"]).then_inc(C.qG, 16)
    P.gpsimd.dma_start(out=W0B, in_=aux["a_w0"]).then_inc(C.qG, 16)
    P.sync.dma_start(out=W1T, in_=aux["a_w1"]).then_inc(C.qS, 16)
    P.sync.dma_start(out=M2T, in_=aux["a_m2"]).then_inc(C.qS, 16)
    P.sync.dma_start(out=ALNW, in_=aux["a_alnw"]).then_inc(C.qS, 16)
    P.sync.dma_start(out=ALNB, in_=aux["a_alnb"]).then_inc(C.qS, 16)
    P.sync.dma_start(out=PBT, in_=aux["a_pb"]).then_inc(C.qS, 16)
    P.sync.dma_start(out=IF32, in_=aux["I128"]).then_inc(C.qS, 16)
    P.sync.dma_start(out=B0T, in_=aux["a_b0"]).then_inc(C.qS, 16)
    P.sync.dma_start(out=B1T, in_=aux["a_b1"]).then_inc(C.qS, 16)
    pre_g, pre_s = C.qG.val, C.qS.val

    for sb in range(8):
        P.sync.wait_ge(C.cV, C.cV.val)
        P.sync.wait_ge(C.cT, C.cT.val)
        P.sync.dma_start(out=C.XT2[:, :], in_=aux["xrows"][128 * sb:128 * (sb + 1), :]).then_inc(C.qE[0], 16)
        P.vector.wait_ge(C.qE[0], C.qE[0].val)
        if sb == 0:
            P.vector.wait_ge(C.qS, pre_s)
            P.vector.wait_ge(C.qG, pre_g)
        P.vector.tensor_reduce(RST[:, 0:1], C.XT2[:, :], axis=AXX, op=ALU.add)
        P.vector.tensor_scalar(RST[:, 0:1], RST[:, 0:1], 1.0 / D, None, ALU.mult)
        P.vector.tensor_scalar(C.XT2[:, :], C.XT2[:, :], RST[:, 0:1], None, ALU.subtract)
        P.vector.tensor_tensor(SQF, C.XT2[:, :], C.XT2[:, :], ALU.mult)
        P.vector.tensor_reduce(RST[:, 1:2], SQF, axis=AXX, op=ALU.add)
        P.vector.tensor_scalar(RST[:, 1:2], RST[:, 1:2], 1.0 / D, None, ALU.mult)
        P.vector.tensor_scalar(RST[:, 1:2], RST[:, 1:2], EPS, None, ALU.add)
        P.scalar.wait_ge(C.cV, C.cV.val)
        P.scalar.activation(RST[:, 1:2], RST[:, 1:2], AF.Sqrt)
        P.vector.wait_ge(C.cA, C.cA.val)
        P.vector.reciprocal(RST[:, 1:2], RST[:, 1:2])
        P.vector.tensor_scalar(C.XT2[:, :], C.XT2[:, :], RST[:, 1:2], None, ALU.mult)
        P.vector.tensor_copy(XNB, C.XT2[:, :])
        P.vector.memset(O1F[:, :], 0.0)
        P.vector.memset(O1B[:, :], 0.0)
        ln_ok = C.cV.val

        tb_ev = []
        for g in range(4):
            P.tensor.wait_ge(C.cV, ln_ok)
            for q in range(4):
                j = 4 * g + q
                P.tensor.matmul(C.PS[g % 4][:, 128 * q:128 * (q + 1)], XNB[:, 128 * j:128 * (j + 1)], IBF,
                                start=True, stop=True)
            P.mark_last('tensor', C.cT)
            P.vector.wait_ge(C.cT, C.cT.val)
            P.vector.tensor_copy(TT_[:, 512 * g:512 * (g + 1)], C.PS[g % 4][:, :])
            tb_ev.append(C.cV.val)

        outs = [AOUT, COUT, DOUT, D2OUT, EOUT, E2OUT]
        bd_ev = []
        for ci in range(6):
            for g in range(4):
                i = 4 * ci + g
                P.tensor.wait_ge(C.cV, tb_ev[g] if i < 4 else bd_ev[i - 4])
                for q in range(4):
                    j = 4 * g + q
                    P.tensor.matmul(C.PS[4 + g % 4][:, 128 * q:128 * (q + 1)], BD[ci],
                                    TT_[:, 128 * j:128 * (j + 1)], start=True, stop=True)
                P.mark_last('tensor', C.cT)
                P.vector.wait_ge(C.cT, C.cT.val)
                P.vector.tensor_copy(outs[ci][:, 512 * g:512 * (g + 1)], C.PS[4 + g % 4][:, :])
                bd_ev.append(C.cV.val)

        cp_ev = []
        for si, (srcT, dstT) in enumerate(((AOUT, CQK), (COUT, CVt))):
            for g in range(4):
                i = 4 * si + g
                P.tensor.wait_ge(C.cV, bd_ev[4 * si + g] if i < 4 else cp_ev[i - 4])
                for q in range(4):
                    j = 4 * g + q
                    P.tensor.matmul(C.PS[g % 4][:, 128 * q:128 * (q + 1)], srcT[:, 128 * j:128 * (j + 1)], IBF,
                                    start=True, stop=True)
                P.mark_last('tensor', C.cT)
                P.vector.wait_ge(C.cT, C.cT.val)
                P.vector.tensor_copy(dstT[:, 512 * g:512 * (g + 1)], C.PS[g % 4][:, :])
                cp_ev.append(C.cV.val)


        cv_ap = CVt[:, :].rearrange("p (j u c) -> p j u c", j=16, u=8, c=16)
        for uh in range(2):
            Dsrc = DOUT if uh == 0 else D2OUT
            Esrc = EOUT if uh == 0 else E2OUT
            P.tensor.wait_ge(C.cV, C.cV.val)
            for j in range(16):
                for r in range(4):
                    for band in range(4):
                        P.tensor.matmul(
                            C.PS[4 + r][32 * band:32 * band + 32, 32 * j:32 * j + 32],
                            Dsrc[32 * r:32 * r + 8, 128 * j + 32 * band:128 * j + 32 * band + 32],
                            Esrc[32 * r:32 * r + 8, 128 * j + 32 * band:128 * j + 32 * band + 32],
                            start=True, stop=True, tile_position=(32 * r, 32 * band),
                        )
            P.mark_last('tensor', C.cT)
            gram_done = C.cT.val
            O2 = O2T1 if uh == 0 else O2T2
            for r in range(4):
                P.vector.wait_ge(C.cT, gram_done)
                P.vector.tensor_tensor(S2F[:, :], C.PS[4 + r][:, :], M2T, ALU.add)
                P.vector.tensor_reduce(RST[:, 0:16], S2F[:, :].rearrange("p (j b) -> p j b", j=16),
                                       axis=AXX, op=ALU.max)
                P.vector.tensor_tensor(S2F[:, :].rearrange("p (j b) -> p j b", j=16),
                                       S2F[:, :].rearrange("p (j b) -> p j b", j=16),
                                       RST[:, 0:16].unsqueeze(2).to_broadcast([128, 16, 32]), ALU.subtract)
                P.scalar.wait_ge(C.cV, C.cV.val)
                P.scalar.activation(S2B[:, 0:512], S2F[:, :], AF.Exp, bias=0.0, scale=float(1.0 / np.sqrt(8.0)))
                P.vector.wait_ge(C.cA, C.cA.val)
                P.vector.tensor_reduce(RST[:, 0:16], S2B[:, 0:512].rearrange("p (j b) -> p j b", j=16),
                                       axis=AXX, op=ALU.add)
                P.vector.reciprocal(RST[:, 0:16], RST[:, 0:16])
                P.vector.tensor_copy(SQ16[:, 0:16], RST[:, 0:16])
                P.vector.tensor_tensor(S2B[:, 0:512].rearrange("p (j b) -> p j b", j=16),
                                       S2B[:, 0:512].rearrange("p (j b) -> p j b", j=16),
                                       SQ16[:, 0:16].unsqueeze(2).to_broadcast([128, 16, 32]), ALU.mult)
                P.vector.wait_ge(C.cT, C.cT.val)
                P.vector.transpose(P2T[:, :], S2B[:, 0:512])
                P.tensor.wait_ge(C.cV, C.cV.val)
                for j in range(16):
                    for band in range(4):
                        P.tensor.matmul(
                            C.PS[band][32 * r:32 * r + 8, 32 * j:32 * j + 32],
                            cv_ap[32 * band:32 * band + 32, j, 4 * uh + r, 8:16],
                            P2T[32 * band:32 * band + 32, 32 * j:32 * j + 32],
                            start=True, stop=True, tile_position=(32 * band, 32 * r),
                        )
                P.mark_last('tensor', C.cT)
            P.vector.wait_ge(C.cT, C.cT.val)
            for band in range(4):
                P.vector.tensor_copy(
                    O2[:, :].rearrange("p (j rl) -> p j rl", j=16)[:, :, 32 * band:32 * band + 32],
                    C.PS[band][:, :].rearrange("p (j b) -> p j b", j=16))
        for g in range(4):
            q_ap = CQK[:, 512 * g:512 * (g + 1)].rearrange("p (j u c) -> p j u c", j=4, u=8, c=16)
            qq = q_ap[:, :, :, 0:8]
            kk_ = q_ap[:, :, :, 8:16]
            for jj in range(4):
                P.vector.tensor_tensor(PR1F[:, 512 * jj:512 * (jj + 1)].rearrange("p (a b c) -> p a b c", a=8, b=8, c=8),
                                       qq[:, jj].unsqueeze(2).to_broadcast([128, 8, 8, 8]),
                                       kk_[:, jj].unsqueeze(1).to_broadcast([128, 8, 8, 8]), ALU.mult)
            P.vector.tensor_reduce(S1F[:, 0:256],
                                   PR1F[:, 0:2048].rearrange("p (x c) -> p x c", c=8), axis=AXX, op=ALU.add)
            P.vector.tensor_reduce(ST32[:, 0:32], S1F[:, 0:256].rearrange("p (x b) -> p x b", b=8),
                                   axis=AXX, op=ALU.max)
            P.vector.tensor_tensor(S1F[:, 0:256].rearrange("p (x b) -> p x b", b=8),
                                   S1F[:, 0:256].rearrange("p (x b) -> p x b", b=8),
                                   ST32[:, 0:32].unsqueeze(2).to_broadcast([128, 32, 8]), ALU.subtract)
            P.scalar.wait_ge(C.cV, C.cV.val)
            P.scalar.activation(P1B[:, 256 * g:256 * (g + 1)], S1F[:, 0:256], AF.Exp,
                                bias=0.0, scale=float(1.0 / np.sqrt(8.0)))
            P.vector.wait_ge(C.cA, C.cA.val)
            P.vector.tensor_reduce(ST32[:, 0:32], P1B[:, 256 * g:256 * (g + 1)].rearrange("p (x b) -> p x b", b=8),
                                   axis=AXX, op=ALU.add)
            P.vector.reciprocal(ST32[:, 0:32], ST32[:, 0:32])
            P.vector.tensor_copy(SQ16[:, 0:32], ST32[:, 0:32])
            P.vector.tensor_tensor(P1B[:, 256 * g:256 * (g + 1)].rearrange("p (x b) -> p x b", b=8),
                                   P1B[:, 256 * g:256 * (g + 1)].rearrange("p (x b) -> p x b", b=8),
                                   SQ16[:, 0:32].unsqueeze(2).to_broadcast([128, 32, 8]), ALU.mult)
            p_ap = P1B[:, 256 * g:256 * (g + 1)].rearrange("p (j i b) -> p j i b", j=4, i=8, b=8)
            v1 = CVt[:, 512 * g:512 * (g + 1)].rearrange("p (j u c) -> p j u c", j=4, u=8, c=16)[:, :, :, 0:8]
            for jj in range(4):
                P.vector.tensor_tensor(PR1F[:, 512 * jj:512 * (jj + 1)].rearrange("p (a c b) -> p a c b", a=8, c=8, b=8),
                                       p_ap[:, jj].unsqueeze(2).to_broadcast([128, 8, 8, 8]),
                                       v1[:, jj].rearrange("p b c -> p c b").unsqueeze(1).to_broadcast([128, 8, 8, 8]),
                                       ALU.mult)
            o_ap = O1F[:, 512 * g:512 * (g + 1)].rearrange("p (j u c) -> p j u c", j=4, u=8, c=16)
            P.vector.tensor_reduce(o_ap[:, :, :, 0:8], PR1F[:, 0:2048].rearrange("p (x b) -> p x b", b=8),
                                   axis=AXX, op=ALU.add)
        cv_ap = CVt[:, :].rearrange("p (j u c) -> p j u c", j=16, u=8, c=16)
        o_all = O1F[:, :].rearrange("p (j u c) -> p j u c", j=16, u=8, c=16)
        for du, (o_lo, o_hi, v_lo) in ((0, (1, 8, 0)), (1, (0, 8, 0)), (2, (0, 7, 1))):
            n_u = o_hi - o_lo
            w0_ap = W0B[:, 64 * du:64 * du + 64].rearrange("p (u c) -> p u c", u=8, c=8)[:, o_lo:o_hi, :]
            tmp = PR1F[:, 0:16 * n_u * 8].rearrange("p (j u c) -> p j u c", j=16, u=n_u, c=8)
            P.vector.tensor_tensor(tmp, cv_ap[:, :, v_lo:v_lo + n_u, 0:8],
                                   w0_ap.unsqueeze(1).to_broadcast([128, 16, n_u, 8]), ALU.mult)
            P.vector.tensor_tensor(o_all[:, :, o_lo:o_hi, 0:8], o_all[:, :, o_lo:o_hi, 0:8], tmp, ALU.add)
        P.vector.tensor_tensor(o_all[:, :, :, 0:8], o_all[:, :, :, 0:8],
                               B0T.rearrange("p (u c) -> p u c", u=8)[:, :, :].unsqueeze(1).to_broadcast([128, 16, 8, 8]),
                               ALU.add)
        P.vector.tensor_copy(O1B[:, :], O1F[:, :])
        b1_done = C.cV.val
        P.vector.memset(L2F[:, :], 0.0)
        c2_ap = COUT[:, :].rearrange("p (j band t r4) -> p j band t r4", j=16, band=4, t=8, r4=4)
        l_ap = L2F[:, :].rearrange("p (j band t r4) -> p j band t r4", j=16, band=4, t=8, r4=4)
        for dt, (t_lo, t_hi, s_lo) in ((0, (1, 8, 0)), (1, (0, 8, 0)), (2, (0, 7, 1))):
            n_t = t_hi - t_lo
            for band in range(4):
                tmp = PR1F[:, 0:16 * n_t * 4].rearrange("p (j t r4) -> p j t r4", j=16, t=n_t, r4=4)
                P.vector.tensor_scalar(tmp, c2_ap[:, :, band, s_lo:s_lo + n_t, :], W1T[:, dt:dt + 1], None, ALU.mult)
                P.vector.tensor_tensor(l_ap[:, :, band, t_lo:t_hi, :], l_ap[:, :, band, t_lo:t_hi, :], tmp, ALU.add)
        P.vector.tensor_scalar(L2F[:, :], L2F[:, :], B1T[:, 0:1], None, ALU.add)
        P.vector.tensor_copy(LEP2[:, :], L2F[:, :])

        for g in range(4):
            P.tensor.wait_ge(C.cV, C.cV.val)
            for q in range(4):
                j = 4 * g + q
                P.tensor.matmul(C.PS[q][:, 0:128], O1B[:, 128 * j:128 * (j + 1)], IBF, start=True, stop=True)
            P.mark_last('tensor', C.cT)
            P.vector.wait_ge(C.cT, C.cT.val)
            for q in range(4):
                P.vector.tensor_copy(PR1[:, 128 * q:128 * (q + 1)], C.PS[q][:, 0:128])
            P.tensor.wait_ge(C.cV, C.cV.val)
            for q in range(4):
                j = 4 * g + q
                P.tensor.matmul(C.PS[4 + q][:, 0:128], BDP[0], PR1[:, 128 * q:128 * (q + 1)], start=True, stop=False)
                P.tensor.matmul(C.PS[4 + q][:, 0:128], BDP[1], O2T1[:, 128 * j:128 * (j + 1)], start=False, stop=False)
                P.tensor.matmul(C.PS[4 + q][:, 0:128], BDP[2], O2T2[:, 128 * j:128 * (j + 1)], start=False, stop=False)
                P.tensor.matmul(C.PS[4 + q][:, 0:128], BDP[3], LEP2[:, 128 * j:128 * (j + 1)], start=False, stop=True)
            P.mark_last('tensor', C.cT)
            P.vector.wait_ge(C.cT, C.cT.val)
            for q in range(4):
                P.vector.tensor_scalar(PR1[:, 128 * q:128 * (q + 1)], C.PS[4 + q][:, 0:128], PBT[:, 0:1], None, ALU.add)
            P.tensor.wait_ge(C.cV, C.cV.val)
            for q in range(4):
                P.tensor.matmul(C.PS[q][:, 0:128], PR1[:, 128 * q:128 * (q + 1)], IBF, start=True, stop=True)
            P.mark_last('tensor', C.cT)
            P.vector.wait_ge(C.cT, C.cT.val)
            for q in range(4):
                P.vector.tensor_copy(YC[:, 128 * q:128 * (q + 1)], C.PS[q][:, 0:128])
            yap = YC[:, :].rearrange("p (x c) -> p x c", c=16)
            P.vector.tensor_reduce(ST32[:, 0:32], yap, axis=AXX, op=ALU.add)
            P.vector.tensor_scalar(ST32[:, 0:32], ST32[:, 0:32], 1.0 / 16, None, ALU.mult)
            P.vector.tensor_tensor(yap, yap, ST32[:, 0:32].unsqueeze(2).to_broadcast([128, 32, 16]), ALU.subtract)
            P.scalar.wait_ge(C.cV, C.cV.val)
            P.scalar.activation(SQ16[:, 0:512], YC[:, :], AF.Square)
            P.vector.wait_ge(C.cA, C.cA.val)
            P.vector.tensor_reduce(ST32[:, 0:32], SQ16[:, 0:512].rearrange("p (x c) -> p x c", c=16),
                                   axis=AXX, op=ALU.add)
            P.vector.tensor_scalar(ST32[:, 0:32], ST32[:, 0:32], 1.0 / 16, None, ALU.mult)
            P.vector.tensor_scalar(ST32[:, 0:32], ST32[:, 0:32], EPS, None, ALU.add)
            P.scalar.wait_ge(C.cV, C.cV.val)
            P.scalar.activation(ST32[:, 0:32], ST32[:, 0:32], AF.Sqrt)
            P.vector.wait_ge(C.cA, C.cA.val)
            P.vector.reciprocal(ST32[:, 0:32], ST32[:, 0:32])
            P.vector.tensor_tensor(yap, yap, ST32[:, 0:32].unsqueeze(2).to_broadcast([128, 32, 16]), ALU.mult)
            ycv = YC[:, :].rearrange("p (a c) -> p a c", a=4)
            P.vector.tensor_tensor(ycv, ycv, ALNW.unsqueeze(1).to_broadcast([128, 4, 128]), ALU.mult)
            P.vector.tensor_tensor(ycv, ycv, ALNB.unsqueeze(1).to_broadcast([128, 4, 128]), ALU.add)
            P.vector.tensor_tensor(YC[:, :], YC[:, :], C.XT2[:, 512 * g:512 * (g + 1)], ALU.add)
            P.vector.tensor_copy(Yt[:, 512 * g:512 * (g + 1)], YC[:, :])

        y_ap = Yt[:, :].rearrange("p (j u c) -> p j u c", j=16, u=8, c=16)
        P.vector.wait_ge(C.qG, C.qG.val)
        for dd in range(8):
            P.vector.tensor_copy(
                SLAB[:, 256 * dd:256 * (dd + 1)].rearrange("p (j u two) -> p j u two", j=16, u=8, two=2),
                y_ap[:, :, :, 2 * dd:2 * dd + 2])
        P.gpsimd.wait_ge(C.cV, C.cV.val)
        for dd in range(8):
            P.gpsimd.dma_start(out=aux["a2a_in"].ap()[dd, sb], in_=SLAB[:, 256 * dd:256 * (dd + 1)]).then_inc(C.qG, 16)

    P.gpsimd.wait_ge(C.qG, C.qG.val)
    P.gpsimd.collective_compute(
        "AllToAll", ALU.bypass, replica_groups=[list(range(NCORES))],
        ins=[aux["a2a_in"].ap().opt()], outs=[aux["a2a_out"].ap().opt()],
    ).then_inc(C.cc, 1)
    P.gpsimd.wait_ge(C.cc, C.cc.val)
    YTR = bft(A_YTR, 16384)
    for src in range(8):
        for sb in range(8):
            P.gpsimd.wait_ge(C.cT, C.cT.val)
            P.gpsimd.dma_start(out=SLAB[:, 0:256], in_=aux["a2a_out"].ap()[src, sb]).then_inc(C.qR[0], 16)
            P.tensor.wait_ge(C.qR[0], C.qR[0].val)
            P.tensor.wait_ge(C.cV, C.cV.val)
            for par in range(2):
                sel = SLAB[:, 0:256].rearrange("p (j u two) -> p two (j u)", j=16, u=8)[:, par, :]
                P.tensor.matmul(C.PS[par][:, 0:128], sel, IBF, start=True, stop=True)
            P.mark_last('tensor', C.cT)
            P.vector.wait_ge(C.cT, C.cT.val)
            for par in range(2):
                col = 2 * (8 * src + sb) + par
                P.vector.tensor_copy(YTR[:, 128 * col:128 * (col + 1)], C.PS[par][:, 0:128])
    SCR = f32t(A_SCR, 1024)
    X1T = f32t(A_X1T, 1024)
    for F in range(16):
        for rb in range(8):
            if rb == 0:
                P.sync.wait_ge(C.cV, C.cV.val)
            else:
                P.sync.wait_ge(C.cT, C.cT.val)
            P.sync.dma_start(out=f32t(A_STG, 128),
                             in_=aux["xrows"][128 * rb:128 * (rb + 1), 128 * F:128 * (F + 1)]).then_inc(C.qE[1], 16)
            P.tensor.wait_ge(C.qE[1], C.qE[1].val)
            if rb == 0:
                P.tensor.wait_ge(C.cV, C.cV.val)
            pb_ = C.PS[4 + rb // 4]
            P.tensor.matmul(pb_[:, 128 * (rb % 4):128 * (rb % 4 + 1)], f32t(A_STG, 128), IF32, start=True, stop=True)
            P.mark_last('tensor', C.cT)
        P.vector.wait_ge(C.cT, C.cT.val)
        P.vector.wait_ge(C.qD[0], C.qD[0].val)
        ysel = YTR[:, :].rearrange("p (q a g) -> p q a g", q=128, a=8, g=16)[:, :, :, F]
        ysel = ysel.rearrange("p (s2 par) a -> p par s2 a", par=2)
        P.vector.tensor_copy(SCR.rearrange("p (par s a) -> p par s a", par=2, s=64), ysel)
        P.vector.tensor_copy(X1T[:, 0:512], C.PS[4][:, :])
        P.vector.tensor_copy(X1T[:, 512:1024], C.PS[5][:, :])
        P.vector.tensor_tensor(X1T, X1T, SCR, ALU.add)
        P.gpsimd.wait_ge(C.cV, C.cV.val)
        P.gpsimd.dma_start(out=x1_dram[F], in_=X1T).then_inc(C.qD[0], 16)


def build_full():
    kb = KB()
    C = Common(kb)
    din = {
        "lua": kb.din("lua", [D, 16]), "lda": kb.din("lda", [FF, 16]),
        "b_up": kb.din("b_up", [128, FC]), "b_down": kb.din("b_down", [128, KC]),
    }
    lub = kb.din("lub", [16, FF])
    ldb = kb.din("ldb", [16, D])
    wq_u = kb.din("wupT_q", [D, FF], I32)
    ws_u = kb.din("wup_sc", [D, FF])
    wq_d = kb.din("wdnT_q", [FF, D], I32)
    ws_d = kb.din("wdn_sc", [FF, D])
    out_rows = kb.dout("out_rows", [T, D])
    auxa = declare_attna_aux(kb)
    auxb = declare_attnb_aux(kb)
    x1 = kb.dscratch("x1s", [KC, 128, T]).ap()
    x2 = kb.dscratch("x2s", [KC, 128, T]).ap()
    x3 = kb.dscratch("x3s", [KC, 128, T]).ap()
    hm = kb.dscratch("hms", [KC, 128, T]).ap()
    x4 = kb.dscratch("x4s", [KC, 128, T]).ap()
    wup_bf = kb.dscratch("wup_bf", [FC, 128, KC, 128], BF16).ap()
    wdn_bf = kb.dscratch("wdn_bf", [4 * KC, 128, KC, 128], BF16).ap()
    P = kb.prog

    C.preload(din)
    emit_predequant(kb, C, (wq_u, ws_u), (wq_d, ws_d), wup_bf, wdn_bf)
    emit_attn_a(kb, C, auxa, x1)
    P.gpsimd.wait_ge(C.qC[0], C.qC[0].val)
    P.gpsimd.wait_ge(C.qC[1], C.qC[1].val)
    emit_ffn(kb, C, x1, x2, dequant=False,
             wup_src=[wup_bf[m] for m in range(FC)],
             wdn_src=[wdn_bf[st] for st in range(4 * KC)],
             lub_dram=lub, ldb_dram=ldb,
             halo_out=auxb["halo_in"].ap())
    emit_attn_b(kb, C, x2, x3, hm, auxb)
    emit_ffn(kb, C, x3, x4, dequant=False,
             wup_src=[wup_bf[m] for m in range(FC)],
             wdn_src=[wdn_bf[st] for st in range(4 * KC)],
             lub_dram=lub, ldb_dram=ldb)
    IF32 = C.AR0[:, A_IF32:A_IF32 + 128]
    STG = C.AR0[:, A_STG:A_STG + 128]
    P.sync.wait_ge(C.qD[0], C.qD[0].val)
    P.sync.wait_ge(C.qD[1], C.qD[1].val)
    P.sync.wait_ge(C.cT, C.cT.val)
    P.sync.dma_start(out=IF32, in_=auxa["I128"]).then_inc(C.qS, 16)
    P.tensor.wait_ge(C.qS, C.qS.val)
    prev_out = 0
    for rb in range(8):
        for F in range(16):
            P.sync.wait_ge(C.cT, C.cT.val)
            P.sync.dma_start(out=STG, in_=x4[F][:, 128 * rb:128 * (rb + 1)]).then_inc(C.qE[0], 16)
            P.tensor.wait_ge(C.qE[0], C.qE[0].val)
            if F % 4 == 0:
                P.tensor.wait_ge(C.cV, C.cV.val)
            pb_ = C.PS[4 + (F // 4) % 2]
            P.tensor.matmul(pb_[:, 128 * (F % 4):128 * (F % 4 + 1)], STG, IF32, start=True, stop=True)
            P.mark_last('tensor', C.cT)
            if F % 4 == 3:
                P.vector.wait_ge(C.cT, C.cT.val)
                if F == 3:
                    P.vector.wait_ge(C.qD[1], prev_out)
                P.vector.tensor_copy(C.XT2[:, 512 * (F // 4):512 * (F // 4 + 1)], pb_[:, :])
                if F == 15:
                    P.gpsimd.wait_ge(C.cV, C.cV.val)
                    P.gpsimd.dma_start(out=out_rows[128 * rb:128 * (rb + 1), :], in_=C.XT2[:, :]).then_inc(C.qD[1], 16)
                    prev_out = C.qD[1].val
    P.gpsimd.wait_ge(C.qD[1], C.qD[1].val)
    P.gpsimd.wait_ge(C.qD[0], C.qD[0].val)
    P.gpsimd.wait_ge(C.qG, C.qG.val)
    P.sync.wait_ge(C.qS, C.qS.val)
    return kb.finalize()


K_BUILD_CACHE = {}


def full_in_maps(inp):
    maps = prep_inputs(inp)
    ac = prep_attna_consts(inp)
    for i, m in enumerate(maps):
        m.update(ac)
        pp, cmrow = prep_attnb_pack(inp, i)
        m["parpack"] = pp
        m["cmrow"] = cmrow
        for nm in ["Wq", "Wk", "Wv", "Wig", "Wog", "Wo", "Wg", "Wgate", "Wval", "Wout"]:
            m["b_" + nm] = inp["L0_attn_b_" + nm]
        m["lub"] = inp["L0_ffn_w_up_lora_b"]
        m["ldb"] = inp["L0_ffn_w_down_lora_b"]
    return maps


def kernel(**inputs):
    maps = full_in_maps(inputs)
    nc = build_full()
    res = run_bass_kernel_spmd(nc, maps, core_ids=list(range(NCORES)))
    out = np.concatenate([res.results[i]["out_rows"] for i in range(NCORES)], axis=0)
    return out.reshape(1, 8192, 2048)


# version 08
# speedup vs baseline: 1.3046x; 1.3046x over previous
import contextlib
import numpy as np
import ml_dtypes
import concourse.bass as bass
import concourse.mybir as mybir
from concourse.bass_utils import run_bass_kernel_spmd

F32 = mybir.dt.float32
BF16 = mybir.dt.bfloat16
I32 = mybir.dt.int32
ALU = mybir.AluOpType
AF = mybir.ActivationFunctionType
AXX = mybir.AxisListType.X

NCORES = 8
T = 1024
D = 2048
FF = 8192
NH = 16
HD = 128
KC = D // 128
FC = FF // 128
EPS = 1e-5


class Call:
    __slots__ = ("name", "args", "kwargs", "chain")

    def __init__(self, name, args, kwargs):
        self.name, self.args, self.kwargs, self.chain = name, args, kwargs, []

    def then_inc(self, sem, k=1):
        sem.val += k
        self.chain.append((sem, k))
        return self


_NOCHAIN = {"wait_ge", "dma_start", "dma_start_transpose", "nop", "collective_compute"}


class EngRec:

    def __init__(self, prog, key):
        self._prog, self._key = prog, key

    def __getattr__(self, name):
        def call(*a, **kw):
            chain = self._prog.chains.get(self._key)
            if chain is not None and name not in _NOCHAIN:
                if chain.val > 0:
                    self._prog.streams[self._key].append(Call("wait_ge", (chain, chain.val), {}))
                c = Call(name, a, kw)
                self._prog.streams[self._key].append(c)
                c.then_inc(chain, 1)
                return c
            c = Call(name, a, kw)
            self._prog.streams[self._key].append(c)
            return c
        return call


class Prog:
    ENGS = ("tensor", "vector", "scalar", "gpsimd", "sync")

    def __init__(self):
        self.streams = {e: [] for e in self.ENGS}
        self.chains = {}
        for e in self.ENGS:
            setattr(self, e, EngRec(self, e))

    def set_chains(self, sems):
        self.chains.update(sems)

    def mark_last(self, key, sem, k=1):
        self.streams[key][-1].then_inc(sem, k)

    def emit(self, block):
        def mk(key):
            def f(eng):
                for c in self.streams[key]:
                    args = [a.h if isinstance(a, Sem) else a for a in c.args]
                    r = getattr(eng, c.name)(*args, **c.kwargs)
                    for (sem, k) in c.chain:
                        r = r.then_inc(sem.h, k)
            return f
        block.tensor(mk("tensor"))
        block.vector(mk("vector"))
        block.scalar(mk("scalar"))
        block.gpsimd(mk("gpsimd"))
        block.sync(mk("sync"))


class Sem:

    def __init__(self, name):
        self.name, self.val, self.h = name, 0, None

    def inc(self, k=1):
        self.val += k
        return self.val


def prep_inputs(inp):
    x = np.ascontiguousarray(inp["L0_attn_a_x"].reshape(8192, 2048))
    wupT_q = np.ascontiguousarray(inp["L0_ffn_w_up_q"].T)
    wdnT_q = np.ascontiguousarray(inp["L0_ffn_w_down_q"].T)
    wup_sc = np.ascontiguousarray(np.repeat(inp["L0_ffn_w_up_scale"], 64, axis=1).T)
    wdn_sc = np.ascontiguousarray(np.repeat(inp["L0_ffn_w_down_scale"], 64, axis=1).T)
    I128 = np.eye(128, dtype=np.float32)
    base = {
        "I128": I128,
        "wupT_q": wupT_q, "wdnT_q": wdnT_q, "wup_sc": wup_sc, "wdn_sc": wdn_sc,
        "b_up": np.ascontiguousarray(inp["L0_ffn_b_up"].reshape(FC, 128).T),
        "b_down": np.ascontiguousarray(inp["L0_ffn_b_down"].reshape(KC, 128).T),
        "lua": inp["L0_ffn_w_up_lora_a"],
        "lub": inp["L0_ffn_w_up_lora_b"],
        "lda": inp["L0_ffn_w_down_lora_a"],
        "ldb": inp["L0_ffn_w_down_lora_b"],
    }
    maps = []
    for i in range(NCORES):
        m = dict(base)
        m["xrows"] = np.ascontiguousarray(x[T * i:T * (i + 1)])
        maps.append(m)
    return maps


class KB:

    def __init__(self):
        self.nc = bass.Bass("TRN2", target_bir_lowering=False)
        self.stack = contextlib.ExitStack()
        self.prog = Prog()
        self.sems = {}
        self._psum_used = []

    def sem(self, name):
        s = Sem(name)
        self.sems[name] = s
        return s

    def din(self, name, shape, dtype=F32):
        return self.nc.dram_tensor(name, list(shape), dtype, kind="ExternalInput").ap()

    def dout(self, name, shape, dtype=F32):
        return self.nc.dram_tensor(name, list(shape), dtype, kind="ExternalOutput").ap()

    def dscratch(self, name, shape, dtype=F32):
        return self.nc.dram_tensor(name, list(shape), dtype)

    def sb(self, name, shape, dtype=F32):
        return self.stack.enter_context(self.nc.sbuf_tensor(name, list(shape), dtype))

    def ps(self, name, shape, dtype=F32):
        return self.stack.enter_context(self.nc.psum_tensor(name, list(shape), dtype))

    def finalize(self):
        from concourse.aot_env import allow_ici_all_to_all
        with contextlib.ExitStack() as es:
            es.enter_context(allow_ici_all_to_all())
            es.enter_context(self.nc.allow_low_precision(
                reason="bf16 working tiles by design; matmul/reduce accumulate fp32 internally"))
            for s in self.sems.values():
                s.h = es.enter_context(self.nc.semaphore(s.name))
            block = es.enter_context(self.nc.Block())
            self.prog.emit(block)
        self.stack.close()
        return self.nc


def view(t, dtype, shape):
    ap = t[:, :] if not isinstance(t, bass.AP) else t
    ap = ap.bitcast(dtype)
    if ap.ndim == 3:
        ap = ap.rearrange("p a b -> p (a b)")
    if len(shape) == 2:
        return ap.rearrange("p (a b) -> p a b", a=shape[0], b=shape[1])
    if len(shape) == 1:
        return ap
    raise ValueError(shape)


class Common:

    def __init__(self, kb: KB):
        self.kb = kb
        self.PS = [kb.ps(f"PS{i}", [128, 512], F32) for i in range(8)]
        self.AR0 = kb.sb("AR0", [128, 32768], F32)
        ar0b = self.AR0[:, :].bitcast(BF16)
        self.XH = ar0b.rearrange("p (a b) -> p a b", a=FC, b=T)
        self.AB0 = [ar0b[:, 16384 * i:16384 * (i + 1)].rearrange("p (a b) -> p a b", a=KC, b=T) for i in range(4)]
        self.XT2 = kb.sb("XT2", [128, 2048], F32)
        self.XT = [self.XT2[:, 0:1024], self.XT2[:, 1024:2048]]
        self.XB = kb.sb("XB", [128, KC, 1028], BF16)
        self.SA = self.XT2[0:1, 0:1024]
        self.SB = self.XT2[0:1, 1024:2048]
        self.MB = kb.sb("MB", [128, 1024], BF16)
        self.RB = kb.sb("RB", [128, 1024], BF16)
        self.SQ = self.MB
        self.WQF = [kb.sb(f"WQF{i}", [128, 4, 128], F32) for i in range(2)]
        self.SCF = [kb.sb(f"SCF{i}", [128, 4, 128], F32) for i in range(2)]
        self.WUB = [kb.sb(f"WUB{i}", [128, KC, 128], BF16) for i in range(2)]
        self.STB = kb.sb("STB", [1, 2048], BF16)
        self.SMALL = kb.sb("SMALL", [128, 1024], F32)
        self.LUA = kb.sb("LUA", [128, KC, 16], BF16)
        self.LUBd = [kb.sb(f"LUBd{i}", [16, 128], BF16) for i in range(2)]
        self.LDA = kb.sb("LDA", [128, FC, 16], BF16)
        self.LDBd = [kb.sb(f"LDBd{i}", [16, 128], BF16) for i in range(2)]
        self.LT1 = kb.sb("LT1", [16, 1024], BF16)
        self.LT2 = kb.sb("LT2", [16, 1024], BF16)
        self.SM0 = self.LT1[0:1, :]
        self.SR0 = self.LT2[0:1, :]

        self.BUP = kb.sb("BUP", [128, FC], F32)
        self.BDN = kb.sb("BDN", [128, KC], F32)
        self.ones_col_bf = kb.sb("ones_col_bf", [128, 1], BF16)
        self.ones_row_bf = kb.sb("ones_row_bf", [1, 128], BF16)
        self.invd_col_bf = kb.sb("invd_col_bf", [128, 1], BF16)
        self.sqrthd_row_bf = kb.sb("sqrthd_row_bf", [1, 128], BF16)
        self.eps_l2 = kb.sb("eps_l2", [1, 1], F32)
        self.eps_ln = kb.sb("eps_ln", [1, 1], F32)
        self.m75 = kb.sb("m75", [128, 1], F32)
        self.wqf_guard = [None, None]
        self.qS = kb.sem("qS")
        self.qL = kb.sem("qL")
        self.qW = [kb.sem("qW0"), kb.sem("qW1")]
        self.qP = [kb.sem("qP0"), kb.sem("qP1")]
        self.qB = [kb.sem("qB0"), kb.sem("qB1")]
        self.qE = [kb.sem("qE0"), kb.sem("qE1")]
        self.qC = [kb.sem("qC0"), kb.sem("qC1")]
        self.qD = [kb.sem("qD0"), kb.sem("qD1")]
        self.qR = [kb.sem("qR0"), kb.sem("qR1")]
        self.qG = kb.sem("qG")
        self.qH = kb.sem("qH")
        self.cT = kb.sem("cT")
        self.cc = kb.sem("cc")
        self.cV = kb.sem("cV")
        self.cA = kb.sem("cA")
        self.cP = kb.sem("cP")
        kb.prog.set_chains({"vector": self.cV, "scalar": self.cA, "gpsimd": self.cP})
        self.ones_col_f32 = kb.sb("ones_col_f32", [128, 1], F32)
        self.ones_row_f32 = kb.sb("ones_row_f32", [1, 128], F32)
        self.WG = kb.sb("WG", [128, KC, 16], BF16)

    def xb(self, k, n0=0, n1=T):
        return self.XB[:, k, 4 + n0:4 + n1]

    def preload(self, din):
        P, C = self.kb.prog, self
        P.vector.memset(C.ones_col_bf[:, :], 1.0)
        P.vector.memset(C.ones_row_bf[:, :], 1.0)
        P.vector.memset(C.invd_col_bf[:, :], 1.0 / 128.0)
        P.vector.memset(C.sqrthd_row_bf[:, :], float(np.sqrt(128.0)))
        P.vector.memset(C.eps_l2[:, :], 1e-12)
        P.vector.memset(C.eps_ln[:, :], float(EPS * HD))
        P.vector.memset(C.m75[:, :], -7.5)
        P.vector.memset(C.ones_col_f32[:, :], 1.0)
        P.vector.memset(C.ones_row_f32[:, :], 1.0)
        P.gpsimd.dma_start(out=C.LUA[:, :, :], in_=din["lua"].rearrange("(kc p) r -> p kc r", p=128)).then_inc(C.qG, 16)
        P.gpsimd.dma_start(out=C.LDA[:, :, :], in_=din["lda"].rearrange("(h p) r -> p h r", p=128)).then_inc(C.qG, 16)
        P.sync.dma_start(out=C.BUP[:, :], in_=din["b_up"]).then_inc(C.qS, 16)
        P.sync.dma_start(out=C.BDN[:, :], in_=din["b_down"]).then_inc(C.qS, 16)


def emit_ln(kb, C, src_dram, *, fused_w=None, fused_b=None, fused_rho=False,
            out_bf=None, keep_rho=None):
    P = kb.prog
    out_bf = out_bf if out_bf is not None else C.XB

    def OBF(k, n0=0, n1=T):
        if out_bf is C.XB:
            return out_bf[:, k, 4 + n0:4 + n1]
        return out_bf[:, k, n0:n1]
    P.gpsimd.wait_ge(C.cT, C.cT.val)
    P.gpsimd.wait_ge(C.cV, C.cV.val)
    P.gpsimd.wait_ge(C.qG, C.qG.val)
    P.gpsimd.wait_ge(C.qD[0], C.qD[0].val)
    P.gpsimd.wait_ge(C.qD[1], C.qD[1].val)
    P.sync.wait_ge(C.qG, C.qG.val)
    P.sync.wait_ge(C.cT, C.cT.val)
    P.sync.wait_ge(C.qC[0], C.qC[0].val)
    P.sync.wait_ge(C.qC[1], C.qC[1].val)
    P.tensor.wait_ge(C.cV, C.cV.val)
    for k in range(KC):
        P.gpsimd.dma_start(out=OBF(k), in_=src_dram[k]).then_inc(C.qL, 16)
    P.scalar.wait_ge(C.qL, C.qL.val)
    for k in range(KC):
        if k > 0:
            P.scalar.wait_ge(C.cT, C.cT.val)
        P.scalar.activation(C.SQ[:, :], OBF(k), AF.Square)
        P.tensor.wait_ge(C.cA, C.cA.val)
        P.tensor.matmul(C.PS[0][0:1, :], C.ones_col_bf[:, :], OBF(k, 0, 512),
                        start=(k == 0), stop=(k == KC - 1))
        P.tensor.matmul(C.PS[1][0:1, :], C.ones_col_bf[:, :], OBF(k, 512, 1024),
                        start=(k == 0), stop=(k == KC - 1))
        P.tensor.matmul(C.PS[2][0:1, :], C.ones_col_bf[:, :], C.SQ[:, 0:512],
                        start=(k == 0), stop=(k == KC - 1))
        P.tensor.matmul(C.PS[3][0:1, :], C.ones_col_bf[:, :], C.SQ[:, 512:1024],
                        start=(k == 0), stop=(k == KC - 1)).then_inc(C.cT, 1)
    P.vector.wait_ge(C.cT, C.cT.val)
    P.vector.tensor_scalar(C.SA[:, 0:512], C.PS[0][0:1, :], 1.0 / D, None, ALU.mult)
    P.vector.tensor_scalar(C.SA[:, 512:1024], C.PS[1][0:1, :], 1.0 / D, None, ALU.mult)
    P.vector.tensor_copy(C.STB[0:1, 0:1024], C.SA)
    P.vector.tensor_scalar(C.SB[:, 0:512], C.PS[2][0:1, :], 1.0 / D, None, ALU.mult)
    P.vector.tensor_scalar(C.SB[:, 512:1024], C.PS[3][0:1, :], 1.0 / D, None, ALU.mult)
    P.vector.tensor_tensor(C.STB[0:1, 1024:2048], C.SA, C.SA, ALU.mult)
    P.vector.tensor_tensor(C.SB, C.SB, C.STB[0:1, 1024:2048], ALU.subtract)
    P.vector.tensor_scalar(C.SA, C.SB, EPS, None, ALU.add)
    P.vector.reciprocal(C.SA, C.SA)
    if fused_rho or keep_rho is not None:
        P.vector.tensor_tensor(C.SB, C.SB, C.SA, ALU.mult)
        P.vector.tensor_scalar(C.SB, C.SB, EPS, None, ALU.add)
        P.scalar.wait_ge(C.cV, C.cV.val)
        P.scalar.activation(C.SB, C.SB, AF.Sqrt)
        P.scalar.activation(C.SA, C.SA, AF.Sqrt)
        P.vector.wait_ge(C.cA, C.cA.val)
        P.vector.reciprocal(C.SB, C.SB)
        if keep_rho is not None:
            P.vector.tensor_copy(C.SM0, C.STB[0:1, 0:1024])
            P.vector.tensor_copy(C.SR0, C.SA)
        if fused_rho:
            P.vector.tensor_tensor(C.SA, C.SA, C.SB, ALU.mult)
    else:
        P.vector.wait_ge(C.cV, C.cV.val)
        P.scalar.wait_ge(C.cV, C.cV.val)
        P.scalar.activation(C.SA, C.SA, AF.Sqrt)
        P.vector.wait_ge(C.cA, C.cA.val)
    P.vector.tensor_copy(C.STB[0:1, 1024:2048], C.SA)
    P.tensor.wait_ge(C.cV, C.cV.val)
    for n in range(2):
        P.tensor.matmul(C.PS[0 + n][:, :], C.ones_row_bf[:, :], C.STB[0:1, 512 * n:512 * (n + 1)], start=True, stop=True)
        P.tensor.matmul(C.PS[2 + n][:, :], C.ones_row_bf[:, :], C.STB[0:1, 1024 + 512 * n:1024 + 512 * (n + 1)], start=True, stop=True)
    P.mark_last('tensor', C.cT)
    P.vector.wait_ge(C.cT, C.cT.val)
    P.vector.tensor_copy(C.MB[:, 0:512], C.PS[0][:, :])
    P.vector.tensor_copy(C.MB[:, 512:1024], C.PS[1][:, :])
    P.vector.tensor_copy(C.RB[:, 0:512], C.PS[2][:, :])
    P.vector.tensor_copy(C.RB[:, 512:1024], C.PS[3][:, :])
    for k in range(KC):
        P.vector.tensor_tensor(OBF(k), OBF(k), C.MB[:, :], ALU.subtract)
        P.vector.tensor_tensor(OBF(k), OBF(k), C.RB[:, :], ALU.mult)
        if fused_w is not None:
            P.vector.tensor_scalar(OBF(k), OBF(k), fused_w[:, k:k + 1], fused_b[:, k:k + 1], ALU.mult, ALU.add)
    P.vector.nop()


def _stream_lhsT(kb, C, par, src, cache, mcols, rows0, dequant, mm_wait=None):
    P = kb.prog
    if not dequant:
        for w in (mm_wait or []):
            P.sync.wait_ge(*w)
        P.sync.dma_start(out=C.WUB[par][:, :, :], in_=src).then_inc(C.qP[par], 16)
        return [(C.qP[par], C.qP[par].val)]
    qsrc, ssrc = src
    done = []
    for q in range(4):
        b = q % 2
        r = rows0 + 512 * q
        g = C.wqf_guard[b]
        if g is not None:
            P.gpsimd.wait_ge(*g)
        if q == 0:
            for w in (mm_wait or []):
                P.gpsimd.wait_ge(*w)
        P.gpsimd.dma_start(out=C.WQF[b][:, :, :],
                           in_=qsrc[r:r + 512, mcols].rearrange("(kc p) n -> p kc n", p=128)).then_inc(C.qW[b], 16)
        P.gpsimd.dma_start(out=C.SCF[b][:, :, :],
                           in_=ssrc[r:r + 512, mcols].rearrange("(kc p) n -> p kc n", p=128)).then_inc(C.qW[b], 16)
        eng, cm = (P.vector, C.cV) if b == 0 else (P.gpsimd, C.cP)
        eng.wait_ge(C.qW[b], C.qW[b].val)
        for w in (mm_wait or []):
            eng.wait_ge(*w)
        if b == 0:
            eng.scalar_tensor_tensor(
                C.WUB[par][:, 4 * q:4 * (q + 1), :].rearrange("p a b -> p (a b)"),
                C.WQF[b][:, :, :].rearrange("p a b -> p (a b)"), -7.5,
                C.SCF[b][:, :, :].rearrange("p a b -> p (a b)"),
                ALU.add, ALU.mult)
        else:
            eng.tensor_tensor(C.WQF[b][:, :, :].rearrange("p a b -> p (a b)"),
                              C.WQF[b][:, :, :].rearrange("p a b -> p (a b)"),
                              C.m75[:, 0:1].to_broadcast([128, 512]), ALU.add)
            eng.tensor_tensor(
                C.WUB[par][:, 4 * q:4 * (q + 1), :].rearrange("p a b -> p (a b)"),
                C.WQF[b][:, :, :].rearrange("p a b -> p (a b)"),
                C.SCF[b][:, :, :].rearrange("p a b -> p (a b)"),
                ALU.mult)
        C.wqf_guard[b] = (cm, cm.val)
        done.append((cm, cm.val))
    if cache is not None:
        for (sm, vv) in done[-2:]:
            P.gpsimd.wait_ge(sm, vv)
        P.gpsimd.dma_start(out=cache, in_=C.WUB[par][:, :, :]).then_inc(C.qC[par], 16)
    return done


def make_predequant(kb, C, wup_src, wdn_src, wup_cache, wdn_cache, n_chunks=8):
    P = kb.prog
    blocks = []
    for m in range(FC):
        blocks.append((wup_src, wup_cache[m], slice(128 * m, 128 * (m + 1)), 0))
    for d in range(KC):
        for r4 in range(4):
            blocks.append((wdn_src, wdn_cache[4 * d + r4], slice(128 * d, 128 * (d + 1)), 2048 * r4))
    quarters = []
    for bi, (src, cache, mcols, rows0) in enumerate(blocks):
        for q in range(4):
            quarters.append((bi, q, src, cache, mcols, rows0 + 512 * q))
    NQ = len(quarters)
    ldq = {}
    lds = {}
    tt_done = {}
    wr = {}
    st = {"started": False}

    def issue_quant(i):
        bi, q, (qsrc, ssrc), cache, mcols, r = quarters[i]
        b = i % 2
        if i >= 2:
            P.gpsimd.wait_ge(C.cP, tt_done[i - 2])
        P.gpsimd.dma_start(out=C.WQF[b][:, :, :],
                           in_=qsrc[r:r + 512, mcols].rearrange("(kc p) n -> p kc n", p=128)).then_inc(C.qW[b], 16)
        ldq[i] = C.qW[b].val

    def issue_scale(i):
        bi, q, (qsrc, ssrc), cache, mcols, r = quarters[i]
        b = i % 2
        if i >= 2:
            P.sync.wait_ge(C.cP, tt_done[i - 2])
        P.sync.dma_start(out=C.SCF[b][:, :, :],
                         in_=ssrc[r:r + 512, mcols].rearrange("(kc p) n -> p kc n", p=128)).then_inc(C.qP[b], 16)
        lds[i] = C.qP[b].val

    per = NQ // n_chunks

    def emit_chunk():
        if not st["started"]:
            st["started"] = True
            P.gpsimd.wait_ge(C.cV, C.cV.val)
            issue_quant(0)
            issue_quant(1)
            issue_scale(0)
            issue_scale(1)
            st["lo"] = 0
        lo = st["lo"]
        hi = min(lo + per, NQ)
        st["lo"] = hi
        for i in range(lo, hi):
            bi, q, src, cache, mcols, r = quarters[i]
            b = i % 2
            par = bi % 2
            P.gpsimd.wait_ge(C.qW[b], ldq[i])
            P.gpsimd.wait_ge(C.qP[b], lds[i])
            if q == 0 and bi >= 2:
                P.gpsimd.wait_ge(C.qC[par], wr[bi - 2])
            P.gpsimd.tensor_tensor(C.WQF[b][:, :, :].rearrange("p a b -> p (a b)"),
                                   C.WQF[b][:, :, :].rearrange("p a b -> p (a b)"),
                                   C.m75[:, 0:1].to_broadcast([128, 512]), ALU.add)
            P.gpsimd.tensor_tensor(C.WUB[par][:, 4 * q:4 * (q + 1), :].rearrange("p a b -> p (a b)"),
                                   C.WQF[b][:, :, :].rearrange("p a b -> p (a b)"),
                                   C.SCF[b][:, :, :].rearrange("p a b -> p (a b)"), ALU.mult)
            tt_done[i] = C.cP.val
            if i + 2 < NQ:
                issue_quant(i + 2)
                issue_scale(i + 2)
            if q == 3:
                P.gpsimd.wait_ge(C.cP, C.cP.val)
                P.gpsimd.dma_start(out=cache, in_=C.WUB[par][:, :, :]).then_inc(C.qC[par], 16)
                wr[bi] = C.qC[par].val

    return emit_chunk


def emit_ffn(kb, C, src_dram, dst_dram, *, dequant, wup_src, wdn_src, lub_dram, ldb_dram,
             wup_cache=None, wdn_cache=None, halo_out=None):
    P = kb.prog
    emit_ln(kb, C, src_dram)
    P.tensor.wait_ge(C.cV, C.cV.val)
    for n in range(2):
        if n == 1:
            P.tensor.wait_ge(C.cV, C.cV.val)
        for k in range(KC):
            P.tensor.matmul(C.PS[4][0:16, :], C.LUA[:, k, :], C.xb(k, 512 * n, 512 * (n + 1)),
                            start=(k == 0), stop=(k == KC - 1))
        P.mark_last('tensor', C.cT)
        P.vector.wait_ge(C.cT, C.cT.val)
        P.vector.tensor_copy(C.LT1[:, 512 * n:512 * (n + 1)], C.PS[4][0:16, :])
    lt1_ready = C.cV.val

    mm_done = {}
    evac_done = {}
    cache_done = {}
    for m in range(FC):
        par = m % 2
        mm_wait = []
        if m >= 2:
            mm_wait.append((C.cT, mm_done[m - 2]))
            if m - 2 in cache_done:
                mm_wait.append((C.qC[par], cache_done[m - 2]))
        if dequant:
            src = wup_src
        else:
            src = wup_src[m]
        rdy = _stream_lhsT(kb, C, par, src, wup_cache[m] if wup_cache is not None else None,
                           slice(128 * m, 128 * (m + 1)), 0, dequant, mm_wait)
        if wup_cache is not None:
            cache_done[m] = C.qC[par].val
        P.gpsimd.wait_ge(C.cT, mm_done[m - 2] if m >= 2 else 0)
        P.gpsimd.dma_start(out=C.LUBd[par][:, :], in_=lub_dram[0:16, 128 * m:128 * (m + 1)]).then_inc(C.qB[par], 16)
        pu, pu2 = C.PS[4 + 2 * par], C.PS[5 + 2 * par]
        for (sm, vv) in rdy:
            P.tensor.wait_ge(sm, vv)
        P.tensor.wait_ge(C.qB[par], C.qB[par].val)
        if m == 0:
            P.tensor.wait_ge(C.cV, lt1_ready)
        if m >= 2:
            P.tensor.wait_ge(C.cA, evac_done[m - 2])
        for k in range(KC):
            P.tensor.matmul(pu[:, :], C.WUB[par][:, k, :], C.xb(k, 0, 512), start=(k == 0), stop=False)
            P.tensor.matmul(pu2[:, :], C.WUB[par][:, k, :], C.xb(k, 512, 1024), start=(k == 0), stop=False)
        P.tensor.matmul(pu[:, :], C.LUBd[par][:, :], C.LT1[:, 0:512], start=False, stop=True)
        P.tensor.matmul(pu2[:, :], C.LUBd[par][:, :], C.LT1[:, 512:1024], start=False, stop=True).then_inc(C.cT, 1)
        mm_done[m] = C.cT.val
        P.scalar.wait_ge(C.cT, C.cT.val)
        if m == 0:
            P.scalar.wait_ge(C.qS, C.qS.val)
        P.scalar.activation(C.XH[:, m, 0:512], pu[:, :], AF.Relu, bias=C.BUP[:, m:m + 1], scale=1.0)
        P.scalar.activation(C.XH[:, m, 512:1024], pu2[:, :], AF.Relu, bias=C.BUP[:, m:m + 1], scale=1.0)
        evac_done[m] = C.cA.val
    up_done = C.cA.val

    P.tensor.wait_ge(C.cA, up_done)
    for n in range(2):
        if n == 1:
            P.tensor.wait_ge(C.cV, C.cV.val)
        for h in range(FC):
            P.tensor.matmul(C.PS[0][0:16, :], C.LDA[:, h, :], C.XH[:, h, 512 * n:512 * (n + 1)],
                            start=(h == 0), stop=(h == FC - 1))
        P.mark_last('tensor', C.cT)
        P.vector.wait_ge(C.cT, C.cT.val)
        P.vector.tensor_copy(C.LT2[:, 512 * n:512 * (n + 1)], C.PS[0][0:16, :])
    lt2_ready = C.cV.val

    mmd_done = {}
    ep_done = {}
    out_done = {}
    cached_done = {}
    for d in range(KC):
        pd, pd2 = C.PS[4 + 2 * (d % 2)], C.PS[5 + 2 * (d % 2)]
        P.gpsimd.wait_ge(C.cT, mmd_done[4 * (d - 2) + 3] if d >= 2 else C.cT.val)
        P.gpsimd.dma_start(out=C.LDBd[d % 2][:, :], in_=ldb_dram[0:16, 128 * d:128 * (d + 1)]).then_inc(C.qB[d % 2], 16)
        for r4 in range(4):
            par = r4 % 2
            step = 4 * d + r4
            mm_wait = []
            if step >= 2:
                mm_wait.append((C.cT, mmd_done[step - 2]))
                if step - 2 in cached_done:
                    mm_wait.append((C.qC[par], cached_done[step - 2]))
            else:
                mm_wait.append((C.cT, C.cT.val))
                mm_wait.append((C.qC[par], C.qC[par].val))
            if dequant:
                src = wdn_src
            else:
                src = wdn_src[step]
            rdy = _stream_lhsT(kb, C, par, src, wdn_cache[step] if wdn_cache is not None else None,
                               slice(128 * d, 128 * (d + 1)), 2048 * r4, dequant, mm_wait)
            if wdn_cache is not None:
                cached_done[step] = C.qC[par].val
            for (sm, vv) in rdy:
                P.tensor.wait_ge(sm, vv)
            if step == 0:
                P.tensor.wait_ge(C.cV, lt2_ready)
            if r4 == 0 and d >= 2:
                P.tensor.wait_ge(C.cV, ep_done[d - 2])
            for kk in range(KC):
                h = KC * r4 + kk
                P.tensor.matmul(pd[:, :], C.WUB[par][:, kk, :], C.XH[:, h, 0:512], start=(step % 4 == 0 and kk == 0), stop=False)
                P.tensor.matmul(pd2[:, :], C.WUB[par][:, kk, :], C.XH[:, h, 512:1024], start=(step % 4 == 0 and kk == 0), stop=False)
            if r4 == 3:
                P.tensor.wait_ge(C.qB[d % 2], C.qB[d % 2].val)
                P.tensor.matmul(pd[:, :], C.LDBd[d % 2][:, :], C.LT2[:, 0:512], start=False, stop=True)
                P.tensor.matmul(pd2[:, :], C.LDBd[d % 2][:, :], C.LT2[:, 512:1024], start=False, stop=True)
            P.mark_last('tensor', C.cT)
            mmd_done[step] = C.cT.val
        buf = C.XT[d % 2]
        if d == 0:
            P.sync.wait_ge(C.cV, C.cV.val)
            P.sync.wait_ge(C.cA, C.cA.val)
        if d >= 2:
            P.sync.wait_ge(C.cV, ep_done[d - 2])
            P.sync.wait_ge(C.qD[d % 2], out_done[d - 2])
        P.sync.dma_start(out=buf, in_=src_dram[d]).then_inc(C.qE[d % 2], 16)
        P.vector.wait_ge(C.qE[d % 2], C.qE[d % 2].val)
        P.vector.wait_ge(C.cT, C.cT.val)
        if d == 0:
            P.vector.wait_ge(C.qS, C.qS.val)
        P.vector.scalar_tensor_tensor(buf[:, 0:512], pd[:, :], C.BDN[:, d:d + 1], buf[:, 0:512], ALU.add, ALU.add)
        P.vector.scalar_tensor_tensor(buf[:, 512:1024], pd2[:, :], C.BDN[:, d:d + 1], buf[:, 512:1024], ALU.add, ALU.add)
        ep_done[d] = C.cV.val
        P.gpsimd.wait_ge(C.cV, C.cV.val)
        P.gpsimd.dma_start(out=dst_dram[d], in_=buf).then_inc(C.qD[d % 2], 16)
        if halo_out is not None:
            P.gpsimd.dma_start(out=halo_out[:, d, :], in_=buf[:, 1021:1024]).then_inc(C.qD[d % 2], 16)
        out_done[d] = C.qD[d % 2].val


PP = 512
PP_CONVW = PP + 0
PP_CONVB = PP + 64
PP_VNW = PP + 80
PP_VNB = PP + 81
PP_MNW = PP + 82
PP_MNB = PP + 83
PP_GNW = PP + 84
PP_GNB = PP + 100
PP_BIG = PP + 116
PP_BOG = PP + 132
PP_LN1W = PP + 148
PP_LN1B = PP + 164
PP_LN2W = PP + 180
PP_LN2B = PP + 196
PP_BG = PP + 212
PP_CMS = PP + 213
PP_HM8 = PP + 221
PPN = 229
SM_SD = 741
SM_TMP = 757
SM_CARRY = 773
SM_HALO = 789
SM_HBT = 837


def emit_attn_b(kb, C, src_dram, dst_dram, hm_dram, aux):
    P = kb.prog
    SM = C.SMALL
    xcb, MEM, QB, OUTB = C.AB0
    T0 = C.XT2[:, 1024:1536].bitcast(BF16)
    T1 = C.XT2[:, 1536:2048].bitcast(BF16)
    S0 = SM[:, 0:512].bitcast(BF16)
    GAMS = C.XT2[0:16, 0:1024]

    def wslab(segs, wsrc, col, sem):
        for (dst, r0, r1) in segs:
            P.gpsimd.dma_start(
                out=dst,
                in_=wsrc[r0:r1, 128 * col:128 * (col + 1)].rearrange("(kc p) n -> p kc n", p=128),
            ).then_inc(sem, 16)

    WUB0q = [(C.WUB[0][:, 0:16, :], 0, 2048)]
    WUB1q = [(C.WUB[1][:, 0:16, :], 0, 2048)]
    WQFv = [view(C.WQF[0][:, :, :], BF16, (8, 128)), view(C.WQF[1][:, :, :], BF16, (8, 128))]
    SCFv = [view(C.SCF[0][:, :, :], BF16, (8, 128)), view(C.SCF[1][:, :, :], BF16, (8, 128))]
    WQFvq = [(WQFv[0][:, 0:8, :], 0, 1024), (WQFv[1][:, 0:8, :], 1024, 2048)]
    SCFvq = [(SCFv[0][:, 0:8, :], 0, 1024), (SCFv[1][:, 0:8, :], 1024, 2048)]

    def vtile(kc):
        return WQFv[kc // 8][:, kc % 8, :]

    def itile(kc):
        return SCFv[kc // 8][:, kc % 8, :]

    P.gpsimd.dma_start(out=SM[:, PP:PP + PPN], in_=aux["parpack"]).then_inc(C.qG, 16)
    P.gpsimd.dma_start(out=C.WG[:, :, :], in_=aux["Wg"].rearrange("(kc p) n -> p kc n", p=128)).then_inc(C.qG, 16)

    emit_ln(kb, C, src_dram, fused_w=SM[:, PP_LN1W:PP_LN1W + 16], fused_b=SM[:, PP_LN1B:PP_LN1B + 16],
            fused_rho=True, keep_rho=True)
    ln_done = C.cV.val

    P.gpsimd.wait_ge(C.qD[0], C.qD[0].val)
    P.gpsimd.wait_ge(C.qD[1], C.qD[1].val)
    P.gpsimd.collective_compute(
        "AllGather", ALU.bypass, replica_groups=[list(range(NCORES))],
        ins=[aux["halo_in"].ap().opt()], outs=[aux["halo_all"].ap().opt()],
    ).then_inc(C.cc, 1)
    P.gpsimd.wait_ge(C.cc, C.cc.val)
    HLD = C.XT2[:, 1024:1408]
    P.gpsimd.wait_ge(C.cV, C.cV.val)
    P.gpsimd.wait_ge(C.cA, C.cA.val)
    P.gpsimd.wait_ge(C.cT, C.cT.val)
    P.gpsimd.dma_start(out=HLD.rearrange("p (j k t) -> p j k t", j=8, k=16, t=3),
                       in_=aux["halo_all"].ap().rearrange("j p k t -> p j k t")).then_inc(C.qG, 16)
    P.vector.wait_ge(C.qG, C.qG.val)
    P.vector.memset(SM[:, SM_HALO:SM_HALO + 48], 0.0)
    for j in range(NCORES):
        P.vector.scalar_tensor_tensor(SM[:, SM_HALO:SM_HALO + 48], HLD[:, 48 * j:48 * (j + 1)],
                                      SM[:, PP_HM8 + j:PP_HM8 + j + 1], SM[:, SM_HALO:SM_HALO + 48],
                                      ALU.mult, ALU.add)
    P.tensor.wait_ge(C.cV, C.cV.val)
    P.tensor.matmul(C.PS[0][0:1, 0:48], C.ones_col_f32[:, :], SM[:, SM_HALO:SM_HALO + 48], start=True, stop=True)
    P.mark_last('tensor', C.cT)
    P.vector.wait_ge(C.cT, C.cT.val)
    SQH = C.XT2[:, 1408:1456]
    P.vector.tensor_tensor(SQH, SM[:, SM_HALO:SM_HALO + 48], SM[:, SM_HALO:SM_HALO + 48], ALU.mult)
    P.tensor.wait_ge(C.cV, C.cV.val)
    P.tensor.matmul(C.PS[1][0:1, 0:48], C.ones_col_f32[:, :], SQH, start=True, stop=True)
    P.mark_last('tensor', C.cT)
    P.vector.wait_ge(C.cT, C.cT.val)
    HS = C.XT2[0:1, 1456:1459]
    HQ = C.XT2[0:1, 1459:1462]
    P.vector.tensor_reduce(HS, C.PS[0][0:1, 0:48].rearrange("p (k t) -> p t k", k=16), axis=AXX, op=ALU.add)
    P.vector.tensor_reduce(HQ, C.PS[1][0:1, 0:48].rearrange("p (k t) -> p t k", k=16), axis=AXX, op=ALU.add)
    HM_ = C.XT2[0:1, 1462:1465]
    HR = C.XT2[0:1, 1465:1468]
    P.vector.tensor_scalar(HM_, HS, 1.0 / D, None, ALU.mult)
    P.vector.tensor_scalar(HQ, HQ, 1.0 / D, None, ALU.mult)
    P.vector.tensor_tensor(HS, HM_, HM_, ALU.mult)
    P.vector.tensor_tensor(HQ, HQ, HS, ALU.subtract)
    P.vector.tensor_scalar(HS, HQ, EPS, None, ALU.add)
    P.vector.reciprocal(HS, HS)
    P.vector.tensor_tensor(HQ, HQ, HS, ALU.mult)
    P.vector.tensor_scalar(HQ, HQ, EPS, None, ALU.add)
    P.scalar.wait_ge(C.cV, C.cV.val)
    P.scalar.activation(HQ, HQ, AF.Sqrt)
    P.scalar.activation(HR, HS, AF.Sqrt)
    P.vector.wait_ge(C.cA, C.cA.val)
    P.vector.reciprocal(HQ, HQ)
    P.vector.tensor_tensor(HR, HR, HQ, ALU.mult)
    P.tensor.wait_ge(C.cV, C.cV.val)
    P.tensor.matmul(C.PS[2][:, 0:3], C.ones_row_f32[:, :], HM_, start=True, stop=True)
    P.tensor.matmul(C.PS[3][:, 0:3], C.ones_row_f32[:, :], HR, start=True, stop=True)
    P.mark_last('tensor', C.cT)
    P.vector.wait_ge(C.cT, C.cT.val)
    HBTb = SM[:, SM_HBT:SM_HBT + 24].bitcast(BF16)
    for k in range(KC):
        hk = SM[:, SM_HALO + 3 * k:SM_HALO + 3 * (k + 1)]
        P.vector.tensor_tensor(hk, hk, C.PS[2][:, 0:3], ALU.subtract)
        P.vector.tensor_tensor(hk, hk, C.PS[3][:, 0:3], ALU.mult)
        P.vector.tensor_scalar(HBTb[:, 3 * k:3 * (k + 1)], hk, SM[:, PP_LN1W + k:PP_LN1W + k + 1],
                               SM[:, PP_LN1B + k:PP_LN1B + k + 1], ALU.mult, ALU.add)
    for k in range(KC):
        P.vector.tensor_copy(C.XB[:, k, 1:4], HBTb[:, 3 * k:3 * (k + 1)])
    P.vector.memset(C.XB[:, 0, 0:1], 0.0)

    for k in range(KC):
        acc = T0
        if k > 0:
            P.vector.wait_ge(C.cA, C.cA.val)
        P.vector.tensor_scalar(acc[:, :], C.XB[:, k, 1:1025], SM[:, PP_CONVW + 4 * k:PP_CONVW + 4 * k + 1],
                               None, ALU.mult)
        for j in (1, 2):
            P.vector.scalar_tensor_tensor(acc[:, :], C.XB[:, k, 1 + j:1025 + j],
                                          SM[:, PP_CONVW + 4 * k + j:PP_CONVW + 4 * k + j + 1], acc[:, :],
                                          ALU.mult, ALU.add)
        P.vector.scalar_tensor_tensor(acc[:, :], C.xb(k), SM[:, PP_CONVW + 4 * k + 3:PP_CONVW + 4 * k + 4],
                                      acc[:, :], ALU.mult, ALU.add)
        P.vector.tensor_scalar(acc[:, :], acc[:, :], SM[:, PP_CONVB + k:PP_CONVB + k + 1], None, ALU.add)
        P.scalar.wait_ge(C.cV, C.cV.val)
        P.scalar.activation(xcb[:, k, :], acc[:, :], AF.Silu)
    conv_done = C.cA.val

    P.tensor.wait_ge(C.cA, conv_done)
    P.tensor.wait_ge(C.qG, C.qG.val)
    for n in range(2):
        for k in range(KC):
            P.tensor.matmul(C.PS[4 + n][0:16, :], C.WG[:, k, :], xcb[:, k, 512 * n:512 * (n + 1)],
                            start=(k == 0), stop=(k == KC - 1))
    P.mark_last('tensor', C.cT)
    P.scalar.wait_ge(C.cT, C.cT.val)
    P.scalar.activation(GAMS[:, 0:512], C.PS[4][0:16, :], AF.Sigmoid, bias=SM[0:16, PP_BG:PP_BG + 1], scale=1.0)
    P.scalar.activation(GAMS[:, 512:1024], C.PS[5][0:16, :], AF.Sigmoid, bias=SM[0:16, PP_BG:PP_BG + 1], scale=1.0)
    P.gpsimd.wait_ge(C.cA, C.cA.val)
    P.gpsimd.dma_start(out=aux["gam"].ap(), in_=GAMS).then_inc(C.qG, 16)
    ONES16 = C.ones_col_f32[0:16, 0:1].to_broadcast([16, 512])
    P.vector.wait_ge(C.cA, C.cA.val)
    P.vector.tensor_tensor_scan(C.PS[6][0:16, :], GAMS[:, 0:512], ONES16, 1.0, ALU.mult, ALU.mult)
    P.vector.tensor_tensor_scan(C.PS[7][0:16, :], GAMS[:, 512:1024], ONES16, C.PS[6][0:16, 511:512], ALU.mult, ALU.mult)
    P.vector.wait_ge(C.qG, C.qG.val)
    P.vector.wait_ge(C.cP, C.cP.val)
    P.vector.tensor_copy(GAMS[:, 0:512], C.PS[6][0:16, :])
    P.vector.tensor_copy(GAMS[:, 512:1024], C.PS[7][0:16, :])
    P.gpsimd.wait_ge(C.cV, C.cV.val)
    P.gpsimd.dma_start(out=aux["gamc"].ap(), in_=GAMS).then_inc(C.qG, 16)
    P.gpsimd.dma_start(out=aux["abounce"].ap(), in_=GAMS[:, 1023:1024]).then_inc(C.qG, 16)

    def phead_norm(ps_a, ps_b, raw_tile, mmv, l2, w_ap, b_ap, out_tile, from_psum=True, war_cv=None,
                   war_stats_cv=None, war_stats_ca=None, stb_off=0, sq_tile=None):
        STBa = C.STB[0:1, stb_off:stb_off + 512]
        STBb = C.STB[0:1, stb_off + 512:stb_off + 1024]
        sq = C.MB if sq_tile is None else sq_tile
        if from_psum:
            P.scalar.wait_ge(C.cT, mmv)
            P.scalar.wait_ge(C.cV, C.cV.val if war_cv is None else war_cv)
            P.scalar.activation(raw_tile[:, 0:512], ps_a[:, :], AF.Identity)
            P.scalar.activation(raw_tile[:, 512:1024], ps_b[:, :], AF.Identity)
        if not l2:
            P.tensor.wait_ge(C.cA, C.cA.val if war_stats_ca is None else war_stats_ca)
            P.tensor.wait_ge(C.cV, C.cV.val if war_stats_cv is None else war_stats_cv)
            P.tensor.matmul(ps_a[0:1, :], C.invd_col_bf[:, :], raw_tile[:, 0:512], start=True, stop=True)
            P.tensor.matmul(ps_b[0:1, :], C.invd_col_bf[:, :], raw_tile[:, 512:1024], start=True, stop=True)
            P.mark_last('tensor', C.cT)
            P.scalar.wait_ge(C.cT, C.cT.val)
            P.scalar.activation(STBa, ps_a[0:1, :], AF.Identity)
            P.scalar.activation(STBb, ps_b[0:1, :], AF.Identity)
            P.tensor.wait_ge(C.cA, C.cA.val)
            P.tensor.matmul(ps_a[:, :], C.ones_row_bf[:, :], STBa, start=True, stop=True)
            P.tensor.matmul(ps_b[:, :], C.ones_row_bf[:, :], STBb, start=True, stop=True)
            P.mark_last('tensor', C.cT)
            P.vector.wait_ge(C.cT, C.cT.val)
            P.vector.tensor_tensor(raw_tile[:, 0:512], raw_tile[:, 0:512], ps_a[:, :], ALU.subtract)
            P.vector.tensor_tensor(raw_tile[:, 512:1024], raw_tile[:, 512:1024], ps_b[:, :], ALU.subtract)
        P.scalar.wait_ge(C.cV, C.cV.val)
        P.scalar.wait_ge(C.cT, C.cT.val)
        P.scalar.activation(sq[:, :], raw_tile[:, :], AF.Square)
        if l2:
            P.tensor.wait_ge(C.cV, C.cV.val if war_stats_cv is None else war_stats_cv)
        P.tensor.wait_ge(C.cA, C.cA.val)
        P.tensor.matmul(ps_a[0:1, :], C.ones_col_bf[:, :], sq[:, 0:512], start=True, stop=True)
        P.tensor.matmul(ps_b[0:1, :], C.ones_col_bf[:, :], sq[:, 512:1024], start=True, stop=True)
        P.mark_last('tensor', C.cT)
        epsap = C.eps_l2[0:1, 0:1] if l2 else C.eps_ln[0:1, 0:1]
        P.scalar.wait_ge(C.cT, C.cT.val)
        P.scalar.activation(ps_a[0:1, :], ps_a[0:1, :], AF.Ln, bias=epsap, scale=1.0)
        P.scalar.activation(ps_b[0:1, :], ps_b[0:1, :], AF.Ln, bias=epsap, scale=1.0)
        P.scalar.activation(STBa, ps_a[0:1, :], AF.Exp, bias=0.0, scale=-0.5)
        P.scalar.activation(STBb, ps_b[0:1, :], AF.Exp, bias=0.0, scale=-0.5)
        row = C.ones_row_bf if l2 else C.sqrthd_row_bf
        P.tensor.wait_ge(C.cA, C.cA.val)
        P.tensor.matmul(ps_a[:, :], row[:, :], STBa, start=True, stop=True)
        P.tensor.matmul(ps_b[:, :], row[:, :], STBb, start=True, stop=True)
        P.mark_last('tensor', C.cT)
        P.vector.wait_ge(C.cT, C.cT.val)
        P.vector.tensor_tensor(raw_tile[:, 0:512], raw_tile[:, 0:512], ps_a[:, :], ALU.mult)
        P.vector.tensor_tensor(raw_tile[:, 512:1024], raw_tile[:, 512:1024], ps_b[:, :], ALU.mult)
        if w_ap is not None:
            P.vector.tensor_scalar(out_tile[:, :], raw_tile[:, :], w_ap, b_ap, ALU.mult, ALU.add)
        elif out_tile is not raw_tile:
            P.vector.tensor_copy(out_tile[:, :], raw_tile[:, :])

    q_mm = {}
    k_mm = {}
    v_mm = {}
    i_mm = {}
    qn_done = {}
    kn_done = {}
    vn_done = {}
    ig_ev = {}
    gb_ev = {}
    u_done = {}
    scan_done = {}
    GBT = C.XT2[:, 0:1024]
    for h in range(NH):
        if h == 0:
            P.gpsimd.wait_ge(C.cT, C.cT.val)
            P.gpsimd.wait_ge(C.qC[0], C.qC[0].val)
            P.gpsimd.wait_ge(C.qC[1], C.qC[1].val)
        P.gpsimd.wait_ge(C.cT, q_mm.get(h - 1, 0))
        wslab(WUB0q, aux["Wq"], h, C.qW[0])
        ld_q = C.qW[0].val
        P.gpsimd.wait_ge(C.cT, k_mm.get(h - 1, 0))
        wslab(WUB1q, aux["Wk"], h, C.qW[1])
        ld_k = C.qW[1].val
        P.gpsimd.wait_ge(C.cT, v_mm.get(h - 1, 0))
        wslab(WQFvq, aux["Wv"], h, C.qB[0])
        ld_v = C.qB[0].val
        P.gpsimd.wait_ge(C.cT, i_mm.get(h - 1, 0))
        wslab(SCFvq, aux["Wig"], h, C.qB[1])
        ld_i = C.qB[1].val
        P.tensor.wait_ge(C.qB[1], ld_i)
        if h == 0:
            P.tensor.wait_ge(C.cV, C.cV.val)
        else:
            P.tensor.wait_ge(C.cA, gb_ev[h - 1])
        for k in range(KC):
            P.tensor.matmul(C.PS[6][:, :], itile(k), C.xb(k, 0, 512), start=(k == 0), stop=(k == KC - 1))
            P.tensor.matmul(C.PS[7][:, :], itile(k), C.xb(k, 512, 1024), start=(k == 0), stop=(k == KC - 1))
        P.mark_last('tensor', C.cT)
        i_mm[h] = C.cT.val
        P.tensor.wait_ge(C.qW[0], ld_q)
        if h > 0:
            P.tensor.wait_ge(C.cV, qn_done[h - 1])
        for k in range(KC):
            P.tensor.matmul(C.PS[0][:, :], C.WUB[0][:, k, :], C.xb(k, 0, 512), start=(k == 0), stop=(k == KC - 1))
            P.tensor.matmul(C.PS[1][:, :], C.WUB[0][:, k, :], C.xb(k, 512, 1024), start=(k == 0), stop=(k == KC - 1))
        P.mark_last('tensor', C.cT)
        q_mm[h] = C.cT.val
        P.tensor.wait_ge(C.qW[1], ld_k)
        if h > 0:
            P.tensor.wait_ge(C.cV, kn_done[h - 1])
        for k in range(KC):
            P.tensor.matmul(C.PS[2][:, :], C.WUB[1][:, k, :], C.xb(k, 0, 512), start=(k == 0), stop=(k == KC - 1))
            P.tensor.matmul(C.PS[3][:, :], C.WUB[1][:, k, :], C.xb(k, 512, 1024), start=(k == 0), stop=(k == KC - 1))
        P.mark_last('tensor', C.cT)
        k_mm[h] = C.cT.val
        P.tensor.wait_ge(C.qB[0], ld_v)
        if h > 0:
            P.tensor.wait_ge(C.cV, vn_done[h - 1])
        for k in range(KC):
            P.tensor.matmul(C.PS[4][:, :], vtile(k), C.xb(k, 0, 512), start=(k == 0), stop=(k == KC - 1))
            P.tensor.matmul(C.PS[5][:, :], vtile(k), C.xb(k, 512, 1024), start=(k == 0), stop=(k == KC - 1))
        P.mark_last('tensor', C.cT)
        v_mm[h] = C.cT.val

        P.scalar.wait_ge(C.cT, i_mm[h])
        if h > 0:
            P.scalar.wait_ge(C.cV, u_done[h - 1])
        P.scalar.activation(S0[:, 0:512], C.PS[6][:, :], AF.Sigmoid, bias=SM[:, PP_BIG + h:PP_BIG + h + 1], scale=1.0)
        P.scalar.activation(S0[:, 512:1024], C.PS[7][:, :], AF.Sigmoid, bias=SM[:, PP_BIG + h:PP_BIG + h + 1], scale=1.0)
        ig_ev[h] = C.cA.val
        P.gpsimd.wait_ge(C.cA, ig_ev[h])
        P.gpsimd.wait_ge(C.cT, C.cT.val)
        P.gpsimd.wait_ge(C.qG, C.qG.val)
        P.gpsimd.wait_ge(C.cV, C.cV.val)
        P.gpsimd.dma_start(out=C.XT2[0:1, 0:1024], in_=aux["gam"].ap()[h:h + 1, :]).then_inc(C.qR[0], 16)
        P.tensor.wait_ge(C.qR[0], C.qR[0].val)
        P.tensor.matmul(C.PS[6][:, :], C.ones_row_f32[:, :], C.XT2[0:1, 0:512], start=True, stop=True)
        P.tensor.matmul(C.PS[7][:, :], C.ones_row_f32[:, :], C.XT2[0:1, 512:1024], start=True, stop=True)
        P.mark_last('tensor', C.cT)
        P.scalar.wait_ge(C.cT, C.cT.val)
        P.scalar.activation(GBT[:, 0:512], C.PS[6][:, :], AF.Identity)
        P.scalar.activation(GBT[:, 512:1024], C.PS[7][:, :], AF.Identity)
        gb_ev[h] = C.cA.val

        phead_norm(C.PS[0], C.PS[1], QB[:, h, :], q_mm[h], True, None, None, QB[:, h, :], war_cv=0)
        qn_done[h] = C.cV.val
        phead_norm(C.PS[2], C.PS[3], T0, k_mm[h], True, None, None, T0,
                   war_cv=u_done.get(h - 1, 0))
        kn_done[h] = C.cV.val
        phead_norm(C.PS[4], C.PS[5], T1, v_mm[h], False,
                   SM[:, PP_VNW:PP_VNW + 1], SM[:, PP_VNB:PP_VNB + 1], T1,
                   war_cv=scan_done.get(h - 1, 0))
        vn_done[h] = C.cV.val
        P.vector.wait_ge(C.cA, ig_ev[h])
        P.vector.tensor_tensor(T1[:, :], T1[:, :], T0[:, :], ALU.mult)
        P.vector.tensor_tensor(T1[:, :], T1[:, :], S0[:, :], ALU.mult)
        u_done[h] = C.cV.val
        P.vector.wait_ge(C.cA, gb_ev[h])
        P.vector.tensor_tensor_scan(MEM[:, h, 0:512], GBT[:, 0:512], T1[:, 0:512], 0.0, ALU.mult, ALU.add)
        P.vector.tensor_tensor_scan(MEM[:, h, 512:1024], GBT[:, 512:1024], T1[:, 512:1024],
                                    MEM[:, h, 511:512], ALU.mult, ALU.add)
        scan_done[h] = C.cV.val

    P.vector.tensor_copy(S0[:, 0:16], MEM[:, :, 1023:1024].rearrange("p a b -> p (a b)"))
    P.gpsimd.wait_ge(C.cV, C.cV.val)
    P.gpsimd.dma_start(out=aux["sbounce"].ap(), in_=S0[:, 0:16]).then_inc(C.qG, 16)
    P.gpsimd.wait_ge(C.qG, C.qG.val)
    P.gpsimd.collective_compute(
        "AllGather", ALU.bypass, replica_groups=[list(range(NCORES))],
        ins=[aux["sbounce"].ap().opt()], outs=[aux["sa2_all"].ap().opt()],
    ).then_inc(C.cc, 1)
    P.gpsimd.collective_compute(
        "AllGather", ALU.bypass, replica_groups=[list(range(NCORES))],
        ins=[aux["abounce"].ap().opt()], outs=[aux["aa_all"].ap().opt()],
    ).then_inc(C.cc, 1)
    P.gpsimd.wait_ge(C.cc, C.cc.val)
    P.gpsimd.dma_start(out=C.XT2[0:1, 0:128], in_=aux["aa_all"].ap().rearrange("a b -> (a b)").unsqueeze(0)).then_inc(C.qG, 16)
    P.gpsimd.dma_start(out=C.XT2[0:1, 128:256], in_=aux["cmrow"][0:1, :]).then_inc(C.qG, 16)
    P.gpsimd.dma_start(out=C.XT2[0:1, 256:384], in_=aux["cmrow"][1:2, :]).then_inc(C.qG, 16)
    P.vector.wait_ge(C.qG, C.qG.val)
    P.vector.tensor_tensor(C.XT2[0:1, 0:128], C.XT2[0:1, 0:128], C.XT2[0:1, 128:256], ALU.mult)
    P.vector.tensor_tensor(C.XT2[0:1, 0:128], C.XT2[0:1, 0:128], C.XT2[0:1, 256:384], ALU.add)
    P.vector.tensor_copy(C.STB[0:1, 0:128], C.XT2[0:1, 0:128])
    P.tensor.wait_ge(C.cV, C.cV.val)
    P.tensor.matmul(C.PS[0][:, 0:128], C.ones_row_bf[:, :], C.STB[0:1, 0:128], start=True, stop=True)
    P.mark_last('tensor', C.cT)
    P.vector.wait_ge(C.cT, C.cT.val)
    P.vector.memset(SM[:, SM_CARRY:SM_CARRY + 16], 0.0)
    for j in range(NCORES):
        P.gpsimd.wait_ge(C.cV, C.cV.val)
        P.gpsimd.dma_start(out=SM[:, SM_SD:SM_SD + 16], in_=aux["sa2_all"].ap()[128 * j:128 * (j + 1), :]).then_inc(C.qR[1], 16)
        P.vector.wait_ge(C.qR[1], C.qR[1].val)
        P.vector.tensor_scalar(SM[:, SM_SD:SM_SD + 16], SM[:, SM_SD:SM_SD + 16],
                               SM[:, PP_CMS + j:PP_CMS + j + 1], None, ALU.mult)
        P.vector.tensor_tensor(SM[:, SM_CARRY:SM_CARRY + 16], SM[:, SM_CARRY:SM_CARRY + 16],
                               C.PS[0][:, 16 * j:16 * (j + 1)], ALU.mult)
        P.vector.tensor_tensor(SM[:, SM_CARRY:SM_CARRY + 16], SM[:, SM_CARRY:SM_CARRY + 16],
                               SM[:, SM_SD:SM_SD + 16], ALU.add)

    og_mm = {}
    og_sig = {}
    outb_done = {}
    fix_done = {}
    cv0 = C.cV.val
    ca0 = C.cA.val
    pair_cv = {0: cv0, 1: cv0}
    pair_ca = {0: ca0, 1: ca0}
    for h in range(NH):
        p = h % 2
        sa, sb = (C.PS[0], C.PS[1]) if p == 0 else (C.PS[4], C.PS[5])
        P.gpsimd.wait_ge(C.cT, C.cT.val)
        P.gpsimd.wait_ge(C.qG, C.qG.val)
        P.gpsimd.dma_start(out=C.XT2[0:1, 0:1024], in_=aux["gamc"].ap()[h:h + 1, :]).then_inc(C.qR[0], 16)
        P.tensor.wait_ge(C.qR[0], C.qR[0].val)
        P.tensor.wait_ge(C.cV, fix_done.get(h - 1, cv0))
        P.tensor.matmul(C.PS[6][:, :], C.ones_row_f32[:, :], C.XT2[0:1, 0:512], start=True, stop=True)
        P.tensor.matmul(C.PS[7][:, :], C.ones_row_f32[:, :], C.XT2[0:1, 512:1024], start=True, stop=True)
        P.mark_last('tensor', C.cT)
        gamc_mm = C.cT.val
        P.gpsimd.wait_ge(C.cT, og_mm.get(h - 1, 0))
        wslab(WUB0q, aux["Wog"], h, C.qW[0])
        P.tensor.wait_ge(C.qW[0], C.qW[0].val)
        P.tensor.wait_ge(C.cA, og_sig.get(h - 1, ca0))
        if h == 0:
            P.tensor.wait_ge(C.cV, cv0)
        for k in range(KC):
            P.tensor.matmul(C.PS[2][:, :], C.WUB[0][:, k, :], xcb[:, k, 0:512], start=(k == 0), stop=(k == KC - 1))
            P.tensor.matmul(C.PS[3][:, :], C.WUB[0][:, k, :], xcb[:, k, 512:1024], start=(k == 0), stop=(k == KC - 1))
        P.mark_last('tensor', C.cT)
        og_mm[h] = C.cT.val
        memv = MEM[:, h, :]
        P.vector.wait_ge(C.cT, gamc_mm)
        P.vector.scalar_tensor_tensor(memv[:, 0:512], C.PS[6][:, :], SM[:, SM_CARRY + h:SM_CARRY + h + 1],
                                      memv[:, 0:512], ALU.mult, ALU.add)
        P.vector.scalar_tensor_tensor(memv[:, 512:1024], C.PS[7][:, :], SM[:, SM_CARRY + h:SM_CARRY + h + 1],
                                      memv[:, 512:1024], ALU.mult, ALU.add)
        fix_done[h] = C.cV.val
        phead_norm(sa, sb, memv, 0, False,
                   SM[:, PP_MNW:PP_MNW + 1], SM[:, PP_MNB:PP_MNB + 1], T0, from_psum=False,
                   war_stats_cv=max(fix_done[h], pair_cv[p]), war_stats_ca=pair_ca[p], stb_off=1024 * p)
        P.vector.tensor_tensor(T0[:, :], T0[:, :], QB[:, h, :], ALU.mult)
        phead_norm(sa, sb, T0, 0, False,
                   SM[:, PP_GNW + h:PP_GNW + h + 1], SM[:, PP_GNB + h:PP_GNB + h + 1], T0,
                   from_psum=False, stb_off=1024 * p)
        pair_cv[p] = C.cV.val
        pair_ca[p] = C.cA.val
        P.scalar.wait_ge(C.cT, og_mm[h])
        P.scalar.wait_ge(C.cV, outb_done.get(h - 1, cv0))
        P.scalar.activation(S0[:, 0:512], C.PS[2][:, :], AF.Sigmoid, bias=SM[:, PP_BOG + h:PP_BOG + h + 1], scale=1.0)
        P.scalar.activation(S0[:, 512:1024], C.PS[3][:, :], AF.Sigmoid, bias=SM[:, PP_BOG + h:PP_BOG + h + 1], scale=1.0)
        og_sig[h] = C.cA.val
        P.vector.wait_ge(C.cA, og_sig[h])
        P.vector.tensor_tensor(OUTB[:, h, :], T0[:, :], S0[:, :], ALU.mult)
        outb_done[h] = C.cV.val
    p2_done = C.cV.val

    M0B = C.AR0[:, 0:1024]
    R0B = C.AR0[:, 1024:2048]
    Q0B = C.AR0[:, 2048:3072]
    P.tensor.wait_ge(C.cV, p2_done)
    P.tensor.matmul(C.PS[4][:, :], C.ones_row_bf[:, :], C.SM0[:, 0:512], start=True, stop=True)
    P.tensor.matmul(C.PS[5][:, :], C.ones_row_bf[:, :], C.SM0[:, 512:1024], start=True, stop=True)
    P.tensor.matmul(C.PS[6][:, :], C.ones_row_bf[:, :], C.SR0[:, 0:512], start=True, stop=True)
    P.tensor.matmul(C.PS[7][:, :], C.ones_row_bf[:, :], C.SR0[:, 512:1024], start=True, stop=True)
    P.mark_last('tensor', C.cT)
    P.vector.wait_ge(C.cT, C.cT.val)
    P.vector.tensor_copy(M0B[:, 0:512], C.PS[4][:, :])
    P.vector.tensor_copy(M0B[:, 512:1024], C.PS[5][:, :])
    P.vector.tensor_copy(R0B[:, 0:512], C.PS[6][:, :])
    P.vector.tensor_copy(R0B[:, 512:1024], C.PS[7][:, :])
    P.vector.tensor_tensor(Q0B, M0B, R0B, ALU.mult)
    wo_mm = {}
    ep = {}
    outw = {}
    for m in range(KC):
        par = m % 2
        P.gpsimd.wait_ge(C.cT, wo_mm.get(m - 2, C.cT.val if m < 2 else 0))
        wslab(WUB0q if par == 0 else WUB1q, aux["Wo"], m, C.qW[par])
        pd, pd2 = C.PS[0 + 2 * par], C.PS[1 + 2 * par]
        P.tensor.wait_ge(C.qW[par], C.qW[par].val)
        if m < 2:
            P.tensor.wait_ge(C.cV, C.cV.val)
        else:
            P.tensor.wait_ge(C.cV, ep.get(m - 2, 0))
        for k in range(NH):
            P.tensor.matmul(pd[:, :], (C.WUB[0] if par == 0 else C.WUB[1])[:, k, :], OUTB[:, k, 0:512],
                            start=(k == 0), stop=(k == NH - 1))
            P.tensor.matmul(pd2[:, :], (C.WUB[0] if par == 0 else C.WUB[1])[:, k, :], OUTB[:, k, 512:1024],
                            start=(k == 0), stop=(k == NH - 1))
        P.mark_last('tensor', C.cT)
        wo_mm[m] = C.cT.val
        buf = C.XT[par]
        if m >= 2:
            P.sync.wait_ge(C.cV, ep[m - 2])
            P.sync.wait_ge(C.qD[par], outw[m - 2])
        else:
            P.sync.wait_ge(C.cV, C.cV.val)
        P.sync.dma_start(out=buf, in_=src_dram[m]).then_inc(C.qE[par], 16)
        P.vector.wait_ge(C.qE[par], C.qE[par].val)
        P.vector.wait_ge(C.cT, C.cT.val)
        P.vector.tensor_tensor(buf, buf, R0B, ALU.mult)
        P.vector.tensor_tensor(buf, buf, Q0B, ALU.subtract)
        P.vector.tensor_tensor(buf[:, 0:512], buf[:, 0:512], pd[:, :], ALU.add)
        P.vector.tensor_tensor(buf[:, 512:1024], buf[:, 512:1024], pd2[:, :], ALU.add)
        ep[m] = C.cV.val
        P.gpsimd.wait_ge(C.cV, C.cV.val)
        P.gpsimd.dma_start(out=hm_dram[m], in_=buf).then_inc(C.qD[par], 16)
        outw[m] = C.qD[par].val

    emit_ln(kb, C, hm_dram, fused_w=SM[:, PP_LN2W:PP_LN2W + 16], fused_b=SM[:, PP_LN2B:PP_LN2B + 16])
    g_mm = {}
    ev2 = {}
    for m in range(FC):
        par = m % 2
        P.gpsimd.wait_ge(C.cT, g_mm.get(m - 1, C.cT.val if m < 1 else 0))
        wslab(WUB0q, aux["Wgate"], m, C.qW[0])
        wslab(WUB1q, aux["Wval"], m, C.qW[1])
        pg, pg2 = C.PS[0 + 2 * par], C.PS[1 + 2 * par]
        pv, pv2 = C.PS[4 + 2 * par], C.PS[5 + 2 * par]
        P.tensor.wait_ge(C.qW[0], C.qW[0].val)
        P.tensor.wait_ge(C.qW[1], C.qW[1].val)
        if m < 2:
            P.tensor.wait_ge(C.cV, C.cV.val)
        else:
            P.tensor.wait_ge(C.cV, ev2[m - 2])
        for k in range(KC):
            P.tensor.matmul(pg[:, :], C.WUB[0][:, k, :], C.xb(k, 0, 512), start=(k == 0), stop=(k == KC - 1))
            P.tensor.matmul(pg2[:, :], C.WUB[0][:, k, :], C.xb(k, 512, 1024), start=(k == 0), stop=(k == KC - 1))
            P.tensor.matmul(pv[:, :], C.WUB[1][:, k, :], C.xb(k, 0, 512), start=(k == 0), stop=(k == KC - 1))
            P.tensor.matmul(pv2[:, :], C.WUB[1][:, k, :], C.xb(k, 512, 1024), start=(k == 0), stop=(k == KC - 1))
        P.mark_last('tensor', C.cT)
        g_mm[m] = C.cT.val
        P.scalar.wait_ge(C.cT, C.cT.val)
        if m > 0:
            P.scalar.wait_ge(C.cV, ev2[m - 1])
        P.scalar.activation(S0[:, 0:512], pg[:, :], AF.Silu)
        P.scalar.activation(S0[:, 512:1024], pg2[:, :], AF.Silu)
        P.vector.wait_ge(C.cA, C.cA.val)
        P.vector.tensor_tensor(C.XH[:, m, 0:512], S0[:, 0:512], pv[:, :], ALU.mult)
        P.vector.tensor_tensor(C.XH[:, m, 512:1024], S0[:, 512:1024], pv2[:, :], ALU.mult)
        ev2[m] = C.cV.val
    mlp_up_done = C.cV.val

    wom = {}
    epo = {}
    outd = {}
    for d in range(KC):
        pd, pd2 = C.PS[0 + 2 * (d % 2)], C.PS[1 + 2 * (d % 2)]
        for r4 in range(4):
            par = r4 % 2
            step = 4 * d + r4
            P.gpsimd.wait_ge(C.cT, wom.get(step - 2, C.cT.val if step < 2 else 0))
            src_ap = aux["Wout"][2048 * r4:2048 * (r4 + 1), 128 * d:128 * (d + 1)]
            for q in range(4):
                P.gpsimd.dma_start(
                    out=C.WUB[par][:, 4 * q:4 * (q + 1), :],
                    in_=src_ap[512 * q:512 * (q + 1), :].rearrange("(kc p) n -> p kc n", p=128),
                ).then_inc(C.qW[par], 16)
            P.tensor.wait_ge(C.qW[par], C.qW[par].val)
            if step < 2:
                P.tensor.wait_ge(C.cV, mlp_up_done)
            if r4 == 0 and d >= 2:
                P.tensor.wait_ge(C.cV, epo[d - 2])
            for kk in range(KC):
                h = KC * r4 + kk
                st_ = (r4 == 0 and kk == 0)
                sp_ = (r4 == 3 and kk == KC - 1)
                P.tensor.matmul(pd[:, :], C.WUB[par][:, kk, :], C.XH[:, h, 0:512], start=st_, stop=sp_)
                P.tensor.matmul(pd2[:, :], C.WUB[par][:, kk, :], C.XH[:, h, 512:1024], start=st_, stop=sp_)
            P.mark_last('tensor', C.cT)
            wom[step] = C.cT.val
        if d >= 1:
            P.sync.wait_ge(C.cV, epo[d - 1])
            P.sync.wait_ge(C.qD[0], outd[d - 1])
        else:
            P.sync.wait_ge(C.cV, C.cV.val)
        P.sync.dma_start(out=C.XT[0], in_=src_dram[d]).then_inc(C.qE[0], 16)
        P.sync.dma_start(out=C.XT[1], in_=hm_dram[d]).then_inc(C.qE[1], 16)
        P.vector.wait_ge(C.qE[0], C.qE[0].val)
        P.vector.wait_ge(C.qE[1], C.qE[1].val)
        P.vector.wait_ge(C.cT, C.cT.val)
        P.vector.tensor_tensor(C.XT[0], C.XT[0], C.XT[1], ALU.add)
        P.vector.tensor_tensor(C.XT[0][:, 0:512], C.XT[0][:, 0:512], pd[:, :], ALU.add)
        P.vector.tensor_tensor(C.XT[0][:, 512:1024], C.XT[0][:, 512:1024], pd2[:, :], ALU.add)
        epo[d] = C.cV.val
        P.gpsimd.wait_ge(C.cV, C.cV.val)
        P.gpsimd.dma_start(out=dst_dram[d], in_=C.XT[0]).then_inc(C.qD[0], 16)
        outd[d] = C.qD[0].val


def prep_attnb_pack(inp, core):
    pp = np.zeros((128, PPN), dtype=np.float32)
    def col(c):
        return c - PP
    cw = inp["L0_attn_b_conv_w"].reshape(D, 4)
    for k in range(KC):
        for j in range(4):
            pp[:, col(PP_CONVW) + 4 * k + j] = cw[128 * k:128 * (k + 1), j]
        pp[:, col(PP_CONVB) + k] = inp["L0_attn_b_conv_b"][128 * k:128 * (k + 1)]
        pp[:, col(PP_LN1W) + k] = inp["L0_attn_b_ln1w"][128 * k:128 * (k + 1)]
        pp[:, col(PP_LN1B) + k] = inp["L0_attn_b_ln1b"][128 * k:128 * (k + 1)]
        pp[:, col(PP_LN2W) + k] = inp["L0_attn_b_ln2w"][128 * k:128 * (k + 1)]
        pp[:, col(PP_LN2B) + k] = inp["L0_attn_b_ln2b"][128 * k:128 * (k + 1)]
    pp[:, col(PP_VNW)] = inp["L0_attn_b_vnw"]
    pp[:, col(PP_VNB)] = inp["L0_attn_b_vnb"]
    pp[:, col(PP_MNW)] = inp["L0_attn_b_mnw"]
    pp[:, col(PP_MNB)] = inp["L0_attn_b_mnb"]
    for h in range(NH):
        pp[:, col(PP_GNW) + h] = inp["L0_attn_b_gnw"][128 * h:128 * (h + 1)]
        pp[:, col(PP_GNB) + h] = inp["L0_attn_b_gnb"][128 * h:128 * (h + 1)]
        pp[:, col(PP_BIG) + h] = inp["L0_attn_b_big"][128 * h:128 * (h + 1)]
        pp[:, col(PP_BOG) + h] = inp["L0_attn_b_bog"][128 * h:128 * (h + 1)]
    pp[0:16, col(PP_BG)] = inp["L0_attn_b_bg"]
    for j in range(NCORES):
        pp[:, col(PP_CMS) + j] = 1.0 if j < core else 0.0
        pp[:, col(PP_HM8) + j] = 1.0 if j == core - 1 else 0.0
    cmrow = np.zeros((2, 128), dtype=np.float32)
    for j in range(NCORES):
        for h in range(NH):
            cmrow[0, 16 * j + h] = 1.0 if j < core else 0.0
    cmrow[1] = 1.0 - cmrow[0]
    return np.ascontiguousarray(pp), cmrow


def declare_attnb_aux(kb, with_inputs=True):
    aux = {}
    for nm, shp in [("Wq", [D, D]), ("Wk", [D, D]), ("Wv", [D, D]), ("Wig", [D, D]),
                    ("Wog", [D, D]), ("Wo", [D, D]), ("Wg", [D, NH]),
                    ("Wgate", [D, FF]), ("Wval", [D, FF]), ("Wout", [FF, D])]:
        aux[nm] = kb.din("b_" + nm, shp)
    aux["parpack"] = kb.din("parpack", [128, PPN])
    aux["cmrow"] = kb.din("cmrow", [2, 128])
    aux["gam"] = kb.dscratch("gam", [NH, T])
    aux["gamc"] = kb.dscratch("gamc", [NH, T])
    aux["halo_in"] = kb.dscratch("halo_in", [128, KC, 3])
    aux["halo_all"] = kb.dscratch("halo_all", [NCORES, 128, KC, 3])
    aux["sbounce"] = kb.dscratch("sbounce", [128, NH])
    aux["sa2_all"] = kb.dscratch("sa2_all", [NCORES * 128, NH])
    aux["abounce"] = kb.dscratch("abounce", [NH, 1])
    aux["aa_all"] = kb.dscratch("aa_all", [NCORES * NH, 1])
    return aux


A_XNB, A_TT, A_AOUT, A_COUT = 0, 1024, 2048, 3072
A_DOUT, A_D2, A_EOUT, A_E2 = 4096, 5120, 6144, 7168
A_CQK, A_CV = 8192, 9216
A_O1B = 10240
A_O2T1, A_O2T2, A_LEP2, A_Y = 12288, 13312, 14336, 15360
A_PR1, A_S1F, A_P1B = 16384, 17408, 17920
A_S2F, A_S2B, A_P2T, A_RST = 18432, 18944, 19200, 19456
A_IBF, A_BD, A_BDP = 19472, 19536, 19920
A_W0B, A_W1, A_ALNW, A_ALNB, A_PB = 20176, 20272, 20275, 20403, 20531
A_M2, A_SLAB, A_YC, A_ST16, A_SQ16 = 20532, 21044, 22068, 22580, 22624
A_IF32 = 22884
A_PR1F, A_O1F, A_L2F = 23040, 25088, 27136
A_B0, A_B1 = 29184, 29248
A_SQF = 2048
A_YTR, A_SCR, A_X1T, A_STG = 0, 8192, 9216, 10240


def prep_attna_consts(inp):
    qw = inp["L0_attn_a_qkv_w"]
    pw = inp["L0_attn_a_proj_w"]
    cw0 = inp["L0_attn_a_conv_w0"]
    cw1 = inp["L0_attn_a_conv_w1"]
    def bd(colmap, umatch):
        m = np.zeros((128, 128), dtype=np.float32)
        for u in range(8):
            for c in range(16):
                for oc, (uu, col) in colmap(u):
                    pass
        return m
    bdA = np.zeros((128, 128), np.float32)
    bdC = np.zeros((128, 128), np.float32)
    bdD = np.zeros((128, 128), np.float32)
    bdD2 = np.zeros((128, 128), np.float32)
    bdE = np.zeros((128, 128), np.float32)
    bdE2 = np.zeros((128, 128), np.float32)
    for u in range(8):
        for c in range(16):
            r = 16 * u + c
            for oc in range(8):
                bdA[r, 16 * u + oc] = qw[c, 0 + oc]
                bdA[r, 16 * u + 8 + oc] = qw[c, 16 + oc]
            for oc in range(16):
                bdC[r, 16 * u + oc] = qw[c, 32 + oc]
            if u < 4:
                for oc in range(8):
                    bdD[r, 32 * u + oc] = qw[c, 8 + oc]
                    bdE[r, 32 * u + oc] = qw[c, 24 + oc]
            else:
                for oc in range(8):
                    bdD2[r, 32 * (u - 4) + oc] = qw[c, 8 + oc]
                    bdE2[r, 32 * (u - 4) + oc] = qw[c, 24 + oc]
    bdP1 = np.zeros((128, 128), np.float32)
    bdPL = np.zeros((128, 128), np.float32)
    for u in range(8):
        for c in range(8):
            for oc in range(16):
                bdP1[16 * u + c, 16 * u + oc] = pw[c, oc]
                bdPL[16 * u + 8 + c, 16 * u + oc] = pw[8 + c, oc]
    bdPG = np.zeros((128, 128), np.float32)
    bdPG2 = np.zeros((128, 128), np.float32)
    for r in range(4):
        for c in range(8):
            for oc in range(16):
                bdPG[32 * r + c, 16 * r + oc] = pw[8 + c, oc]
                bdPG2[32 * r + c, 16 * (r + 4) + oc] = pw[8 + c, oc]
    W0 = np.zeros((128, 192), np.float32)
    for du in range(3):
        for i in range(8):
            for c in range(8):
                W0[:, 64 * du + 8 * i + c] = cw0[c, 0, 1, du]
    W1 = np.zeros((128, 3), np.float32)
    for dt in range(3):
        for u in range(8):
            for c in range(8):
                W1[16 * u + 8 + c, dt] = cw1[c, 0, dt, 1]
    m2 = np.zeros((32, 32), np.float32)
    for a in range(32):
        for b in range(32):
            if a % 4 != b % 4:
                m2[a, b] = -1e9
    M2 = np.tile(m2, (4, 16))
    alnw = np.zeros((128, 128), np.float32)
    alnb = np.zeros((128, 128), np.float32)
    for u in range(8):
        for c in range(16):
            alnw[:, 16 * u + c] = inp["L0_attn_a_ln_w"][c]
            alnb[:, 16 * u + c] = inp["L0_attn_a_ln_b"][c]
    pb = np.zeros((128, 1), np.float32)
    for u in range(8):
        for c in range(16):
            pb[16 * u + c, 0] = inp["L0_attn_a_proj_b"][c]
    b0 = np.zeros((128, 64), np.float32)
    for i in range(8):
        for c in range(8):
            b0[:, 8 * i + c] = inp["L0_attn_a_conv_b0"][c]
    b1 = np.zeros((128, 1), np.float32)
    for u in range(8):
        for c in range(8):
            b1[16 * u + 8 + c, 0] = inp["L0_attn_a_conv_b1"][c]
    bds = np.stack([bdA, bdC, bdD, bdD2, bdE, bdE2], 0)
    bdp = np.stack([bdP1, bdPG, bdPG2, bdPL], 0)
    return {"a_bds": bds, "a_bdp": bdp, "a_w0": W0, "a_w1": W1, "a_m2": M2,
            "a_alnw": alnw, "a_alnb": alnb, "a_pb": pb, "a_b0": b0, "a_b1": b1}


def declare_attna_aux(kb):
    aux = {}
    aux["a_bds"] = kb.din("a_bds", [6, 128, 128])
    aux["a_bdp"] = kb.din("a_bdp", [4, 128, 128])
    aux["a_w0"] = kb.din("a_w0", [128, 192])
    aux["a_w1"] = kb.din("a_w1", [128, 3])
    aux["a_m2"] = kb.din("a_m2", [128, 512])
    aux["a_alnw"] = kb.din("a_alnw", [128, 128])
    aux["a_alnb"] = kb.din("a_alnb", [128, 128])
    aux["a_pb"] = kb.din("a_pb", [128, 1])
    aux["a_b0"] = kb.din("a_b0", [128, 64])
    aux["a_b1"] = kb.din("a_b1", [128, 1])
    aux["I128"] = kb.din("I128", [128, 128])
    aux["xrows"] = kb.din("xrows", [T, D])
    aux["a2a_in"] = kb.dscratch("a2a_in", [8, 8, 128, 256], BF16)
    aux["a2a_out"] = kb.dscratch("a2a_out", [8, 8, 128, 256], BF16)
    return aux


def emit_attn_a(kb, C, aux, x1_dram, chunk_cb=None):
    P = kb.prog
    A = C.AR0
    Ab = A[:, :].bitcast(BF16)

    def bft(c0, n):
        return Ab[:, 2 * c0:2 * c0 + n]

    def f32t(c0, n):
        return A[:, c0:c0 + n]

    XNB = bft(A_XNB, 2048)
    TT_ = bft(A_TT, 2048)
    AOUT = bft(A_AOUT, 2048)
    COUT = bft(A_COUT, 2048)
    DOUT, D2OUT = bft(A_DOUT, 2048), bft(A_D2, 2048)
    EOUT, E2OUT = bft(A_EOUT, 2048), bft(A_E2, 2048)
    CQK, CVt = bft(A_CQK, 2048), bft(A_CV, 2048)
    O1B = bft(A_O1B, 2048)
    O2T1, O2T2 = bft(A_O2T1, 2048), bft(A_O2T2, 2048)
    LEP2, Yt = bft(A_LEP2, 2048), bft(A_Y, 2048)
    PR1 = bft(A_PR1, 2048)
    S1F = f32t(A_S1F, 512)
    P1B = bft(A_P1B, 1024)
    S2F = f32t(A_S2F, 512)
    S2B = bft(A_S2B, 1024)
    P2T = bft(A_P2T, 512)
    RST = f32t(A_RST, 16)
    IBF = bft(A_IBF, 128)
    BD = [bft(A_BD + 64 * i, 128) for i in range(6)]
    BDP = [bft(A_BDP + 64 * i, 128) for i in range(4)]
    W0B = bft(A_W0B, 192)
    W1T = f32t(A_W1, 3)
    ALNW, ALNB = f32t(A_ALNW, 128), f32t(A_ALNB, 128)
    PBT = f32t(A_PB, 1)
    M2T = f32t(A_M2, 512)
    SLAB = bft(A_SLAB, 2048)
    YC = f32t(A_YC, 512)
    ST32 = f32t(A_ST16, 32)
    SQ16 = bft(A_SQ16, 512)
    IF32 = f32t(A_IF32, 128)
    SQF = f32t(A_SQF, 2048)
    PR1F = f32t(A_PR1F, 2048)
    O1F = f32t(A_O1F, 2048)
    L2F = f32t(A_L2F, 2048)
    B0T = f32t(A_B0, 64)
    B1T = f32t(A_B1, 1)

    for i in range(6):
        P.gpsimd.dma_start(out=BD[i], in_=aux["a_bds"][i]).then_inc(C.qG, 16)
    for i in range(4):
        P.gpsimd.dma_start(out=BDP[i], in_=aux["a_bdp"][i]).then_inc(C.qG, 16)
    P.gpsimd.dma_start(out=IBF, in_=aux["I128"]).then_inc(C.qG, 16)
    P.gpsimd.dma_start(out=W0B, in_=aux["a_w0"]).then_inc(C.qG, 16)
    P.sync.dma_start(out=W1T, in_=aux["a_w1"]).then_inc(C.qS, 16)
    P.sync.dma_start(out=M2T, in_=aux["a_m2"]).then_inc(C.qS, 16)
    P.sync.dma_start(out=ALNW, in_=aux["a_alnw"]).then_inc(C.qS, 16)
    P.sync.dma_start(out=ALNB, in_=aux["a_alnb"]).then_inc(C.qS, 16)
    P.sync.dma_start(out=PBT, in_=aux["a_pb"]).then_inc(C.qS, 16)
    P.sync.dma_start(out=IF32, in_=aux["I128"]).then_inc(C.qS, 16)
    P.sync.dma_start(out=B0T, in_=aux["a_b0"]).then_inc(C.qS, 16)
    P.sync.dma_start(out=B1T, in_=aux["a_b1"]).then_inc(C.qS, 16)
    pre_g, pre_s = C.qG.val, C.qS.val

    for sb in range(8):
        P.sync.wait_ge(C.cV, C.cV.val)
        P.sync.wait_ge(C.cT, C.cT.val)
        P.sync.dma_start(out=C.XT2[:, :], in_=aux["xrows"][128 * sb:128 * (sb + 1), :]).then_inc(C.qE[0], 16)
        P.vector.wait_ge(C.qE[0], C.qE[0].val)
        if sb == 0:
            P.vector.wait_ge(C.qS, pre_s)
            P.vector.wait_ge(C.qG, pre_g)
        P.vector.tensor_reduce(RST[:, 0:1], C.XT2[:, :], axis=AXX, op=ALU.add)
        P.vector.tensor_scalar(RST[:, 0:1], RST[:, 0:1], 1.0 / D, None, ALU.mult)
        P.vector.tensor_scalar(C.XT2[:, :], C.XT2[:, :], RST[:, 0:1], None, ALU.subtract)
        P.vector.tensor_tensor(SQF, C.XT2[:, :], C.XT2[:, :], ALU.mult)
        P.vector.tensor_reduce(RST[:, 1:2], SQF, axis=AXX, op=ALU.add)
        P.vector.tensor_scalar(RST[:, 1:2], RST[:, 1:2], 1.0 / D, None, ALU.mult)
        P.vector.tensor_scalar(RST[:, 1:2], RST[:, 1:2], EPS, None, ALU.add)
        P.scalar.wait_ge(C.cV, C.cV.val)
        P.scalar.activation(RST[:, 1:2], RST[:, 1:2], AF.Sqrt)
        P.vector.wait_ge(C.cA, C.cA.val)
        P.vector.reciprocal(RST[:, 1:2], RST[:, 1:2])
        P.vector.tensor_scalar(C.XT2[:, :], C.XT2[:, :], RST[:, 1:2], None, ALU.mult)
        P.vector.tensor_copy(XNB, C.XT2[:, :])
        P.vector.memset(O1F[:, :], 0.0)
        P.vector.memset(O1B[:, :], 0.0)
        ln_ok = C.cV.val
        if chunk_cb is not None:
            chunk_cb()

        tb_ev = []
        for g in range(4):
            P.tensor.wait_ge(C.cV, ln_ok)
            for q in range(4):
                j = 4 * g + q
                P.tensor.matmul(C.PS[g % 4][:, 128 * q:128 * (q + 1)], XNB[:, 128 * j:128 * (j + 1)], IBF,
                                start=True, stop=True)
            P.mark_last('tensor', C.cT)
            P.vector.wait_ge(C.cT, C.cT.val)
            P.vector.tensor_copy(TT_[:, 512 * g:512 * (g + 1)], C.PS[g % 4][:, :])
            tb_ev.append(C.cV.val)

        outs = [AOUT, COUT, DOUT, D2OUT, EOUT, E2OUT]
        bd_ev = []
        for ci in range(6):
            for g in range(4):
                i = 4 * ci + g
                P.tensor.wait_ge(C.cV, tb_ev[g] if i < 4 else bd_ev[i - 4])
                for q in range(4):
                    j = 4 * g + q
                    P.tensor.matmul(C.PS[4 + g % 4][:, 128 * q:128 * (q + 1)], BD[ci],
                                    TT_[:, 128 * j:128 * (j + 1)], start=True, stop=True)
                P.mark_last('tensor', C.cT)
                P.vector.wait_ge(C.cT, C.cT.val)
                P.vector.tensor_copy(outs[ci][:, 512 * g:512 * (g + 1)], C.PS[4 + g % 4][:, :])
                bd_ev.append(C.cV.val)

        cp_ev = []
        for si, (srcT, dstT) in enumerate(((AOUT, CQK), (COUT, CVt))):
            for g in range(4):
                i = 4 * si + g
                P.tensor.wait_ge(C.cV, bd_ev[4 * si + g] if i < 4 else cp_ev[i - 4])
                for q in range(4):
                    j = 4 * g + q
                    P.tensor.matmul(C.PS[g % 4][:, 128 * q:128 * (q + 1)], srcT[:, 128 * j:128 * (j + 1)], IBF,
                                    start=True, stop=True)
                P.mark_last('tensor', C.cT)
                P.vector.wait_ge(C.cT, C.cT.val)
                P.vector.tensor_copy(dstT[:, 512 * g:512 * (g + 1)], C.PS[g % 4][:, :])
                cp_ev.append(C.cV.val)


        cv_ap = CVt[:, :].rearrange("p (j u c) -> p j u c", j=16, u=8, c=16)
        for uh in range(2):
            Dsrc = DOUT if uh == 0 else D2OUT
            Esrc = EOUT if uh == 0 else E2OUT
            P.tensor.wait_ge(C.cV, C.cV.val)
            for j in range(16):
                for r in range(4):
                    for band in range(4):
                        P.tensor.matmul(
                            C.PS[4 + r][32 * band:32 * band + 32, 32 * j:32 * j + 32],
                            Dsrc[32 * r:32 * r + 8, 128 * j + 32 * band:128 * j + 32 * band + 32],
                            Esrc[32 * r:32 * r + 8, 128 * j + 32 * band:128 * j + 32 * band + 32],
                            start=True, stop=True, tile_position=(32 * r, 32 * band),
                        )
            P.mark_last('tensor', C.cT)
            gram_done = C.cT.val
            O2 = O2T1 if uh == 0 else O2T2
            for r in range(4):
                P.vector.wait_ge(C.cT, gram_done)
                P.vector.tensor_tensor(S2F[:, :], C.PS[4 + r][:, :], M2T, ALU.add)
                P.vector.tensor_reduce(RST[:, 0:16], S2F[:, :].rearrange("p (j b) -> p j b", j=16),
                                       axis=AXX, op=ALU.max)
                P.vector.tensor_tensor(S2F[:, :].rearrange("p (j b) -> p j b", j=16),
                                       S2F[:, :].rearrange("p (j b) -> p j b", j=16),
                                       RST[:, 0:16].unsqueeze(2).to_broadcast([128, 16, 32]), ALU.subtract)
                P.scalar.wait_ge(C.cV, C.cV.val)
                P.scalar.activation(S2B[:, 0:512], S2F[:, :], AF.Exp, bias=0.0, scale=float(1.0 / np.sqrt(8.0)))
                P.vector.wait_ge(C.cA, C.cA.val)
                P.vector.tensor_reduce(RST[:, 0:16], S2B[:, 0:512].rearrange("p (j b) -> p j b", j=16),
                                       axis=AXX, op=ALU.add)
                P.vector.reciprocal(RST[:, 0:16], RST[:, 0:16])
                P.vector.tensor_copy(SQ16[:, 0:16], RST[:, 0:16])
                P.vector.tensor_tensor(S2B[:, 0:512].rearrange("p (j b) -> p j b", j=16),
                                       S2B[:, 0:512].rearrange("p (j b) -> p j b", j=16),
                                       SQ16[:, 0:16].unsqueeze(2).to_broadcast([128, 16, 32]), ALU.mult)
                P.vector.wait_ge(C.cT, C.cT.val)
                P.vector.transpose(P2T[:, :], S2B[:, 0:512])
                P.tensor.wait_ge(C.cV, C.cV.val)
                for j in range(16):
                    for band in range(4):
                        P.tensor.matmul(
                            C.PS[band][32 * r:32 * r + 8, 32 * j:32 * j + 32],
                            cv_ap[32 * band:32 * band + 32, j, 4 * uh + r, 8:16],
                            P2T[32 * band:32 * band + 32, 32 * j:32 * j + 32],
                            start=True, stop=True, tile_position=(32 * band, 32 * r),
                        )
                P.mark_last('tensor', C.cT)
            P.vector.wait_ge(C.cT, C.cT.val)
            for band in range(4):
                P.vector.tensor_copy(
                    O2[:, :].rearrange("p (j rl) -> p j rl", j=16)[:, :, 32 * band:32 * band + 32],
                    C.PS[band][:, :].rearrange("p (j b) -> p j b", j=16))
        for g in range(4):
            q_ap = CQK[:, 512 * g:512 * (g + 1)].rearrange("p (j u c) -> p j u c", j=4, u=8, c=16)
            qq = q_ap[:, :, :, 0:8]
            kk_ = q_ap[:, :, :, 8:16]
            for jj in range(4):
                P.vector.tensor_tensor(PR1F[:, 512 * jj:512 * (jj + 1)].rearrange("p (a b c) -> p a b c", a=8, b=8, c=8),
                                       qq[:, jj].unsqueeze(2).to_broadcast([128, 8, 8, 8]),
                                       kk_[:, jj].unsqueeze(1).to_broadcast([128, 8, 8, 8]), ALU.mult)
            P.vector.tensor_reduce(S1F[:, 0:256],
                                   PR1F[:, 0:2048].rearrange("p (x c) -> p x c", c=8), axis=AXX, op=ALU.add)
            P.vector.tensor_reduce(ST32[:, 0:32], S1F[:, 0:256].rearrange("p (x b) -> p x b", b=8),
                                   axis=AXX, op=ALU.max)
            P.vector.tensor_tensor(S1F[:, 0:256].rearrange("p (x b) -> p x b", b=8),
                                   S1F[:, 0:256].rearrange("p (x b) -> p x b", b=8),
                                   ST32[:, 0:32].unsqueeze(2).to_broadcast([128, 32, 8]), ALU.subtract)
            P.scalar.wait_ge(C.cV, C.cV.val)
            P.scalar.activation(P1B[:, 256 * g:256 * (g + 1)], S1F[:, 0:256], AF.Exp,
                                bias=0.0, scale=float(1.0 / np.sqrt(8.0)))
            P.vector.wait_ge(C.cA, C.cA.val)
            P.vector.tensor_reduce(ST32[:, 0:32], P1B[:, 256 * g:256 * (g + 1)].rearrange("p (x b) -> p x b", b=8),
                                   axis=AXX, op=ALU.add)
            P.vector.reciprocal(ST32[:, 0:32], ST32[:, 0:32])
            P.vector.tensor_copy(SQ16[:, 0:32], ST32[:, 0:32])
            P.vector.tensor_tensor(P1B[:, 256 * g:256 * (g + 1)].rearrange("p (x b) -> p x b", b=8),
                                   P1B[:, 256 * g:256 * (g + 1)].rearrange("p (x b) -> p x b", b=8),
                                   SQ16[:, 0:32].unsqueeze(2).to_broadcast([128, 32, 8]), ALU.mult)
            p_ap = P1B[:, 256 * g:256 * (g + 1)].rearrange("p (j i b) -> p j i b", j=4, i=8, b=8)
            v1 = CVt[:, 512 * g:512 * (g + 1)].rearrange("p (j u c) -> p j u c", j=4, u=8, c=16)[:, :, :, 0:8]
            for jj in range(4):
                P.vector.tensor_tensor(PR1F[:, 512 * jj:512 * (jj + 1)].rearrange("p (a c b) -> p a c b", a=8, c=8, b=8),
                                       p_ap[:, jj].unsqueeze(2).to_broadcast([128, 8, 8, 8]),
                                       v1[:, jj].rearrange("p b c -> p c b").unsqueeze(1).to_broadcast([128, 8, 8, 8]),
                                       ALU.mult)
            o_ap = O1F[:, 512 * g:512 * (g + 1)].rearrange("p (j u c) -> p j u c", j=4, u=8, c=16)
            P.vector.tensor_reduce(o_ap[:, :, :, 0:8], PR1F[:, 0:2048].rearrange("p (x b) -> p x b", b=8),
                                   axis=AXX, op=ALU.add)
        cv_ap = CVt[:, :].rearrange("p (j u c) -> p j u c", j=16, u=8, c=16)
        o_all = O1F[:, :].rearrange("p (j u c) -> p j u c", j=16, u=8, c=16)
        for du, (o_lo, o_hi, v_lo) in ((0, (1, 8, 0)), (1, (0, 8, 0)), (2, (0, 7, 1))):
            n_u = o_hi - o_lo
            w0_ap = W0B[:, 64 * du:64 * du + 64].rearrange("p (u c) -> p u c", u=8, c=8)[:, o_lo:o_hi, :]
            tmp = PR1F[:, 0:16 * n_u * 8].rearrange("p (j u c) -> p j u c", j=16, u=n_u, c=8)
            P.vector.tensor_tensor(tmp, cv_ap[:, :, v_lo:v_lo + n_u, 0:8],
                                   w0_ap.unsqueeze(1).to_broadcast([128, 16, n_u, 8]), ALU.mult)
            P.vector.tensor_tensor(o_all[:, :, o_lo:o_hi, 0:8], o_all[:, :, o_lo:o_hi, 0:8], tmp, ALU.add)
        P.vector.tensor_tensor(o_all[:, :, :, 0:8], o_all[:, :, :, 0:8],
                               B0T.rearrange("p (u c) -> p u c", u=8)[:, :, :].unsqueeze(1).to_broadcast([128, 16, 8, 8]),
                               ALU.add)
        P.vector.tensor_copy(O1B[:, :], O1F[:, :])
        b1_done = C.cV.val
        P.vector.memset(L2F[:, :], 0.0)
        c2_ap = COUT[:, :].rearrange("p (j band t r4) -> p j band t r4", j=16, band=4, t=8, r4=4)
        l_ap = L2F[:, :].rearrange("p (j band t r4) -> p j band t r4", j=16, band=4, t=8, r4=4)
        for dt, (t_lo, t_hi, s_lo) in ((0, (1, 8, 0)), (1, (0, 8, 0)), (2, (0, 7, 1))):
            n_t = t_hi - t_lo
            for band in range(4):
                tmp = PR1F[:, 0:16 * n_t * 4].rearrange("p (j t r4) -> p j t r4", j=16, t=n_t, r4=4)
                P.vector.tensor_scalar(tmp, c2_ap[:, :, band, s_lo:s_lo + n_t, :], W1T[:, dt:dt + 1], None, ALU.mult)
                P.vector.tensor_tensor(l_ap[:, :, band, t_lo:t_hi, :], l_ap[:, :, band, t_lo:t_hi, :], tmp, ALU.add)
        P.vector.tensor_scalar(L2F[:, :], L2F[:, :], B1T[:, 0:1], None, ALU.add)
        P.vector.tensor_copy(LEP2[:, :], L2F[:, :])

        for g in range(4):
            P.tensor.wait_ge(C.cV, C.cV.val)
            for q in range(4):
                j = 4 * g + q
                P.tensor.matmul(C.PS[q][:, 0:128], O1B[:, 128 * j:128 * (j + 1)], IBF, start=True, stop=True)
            P.mark_last('tensor', C.cT)
            P.vector.wait_ge(C.cT, C.cT.val)
            for q in range(4):
                P.vector.tensor_copy(PR1[:, 128 * q:128 * (q + 1)], C.PS[q][:, 0:128])
            P.tensor.wait_ge(C.cV, C.cV.val)
            for q in range(4):
                j = 4 * g + q
                P.tensor.matmul(C.PS[4 + q][:, 0:128], BDP[0], PR1[:, 128 * q:128 * (q + 1)], start=True, stop=False)
                P.tensor.matmul(C.PS[4 + q][:, 0:128], BDP[1], O2T1[:, 128 * j:128 * (j + 1)], start=False, stop=False)
                P.tensor.matmul(C.PS[4 + q][:, 0:128], BDP[2], O2T2[:, 128 * j:128 * (j + 1)], start=False, stop=False)
                P.tensor.matmul(C.PS[4 + q][:, 0:128], BDP[3], LEP2[:, 128 * j:128 * (j + 1)], start=False, stop=True)
            P.mark_last('tensor', C.cT)
            P.vector.wait_ge(C.cT, C.cT.val)
            for q in range(4):
                P.vector.tensor_scalar(PR1[:, 128 * q:128 * (q + 1)], C.PS[4 + q][:, 0:128], PBT[:, 0:1], None, ALU.add)
            P.tensor.wait_ge(C.cV, C.cV.val)
            for q in range(4):
                P.tensor.matmul(C.PS[q][:, 0:128], PR1[:, 128 * q:128 * (q + 1)], IBF, start=True, stop=True)
            P.mark_last('tensor', C.cT)
            P.vector.wait_ge(C.cT, C.cT.val)
            for q in range(4):
                P.vector.tensor_copy(YC[:, 128 * q:128 * (q + 1)], C.PS[q][:, 0:128])
            yap = YC[:, :].rearrange("p (x c) -> p x c", c=16)
            P.vector.tensor_reduce(ST32[:, 0:32], yap, axis=AXX, op=ALU.add)
            P.vector.tensor_scalar(ST32[:, 0:32], ST32[:, 0:32], 1.0 / 16, None, ALU.mult)
            P.vector.tensor_tensor(yap, yap, ST32[:, 0:32].unsqueeze(2).to_broadcast([128, 32, 16]), ALU.subtract)
            P.scalar.wait_ge(C.cV, C.cV.val)
            P.scalar.activation(SQ16[:, 0:512], YC[:, :], AF.Square)
            P.vector.wait_ge(C.cA, C.cA.val)
            P.vector.tensor_reduce(ST32[:, 0:32], SQ16[:, 0:512].rearrange("p (x c) -> p x c", c=16),
                                   axis=AXX, op=ALU.add)
            P.vector.tensor_scalar(ST32[:, 0:32], ST32[:, 0:32], 1.0 / 16, None, ALU.mult)
            P.vector.tensor_scalar(ST32[:, 0:32], ST32[:, 0:32], EPS, None, ALU.add)
            P.scalar.wait_ge(C.cV, C.cV.val)
            P.scalar.activation(ST32[:, 0:32], ST32[:, 0:32], AF.Sqrt)
            P.vector.wait_ge(C.cA, C.cA.val)
            P.vector.reciprocal(ST32[:, 0:32], ST32[:, 0:32])
            P.vector.tensor_tensor(yap, yap, ST32[:, 0:32].unsqueeze(2).to_broadcast([128, 32, 16]), ALU.mult)
            ycv = YC[:, :].rearrange("p (a c) -> p a c", a=4)
            P.vector.tensor_tensor(ycv, ycv, ALNW.unsqueeze(1).to_broadcast([128, 4, 128]), ALU.mult)
            P.vector.tensor_tensor(ycv, ycv, ALNB.unsqueeze(1).to_broadcast([128, 4, 128]), ALU.add)
            P.vector.tensor_tensor(YC[:, :], YC[:, :], C.XT2[:, 512 * g:512 * (g + 1)], ALU.add)
            P.vector.tensor_copy(Yt[:, 512 * g:512 * (g + 1)], YC[:, :])

        y_ap = Yt[:, :].rearrange("p (j u c) -> p j u c", j=16, u=8, c=16)
        P.vector.wait_ge(C.qG, C.qG.val)
        for dd in range(8):
            P.vector.tensor_copy(
                SLAB[:, 256 * dd:256 * (dd + 1)].rearrange("p (j u two) -> p j u two", j=16, u=8, two=2),
                y_ap[:, :, :, 2 * dd:2 * dd + 2])
        P.gpsimd.wait_ge(C.cV, C.cV.val)
        for dd in range(8):
            P.gpsimd.dma_start(out=aux["a2a_in"].ap()[dd, sb], in_=SLAB[:, 256 * dd:256 * (dd + 1)]).then_inc(C.qG, 16)

    P.gpsimd.wait_ge(C.qG, C.qG.val)
    P.gpsimd.collective_compute(
        "AllToAll", ALU.bypass, replica_groups=[list(range(NCORES))],
        ins=[aux["a2a_in"].ap().opt()], outs=[aux["a2a_out"].ap().opt()],
    ).then_inc(C.cc, 1)
    P.gpsimd.wait_ge(C.cc, C.cc.val)
    YTR = bft(A_YTR, 16384)
    for src in range(8):
        for sb in range(8):
            P.gpsimd.wait_ge(C.cT, C.cT.val)
            P.gpsimd.dma_start(out=SLAB[:, 0:256], in_=aux["a2a_out"].ap()[src, sb]).then_inc(C.qR[0], 16)
            P.tensor.wait_ge(C.qR[0], C.qR[0].val)
            P.tensor.wait_ge(C.cV, C.cV.val)
            for par in range(2):
                sel = SLAB[:, 0:256].rearrange("p (j u two) -> p two (j u)", j=16, u=8)[:, par, :]
                P.tensor.matmul(C.PS[par][:, 0:128], sel, IBF, start=True, stop=True)
            P.mark_last('tensor', C.cT)
            P.vector.wait_ge(C.cT, C.cT.val)
            for par in range(2):
                col = 2 * (8 * src + sb) + par
                P.vector.tensor_copy(YTR[:, 128 * col:128 * (col + 1)], C.PS[par][:, 0:128])
    SCR = f32t(A_SCR, 1024)
    X1T = f32t(A_X1T, 1024)
    for F in range(16):
        for rb in range(8):
            if rb == 0:
                P.sync.wait_ge(C.cV, C.cV.val)
            else:
                P.sync.wait_ge(C.cT, C.cT.val)
            P.sync.dma_start(out=f32t(A_STG, 128),
                             in_=aux["xrows"][128 * rb:128 * (rb + 1), 128 * F:128 * (F + 1)]).then_inc(C.qE[1], 16)
            P.tensor.wait_ge(C.qE[1], C.qE[1].val)
            if rb == 0:
                P.tensor.wait_ge(C.cV, C.cV.val)
            pb_ = C.PS[4 + rb // 4]
            P.tensor.matmul(pb_[:, 128 * (rb % 4):128 * (rb % 4 + 1)], f32t(A_STG, 128), IF32, start=True, stop=True)
            P.mark_last('tensor', C.cT)
        P.vector.wait_ge(C.cT, C.cT.val)
        P.vector.wait_ge(C.qD[0], C.qD[0].val)
        ysel = YTR[:, :].rearrange("p (q a g) -> p q a g", q=128, a=8, g=16)[:, :, :, F]
        ysel = ysel.rearrange("p (s2 par) a -> p par s2 a", par=2)
        P.vector.tensor_copy(SCR.rearrange("p (par s a) -> p par s a", par=2, s=64), ysel)
        P.vector.tensor_copy(X1T[:, 0:512], C.PS[4][:, :])
        P.vector.tensor_copy(X1T[:, 512:1024], C.PS[5][:, :])
        P.vector.tensor_tensor(X1T, X1T, SCR, ALU.add)
        P.gpsimd.wait_ge(C.cV, C.cV.val)
        P.gpsimd.dma_start(out=x1_dram[F], in_=X1T).then_inc(C.qD[0], 16)


def build_full():
    kb = KB()
    C = Common(kb)
    din = {
        "lua": kb.din("lua", [D, 16]), "lda": kb.din("lda", [FF, 16]),
        "b_up": kb.din("b_up", [128, FC]), "b_down": kb.din("b_down", [128, KC]),
    }
    lub = kb.din("lub", [16, FF])
    ldb = kb.din("ldb", [16, D])
    wq_u = kb.din("wupT_q", [D, FF], I32)
    ws_u = kb.din("wup_sc", [D, FF])
    wq_d = kb.din("wdnT_q", [FF, D], I32)
    ws_d = kb.din("wdn_sc", [FF, D])
    out_rows = kb.dout("out_rows", [T, D])
    auxa = declare_attna_aux(kb)
    auxb = declare_attnb_aux(kb)
    x1 = kb.dscratch("x1s", [KC, 128, T]).ap()
    x2 = kb.dscratch("x2s", [KC, 128, T]).ap()
    x3 = kb.dscratch("x3s", [KC, 128, T]).ap()
    hm = kb.dscratch("hms", [KC, 128, T]).ap()
    x4 = kb.dscratch("x4s", [KC, 128, T]).ap()
    wup_bf = kb.dscratch("wup_bf", [FC, 128, KC, 128], BF16).ap()
    wdn_bf = kb.dscratch("wdn_bf", [4 * KC, 128, KC, 128], BF16).ap()
    P = kb.prog

    C.preload(din)
    predeq_chunk = make_predequant(kb, C, (wq_u, ws_u), (wq_d, ws_d), wup_bf, wdn_bf)
    emit_attn_a(kb, C, auxa, x1, chunk_cb=predeq_chunk)
    P.gpsimd.wait_ge(C.qC[0], C.qC[0].val)
    P.gpsimd.wait_ge(C.qC[1], C.qC[1].val)
    emit_ffn(kb, C, x1, x2, dequant=False,
             wup_src=[wup_bf[m] for m in range(FC)],
             wdn_src=[wdn_bf[st] for st in range(4 * KC)],
             lub_dram=lub, ldb_dram=ldb,
             halo_out=auxb["halo_in"].ap())
    emit_attn_b(kb, C, x2, x3, hm, auxb)
    emit_ffn(kb, C, x3, x4, dequant=False,
             wup_src=[wup_bf[m] for m in range(FC)],
             wdn_src=[wdn_bf[st] for st in range(4 * KC)],
             lub_dram=lub, ldb_dram=ldb)
    IF32 = C.AR0[:, A_IF32:A_IF32 + 128]
    STG = C.AR0[:, A_STG:A_STG + 128]
    P.sync.wait_ge(C.qD[0], C.qD[0].val)
    P.sync.wait_ge(C.qD[1], C.qD[1].val)
    P.sync.wait_ge(C.cT, C.cT.val)
    P.sync.dma_start(out=IF32, in_=auxa["I128"]).then_inc(C.qS, 16)
    P.tensor.wait_ge(C.qS, C.qS.val)
    prev_out = 0
    for rb in range(8):
        for F in range(16):
            P.sync.wait_ge(C.cT, C.cT.val)
            P.sync.dma_start(out=STG, in_=x4[F][:, 128 * rb:128 * (rb + 1)]).then_inc(C.qE[0], 16)
            P.tensor.wait_ge(C.qE[0], C.qE[0].val)
            if F % 4 == 0:
                P.tensor.wait_ge(C.cV, C.cV.val)
            pb_ = C.PS[4 + (F // 4) % 2]
            P.tensor.matmul(pb_[:, 128 * (F % 4):128 * (F % 4 + 1)], STG, IF32, start=True, stop=True)
            P.mark_last('tensor', C.cT)
            if F % 4 == 3:
                P.vector.wait_ge(C.cT, C.cT.val)
                if F == 3:
                    P.vector.wait_ge(C.qD[1], prev_out)
                P.vector.tensor_copy(C.XT2[:, 512 * (F // 4):512 * (F // 4 + 1)], pb_[:, :])
                if F == 15:
                    P.gpsimd.wait_ge(C.cV, C.cV.val)
                    P.gpsimd.dma_start(out=out_rows[128 * rb:128 * (rb + 1), :], in_=C.XT2[:, :]).then_inc(C.qD[1], 16)
                    prev_out = C.qD[1].val
    P.gpsimd.wait_ge(C.qD[1], C.qD[1].val)
    P.gpsimd.wait_ge(C.qD[0], C.qD[0].val)
    P.gpsimd.wait_ge(C.qG, C.qG.val)
    P.sync.wait_ge(C.qS, C.qS.val)
    return kb.finalize()


K_BUILD_CACHE = {}


def full_in_maps(inp):
    maps = prep_inputs(inp)
    ac = prep_attna_consts(inp)
    for i, m in enumerate(maps):
        m.update(ac)
        pp, cmrow = prep_attnb_pack(inp, i)
        m["parpack"] = pp
        m["cmrow"] = cmrow
        for nm in ["Wq", "Wk", "Wv", "Wig", "Wog", "Wo", "Wg", "Wgate", "Wval", "Wout"]:
            m["b_" + nm] = inp["L0_attn_b_" + nm]
        m["lub"] = inp["L0_ffn_w_up_lora_b"]
        m["ldb"] = inp["L0_ffn_w_down_lora_b"]
    return maps


def kernel(**inputs):
    maps = full_in_maps(inputs)
    nc = build_full()
    res = run_bass_kernel_spmd(nc, maps, core_ids=list(range(NCORES)))
    out = np.concatenate([res.results[i]["out_rows"] for i in range(NCORES)], axis=0)
    return out.reshape(1, 8192, 2048)


# version 10 (final)
# speedup vs baseline: 1.3915x; 1.0666x over previous
import contextlib
import numpy as np
import ml_dtypes
import concourse.bass as bass
import concourse.mybir as mybir
from concourse.bass_utils import run_bass_kernel_spmd

F32 = mybir.dt.float32
BF16 = mybir.dt.bfloat16
I32 = mybir.dt.int32
ALU = mybir.AluOpType
AF = mybir.ActivationFunctionType
AXX = mybir.AxisListType.X

NCORES = 8
T = 1024
D = 2048
FF = 8192
NH = 16
HD = 128
KC = D // 128
FC = FF // 128
EPS = 1e-5


class Call:
    __slots__ = ("name", "args", "kwargs", "chain")

    def __init__(self, name, args, kwargs):
        self.name, self.args, self.kwargs, self.chain = name, args, kwargs, []

    def then_inc(self, sem, k=1):
        sem.val += k
        self.chain.append((sem, k))
        return self


_NOCHAIN = {"wait_ge", "dma_start", "dma_start_transpose", "nop", "collective_compute"}


class EngRec:

    def __init__(self, prog, key):
        self._prog, self._key = prog, key

    def __getattr__(self, name):
        def call(*a, **kw):
            chain = self._prog.chains.get(self._key)
            if chain is not None and name not in _NOCHAIN:
                if chain.val > 0:
                    self._prog.streams[self._key].append(Call("wait_ge", (chain, chain.val), {}))
                c = Call(name, a, kw)
                self._prog.streams[self._key].append(c)
                c.then_inc(chain, 1)
                return c
            c = Call(name, a, kw)
            self._prog.streams[self._key].append(c)
            return c
        return call


class Prog:
    ENGS = ("tensor", "vector", "scalar", "gpsimd", "sync")

    def __init__(self):
        self.streams = {e: [] for e in self.ENGS}
        self.chains = {}
        for e in self.ENGS:
            setattr(self, e, EngRec(self, e))

    def set_chains(self, sems):
        self.chains.update(sems)

    def mark_last(self, key, sem, k=1):
        self.streams[key][-1].then_inc(sem, k)

    def emit(self, block):
        def mk(key):
            def f(eng):
                for c in self.streams[key]:
                    args = [a.h if isinstance(a, Sem) else a for a in c.args]
                    r = getattr(eng, c.name)(*args, **c.kwargs)
                    for (sem, k) in c.chain:
                        r = r.then_inc(sem.h, k)
            return f
        block.tensor(mk("tensor"))
        block.vector(mk("vector"))
        block.scalar(mk("scalar"))
        block.gpsimd(mk("gpsimd"))
        block.sync(mk("sync"))


class Sem:

    def __init__(self, name):
        self.name, self.val, self.h = name, 0, None

    def inc(self, k=1):
        self.val += k
        return self.val


def prep_inputs(inp):
    x = np.ascontiguousarray(inp["L0_attn_a_x"].reshape(8192, 2048))
    wupT_q = np.ascontiguousarray(inp["L0_ffn_w_up_q"].T)
    wdnT_q = np.ascontiguousarray(inp["L0_ffn_w_down_q"].T)
    wup_sc = np.ascontiguousarray(np.repeat(inp["L0_ffn_w_up_scale"], 64, axis=1).T)
    wdn_sc = np.ascontiguousarray(np.repeat(inp["L0_ffn_w_down_scale"], 64, axis=1).T)
    I128 = np.eye(128, dtype=np.float32)
    base = {
        "I128": I128,
        "wupT_q": wupT_q, "wdnT_q": wdnT_q, "wup_sc": wup_sc, "wdn_sc": wdn_sc,
        "b_up": np.ascontiguousarray(inp["L0_ffn_b_up"].reshape(FC, 128).T),
        "b_down": np.ascontiguousarray(inp["L0_ffn_b_down"].reshape(KC, 128).T),
        "lua": inp["L0_ffn_w_up_lora_a"],
        "lub": inp["L0_ffn_w_up_lora_b"],
        "lda": inp["L0_ffn_w_down_lora_a"],
        "ldb": inp["L0_ffn_w_down_lora_b"],
    }
    maps = []
    for i in range(NCORES):
        m = dict(base)
        m["xrows"] = np.ascontiguousarray(x[T * i:T * (i + 1)])
        maps.append(m)
    return maps


class KB:

    def __init__(self):
        self.nc = bass.Bass("TRN2", target_bir_lowering=False)
        self.stack = contextlib.ExitStack()
        self.prog = Prog()
        self.sems = {}
        self._psum_used = []

    def sem(self, name):
        s = Sem(name)
        self.sems[name] = s
        return s

    def din(self, name, shape, dtype=F32):
        return self.nc.dram_tensor(name, list(shape), dtype, kind="ExternalInput").ap()

    def dout(self, name, shape, dtype=F32):
        return self.nc.dram_tensor(name, list(shape), dtype, kind="ExternalOutput").ap()

    def dscratch(self, name, shape, dtype=F32):
        return self.nc.dram_tensor(name, list(shape), dtype)

    def sb(self, name, shape, dtype=F32):
        return self.stack.enter_context(self.nc.sbuf_tensor(name, list(shape), dtype))

    def ps(self, name, shape, dtype=F32):
        return self.stack.enter_context(self.nc.psum_tensor(name, list(shape), dtype))

    def finalize(self):
        from concourse.aot_env import allow_ici_all_to_all
        with contextlib.ExitStack() as es:
            es.enter_context(allow_ici_all_to_all())
            es.enter_context(self.nc.allow_low_precision(
                reason="bf16 working tiles by design; matmul/reduce accumulate fp32 internally"))
            for s in self.sems.values():
                s.h = es.enter_context(self.nc.semaphore(s.name))
            block = es.enter_context(self.nc.Block())
            self.prog.emit(block)
        self.stack.close()
        return self.nc


def view(t, dtype, shape):
    ap = t[:, :] if not isinstance(t, bass.AP) else t
    ap = ap.bitcast(dtype)
    if ap.ndim == 3:
        ap = ap.rearrange("p a b -> p (a b)")
    if len(shape) == 2:
        return ap.rearrange("p (a b) -> p a b", a=shape[0], b=shape[1])
    if len(shape) == 1:
        return ap
    raise ValueError(shape)


class Common:

    def __init__(self, kb: KB):
        self.kb = kb
        self.PS = [kb.ps(f"PS{i}", [128, 512], F32) for i in range(8)]
        self.AR0 = kb.sb("AR0", [128, 32768], F32)
        ar0b = self.AR0[:, :].bitcast(BF16)
        self.XH = ar0b.rearrange("p (a b) -> p a b", a=FC, b=T)
        self.AB0 = [ar0b[:, 16384 * i:16384 * (i + 1)].rearrange("p (a b) -> p a b", a=KC, b=T) for i in range(4)]
        self.XT2 = kb.sb("XT2", [128, 2048], F32)
        self.XT = [self.XT2[:, 0:1024], self.XT2[:, 1024:2048]]
        self.XB = kb.sb("XB", [128, KC, 1028], BF16)
        self.SA = self.XT2[0:1, 0:1024]
        self.SB = self.XT2[0:1, 1024:2048]
        self.MB = kb.sb("MB", [128, 1024], BF16)
        self.RB = kb.sb("RB", [128, 1024], BF16)
        self.SQ = self.MB
        self.WQF = [kb.sb(f"WQF{i}", [128, 4, 128], F32) for i in range(2)]
        self.SCF = [kb.sb(f"SCF{i}", [128, 4, 128], F32) for i in range(2)]
        self.WUB = [kb.sb(f"WUB{i}", [128, KC, 128], BF16) for i in range(2)]
        self.STB = kb.sb("STB", [1, 2048], BF16)
        self.SMALL = kb.sb("SMALL", [128, 1024], F32)
        self.LUA = kb.sb("LUA", [128, KC, 16], BF16)
        self.LUBd = [kb.sb(f"LUBd{i}", [16, 128], BF16) for i in range(2)]
        self.LDA = kb.sb("LDA", [128, FC, 16], BF16)
        self.LDBd = [kb.sb(f"LDBd{i}", [16, 128], BF16) for i in range(2)]
        self.LT1 = kb.sb("LT1", [16, 1024], BF16)
        self.LT2 = kb.sb("LT2", [16, 1024], BF16)
        self.SM0 = self.LT1[0:1, :]
        self.SR0 = self.LT2[0:1, :]

        self.BUP = kb.sb("BUP", [128, FC], F32)
        self.BDN = kb.sb("BDN", [128, KC], F32)
        self.ones_col_bf = kb.sb("ones_col_bf", [128, 1], BF16)
        self.ones_row_bf = kb.sb("ones_row_bf", [1, 128], BF16)
        self.invd_col_bf = kb.sb("invd_col_bf", [128, 1], BF16)
        self.sqrthd_row_bf = kb.sb("sqrthd_row_bf", [1, 128], BF16)
        self.eps_l2 = kb.sb("eps_l2", [1, 1], F32)
        self.eps_ln = kb.sb("eps_ln", [1, 1], F32)
        self.m75 = kb.sb("m75", [128, 1], F32)
        self.wqf_guard = [None, None]
        self.qS = kb.sem("qS")
        self.qL = kb.sem("qL")
        self.qW = [kb.sem("qW0"), kb.sem("qW1")]
        self.qP = [kb.sem("qP0"), kb.sem("qP1")]
        self.qB = [kb.sem("qB0"), kb.sem("qB1")]
        self.qE = [kb.sem("qE0"), kb.sem("qE1")]
        self.qC = [kb.sem("qC0"), kb.sem("qC1")]
        self.qD = [kb.sem("qD0"), kb.sem("qD1")]
        self.qR = [kb.sem("qR0"), kb.sem("qR1")]
        self.qG = kb.sem("qG")
        self.qH = kb.sem("qH")
        self.cT = kb.sem("cT")
        self.cc = kb.sem("cc")
        self.cV = kb.sem("cV")
        self.cA = kb.sem("cA")
        self.cP = kb.sem("cP")
        kb.prog.set_chains({"vector": self.cV, "scalar": self.cA, "gpsimd": self.cP})
        self.ones_col_f32 = kb.sb("ones_col_f32", [128, 1], F32)
        self.ones_row_f32 = kb.sb("ones_row_f32", [1, 128], F32)
        self.WG = kb.sb("WG", [128, KC, 16], BF16)

    def xb(self, k, n0=0, n1=T):
        return self.XB[:, k, 4 + n0:4 + n1]

    def preload(self, din):
        P, C = self.kb.prog, self
        P.vector.memset(C.ones_col_bf[:, :], 1.0)
        P.vector.memset(C.ones_row_bf[:, :], 1.0)
        P.vector.memset(C.invd_col_bf[:, :], 1.0 / 128.0)
        P.vector.memset(C.sqrthd_row_bf[:, :], float(np.sqrt(128.0)))
        P.vector.memset(C.eps_l2[:, :], 1e-12)
        P.vector.memset(C.eps_ln[:, :], float(EPS * HD))
        P.vector.memset(C.m75[:, :], -7.5)
        P.vector.memset(C.ones_col_f32[:, :], 1.0)
        P.vector.memset(C.ones_row_f32[:, :], 1.0)
        P.gpsimd.dma_start(out=C.LUA[:, :, :], in_=din["lua"].rearrange("(kc p) r -> p kc r", p=128)).then_inc(C.qG, 16)
        P.gpsimd.dma_start(out=C.LDA[:, :, :], in_=din["lda"].rearrange("(h p) r -> p h r", p=128)).then_inc(C.qG, 16)
        P.sync.dma_start(out=C.BUP[:, :], in_=din["b_up"]).then_inc(C.qS, 16)
        P.sync.dma_start(out=C.BDN[:, :], in_=din["b_down"]).then_inc(C.qS, 16)


def emit_ln(kb, C, src_dram, *, fused_w=None, fused_b=None, fused_rho=False,
            out_bf=None, keep_rho=None):
    P = kb.prog
    out_bf = out_bf if out_bf is not None else C.XB

    def OBF(k, n0=0, n1=T):
        if out_bf is C.XB:
            return out_bf[:, k, 4 + n0:4 + n1]
        return out_bf[:, k, n0:n1]
    P.gpsimd.wait_ge(C.cT, C.cT.val)
    P.gpsimd.wait_ge(C.cV, C.cV.val)
    P.gpsimd.wait_ge(C.qG, C.qG.val)
    P.gpsimd.wait_ge(C.qD[0], C.qD[0].val)
    P.gpsimd.wait_ge(C.qD[1], C.qD[1].val)
    P.sync.wait_ge(C.qG, C.qG.val)
    P.sync.wait_ge(C.cT, C.cT.val)
    P.sync.wait_ge(C.qC[0], C.qC[0].val)
    P.sync.wait_ge(C.qC[1], C.qC[1].val)
    P.tensor.wait_ge(C.cV, C.cV.val)
    for k in range(KC):
        P.gpsimd.dma_start(out=OBF(k), in_=src_dram[k]).then_inc(C.qL, 16)
    P.scalar.wait_ge(C.qL, C.qL.val)
    for k in range(KC):
        if k > 0:
            P.scalar.wait_ge(C.cT, C.cT.val)
        P.scalar.activation(C.SQ[:, :], OBF(k), AF.Square)
        P.tensor.wait_ge(C.cA, C.cA.val)
        P.tensor.matmul(C.PS[0][0:1, :], C.ones_col_bf[:, :], OBF(k, 0, 512),
                        start=(k == 0), stop=(k == KC - 1))
        P.tensor.matmul(C.PS[1][0:1, :], C.ones_col_bf[:, :], OBF(k, 512, 1024),
                        start=(k == 0), stop=(k == KC - 1))
        P.tensor.matmul(C.PS[2][0:1, :], C.ones_col_bf[:, :], C.SQ[:, 0:512],
                        start=(k == 0), stop=(k == KC - 1))
        P.tensor.matmul(C.PS[3][0:1, :], C.ones_col_bf[:, :], C.SQ[:, 512:1024],
                        start=(k == 0), stop=(k == KC - 1)).then_inc(C.cT, 1)
    P.vector.wait_ge(C.cT, C.cT.val)
    P.vector.tensor_scalar(C.SA[:, 0:512], C.PS[0][0:1, :], 1.0 / D, None, ALU.mult)
    P.vector.tensor_scalar(C.SA[:, 512:1024], C.PS[1][0:1, :], 1.0 / D, None, ALU.mult)
    P.vector.tensor_copy(C.STB[0:1, 0:1024], C.SA)
    P.vector.tensor_scalar(C.SB[:, 0:512], C.PS[2][0:1, :], 1.0 / D, None, ALU.mult)
    P.vector.tensor_scalar(C.SB[:, 512:1024], C.PS[3][0:1, :], 1.0 / D, None, ALU.mult)
    P.vector.tensor_tensor(C.STB[0:1, 1024:2048], C.SA, C.SA, ALU.mult)
    P.vector.tensor_tensor(C.SB, C.SB, C.STB[0:1, 1024:2048], ALU.subtract)
    P.vector.tensor_scalar(C.SA, C.SB, EPS, None, ALU.add)
    P.vector.reciprocal(C.SA, C.SA)
    if fused_rho or keep_rho is not None:
        P.vector.tensor_tensor(C.SB, C.SB, C.SA, ALU.mult)
        P.vector.tensor_scalar(C.SB, C.SB, EPS, None, ALU.add)
        P.scalar.wait_ge(C.cV, C.cV.val)
        P.scalar.activation(C.SB, C.SB, AF.Sqrt)
        P.scalar.activation(C.SA, C.SA, AF.Sqrt)
        P.vector.wait_ge(C.cA, C.cA.val)
        P.vector.reciprocal(C.SB, C.SB)
        if keep_rho is not None:
            P.vector.tensor_copy(C.SM0, C.STB[0:1, 0:1024])
            P.vector.tensor_copy(C.SR0, C.SA)
        if fused_rho:
            P.vector.tensor_tensor(C.SA, C.SA, C.SB, ALU.mult)
    else:
        P.vector.wait_ge(C.cV, C.cV.val)
        P.scalar.wait_ge(C.cV, C.cV.val)
        P.scalar.activation(C.SA, C.SA, AF.Sqrt)
        P.vector.wait_ge(C.cA, C.cA.val)
    P.vector.tensor_copy(C.STB[0:1, 1024:2048], C.SA)
    P.tensor.wait_ge(C.cV, C.cV.val)
    for n in range(2):
        P.tensor.matmul(C.PS[0 + n][:, :], C.ones_row_bf[:, :], C.STB[0:1, 512 * n:512 * (n + 1)], start=True, stop=True)
        P.tensor.matmul(C.PS[2 + n][:, :], C.ones_row_bf[:, :], C.STB[0:1, 1024 + 512 * n:1024 + 512 * (n + 1)], start=True, stop=True)
    P.mark_last('tensor', C.cT)
    P.vector.wait_ge(C.cT, C.cT.val)
    P.vector.tensor_copy(C.MB[:, 0:512], C.PS[0][:, :])
    P.vector.tensor_copy(C.MB[:, 512:1024], C.PS[1][:, :])
    P.vector.tensor_copy(C.RB[:, 0:512], C.PS[2][:, :])
    P.vector.tensor_copy(C.RB[:, 512:1024], C.PS[3][:, :])
    for k in range(KC):
        P.vector.tensor_tensor(OBF(k), OBF(k), C.MB[:, :], ALU.subtract)
        P.vector.tensor_tensor(OBF(k), OBF(k), C.RB[:, :], ALU.mult)
        if fused_w is not None:
            P.vector.tensor_scalar(OBF(k), OBF(k), fused_w[:, k:k + 1], fused_b[:, k:k + 1], ALU.mult, ALU.add)
    P.vector.nop()


def _stream_lhsT(kb, C, par, src, cache, mcols, rows0, dequant, mm_wait=None):
    P = kb.prog
    if not dequant:
        for w in (mm_wait or []):
            P.sync.wait_ge(*w)
        P.sync.dma_start(out=C.WUB[par][:, :, :], in_=src).then_inc(C.qP[par], 16)
        return [(C.qP[par], C.qP[par].val)]
    qsrc, ssrc = src
    done = []
    for q in range(4):
        b = q % 2
        r = rows0 + 512 * q
        g = C.wqf_guard[b]
        if g is not None:
            P.gpsimd.wait_ge(*g)
        if q == 0:
            for w in (mm_wait or []):
                P.gpsimd.wait_ge(*w)
        P.gpsimd.dma_start(out=C.WQF[b][:, :, :],
                           in_=qsrc[r:r + 512, mcols].rearrange("(kc p) n -> p kc n", p=128)).then_inc(C.qW[b], 16)
        P.gpsimd.dma_start(out=C.SCF[b][:, :, :],
                           in_=ssrc[r:r + 512, mcols].rearrange("(kc p) n -> p kc n", p=128)).then_inc(C.qW[b], 16)
        eng, cm = (P.vector, C.cV) if b == 0 else (P.gpsimd, C.cP)
        eng.wait_ge(C.qW[b], C.qW[b].val)
        for w in (mm_wait or []):
            eng.wait_ge(*w)
        if b == 0:
            eng.scalar_tensor_tensor(
                C.WUB[par][:, 4 * q:4 * (q + 1), :].rearrange("p a b -> p (a b)"),
                C.WQF[b][:, :, :].rearrange("p a b -> p (a b)"), -7.5,
                C.SCF[b][:, :, :].rearrange("p a b -> p (a b)"),
                ALU.add, ALU.mult)
        else:
            eng.tensor_tensor(C.WQF[b][:, :, :].rearrange("p a b -> p (a b)"),
                              C.WQF[b][:, :, :].rearrange("p a b -> p (a b)"),
                              C.m75[:, 0:1].to_broadcast([128, 512]), ALU.add)
            eng.tensor_tensor(
                C.WUB[par][:, 4 * q:4 * (q + 1), :].rearrange("p a b -> p (a b)"),
                C.WQF[b][:, :, :].rearrange("p a b -> p (a b)"),
                C.SCF[b][:, :, :].rearrange("p a b -> p (a b)"),
                ALU.mult)
        C.wqf_guard[b] = (cm, cm.val)
        done.append((cm, cm.val))
    if cache is not None:
        for (sm, vv) in done[-2:]:
            P.gpsimd.wait_ge(sm, vv)
        P.gpsimd.dma_start(out=cache, in_=C.WUB[par][:, :, :]).then_inc(C.qC[par], 16)
    return done


def make_predequant(kb, C, wup_src, wdn_src, wup_cache, wdn_cache, n_chunks=8):
    P = kb.prog
    blocks = []
    for m in range(FC):
        blocks.append((wup_src, wup_cache[m], slice(128 * m, 128 * (m + 1)), 0))
    for d in range(KC):
        for r4 in range(4):
            blocks.append((wdn_src, wdn_cache[4 * d + r4], slice(128 * d, 128 * (d + 1)), 2048 * r4))
    quarters = []
    for bi, (src, cache, mcols, rows0) in enumerate(blocks):
        for q in range(4):
            quarters.append((bi, q, src, cache, mcols, rows0 + 512 * q))
    NQ = len(quarters)
    ldq = {}
    lds = {}
    tt_done = {}
    wr = {}
    st = {"started": False}

    def issue_quant(i):
        bi, q, (qsrc, ssrc), cache, mcols, r = quarters[i]
        b = i % 2
        if i >= 2:
            P.gpsimd.wait_ge(C.cP, tt_done[i - 2])
        P.gpsimd.dma_start(out=C.WQF[b][:, :, :],
                           in_=qsrc[r:r + 512, mcols].rearrange("(kc p) n -> p kc n", p=128)).then_inc(C.qW[b], 16)
        ldq[i] = C.qW[b].val

    def issue_scale(i):
        bi, q, (qsrc, ssrc), cache, mcols, r = quarters[i]
        b = i % 2
        if i >= 2:
            P.sync.wait_ge(C.cP, tt_done[i - 2])
        P.sync.dma_start(out=C.SCF[b][:, :, :],
                         in_=ssrc[r:r + 512, mcols].rearrange("(kc p) n -> p kc n", p=128)).then_inc(C.qP[b], 16)
        lds[i] = C.qP[b].val

    per = NQ // n_chunks

    def emit_chunk():
        if not st["started"]:
            st["started"] = True
            P.gpsimd.wait_ge(C.cV, C.cV.val)
            issue_quant(0)
            issue_quant(1)
            issue_scale(0)
            issue_scale(1)
            st["lo"] = 0
        lo = st["lo"]
        hi = min(lo + per, NQ)
        st["lo"] = hi
        for i in range(lo, hi):
            bi, q, src, cache, mcols, r = quarters[i]
            b = i % 2
            par = bi % 2
            P.gpsimd.wait_ge(C.qW[b], ldq[i])
            P.gpsimd.wait_ge(C.qP[b], lds[i])
            if q == 0 and bi >= 2:
                P.gpsimd.wait_ge(C.qC[par], wr[bi - 2])
            P.gpsimd.tensor_tensor(C.WQF[b][:, :, :].rearrange("p a b -> p (a b)"),
                                   C.WQF[b][:, :, :].rearrange("p a b -> p (a b)"),
                                   C.m75[:, 0:1].to_broadcast([128, 512]), ALU.add)
            P.gpsimd.tensor_tensor(C.WUB[par][:, 4 * q:4 * (q + 1), :].rearrange("p a b -> p (a b)"),
                                   C.WQF[b][:, :, :].rearrange("p a b -> p (a b)"),
                                   C.SCF[b][:, :, :].rearrange("p a b -> p (a b)"), ALU.mult)
            tt_done[i] = C.cP.val
            if i + 2 < NQ:
                issue_quant(i + 2)
                issue_scale(i + 2)
            if q == 3:
                P.gpsimd.wait_ge(C.cP, C.cP.val)
                P.gpsimd.dma_start(out=cache, in_=C.WUB[par][:, :, :]).then_inc(C.qC[par], 16)
                wr[bi] = C.qC[par].val

    return emit_chunk


def emit_ffn(kb, C, src_dram, dst_dram, *, dequant, wup_src, wdn_src, lub_dram, ldb_dram,
             wup_cache=None, wdn_cache=None, halo_out=None):
    P = kb.prog
    emit_ln(kb, C, src_dram)
    P.tensor.wait_ge(C.cV, C.cV.val)
    for n in range(2):
        if n == 1:
            P.tensor.wait_ge(C.cV, C.cV.val)
        for k in range(KC):
            P.tensor.matmul(C.PS[4][0:16, :], C.LUA[:, k, :], C.xb(k, 512 * n, 512 * (n + 1)),
                            start=(k == 0), stop=(k == KC - 1))
        P.mark_last('tensor', C.cT)
        P.vector.wait_ge(C.cT, C.cT.val)
        P.vector.tensor_copy(C.LT1[:, 512 * n:512 * (n + 1)], C.PS[4][0:16, :])
    lt1_ready = C.cV.val

    mm_done = {}
    evac_done = {}
    cache_done = {}
    for m in range(FC):
        par = m % 2
        mm_wait = []
        if m >= 2:
            mm_wait.append((C.cT, mm_done[m - 2]))
            if m - 2 in cache_done:
                mm_wait.append((C.qC[par], cache_done[m - 2]))
        if dequant:
            src = wup_src
        else:
            src = wup_src[m]
        rdy = _stream_lhsT(kb, C, par, src, wup_cache[m] if wup_cache is not None else None,
                           slice(128 * m, 128 * (m + 1)), 0, dequant, mm_wait)
        if wup_cache is not None:
            cache_done[m] = C.qC[par].val
        P.gpsimd.wait_ge(C.cT, mm_done[m - 2] if m >= 2 else 0)
        P.gpsimd.dma_start(out=C.LUBd[par][:, :], in_=lub_dram[0:16, 128 * m:128 * (m + 1)]).then_inc(C.qB[par], 16)
        pu, pu2 = C.PS[4 + 2 * par], C.PS[5 + 2 * par]
        for (sm, vv) in rdy:
            P.tensor.wait_ge(sm, vv)
        P.tensor.wait_ge(C.qB[par], C.qB[par].val)
        if m == 0:
            P.tensor.wait_ge(C.cV, lt1_ready)
        if m >= 2:
            P.tensor.wait_ge(C.cA, evac_done[m - 2])
        for k in range(KC):
            P.tensor.matmul(pu[:, :], C.WUB[par][:, k, :], C.xb(k, 0, 512), start=(k == 0), stop=False)
            P.tensor.matmul(pu2[:, :], C.WUB[par][:, k, :], C.xb(k, 512, 1024), start=(k == 0), stop=False)
        P.tensor.matmul(pu[:, :], C.LUBd[par][:, :], C.LT1[:, 0:512], start=False, stop=True)
        P.tensor.matmul(pu2[:, :], C.LUBd[par][:, :], C.LT1[:, 512:1024], start=False, stop=True).then_inc(C.cT, 1)
        mm_done[m] = C.cT.val
        P.scalar.wait_ge(C.cT, C.cT.val)
        if m == 0:
            P.scalar.wait_ge(C.qS, C.qS.val)
        P.scalar.activation(C.XH[:, m, 0:512], pu[:, :], AF.Relu, bias=C.BUP[:, m:m + 1], scale=1.0)
        P.scalar.activation(C.XH[:, m, 512:1024], pu2[:, :], AF.Relu, bias=C.BUP[:, m:m + 1], scale=1.0)
        evac_done[m] = C.cA.val
    up_done = C.cA.val

    P.tensor.wait_ge(C.cA, up_done)
    for n in range(2):
        if n == 1:
            P.tensor.wait_ge(C.cV, C.cV.val)
        for h in range(FC):
            P.tensor.matmul(C.PS[0][0:16, :], C.LDA[:, h, :], C.XH[:, h, 512 * n:512 * (n + 1)],
                            start=(h == 0), stop=(h == FC - 1))
        P.mark_last('tensor', C.cT)
        P.vector.wait_ge(C.cT, C.cT.val)
        P.vector.tensor_copy(C.LT2[:, 512 * n:512 * (n + 1)], C.PS[0][0:16, :])
    lt2_ready = C.cV.val

    mmd_done = {}
    ep_done = {}
    out_done = {}
    cached_done = {}
    for d in range(KC):
        pd, pd2 = C.PS[4 + 2 * (d % 2)], C.PS[5 + 2 * (d % 2)]
        P.gpsimd.wait_ge(C.cT, mmd_done[4 * (d - 2) + 3] if d >= 2 else C.cT.val)
        P.gpsimd.dma_start(out=C.LDBd[d % 2][:, :], in_=ldb_dram[0:16, 128 * d:128 * (d + 1)]).then_inc(C.qB[d % 2], 16)
        for r4 in range(4):
            par = r4 % 2
            step = 4 * d + r4
            mm_wait = []
            if step >= 2:
                mm_wait.append((C.cT, mmd_done[step - 2]))
                if step - 2 in cached_done:
                    mm_wait.append((C.qC[par], cached_done[step - 2]))
            else:
                mm_wait.append((C.cT, C.cT.val))
                mm_wait.append((C.qC[par], C.qC[par].val))
            if dequant:
                src = wdn_src
            else:
                src = wdn_src[step]
            rdy = _stream_lhsT(kb, C, par, src, wdn_cache[step] if wdn_cache is not None else None,
                               slice(128 * d, 128 * (d + 1)), 2048 * r4, dequant, mm_wait)
            if wdn_cache is not None:
                cached_done[step] = C.qC[par].val
            for (sm, vv) in rdy:
                P.tensor.wait_ge(sm, vv)
            if step == 0:
                P.tensor.wait_ge(C.cV, lt2_ready)
            if r4 == 0 and d >= 2:
                P.tensor.wait_ge(C.cV, ep_done[d - 2])
            for kk in range(KC):
                h = KC * r4 + kk
                P.tensor.matmul(pd[:, :], C.WUB[par][:, kk, :], C.XH[:, h, 0:512], start=(step % 4 == 0 and kk == 0), stop=False)
                P.tensor.matmul(pd2[:, :], C.WUB[par][:, kk, :], C.XH[:, h, 512:1024], start=(step % 4 == 0 and kk == 0), stop=False)
            if r4 == 3:
                P.tensor.wait_ge(C.qB[d % 2], C.qB[d % 2].val)
                P.tensor.matmul(pd[:, :], C.LDBd[d % 2][:, :], C.LT2[:, 0:512], start=False, stop=True)
                P.tensor.matmul(pd2[:, :], C.LDBd[d % 2][:, :], C.LT2[:, 512:1024], start=False, stop=True)
            P.mark_last('tensor', C.cT)
            mmd_done[step] = C.cT.val
        buf = C.XT[d % 2]
        if d == 0:
            P.sync.wait_ge(C.cV, C.cV.val)
            P.sync.wait_ge(C.cA, C.cA.val)
        if d >= 2:
            P.sync.wait_ge(C.cV, ep_done[d - 2])
            P.sync.wait_ge(C.qD[d % 2], out_done[d - 2])
        P.sync.dma_start(out=buf, in_=src_dram[d]).then_inc(C.qE[d % 2], 16)
        P.vector.wait_ge(C.qE[d % 2], C.qE[d % 2].val)
        P.vector.wait_ge(C.cT, C.cT.val)
        if d == 0:
            P.vector.wait_ge(C.qS, C.qS.val)
        P.vector.scalar_tensor_tensor(buf[:, 0:512], pd[:, :], C.BDN[:, d:d + 1], buf[:, 0:512], ALU.add, ALU.add)
        P.vector.scalar_tensor_tensor(buf[:, 512:1024], pd2[:, :], C.BDN[:, d:d + 1], buf[:, 512:1024], ALU.add, ALU.add)
        ep_done[d] = C.cV.val
        P.gpsimd.wait_ge(C.cV, C.cV.val)
        P.gpsimd.dma_start(out=dst_dram[d], in_=buf).then_inc(C.qD[d % 2], 16)
        if halo_out is not None:
            P.gpsimd.dma_start(out=halo_out[:, d, :], in_=buf[:, 1021:1024]).then_inc(C.qD[d % 2], 16)
        out_done[d] = C.qD[d % 2].val


PP = 512
PP_CONVW = PP + 0
PP_CONVB = PP + 64
PP_VNW = PP + 80
PP_VNB = PP + 81
PP_MNW = PP + 82
PP_MNB = PP + 83
PP_GNW = PP + 84
PP_GNB = PP + 100
PP_BIG = PP + 116
PP_BOG = PP + 132
PP_LN1W = PP + 148
PP_LN1B = PP + 164
PP_LN2W = PP + 180
PP_LN2B = PP + 196
PP_BG = PP + 212
PP_CMS = PP + 213
PP_HM8 = PP + 221
PPN = 229
SM_SD = 741
SM_TMP = 757
SM_CARRY = 773
SM_HALO = 789
SM_HBT = 837


def emit_attn_b(kb, C, src_dram, dst_dram, hm_dram, aux):
    P = kb.prog
    SM = C.SMALL
    xcb, MEM, QB, OUTB = C.AB0
    T0 = C.XT2[:, 1024:1536].bitcast(BF16)
    T1 = C.XT2[:, 1536:2048].bitcast(BF16)
    S0 = SM[:, 0:512].bitcast(BF16)
    GAMS = C.XT2[0:16, 0:1024]

    def wslab(segs, wsrc, col, sem):
        for (dst, r0, r1) in segs:
            P.gpsimd.dma_start(
                out=dst,
                in_=wsrc[r0:r1, 128 * col:128 * (col + 1)].rearrange("(kc p) n -> p kc n", p=128),
            ).then_inc(sem, 16)

    WUB0q = [(C.WUB[0][:, 0:16, :], 0, 2048)]
    WUB1q = [(C.WUB[1][:, 0:16, :], 0, 2048)]
    WQFv = [view(C.WQF[0][:, :, :], BF16, (8, 128)), view(C.WQF[1][:, :, :], BF16, (8, 128))]
    SCFv = [view(C.SCF[0][:, :, :], BF16, (8, 128)), view(C.SCF[1][:, :, :], BF16, (8, 128))]
    WQFvq = [(WQFv[0][:, 0:8, :], 0, 1024), (WQFv[1][:, 0:8, :], 1024, 2048)]
    SCFvq = [(SCFv[0][:, 0:8, :], 0, 1024), (SCFv[1][:, 0:8, :], 1024, 2048)]

    def vtile(kc):
        return WQFv[kc // 8][:, kc % 8, :]

    def itile(kc):
        return SCFv[kc // 8][:, kc % 8, :]

    P.gpsimd.dma_start(out=SM[:, PP:PP + PPN], in_=aux["parpack"]).then_inc(C.qG, 16)
    P.gpsimd.dma_start(out=C.WG[:, :, :], in_=aux["Wg"].rearrange("(kc p) n -> p kc n", p=128)).then_inc(C.qG, 16)

    emit_ln(kb, C, src_dram, fused_w=SM[:, PP_LN1W:PP_LN1W + 16], fused_b=SM[:, PP_LN1B:PP_LN1B + 16],
            fused_rho=True, keep_rho=True)
    ln_done = C.cV.val

    P.gpsimd.wait_ge(C.qD[0], C.qD[0].val)
    P.gpsimd.wait_ge(C.qD[1], C.qD[1].val)
    P.gpsimd.collective_compute(
        "AllGather", ALU.bypass, replica_groups=[list(range(NCORES))],
        ins=[aux["halo_in"].ap().opt()], outs=[aux["halo_all"].ap().opt()],
    ).then_inc(C.cc, 1)
    P.gpsimd.wait_ge(C.cc, C.cc.val)
    HLD = C.XT2[:, 1024:1408]
    P.gpsimd.wait_ge(C.cV, C.cV.val)
    P.gpsimd.wait_ge(C.cA, C.cA.val)
    P.gpsimd.wait_ge(C.cT, C.cT.val)
    P.gpsimd.dma_start(out=HLD.rearrange("p (j k t) -> p j k t", j=8, k=16, t=3),
                       in_=aux["halo_all"].ap().rearrange("j p k t -> p j k t")).then_inc(C.qG, 16)
    P.vector.wait_ge(C.qG, C.qG.val)
    P.vector.memset(SM[:, SM_HALO:SM_HALO + 48], 0.0)
    for j in range(NCORES):
        P.vector.scalar_tensor_tensor(SM[:, SM_HALO:SM_HALO + 48], HLD[:, 48 * j:48 * (j + 1)],
                                      SM[:, PP_HM8 + j:PP_HM8 + j + 1], SM[:, SM_HALO:SM_HALO + 48],
                                      ALU.mult, ALU.add)
    P.tensor.wait_ge(C.cV, C.cV.val)
    P.tensor.matmul(C.PS[0][0:1, 0:48], C.ones_col_f32[:, :], SM[:, SM_HALO:SM_HALO + 48], start=True, stop=True)
    P.mark_last('tensor', C.cT)
    P.vector.wait_ge(C.cT, C.cT.val)
    SQH = C.XT2[:, 1408:1456]
    P.vector.tensor_tensor(SQH, SM[:, SM_HALO:SM_HALO + 48], SM[:, SM_HALO:SM_HALO + 48], ALU.mult)
    P.tensor.wait_ge(C.cV, C.cV.val)
    P.tensor.matmul(C.PS[1][0:1, 0:48], C.ones_col_f32[:, :], SQH, start=True, stop=True)
    P.mark_last('tensor', C.cT)
    P.vector.wait_ge(C.cT, C.cT.val)
    HS = C.XT2[0:1, 1456:1459]
    HQ = C.XT2[0:1, 1459:1462]
    P.vector.tensor_reduce(HS, C.PS[0][0:1, 0:48].rearrange("p (k t) -> p t k", k=16), axis=AXX, op=ALU.add)
    P.vector.tensor_reduce(HQ, C.PS[1][0:1, 0:48].rearrange("p (k t) -> p t k", k=16), axis=AXX, op=ALU.add)
    HM_ = C.XT2[0:1, 1462:1465]
    HR = C.XT2[0:1, 1465:1468]
    P.vector.tensor_scalar(HM_, HS, 1.0 / D, None, ALU.mult)
    P.vector.tensor_scalar(HQ, HQ, 1.0 / D, None, ALU.mult)
    P.vector.tensor_tensor(HS, HM_, HM_, ALU.mult)
    P.vector.tensor_tensor(HQ, HQ, HS, ALU.subtract)
    P.vector.tensor_scalar(HS, HQ, EPS, None, ALU.add)
    P.vector.reciprocal(HS, HS)
    P.vector.tensor_tensor(HQ, HQ, HS, ALU.mult)
    P.vector.tensor_scalar(HQ, HQ, EPS, None, ALU.add)
    P.scalar.wait_ge(C.cV, C.cV.val)
    P.scalar.activation(HQ, HQ, AF.Sqrt)
    P.scalar.activation(HR, HS, AF.Sqrt)
    P.vector.wait_ge(C.cA, C.cA.val)
    P.vector.reciprocal(HQ, HQ)
    P.vector.tensor_tensor(HR, HR, HQ, ALU.mult)
    P.tensor.wait_ge(C.cV, C.cV.val)
    P.tensor.matmul(C.PS[2][:, 0:3], C.ones_row_f32[:, :], HM_, start=True, stop=True)
    P.tensor.matmul(C.PS[3][:, 0:3], C.ones_row_f32[:, :], HR, start=True, stop=True)
    P.mark_last('tensor', C.cT)
    P.vector.wait_ge(C.cT, C.cT.val)
    HBTb = SM[:, SM_HBT:SM_HBT + 24].bitcast(BF16)
    for k in range(KC):
        hk = SM[:, SM_HALO + 3 * k:SM_HALO + 3 * (k + 1)]
        P.vector.tensor_tensor(hk, hk, C.PS[2][:, 0:3], ALU.subtract)
        P.vector.tensor_tensor(hk, hk, C.PS[3][:, 0:3], ALU.mult)
        P.vector.tensor_scalar(HBTb[:, 3 * k:3 * (k + 1)], hk, SM[:, PP_LN1W + k:PP_LN1W + k + 1],
                               SM[:, PP_LN1B + k:PP_LN1B + k + 1], ALU.mult, ALU.add)
    for k in range(KC):
        P.vector.tensor_copy(C.XB[:, k, 1:4], HBTb[:, 3 * k:3 * (k + 1)])
    P.vector.memset(C.XB[:, 0, 0:1], 0.0)

    for k in range(KC):
        acc = T0
        if k > 0:
            P.vector.wait_ge(C.cA, C.cA.val)
        P.vector.tensor_scalar(acc[:, :], C.XB[:, k, 1:1025], SM[:, PP_CONVW + 4 * k:PP_CONVW + 4 * k + 1],
                               None, ALU.mult)
        for j in (1, 2):
            P.vector.scalar_tensor_tensor(acc[:, :], C.XB[:, k, 1 + j:1025 + j],
                                          SM[:, PP_CONVW + 4 * k + j:PP_CONVW + 4 * k + j + 1], acc[:, :],
                                          ALU.mult, ALU.add)
        P.vector.scalar_tensor_tensor(acc[:, :], C.xb(k), SM[:, PP_CONVW + 4 * k + 3:PP_CONVW + 4 * k + 4],
                                      acc[:, :], ALU.mult, ALU.add)
        P.vector.tensor_scalar(acc[:, :], acc[:, :], SM[:, PP_CONVB + k:PP_CONVB + k + 1], None, ALU.add)
        P.scalar.wait_ge(C.cV, C.cV.val)
        P.scalar.activation(xcb[:, k, :], acc[:, :], AF.Silu)
    conv_done = C.cA.val

    P.tensor.wait_ge(C.cA, conv_done)
    P.tensor.wait_ge(C.qG, C.qG.val)
    for n in range(2):
        for k in range(KC):
            P.tensor.matmul(C.PS[4 + n][0:16, :], C.WG[:, k, :], xcb[:, k, 512 * n:512 * (n + 1)],
                            start=(k == 0), stop=(k == KC - 1))
    P.mark_last('tensor', C.cT)
    P.scalar.wait_ge(C.cT, C.cT.val)
    P.scalar.activation(GAMS[:, 0:512], C.PS[4][0:16, :], AF.Sigmoid, bias=SM[0:16, PP_BG:PP_BG + 1], scale=1.0)
    P.scalar.activation(GAMS[:, 512:1024], C.PS[5][0:16, :], AF.Sigmoid, bias=SM[0:16, PP_BG:PP_BG + 1], scale=1.0)
    P.gpsimd.wait_ge(C.cA, C.cA.val)
    P.gpsimd.dma_start(out=aux["gam"].ap(), in_=GAMS).then_inc(C.qG, 16)
    ONES16 = C.ones_col_f32[0:16, 0:1].to_broadcast([16, 512])
    P.vector.wait_ge(C.cA, C.cA.val)
    P.vector.tensor_tensor_scan(C.PS[6][0:16, :], GAMS[:, 0:512], ONES16, 1.0, ALU.mult, ALU.mult)
    P.vector.tensor_tensor_scan(C.PS[7][0:16, :], GAMS[:, 512:1024], ONES16, C.PS[6][0:16, 511:512], ALU.mult, ALU.mult)
    P.vector.wait_ge(C.qG, C.qG.val)
    P.vector.wait_ge(C.cP, C.cP.val)
    P.vector.tensor_copy(GAMS[:, 0:512], C.PS[6][0:16, :])
    P.vector.tensor_copy(GAMS[:, 512:1024], C.PS[7][0:16, :])
    P.gpsimd.wait_ge(C.cV, C.cV.val)
    P.gpsimd.dma_start(out=aux["gamc"].ap(), in_=GAMS).then_inc(C.qG, 16)
    P.gpsimd.dma_start(out=aux["abounce"].ap(), in_=GAMS[:, 1023:1024]).then_inc(C.qG, 16)

    def phead_norm(ps_a, ps_b, raw_tile, mmv, l2, w_ap, b_ap, out_tile, from_psum=True, war_cv=None,
                   war_stats_cv=None, war_stats_ca=None, stb_off=0, sq_tile=None):
        STBa = C.STB[0:1, stb_off:stb_off + 512]
        STBb = C.STB[0:1, stb_off + 512:stb_off + 1024]
        sq = C.MB if sq_tile is None else sq_tile
        if from_psum:
            P.scalar.wait_ge(C.cT, mmv)
            P.scalar.wait_ge(C.cV, C.cV.val if war_cv is None else war_cv)
            P.scalar.activation(raw_tile[:, 0:512], ps_a[:, :], AF.Identity)
            P.scalar.activation(raw_tile[:, 512:1024], ps_b[:, :], AF.Identity)
        if not l2:
            P.tensor.wait_ge(C.cA, C.cA.val if war_stats_ca is None else war_stats_ca)
            P.tensor.wait_ge(C.cV, C.cV.val if war_stats_cv is None else war_stats_cv)
            P.tensor.matmul(ps_a[0:1, :], C.invd_col_bf[:, :], raw_tile[:, 0:512], start=True, stop=True)
            P.tensor.matmul(ps_b[0:1, :], C.invd_col_bf[:, :], raw_tile[:, 512:1024], start=True, stop=True)
            P.mark_last('tensor', C.cT)
            P.scalar.wait_ge(C.cT, C.cT.val)
            P.scalar.activation(STBa, ps_a[0:1, :], AF.Identity)
            P.scalar.activation(STBb, ps_b[0:1, :], AF.Identity)
            P.tensor.wait_ge(C.cA, C.cA.val)
            P.tensor.matmul(ps_a[:, :], C.ones_row_bf[:, :], STBa, start=True, stop=True)
            P.tensor.matmul(ps_b[:, :], C.ones_row_bf[:, :], STBb, start=True, stop=True)
            P.mark_last('tensor', C.cT)
            P.vector.wait_ge(C.cT, C.cT.val)
            P.vector.tensor_tensor(raw_tile[:, 0:512], raw_tile[:, 0:512], ps_a[:, :], ALU.subtract)
            P.vector.tensor_tensor(raw_tile[:, 512:1024], raw_tile[:, 512:1024], ps_b[:, :], ALU.subtract)
        P.scalar.wait_ge(C.cV, C.cV.val)
        P.scalar.wait_ge(C.cT, C.cT.val)
        P.scalar.activation(sq[:, :], raw_tile[:, :], AF.Square)
        if l2:
            P.tensor.wait_ge(C.cV, C.cV.val if war_stats_cv is None else war_stats_cv)
        P.tensor.wait_ge(C.cA, C.cA.val)
        P.tensor.matmul(ps_a[0:1, :], C.ones_col_bf[:, :], sq[:, 0:512], start=True, stop=True)
        P.tensor.matmul(ps_b[0:1, :], C.ones_col_bf[:, :], sq[:, 512:1024], start=True, stop=True)
        P.mark_last('tensor', C.cT)
        epsap = C.eps_l2[0:1, 0:1] if l2 else C.eps_ln[0:1, 0:1]
        P.scalar.wait_ge(C.cT, C.cT.val)
        P.scalar.activation(ps_a[0:1, :], ps_a[0:1, :], AF.Ln, bias=epsap, scale=1.0)
        P.scalar.activation(ps_b[0:1, :], ps_b[0:1, :], AF.Ln, bias=epsap, scale=1.0)
        P.scalar.activation(STBa, ps_a[0:1, :], AF.Exp, bias=0.0, scale=-0.5)
        P.scalar.activation(STBb, ps_b[0:1, :], AF.Exp, bias=0.0, scale=-0.5)
        row = C.ones_row_bf if l2 else C.sqrthd_row_bf
        P.tensor.wait_ge(C.cA, C.cA.val)
        P.tensor.matmul(ps_a[:, :], row[:, :], STBa, start=True, stop=True)
        P.tensor.matmul(ps_b[:, :], row[:, :], STBb, start=True, stop=True)
        P.mark_last('tensor', C.cT)
        P.vector.wait_ge(C.cT, C.cT.val)
        P.vector.tensor_tensor(raw_tile[:, 0:512], raw_tile[:, 0:512], ps_a[:, :], ALU.mult)
        P.vector.tensor_tensor(raw_tile[:, 512:1024], raw_tile[:, 512:1024], ps_b[:, :], ALU.mult)
        if w_ap is not None:
            P.vector.tensor_scalar(out_tile[:, :], raw_tile[:, :], w_ap, b_ap, ALU.mult, ALU.add)
        elif out_tile is not raw_tile:
            P.vector.tensor_copy(out_tile[:, :], raw_tile[:, :])

    q_mm = {}
    k_mm = {}
    v_mm = {}
    i_mm = {}
    qn_done = {}
    kn_done = {}
    vn_done = {}
    ig_ev = {}
    gb_ev = {}
    u_done = {}
    scan_done = {}
    GBT = C.XT2[:, 0:1024]
    for h in range(NH):
        if h == 0:
            P.gpsimd.wait_ge(C.cT, C.cT.val)
            P.gpsimd.wait_ge(C.qC[0], C.qC[0].val)
            P.gpsimd.wait_ge(C.qC[1], C.qC[1].val)
        P.gpsimd.wait_ge(C.cT, q_mm.get(h - 1, 0))
        wslab(WUB0q, aux["Wq"], h, C.qW[0])
        ld_q = C.qW[0].val
        P.gpsimd.wait_ge(C.cT, k_mm.get(h - 1, 0))
        wslab(WUB1q, aux["Wk"], h, C.qW[1])
        ld_k = C.qW[1].val
        P.gpsimd.wait_ge(C.cT, v_mm.get(h - 1, 0))
        wslab(WQFvq, aux["Wv"], h, C.qB[0])
        ld_v = C.qB[0].val
        P.gpsimd.wait_ge(C.cT, i_mm.get(h - 1, 0))
        wslab(SCFvq, aux["Wig"], h, C.qB[1])
        ld_i = C.qB[1].val
        P.tensor.wait_ge(C.qB[1], ld_i)
        if h == 0:
            P.tensor.wait_ge(C.cV, C.cV.val)
        else:
            P.tensor.wait_ge(C.cA, gb_ev[h - 1])
        for k in range(KC):
            P.tensor.matmul(C.PS[6][:, :], itile(k), C.xb(k, 0, 512), start=(k == 0), stop=(k == KC - 1))
            P.tensor.matmul(C.PS[7][:, :], itile(k), C.xb(k, 512, 1024), start=(k == 0), stop=(k == KC - 1))
        P.mark_last('tensor', C.cT)
        i_mm[h] = C.cT.val
        P.tensor.wait_ge(C.qW[0], ld_q)
        if h > 0:
            P.tensor.wait_ge(C.cV, qn_done[h - 1])
        for k in range(KC):
            P.tensor.matmul(C.PS[0][:, :], C.WUB[0][:, k, :], C.xb(k, 0, 512), start=(k == 0), stop=(k == KC - 1))
            P.tensor.matmul(C.PS[1][:, :], C.WUB[0][:, k, :], C.xb(k, 512, 1024), start=(k == 0), stop=(k == KC - 1))
        P.mark_last('tensor', C.cT)
        q_mm[h] = C.cT.val
        P.tensor.wait_ge(C.qW[1], ld_k)
        if h > 0:
            P.tensor.wait_ge(C.cV, kn_done[h - 1])
        for k in range(KC):
            P.tensor.matmul(C.PS[2][:, :], C.WUB[1][:, k, :], C.xb(k, 0, 512), start=(k == 0), stop=(k == KC - 1))
            P.tensor.matmul(C.PS[3][:, :], C.WUB[1][:, k, :], C.xb(k, 512, 1024), start=(k == 0), stop=(k == KC - 1))
        P.mark_last('tensor', C.cT)
        k_mm[h] = C.cT.val
        P.tensor.wait_ge(C.qB[0], ld_v)
        if h > 0:
            P.tensor.wait_ge(C.cV, vn_done[h - 1])
        for k in range(KC):
            P.tensor.matmul(C.PS[4][:, :], vtile(k), C.xb(k, 0, 512), start=(k == 0), stop=(k == KC - 1))
            P.tensor.matmul(C.PS[5][:, :], vtile(k), C.xb(k, 512, 1024), start=(k == 0), stop=(k == KC - 1))
        P.mark_last('tensor', C.cT)
        v_mm[h] = C.cT.val

        P.scalar.wait_ge(C.cT, i_mm[h])
        if h > 0:
            P.scalar.wait_ge(C.cV, u_done[h - 1])
        P.scalar.activation(S0[:, 0:512], C.PS[6][:, :], AF.Sigmoid, bias=SM[:, PP_BIG + h:PP_BIG + h + 1], scale=1.0)
        P.scalar.activation(S0[:, 512:1024], C.PS[7][:, :], AF.Sigmoid, bias=SM[:, PP_BIG + h:PP_BIG + h + 1], scale=1.0)
        ig_ev[h] = C.cA.val
        P.gpsimd.wait_ge(C.cA, ig_ev[h])
        P.gpsimd.wait_ge(C.cT, C.cT.val)
        P.gpsimd.wait_ge(C.qG, C.qG.val)
        P.gpsimd.wait_ge(C.cV, C.cV.val)
        P.gpsimd.dma_start(out=C.XT2[0:1, 0:1024], in_=aux["gam"].ap()[h:h + 1, :]).then_inc(C.qR[0], 16)
        P.tensor.wait_ge(C.qR[0], C.qR[0].val)
        P.tensor.matmul(C.PS[6][:, :], C.ones_row_f32[:, :], C.XT2[0:1, 0:512], start=True, stop=True)
        P.tensor.matmul(C.PS[7][:, :], C.ones_row_f32[:, :], C.XT2[0:1, 512:1024], start=True, stop=True)
        P.mark_last('tensor', C.cT)
        P.scalar.wait_ge(C.cT, C.cT.val)
        P.scalar.activation(GBT[:, 0:512], C.PS[6][:, :], AF.Identity)
        P.scalar.activation(GBT[:, 512:1024], C.PS[7][:, :], AF.Identity)
        gb_ev[h] = C.cA.val

        phead_norm(C.PS[0], C.PS[1], QB[:, h, :], q_mm[h], True, None, None, QB[:, h, :], war_cv=0)
        qn_done[h] = C.cV.val
        phead_norm(C.PS[2], C.PS[3], T0, k_mm[h], True, None, None, T0,
                   war_cv=u_done.get(h - 1, 0))
        kn_done[h] = C.cV.val
        phead_norm(C.PS[4], C.PS[5], T1, v_mm[h], False,
                   SM[:, PP_VNW:PP_VNW + 1], SM[:, PP_VNB:PP_VNB + 1], T1,
                   war_cv=scan_done.get(h - 1, 0))
        vn_done[h] = C.cV.val
        P.vector.wait_ge(C.cA, ig_ev[h])
        P.vector.tensor_tensor(T1[:, :], T1[:, :], T0[:, :], ALU.mult)
        P.vector.tensor_tensor(T1[:, :], T1[:, :], S0[:, :], ALU.mult)
        u_done[h] = C.cV.val
        P.vector.wait_ge(C.cA, gb_ev[h])
        P.vector.tensor_tensor_scan(MEM[:, h, 0:512], GBT[:, 0:512], T1[:, 0:512], 0.0, ALU.mult, ALU.add)
        P.vector.tensor_tensor_scan(MEM[:, h, 512:1024], GBT[:, 512:1024], T1[:, 512:1024],
                                    MEM[:, h, 511:512], ALU.mult, ALU.add)
        scan_done[h] = C.cV.val

    P.vector.tensor_copy(S0[:, 0:16], MEM[:, :, 1023:1024].rearrange("p a b -> p (a b)"))
    P.gpsimd.wait_ge(C.cV, C.cV.val)
    P.gpsimd.dma_start(out=aux["sbounce"].ap(), in_=S0[:, 0:16]).then_inc(C.qG, 16)
    P.gpsimd.wait_ge(C.qG, C.qG.val)
    P.gpsimd.collective_compute(
        "AllGather", ALU.bypass, replica_groups=[list(range(NCORES))],
        ins=[aux["sbounce"].ap().opt()], outs=[aux["sa2_all"].ap().opt()],
    ).then_inc(C.cc, 1)
    P.gpsimd.collective_compute(
        "AllGather", ALU.bypass, replica_groups=[list(range(NCORES))],
        ins=[aux["abounce"].ap().opt()], outs=[aux["aa_all"].ap().opt()],
    ).then_inc(C.cc, 1)
    P.gpsimd.wait_ge(C.cc, C.cc.val)
    P.gpsimd.dma_start(out=C.XT2[0:1, 0:128], in_=aux["aa_all"].ap().rearrange("a b -> (a b)").unsqueeze(0)).then_inc(C.qG, 16)
    P.gpsimd.dma_start(out=C.XT2[0:1, 128:256], in_=aux["cmrow"][0:1, :]).then_inc(C.qG, 16)
    P.gpsimd.dma_start(out=C.XT2[0:1, 256:384], in_=aux["cmrow"][1:2, :]).then_inc(C.qG, 16)
    P.vector.wait_ge(C.qG, C.qG.val)
    P.vector.tensor_tensor(C.XT2[0:1, 0:128], C.XT2[0:1, 0:128], C.XT2[0:1, 128:256], ALU.mult)
    P.vector.tensor_tensor(C.XT2[0:1, 0:128], C.XT2[0:1, 0:128], C.XT2[0:1, 256:384], ALU.add)
    P.vector.tensor_copy(C.STB[0:1, 0:128], C.XT2[0:1, 0:128])
    P.tensor.wait_ge(C.cV, C.cV.val)
    P.tensor.matmul(C.PS[0][:, 0:128], C.ones_row_bf[:, :], C.STB[0:1, 0:128], start=True, stop=True)
    P.mark_last('tensor', C.cT)
    P.vector.wait_ge(C.cT, C.cT.val)
    P.vector.memset(SM[:, SM_CARRY:SM_CARRY + 16], 0.0)
    for j in range(NCORES):
        P.gpsimd.wait_ge(C.cV, C.cV.val)
        P.gpsimd.dma_start(out=SM[:, SM_SD:SM_SD + 16], in_=aux["sa2_all"].ap()[128 * j:128 * (j + 1), :]).then_inc(C.qR[1], 16)
        P.vector.wait_ge(C.qR[1], C.qR[1].val)
        P.vector.tensor_scalar(SM[:, SM_SD:SM_SD + 16], SM[:, SM_SD:SM_SD + 16],
                               SM[:, PP_CMS + j:PP_CMS + j + 1], None, ALU.mult)
        P.vector.tensor_tensor(SM[:, SM_CARRY:SM_CARRY + 16], SM[:, SM_CARRY:SM_CARRY + 16],
                               C.PS[0][:, 16 * j:16 * (j + 1)], ALU.mult)
        P.vector.tensor_tensor(SM[:, SM_CARRY:SM_CARRY + 16], SM[:, SM_CARRY:SM_CARRY + 16],
                               SM[:, SM_SD:SM_SD + 16], ALU.add)

    og_mm = {}
    og_sig = {}
    outb_done = {}
    fix_done = {}
    cv0 = C.cV.val
    ca0 = C.cA.val
    pair_cv = {0: cv0, 1: cv0}
    pair_ca = {0: ca0, 1: ca0}
    for h in range(NH):
        p = h % 2
        sa, sb = (C.PS[0], C.PS[1]) if p == 0 else (C.PS[4], C.PS[5])
        P.gpsimd.wait_ge(C.cT, C.cT.val)
        P.gpsimd.wait_ge(C.qG, C.qG.val)
        P.gpsimd.dma_start(out=C.XT2[0:1, 0:1024], in_=aux["gamc"].ap()[h:h + 1, :]).then_inc(C.qR[0], 16)
        P.tensor.wait_ge(C.qR[0], C.qR[0].val)
        P.tensor.wait_ge(C.cV, fix_done.get(h - 1, cv0))
        P.tensor.matmul(C.PS[6][:, :], C.ones_row_f32[:, :], C.XT2[0:1, 0:512], start=True, stop=True)
        P.tensor.matmul(C.PS[7][:, :], C.ones_row_f32[:, :], C.XT2[0:1, 512:1024], start=True, stop=True)
        P.mark_last('tensor', C.cT)
        gamc_mm = C.cT.val
        P.gpsimd.wait_ge(C.cT, og_mm.get(h - 1, 0))
        wslab(WUB0q, aux["Wog"], h, C.qW[0])
        P.tensor.wait_ge(C.qW[0], C.qW[0].val)
        P.tensor.wait_ge(C.cA, og_sig.get(h - 1, ca0))
        if h == 0:
            P.tensor.wait_ge(C.cV, cv0)
        for k in range(KC):
            P.tensor.matmul(C.PS[2][:, :], C.WUB[0][:, k, :], xcb[:, k, 0:512], start=(k == 0), stop=(k == KC - 1))
            P.tensor.matmul(C.PS[3][:, :], C.WUB[0][:, k, :], xcb[:, k, 512:1024], start=(k == 0), stop=(k == KC - 1))
        P.mark_last('tensor', C.cT)
        og_mm[h] = C.cT.val
        memv = MEM[:, h, :]
        P.vector.wait_ge(C.cT, gamc_mm)
        P.vector.scalar_tensor_tensor(memv[:, 0:512], C.PS[6][:, :], SM[:, SM_CARRY + h:SM_CARRY + h + 1],
                                      memv[:, 0:512], ALU.mult, ALU.add)
        P.vector.scalar_tensor_tensor(memv[:, 512:1024], C.PS[7][:, :], SM[:, SM_CARRY + h:SM_CARRY + h + 1],
                                      memv[:, 512:1024], ALU.mult, ALU.add)
        fix_done[h] = C.cV.val
        phead_norm(sa, sb, memv, 0, False,
                   SM[:, PP_MNW:PP_MNW + 1], SM[:, PP_MNB:PP_MNB + 1], T0, from_psum=False,
                   war_stats_cv=max(fix_done[h], pair_cv[p]), war_stats_ca=pair_ca[p], stb_off=1024 * p)
        P.vector.tensor_tensor(T0[:, :], T0[:, :], QB[:, h, :], ALU.mult)
        phead_norm(sa, sb, T0, 0, False,
                   SM[:, PP_GNW + h:PP_GNW + h + 1], SM[:, PP_GNB + h:PP_GNB + h + 1], T0,
                   from_psum=False, stb_off=1024 * p)
        pair_cv[p] = C.cV.val
        pair_ca[p] = C.cA.val
        P.scalar.wait_ge(C.cT, og_mm[h])
        P.scalar.wait_ge(C.cV, outb_done.get(h - 1, cv0))
        P.scalar.activation(S0[:, 0:512], C.PS[2][:, :], AF.Sigmoid, bias=SM[:, PP_BOG + h:PP_BOG + h + 1], scale=1.0)
        P.scalar.activation(S0[:, 512:1024], C.PS[3][:, :], AF.Sigmoid, bias=SM[:, PP_BOG + h:PP_BOG + h + 1], scale=1.0)
        og_sig[h] = C.cA.val
        P.vector.wait_ge(C.cA, og_sig[h])
        P.vector.tensor_tensor(OUTB[:, h, :], T0[:, :], S0[:, :], ALU.mult)
        outb_done[h] = C.cV.val
    p2_done = C.cV.val

    M0B = C.AR0[:, 0:1024]
    R0B = C.AR0[:, 1024:2048]
    Q0B = C.AR0[:, 2048:3072]
    P.tensor.wait_ge(C.cV, p2_done)
    P.tensor.matmul(C.PS[4][:, :], C.ones_row_bf[:, :], C.SM0[:, 0:512], start=True, stop=True)
    P.tensor.matmul(C.PS[5][:, :], C.ones_row_bf[:, :], C.SM0[:, 512:1024], start=True, stop=True)
    P.tensor.matmul(C.PS[6][:, :], C.ones_row_bf[:, :], C.SR0[:, 0:512], start=True, stop=True)
    P.tensor.matmul(C.PS[7][:, :], C.ones_row_bf[:, :], C.SR0[:, 512:1024], start=True, stop=True)
    P.mark_last('tensor', C.cT)
    P.vector.wait_ge(C.cT, C.cT.val)
    P.vector.tensor_copy(M0B[:, 0:512], C.PS[4][:, :])
    P.vector.tensor_copy(M0B[:, 512:1024], C.PS[5][:, :])
    P.vector.tensor_copy(R0B[:, 0:512], C.PS[6][:, :])
    P.vector.tensor_copy(R0B[:, 512:1024], C.PS[7][:, :])
    P.vector.tensor_tensor(Q0B, M0B, R0B, ALU.mult)
    wo_mm = {}
    ep = {}
    outw = {}
    for m in range(KC):
        par = m % 2
        P.gpsimd.wait_ge(C.cT, wo_mm.get(m - 2, C.cT.val if m < 2 else 0))
        wslab(WUB0q if par == 0 else WUB1q, aux["Wo"], m, C.qW[par])
        pd, pd2 = C.PS[0 + 2 * par], C.PS[1 + 2 * par]
        P.tensor.wait_ge(C.qW[par], C.qW[par].val)
        if m < 2:
            P.tensor.wait_ge(C.cV, C.cV.val)
        else:
            P.tensor.wait_ge(C.cV, ep.get(m - 2, 0))
        for k in range(NH):
            P.tensor.matmul(pd[:, :], (C.WUB[0] if par == 0 else C.WUB[1])[:, k, :], OUTB[:, k, 0:512],
                            start=(k == 0), stop=(k == NH - 1))
            P.tensor.matmul(pd2[:, :], (C.WUB[0] if par == 0 else C.WUB[1])[:, k, :], OUTB[:, k, 512:1024],
                            start=(k == 0), stop=(k == NH - 1))
        P.mark_last('tensor', C.cT)
        wo_mm[m] = C.cT.val
        buf = C.XT[par]
        if m >= 2:
            P.sync.wait_ge(C.cV, ep[m - 2])
            P.sync.wait_ge(C.qD[par], outw[m - 2])
        else:
            P.sync.wait_ge(C.cV, C.cV.val)
        P.sync.dma_start(out=buf, in_=src_dram[m]).then_inc(C.qE[par], 16)
        P.vector.wait_ge(C.qE[par], C.qE[par].val)
        P.vector.wait_ge(C.cT, C.cT.val)
        P.vector.tensor_tensor(buf, buf, R0B, ALU.mult)
        P.vector.tensor_tensor(buf, buf, Q0B, ALU.subtract)
        P.vector.tensor_tensor(buf[:, 0:512], buf[:, 0:512], pd[:, :], ALU.add)
        P.vector.tensor_tensor(buf[:, 512:1024], buf[:, 512:1024], pd2[:, :], ALU.add)
        ep[m] = C.cV.val
        P.gpsimd.wait_ge(C.cV, C.cV.val)
        P.gpsimd.dma_start(out=hm_dram[m], in_=buf).then_inc(C.qD[par], 16)
        outw[m] = C.qD[par].val

    emit_ln(kb, C, hm_dram, fused_w=SM[:, PP_LN2W:PP_LN2W + 16], fused_b=SM[:, PP_LN2B:PP_LN2B + 16])
    g_mm = {}
    ev2 = {}
    WQS = [(WQFv[0][:, 0:8, :], 0, 1024), (WQFv[1][:, 0:8, :], 1024, 2048)]
    SCS = [(SCFv[0][:, 0:8, :], 0, 1024), (SCFv[1][:, 0:8, :], 1024, 2048)]

    def gtile(m, k):
        return C.WUB[0][:, k, :] if m % 2 == 0 else WQFv[k // 8][:, k % 8, :]

    def vtile2(m, k):
        return C.WUB[1][:, k, :] if m % 2 == 0 else SCFv[k // 8][:, k % 8, :]

    for m in range(FC):
        par = m % 2
        sg, sv = (C.qW[0], C.qW[1]) if par == 0 else (C.qB[0], C.qB[1])
        P.gpsimd.wait_ge(C.cT, g_mm.get(m - 2, C.cT.val if m < 2 else 0))
        wslab(WUB0q if par == 0 else WQS, aux["Wgate"], m, sg)
        wslab(WUB1q if par == 0 else SCS, aux["Wval"], m, sv)
        pg, pg2 = C.PS[0 + 2 * par], C.PS[1 + 2 * par]
        pv, pv2 = C.PS[4 + 2 * par], C.PS[5 + 2 * par]
        P.tensor.wait_ge(sg, sg.val)
        P.tensor.wait_ge(sv, sv.val)
        if m < 2:
            P.tensor.wait_ge(C.cV, C.cV.val)
        else:
            P.tensor.wait_ge(C.cV, ev2[m - 2])
        for k in range(KC):
            P.tensor.matmul(pg[:, :], gtile(m, k), C.xb(k, 0, 512), start=(k == 0), stop=(k == KC - 1))
            P.tensor.matmul(pg2[:, :], gtile(m, k), C.xb(k, 512, 1024), start=(k == 0), stop=(k == KC - 1))
            P.tensor.matmul(pv[:, :], vtile2(m, k), C.xb(k, 0, 512), start=(k == 0), stop=(k == KC - 1))
            P.tensor.matmul(pv2[:, :], vtile2(m, k), C.xb(k, 512, 1024), start=(k == 0), stop=(k == KC - 1))
        P.mark_last('tensor', C.cT)
        g_mm[m] = C.cT.val
        P.scalar.wait_ge(C.cT, C.cT.val)
        if m > 0:
            P.scalar.wait_ge(C.cV, ev2[m - 1])
        P.scalar.activation(S0[:, 0:512], pg[:, :], AF.Silu)
        P.scalar.activation(S0[:, 512:1024], pg2[:, :], AF.Silu)
        P.vector.wait_ge(C.cA, C.cA.val)
        P.vector.tensor_tensor(C.XH[:, m, 0:512], S0[:, 0:512], pv[:, :], ALU.mult)
        P.vector.tensor_tensor(C.XH[:, m, 512:1024], S0[:, 512:1024], pv2[:, :], ALU.mult)
        ev2[m] = C.cV.val
    mlp_up_done = C.cV.val

    wom = {}
    epo = {}
    outd = {}
    for d in range(KC):
        pd, pd2 = C.PS[0 + 2 * (d % 2)], C.PS[1 + 2 * (d % 2)]
        for r4 in range(4):
            par = r4 % 2
            step = 4 * d + r4
            P.gpsimd.wait_ge(C.cT, wom.get(step - 2, C.cT.val if step < 2 else 0))
            src_ap = aux["Wout"][2048 * r4:2048 * (r4 + 1), 128 * d:128 * (d + 1)]
            P.gpsimd.dma_start(
                out=C.WUB[par][:, 0:16, :],
                in_=src_ap.rearrange("(kc p) n -> p kc n", p=128),
            ).then_inc(C.qW[par], 16)
            P.tensor.wait_ge(C.qW[par], C.qW[par].val)
            if step < 2:
                P.tensor.wait_ge(C.cV, mlp_up_done)
            if r4 == 0 and d >= 2:
                P.tensor.wait_ge(C.cV, epo[d - 2])
            for kk in range(KC):
                h = KC * r4 + kk
                st_ = (r4 == 0 and kk == 0)
                sp_ = (r4 == 3 and kk == KC - 1)
                P.tensor.matmul(pd[:, :], C.WUB[par][:, kk, :], C.XH[:, h, 0:512], start=st_, stop=sp_)
                P.tensor.matmul(pd2[:, :], C.WUB[par][:, kk, :], C.XH[:, h, 512:1024], start=st_, stop=sp_)
            P.mark_last('tensor', C.cT)
            wom[step] = C.cT.val
        if d >= 1:
            P.sync.wait_ge(C.cV, epo[d - 1])
            P.sync.wait_ge(C.qD[0], outd[d - 1])
        else:
            P.sync.wait_ge(C.cV, C.cV.val)
        P.sync.dma_start(out=C.XT[0], in_=src_dram[d]).then_inc(C.qE[0], 16)
        P.sync.dma_start(out=C.XT[1], in_=hm_dram[d]).then_inc(C.qE[1], 16)
        P.vector.wait_ge(C.qE[0], C.qE[0].val)
        P.vector.wait_ge(C.qE[1], C.qE[1].val)
        P.vector.wait_ge(C.cT, C.cT.val)
        P.vector.tensor_tensor(C.XT[0], C.XT[0], C.XT[1], ALU.add)
        P.vector.tensor_tensor(C.XT[0][:, 0:512], C.XT[0][:, 0:512], pd[:, :], ALU.add)
        P.vector.tensor_tensor(C.XT[0][:, 512:1024], C.XT[0][:, 512:1024], pd2[:, :], ALU.add)
        epo[d] = C.cV.val
        P.gpsimd.wait_ge(C.cV, C.cV.val)
        P.gpsimd.dma_start(out=dst_dram[d], in_=C.XT[0]).then_inc(C.qD[0], 16)
        outd[d] = C.qD[0].val


def prep_attnb_pack(inp, core):
    pp = np.zeros((128, PPN), dtype=np.float32)
    def col(c):
        return c - PP
    cw = inp["L0_attn_b_conv_w"].reshape(D, 4)
    for k in range(KC):
        for j in range(4):
            pp[:, col(PP_CONVW) + 4 * k + j] = cw[128 * k:128 * (k + 1), j]
        pp[:, col(PP_CONVB) + k] = inp["L0_attn_b_conv_b"][128 * k:128 * (k + 1)]
        pp[:, col(PP_LN1W) + k] = inp["L0_attn_b_ln1w"][128 * k:128 * (k + 1)]
        pp[:, col(PP_LN1B) + k] = inp["L0_attn_b_ln1b"][128 * k:128 * (k + 1)]
        pp[:, col(PP_LN2W) + k] = inp["L0_attn_b_ln2w"][128 * k:128 * (k + 1)]
        pp[:, col(PP_LN2B) + k] = inp["L0_attn_b_ln2b"][128 * k:128 * (k + 1)]
    pp[:, col(PP_VNW)] = inp["L0_attn_b_vnw"]
    pp[:, col(PP_VNB)] = inp["L0_attn_b_vnb"]
    pp[:, col(PP_MNW)] = inp["L0_attn_b_mnw"]
    pp[:, col(PP_MNB)] = inp["L0_attn_b_mnb"]
    for h in range(NH):
        pp[:, col(PP_GNW) + h] = inp["L0_attn_b_gnw"][128 * h:128 * (h + 1)]
        pp[:, col(PP_GNB) + h] = inp["L0_attn_b_gnb"][128 * h:128 * (h + 1)]
        pp[:, col(PP_BIG) + h] = inp["L0_attn_b_big"][128 * h:128 * (h + 1)]
        pp[:, col(PP_BOG) + h] = inp["L0_attn_b_bog"][128 * h:128 * (h + 1)]
    pp[0:16, col(PP_BG)] = inp["L0_attn_b_bg"]
    for j in range(NCORES):
        pp[:, col(PP_CMS) + j] = 1.0 if j < core else 0.0
        pp[:, col(PP_HM8) + j] = 1.0 if j == core - 1 else 0.0
    cmrow = np.zeros((2, 128), dtype=np.float32)
    for j in range(NCORES):
        for h in range(NH):
            cmrow[0, 16 * j + h] = 1.0 if j < core else 0.0
    cmrow[1] = 1.0 - cmrow[0]
    return np.ascontiguousarray(pp), cmrow


def declare_attnb_aux(kb, with_inputs=True):
    aux = {}
    for nm, shp in [("Wq", [D, D]), ("Wk", [D, D]), ("Wv", [D, D]), ("Wig", [D, D]),
                    ("Wog", [D, D]), ("Wo", [D, D]), ("Wg", [D, NH]),
                    ("Wgate", [D, FF]), ("Wval", [D, FF]), ("Wout", [FF, D])]:
        aux[nm] = kb.din("b_" + nm, shp)
    aux["parpack"] = kb.din("parpack", [128, PPN])
    aux["cmrow"] = kb.din("cmrow", [2, 128])
    aux["gam"] = kb.dscratch("gam", [NH, T])
    aux["gamc"] = kb.dscratch("gamc", [NH, T])
    aux["halo_in"] = kb.dscratch("halo_in", [128, KC, 3])
    aux["halo_all"] = kb.dscratch("halo_all", [NCORES, 128, KC, 3])
    aux["sbounce"] = kb.dscratch("sbounce", [128, NH])
    aux["sa2_all"] = kb.dscratch("sa2_all", [NCORES * 128, NH])
    aux["abounce"] = kb.dscratch("abounce", [NH, 1])
    aux["aa_all"] = kb.dscratch("aa_all", [NCORES * NH, 1])
    return aux


A_XNB, A_TT, A_AOUT, A_COUT = 0, 1024, 2048, 3072
A_DOUT, A_D2, A_EOUT, A_E2 = 4096, 5120, 6144, 7168
A_CQK, A_CV = 8192, 9216
A_O1B = 10240
A_O2T1, A_O2T2, A_LEP2, A_Y = 12288, 13312, 14336, 15360
A_PR1, A_S1F, A_P1B = 16384, 17408, 17920
A_S2F, A_S2B, A_P2T, A_RST = 18432, 18944, 19200, 19456
A_IBF, A_BD, A_BDP = 19472, 19536, 19920
A_W0B, A_W1, A_ALNW, A_ALNB, A_PB = 20176, 20272, 20275, 20403, 20531
A_M2, A_SLAB, A_YC, A_ST16, A_SQ16 = 20532, 21044, 22068, 22580, 22624
A_IF32 = 22884
A_PR1F, A_O1F, A_L2F = 23040, 25088, 27136
A_B0, A_B1 = 29184, 29248
A_SQF = 2048
A_YTR, A_SCR, A_X1T, A_STG = 0, 8192, 9216, 10240


def prep_attna_consts(inp):
    qw = inp["L0_attn_a_qkv_w"]
    pw = inp["L0_attn_a_proj_w"]
    cw0 = inp["L0_attn_a_conv_w0"]
    cw1 = inp["L0_attn_a_conv_w1"]
    def bd(colmap, umatch):
        m = np.zeros((128, 128), dtype=np.float32)
        for u in range(8):
            for c in range(16):
                for oc, (uu, col) in colmap(u):
                    pass
        return m
    bdA = np.zeros((128, 128), np.float32)
    bdC = np.zeros((128, 128), np.float32)
    bdD = np.zeros((128, 128), np.float32)
    bdD2 = np.zeros((128, 128), np.float32)
    bdE = np.zeros((128, 128), np.float32)
    bdE2 = np.zeros((128, 128), np.float32)
    for u in range(8):
        for c in range(16):
            r = 16 * u + c
            for oc in range(8):
                bdA[r, 16 * u + oc] = qw[c, 0 + oc]
                bdA[r, 16 * u + 8 + oc] = qw[c, 16 + oc]
            for oc in range(16):
                bdC[r, 16 * u + oc] = qw[c, 32 + oc]
            if u < 4:
                for oc in range(8):
                    bdD[r, 32 * u + oc] = qw[c, 8 + oc]
                    bdE[r, 32 * u + oc] = qw[c, 24 + oc]
            else:
                for oc in range(8):
                    bdD2[r, 32 * (u - 4) + oc] = qw[c, 8 + oc]
                    bdE2[r, 32 * (u - 4) + oc] = qw[c, 24 + oc]
    bdP1 = np.zeros((128, 128), np.float32)
    bdPL = np.zeros((128, 128), np.float32)
    for u in range(8):
        for c in range(8):
            for oc in range(16):
                bdP1[16 * u + c, 16 * u + oc] = pw[c, oc]
                bdPL[16 * u + 8 + c, 16 * u + oc] = pw[8 + c, oc]
    bdPG = np.zeros((128, 128), np.float32)
    bdPG2 = np.zeros((128, 128), np.float32)
    for r in range(4):
        for c in range(8):
            for oc in range(16):
                bdPG[32 * r + c, 16 * r + oc] = pw[8 + c, oc]
                bdPG2[32 * r + c, 16 * (r + 4) + oc] = pw[8 + c, oc]
    W0 = np.zeros((128, 192), np.float32)
    for du in range(3):
        for i in range(8):
            for c in range(8):
                W0[:, 64 * du + 8 * i + c] = cw0[c, 0, 1, du]
    W1 = np.zeros((128, 3), np.float32)
    for dt in range(3):
        for u in range(8):
            for c in range(8):
                W1[16 * u + 8 + c, dt] = cw1[c, 0, dt, 1]
    m2 = np.zeros((32, 32), np.float32)
    for a in range(32):
        for b in range(32):
            if a % 4 != b % 4:
                m2[a, b] = -1e9
    M2 = np.tile(m2, (4, 16))
    alnw = np.zeros((128, 128), np.float32)
    alnb = np.zeros((128, 128), np.float32)
    for u in range(8):
        for c in range(16):
            alnw[:, 16 * u + c] = inp["L0_attn_a_ln_w"][c]
            alnb[:, 16 * u + c] = inp["L0_attn_a_ln_b"][c]
    pb = np.zeros((128, 1), np.float32)
    for u in range(8):
        for c in range(16):
            pb[16 * u + c, 0] = inp["L0_attn_a_proj_b"][c]
    b0 = np.zeros((128, 64), np.float32)
    for i in range(8):
        for c in range(8):
            b0[:, 8 * i + c] = inp["L0_attn_a_conv_b0"][c]
    b1 = np.zeros((128, 1), np.float32)
    for u in range(8):
        for c in range(8):
            b1[16 * u + 8 + c, 0] = inp["L0_attn_a_conv_b1"][c]
    bds = np.stack([bdA, bdC, bdD, bdD2, bdE, bdE2], 0)
    bdp = np.stack([bdP1, bdPG, bdPG2, bdPL], 0)
    return {"a_bds": bds, "a_bdp": bdp, "a_w0": W0, "a_w1": W1, "a_m2": M2,
            "a_alnw": alnw, "a_alnb": alnb, "a_pb": pb, "a_b0": b0, "a_b1": b1}


def declare_attna_aux(kb):
    aux = {}
    aux["a_bds"] = kb.din("a_bds", [6, 128, 128])
    aux["a_bdp"] = kb.din("a_bdp", [4, 128, 128])
    aux["a_w0"] = kb.din("a_w0", [128, 192])
    aux["a_w1"] = kb.din("a_w1", [128, 3])
    aux["a_m2"] = kb.din("a_m2", [128, 512])
    aux["a_alnw"] = kb.din("a_alnw", [128, 128])
    aux["a_alnb"] = kb.din("a_alnb", [128, 128])
    aux["a_pb"] = kb.din("a_pb", [128, 1])
    aux["a_b0"] = kb.din("a_b0", [128, 64])
    aux["a_b1"] = kb.din("a_b1", [128, 1])
    aux["I128"] = kb.din("I128", [128, 128])
    aux["xrows"] = kb.din("xrows", [T, D])
    aux["a2a_in"] = kb.dscratch("a2a_in", [8, 8, 128, 256], BF16)
    aux["a2a_out"] = kb.dscratch("a2a_out", [8, 8, 128, 256], BF16)
    return aux


def emit_attn_a(kb, C, aux, x1_dram, chunk_cb=None):
    P = kb.prog
    A = C.AR0
    Ab = A[:, :].bitcast(BF16)

    def bft(c0, n):
        return Ab[:, 2 * c0:2 * c0 + n]

    def f32t(c0, n):
        return A[:, c0:c0 + n]

    XNB = bft(A_XNB, 2048)
    TT_ = bft(A_TT, 2048)
    AOUT = bft(A_AOUT, 2048)
    COUT = bft(A_COUT, 2048)
    DOUT, D2OUT = bft(A_DOUT, 2048), bft(A_D2, 2048)
    EOUT, E2OUT = bft(A_EOUT, 2048), bft(A_E2, 2048)
    CQK, CVt = bft(A_CQK, 2048), bft(A_CV, 2048)
    O1B = bft(A_O1B, 2048)
    O2T1, O2T2 = bft(A_O2T1, 2048), bft(A_O2T2, 2048)
    LEP2, Yt = bft(A_LEP2, 2048), bft(A_Y, 2048)
    PR1 = bft(A_PR1, 2048)
    S1F = f32t(A_S1F, 512)
    P1B = bft(A_P1B, 1024)
    S2F = f32t(A_S2F, 512)
    S2B = bft(A_S2B, 1024)
    P2T = bft(A_P2T, 512)
    RST = f32t(A_RST, 16)
    IBF = bft(A_IBF, 128)
    BD = [bft(A_BD + 64 * i, 128) for i in range(6)]
    BDP = [bft(A_BDP + 64 * i, 128) for i in range(4)]
    W0B = bft(A_W0B, 192)
    W1T = f32t(A_W1, 3)
    ALNW, ALNB = f32t(A_ALNW, 128), f32t(A_ALNB, 128)
    PBT = f32t(A_PB, 1)
    M2T = f32t(A_M2, 512)
    SLAB = bft(A_SLAB, 2048)
    YC = f32t(A_YC, 512)
    ST32 = f32t(A_ST16, 32)
    SQ16 = bft(A_SQ16, 512)
    IF32 = f32t(A_IF32, 128)
    SQF = f32t(A_SQF, 2048)
    PR1F = f32t(A_PR1F, 2048)
    O1F = f32t(A_O1F, 2048)
    L2F = f32t(A_L2F, 2048)
    B0T = f32t(A_B0, 64)
    B1T = f32t(A_B1, 1)

    for i in range(6):
        P.gpsimd.dma_start(out=BD[i], in_=aux["a_bds"][i]).then_inc(C.qG, 16)
    for i in range(4):
        P.gpsimd.dma_start(out=BDP[i], in_=aux["a_bdp"][i]).then_inc(C.qG, 16)
    P.gpsimd.dma_start(out=IBF, in_=aux["I128"]).then_inc(C.qG, 16)
    P.gpsimd.dma_start(out=W0B, in_=aux["a_w0"]).then_inc(C.qG, 16)
    P.sync.dma_start(out=W1T, in_=aux["a_w1"]).then_inc(C.qS, 16)
    P.sync.dma_start(out=M2T, in_=aux["a_m2"]).then_inc(C.qS, 16)
    P.sync.dma_start(out=ALNW, in_=aux["a_alnw"]).then_inc(C.qS, 16)
    P.sync.dma_start(out=ALNB, in_=aux["a_alnb"]).then_inc(C.qS, 16)
    P.sync.dma_start(out=PBT, in_=aux["a_pb"]).then_inc(C.qS, 16)
    P.sync.dma_start(out=IF32, in_=aux["I128"]).then_inc(C.qS, 16)
    P.sync.dma_start(out=B0T, in_=aux["a_b0"]).then_inc(C.qS, 16)
    P.sync.dma_start(out=B1T, in_=aux["a_b1"]).then_inc(C.qS, 16)
    pre_g, pre_s = C.qG.val, C.qS.val

    for sb in range(8):
        P.sync.wait_ge(C.cV, C.cV.val)
        P.sync.wait_ge(C.cT, C.cT.val)
        if sb > 0:
            P.sync.wait_ge(C.cA, a_xnb)
        P.sync.dma_start(out=C.XT2[:, :], in_=aux["xrows"][128 * sb:128 * (sb + 1), :]).then_inc(C.qE[0], 16)
        P.vector.wait_ge(C.qE[0], C.qE[0].val)
        if sb == 0:
            P.vector.wait_ge(C.qS, pre_s)
            P.vector.wait_ge(C.qG, pre_g)
        P.vector.tensor_reduce(RST[:, 0:1], C.XT2[:, :], axis=AXX, op=ALU.add)
        P.vector.tensor_scalar(RST[:, 0:1], RST[:, 0:1], 1.0 / D, None, ALU.mult)
        P.vector.tensor_scalar(C.XT2[:, :], C.XT2[:, :], RST[:, 0:1], None, ALU.subtract)
        P.vector.tensor_tensor(SQF, C.XT2[:, :], C.XT2[:, :], ALU.mult)
        P.vector.tensor_reduce(RST[:, 1:2], SQF, axis=AXX, op=ALU.add)
        P.vector.tensor_scalar(RST[:, 1:2], RST[:, 1:2], 1.0 / D, None, ALU.mult)
        P.vector.tensor_scalar(RST[:, 1:2], RST[:, 1:2], EPS, None, ALU.add)
        P.scalar.wait_ge(C.cV, C.cV.val)
        P.scalar.activation(RST[:, 1:2], RST[:, 1:2], AF.Sqrt)
        P.vector.wait_ge(C.cA, C.cA.val)
        P.vector.reciprocal(RST[:, 1:2], RST[:, 1:2])
        P.vector.tensor_scalar(C.XT2[:, :], C.XT2[:, :], RST[:, 1:2], None, ALU.mult)
        ln_ok = C.cV.val
        P.scalar.wait_ge(C.cV, ln_ok)
        P.scalar.activation(XNB, C.XT2[:, :], AF.Identity)
        a_xnb = C.cA.val
        P.vector.memset(O1F[:, :], 0.0)
        P.vector.memset(O1B[:, :], 0.0)
        if chunk_cb is not None:
            chunk_cb()

        tb_ev = []
        for g in range(4):
            P.tensor.wait_ge(C.cV, ln_ok)
            if g == 0:
                P.tensor.wait_ge(C.cA, a_xnb)
            if sb > 0:
                P.tensor.wait_ge(C.cA, prev_cp)
            for q in range(4):
                j = 4 * g + q
                P.tensor.matmul(C.PS[g % 4][:, 128 * q:128 * (q + 1)], XNB[:, 128 * j:128 * (j + 1)], IBF,
                                start=True, stop=True)
            P.mark_last('tensor', C.cT)
            P.scalar.wait_ge(C.cT, C.cT.val)
            P.scalar.activation(TT_[:, 512 * g:512 * (g + 1)], C.PS[g % 4][:, :], AF.Identity)
            tb_ev.append(C.cA.val)

        outs = [AOUT, COUT, DOUT, D2OUT, EOUT, E2OUT]
        bd_ev = []
        for ci in range(6):
            for g in range(4):
                i = 4 * ci + g
                P.tensor.wait_ge(C.cA, tb_ev[g] if i < 4 else bd_ev[i - 4])
                for q in range(4):
                    j = 4 * g + q
                    P.tensor.matmul(C.PS[4 + g % 4][:, 128 * q:128 * (q + 1)], BD[ci],
                                    TT_[:, 128 * j:128 * (j + 1)], start=True, stop=True)
                P.mark_last('tensor', C.cT)
                P.scalar.wait_ge(C.cT, C.cT.val)
                P.scalar.activation(outs[ci][:, 512 * g:512 * (g + 1)], C.PS[4 + g % 4][:, :], AF.Identity)
                bd_ev.append(C.cA.val)

        cp_ev = []
        for si, (srcT, dstT) in enumerate(((AOUT, CQK), (COUT, CVt))):
            for g in range(4):
                i = 4 * si + g
                P.tensor.wait_ge(C.cA, bd_ev[4 * si + g] if i < 4 else cp_ev[i - 4])
                for q in range(4):
                    j = 4 * g + q
                    P.tensor.matmul(C.PS[g % 4][:, 128 * q:128 * (q + 1)], srcT[:, 128 * j:128 * (j + 1)], IBF,
                                    start=True, stop=True)
                P.mark_last('tensor', C.cT)
                P.scalar.wait_ge(C.cT, C.cT.val)
                P.scalar.activation(dstT[:, 512 * g:512 * (g + 1)], C.PS[g % 4][:, :], AF.Identity)
                cp_ev.append(C.cA.val)
        prev_cp = cp_ev[-1]


        cv_ap = CVt[:, :].rearrange("p (j u c) -> p j u c", j=16, u=8, c=16)
        for uh in range(2):
            Dsrc = DOUT if uh == 0 else D2OUT
            Esrc = EOUT if uh == 0 else E2OUT
            P.tensor.wait_ge(C.cV, C.cV.val)
            P.tensor.wait_ge(C.cA, C.cA.val)
            for j in range(16):
                for r in range(4):
                    for band in range(4):
                        P.tensor.matmul(
                            C.PS[4 + r][32 * band:32 * band + 32, 32 * j:32 * j + 32],
                            Dsrc[32 * r:32 * r + 8, 128 * j + 32 * band:128 * j + 32 * band + 32],
                            Esrc[32 * r:32 * r + 8, 128 * j + 32 * band:128 * j + 32 * band + 32],
                            start=True, stop=True, tile_position=(32 * r, 32 * band),
                        )
            P.mark_last('tensor', C.cT)
            gram_done = C.cT.val
            O2 = O2T1 if uh == 0 else O2T2
            for r in range(4):
                P.vector.wait_ge(C.cT, gram_done)
                P.vector.tensor_tensor(S2F[:, :], C.PS[4 + r][:, :], M2T, ALU.add)
                P.vector.tensor_reduce(RST[:, 0:16], S2F[:, :].rearrange("p (j b) -> p j b", j=16),
                                       axis=AXX, op=ALU.max)
                P.vector.tensor_tensor(S2F[:, :].rearrange("p (j b) -> p j b", j=16),
                                       S2F[:, :].rearrange("p (j b) -> p j b", j=16),
                                       RST[:, 0:16].unsqueeze(2).to_broadcast([128, 16, 32]), ALU.subtract)
                P.scalar.wait_ge(C.cV, C.cV.val)
                P.scalar.activation(S2B[:, 0:512], S2F[:, :], AF.Exp, bias=0.0, scale=float(1.0 / np.sqrt(8.0)))
                P.vector.wait_ge(C.cA, C.cA.val)
                P.vector.tensor_reduce(RST[:, 0:16], S2B[:, 0:512].rearrange("p (j b) -> p j b", j=16),
                                       axis=AXX, op=ALU.add)
                P.vector.reciprocal(RST[:, 0:16], RST[:, 0:16])
                P.vector.tensor_copy(SQ16[:, 0:16], RST[:, 0:16])
                P.vector.tensor_tensor(S2B[:, 0:512].rearrange("p (j b) -> p j b", j=16),
                                       S2B[:, 0:512].rearrange("p (j b) -> p j b", j=16),
                                       SQ16[:, 0:16].unsqueeze(2).to_broadcast([128, 16, 32]), ALU.mult)
                P.vector.wait_ge(C.cT, C.cT.val)
                P.vector.transpose(P2T[:, :], S2B[:, 0:512])
                P.tensor.wait_ge(C.cV, C.cV.val)
                for j in range(16):
                    for band in range(4):
                        P.tensor.matmul(
                            C.PS[band][32 * r:32 * r + 8, 32 * j:32 * j + 32],
                            cv_ap[32 * band:32 * band + 32, j, 4 * uh + r, 8:16],
                            P2T[32 * band:32 * band + 32, 32 * j:32 * j + 32],
                            start=True, stop=True, tile_position=(32 * band, 32 * r),
                        )
                P.mark_last('tensor', C.cT)
            P.vector.wait_ge(C.cT, C.cT.val)
            for band in range(4):
                P.vector.tensor_copy(
                    O2[:, :].rearrange("p (j rl) -> p j rl", j=16)[:, :, 32 * band:32 * band + 32],
                    C.PS[band][:, :].rearrange("p (j b) -> p j b", j=16))
        P.vector.wait_ge(C.cA, cp_ev[-1])
        for g in range(4):
            q_ap = CQK[:, 512 * g:512 * (g + 1)].rearrange("p (j u c) -> p j u c", j=4, u=8, c=16)
            qq = q_ap[:, :, :, 0:8]
            kk_ = q_ap[:, :, :, 8:16]
            for jj in range(4):
                P.vector.tensor_tensor(PR1F[:, 512 * jj:512 * (jj + 1)].rearrange("p (a b c) -> p a b c", a=8, b=8, c=8),
                                       qq[:, jj].unsqueeze(2).to_broadcast([128, 8, 8, 8]),
                                       kk_[:, jj].unsqueeze(1).to_broadcast([128, 8, 8, 8]), ALU.mult)
            P.vector.tensor_reduce(S1F[:, 0:256],
                                   PR1F[:, 0:2048].rearrange("p (x c) -> p x c", c=8), axis=AXX, op=ALU.add)
            P.vector.tensor_reduce(ST32[:, 0:32], S1F[:, 0:256].rearrange("p (x b) -> p x b", b=8),
                                   axis=AXX, op=ALU.max)
            P.vector.tensor_tensor(S1F[:, 0:256].rearrange("p (x b) -> p x b", b=8),
                                   S1F[:, 0:256].rearrange("p (x b) -> p x b", b=8),
                                   ST32[:, 0:32].unsqueeze(2).to_broadcast([128, 32, 8]), ALU.subtract)
            P.scalar.wait_ge(C.cV, C.cV.val)
            P.scalar.activation(P1B[:, 256 * g:256 * (g + 1)], S1F[:, 0:256], AF.Exp,
                                bias=0.0, scale=float(1.0 / np.sqrt(8.0)))
            P.vector.wait_ge(C.cA, C.cA.val)
            P.vector.tensor_reduce(ST32[:, 0:32], P1B[:, 256 * g:256 * (g + 1)].rearrange("p (x b) -> p x b", b=8),
                                   axis=AXX, op=ALU.add)
            P.vector.reciprocal(ST32[:, 0:32], ST32[:, 0:32])
            P.vector.tensor_copy(SQ16[:, 0:32], ST32[:, 0:32])
            P.vector.tensor_tensor(P1B[:, 256 * g:256 * (g + 1)].rearrange("p (x b) -> p x b", b=8),
                                   P1B[:, 256 * g:256 * (g + 1)].rearrange("p (x b) -> p x b", b=8),
                                   SQ16[:, 0:32].unsqueeze(2).to_broadcast([128, 32, 8]), ALU.mult)
            p_ap = P1B[:, 256 * g:256 * (g + 1)].rearrange("p (j i b) -> p j i b", j=4, i=8, b=8)
            v1 = CVt[:, 512 * g:512 * (g + 1)].rearrange("p (j u c) -> p j u c", j=4, u=8, c=16)[:, :, :, 0:8]
            for jj in range(4):
                P.vector.tensor_tensor(PR1F[:, 512 * jj:512 * (jj + 1)].rearrange("p (a c b) -> p a c b", a=8, c=8, b=8),
                                       p_ap[:, jj].unsqueeze(2).to_broadcast([128, 8, 8, 8]),
                                       v1[:, jj].rearrange("p b c -> p c b").unsqueeze(1).to_broadcast([128, 8, 8, 8]),
                                       ALU.mult)
            o_ap = O1F[:, 512 * g:512 * (g + 1)].rearrange("p (j u c) -> p j u c", j=4, u=8, c=16)
            P.vector.tensor_reduce(o_ap[:, :, :, 0:8], PR1F[:, 0:2048].rearrange("p (x b) -> p x b", b=8),
                                   axis=AXX, op=ALU.add)
        cv_ap = CVt[:, :].rearrange("p (j u c) -> p j u c", j=16, u=8, c=16)
        o_all = O1F[:, :].rearrange("p (j u c) -> p j u c", j=16, u=8, c=16)
        for du, (o_lo, o_hi, v_lo) in ((0, (1, 8, 0)), (1, (0, 8, 0)), (2, (0, 7, 1))):
            n_u = o_hi - o_lo
            w0_ap = W0B[:, 64 * du:64 * du + 64].rearrange("p (u c) -> p u c", u=8, c=8)[:, o_lo:o_hi, :]
            tmp = PR1F[:, 0:16 * n_u * 8].rearrange("p (j u c) -> p j u c", j=16, u=n_u, c=8)
            P.vector.tensor_tensor(tmp, cv_ap[:, :, v_lo:v_lo + n_u, 0:8],
                                   w0_ap.unsqueeze(1).to_broadcast([128, 16, n_u, 8]), ALU.mult)
            P.vector.tensor_tensor(o_all[:, :, o_lo:o_hi, 0:8], o_all[:, :, o_lo:o_hi, 0:8], tmp, ALU.add)
        P.vector.tensor_tensor(o_all[:, :, :, 0:8], o_all[:, :, :, 0:8],
                               B0T.rearrange("p (u c) -> p u c", u=8)[:, :, :].unsqueeze(1).to_broadcast([128, 16, 8, 8]),
                               ALU.add)
        P.vector.tensor_copy(O1B[:, :], O1F[:, :])
        b1_done = C.cV.val
        P.vector.wait_ge(C.cA, cp_ev[-1])
        P.vector.memset(L2F[:, :], 0.0)
        c2_ap = COUT[:, :].rearrange("p (j band t r4) -> p j band t r4", j=16, band=4, t=8, r4=4)
        l_ap = L2F[:, :].rearrange("p (j band t r4) -> p j band t r4", j=16, band=4, t=8, r4=4)
        for dt, (t_lo, t_hi, s_lo) in ((0, (1, 8, 0)), (1, (0, 8, 0)), (2, (0, 7, 1))):
            n_t = t_hi - t_lo
            for band in range(4):
                tmp = PR1F[:, 0:16 * n_t * 4].rearrange("p (j t r4) -> p j t r4", j=16, t=n_t, r4=4)
                P.vector.tensor_scalar(tmp, c2_ap[:, :, band, s_lo:s_lo + n_t, :], W1T[:, dt:dt + 1], None, ALU.mult)
                P.vector.tensor_tensor(l_ap[:, :, band, t_lo:t_hi, :], l_ap[:, :, band, t_lo:t_hi, :], tmp, ALU.add)
        P.vector.tensor_scalar(L2F[:, :], L2F[:, :], B1T[:, 0:1], None, ALU.add)
        P.vector.tensor_copy(LEP2[:, :], L2F[:, :])

        for g in range(4):
            P.tensor.wait_ge(C.cV, C.cV.val)
            for q in range(4):
                j = 4 * g + q
                P.tensor.matmul(C.PS[q][:, 0:128], O1B[:, 128 * j:128 * (j + 1)], IBF, start=True, stop=True)
            P.mark_last('tensor', C.cT)
            P.vector.wait_ge(C.cT, C.cT.val)
            for q in range(4):
                P.vector.tensor_copy(PR1[:, 128 * q:128 * (q + 1)], C.PS[q][:, 0:128])
            P.tensor.wait_ge(C.cV, C.cV.val)
            for q in range(4):
                j = 4 * g + q
                P.tensor.matmul(C.PS[4 + q][:, 0:128], BDP[0], PR1[:, 128 * q:128 * (q + 1)], start=True, stop=False)
                P.tensor.matmul(C.PS[4 + q][:, 0:128], BDP[1], O2T1[:, 128 * j:128 * (j + 1)], start=False, stop=False)
                P.tensor.matmul(C.PS[4 + q][:, 0:128], BDP[2], O2T2[:, 128 * j:128 * (j + 1)], start=False, stop=False)
                P.tensor.matmul(C.PS[4 + q][:, 0:128], BDP[3], LEP2[:, 128 * j:128 * (j + 1)], start=False, stop=True)
            P.mark_last('tensor', C.cT)
            P.vector.wait_ge(C.cT, C.cT.val)
            for q in range(4):
                P.vector.tensor_scalar(PR1[:, 128 * q:128 * (q + 1)], C.PS[4 + q][:, 0:128], PBT[:, 0:1], None, ALU.add)
            P.tensor.wait_ge(C.cV, C.cV.val)
            for q in range(4):
                P.tensor.matmul(C.PS[q][:, 0:128], PR1[:, 128 * q:128 * (q + 1)], IBF, start=True, stop=True)
            P.mark_last('tensor', C.cT)
            P.vector.wait_ge(C.cT, C.cT.val)
            for q in range(4):
                P.vector.tensor_copy(YC[:, 128 * q:128 * (q + 1)], C.PS[q][:, 0:128])
            yap = YC[:, :].rearrange("p (x c) -> p x c", c=16)
            P.vector.tensor_reduce(ST32[:, 0:32], yap, axis=AXX, op=ALU.add)
            P.vector.tensor_scalar(ST32[:, 0:32], ST32[:, 0:32], 1.0 / 16, None, ALU.mult)
            P.vector.tensor_tensor(yap, yap, ST32[:, 0:32].unsqueeze(2).to_broadcast([128, 32, 16]), ALU.subtract)
            P.scalar.wait_ge(C.cV, C.cV.val)
            P.scalar.activation(SQ16[:, 0:512], YC[:, :], AF.Square)
            P.vector.wait_ge(C.cA, C.cA.val)
            P.vector.tensor_reduce(ST32[:, 0:32], SQ16[:, 0:512].rearrange("p (x c) -> p x c", c=16),
                                   axis=AXX, op=ALU.add)
            P.vector.tensor_scalar(ST32[:, 0:32], ST32[:, 0:32], 1.0 / 16, None, ALU.mult)
            P.vector.tensor_scalar(ST32[:, 0:32], ST32[:, 0:32], EPS, None, ALU.add)
            P.scalar.wait_ge(C.cV, C.cV.val)
            P.scalar.activation(ST32[:, 0:32], ST32[:, 0:32], AF.Sqrt)
            P.vector.wait_ge(C.cA, C.cA.val)
            P.vector.reciprocal(ST32[:, 0:32], ST32[:, 0:32])
            P.vector.tensor_tensor(yap, yap, ST32[:, 0:32].unsqueeze(2).to_broadcast([128, 32, 16]), ALU.mult)
            ycv = YC[:, :].rearrange("p (a c) -> p a c", a=4)
            P.vector.tensor_tensor(ycv, ycv, ALNW.unsqueeze(1).to_broadcast([128, 4, 128]), ALU.mult)
            P.vector.tensor_tensor(ycv, ycv, ALNB.unsqueeze(1).to_broadcast([128, 4, 128]), ALU.add)
            P.vector.tensor_tensor(YC[:, :], YC[:, :], C.XT2[:, 512 * g:512 * (g + 1)], ALU.add)
            P.vector.tensor_copy(Yt[:, 512 * g:512 * (g + 1)], YC[:, :])

        y_ap = Yt[:, :].rearrange("p (j u c) -> p j u c", j=16, u=8, c=16)
        P.vector.wait_ge(C.qG, C.qG.val)
        for dd in range(8):
            P.vector.tensor_copy(
                SLAB[:, 256 * dd:256 * (dd + 1)].rearrange("p (j u two) -> p j u two", j=16, u=8, two=2),
                y_ap[:, :, :, 2 * dd:2 * dd + 2])
        P.gpsimd.wait_ge(C.cV, C.cV.val)
        for dd in range(8):
            P.gpsimd.dma_start(out=aux["a2a_in"].ap()[dd, sb], in_=SLAB[:, 256 * dd:256 * (dd + 1)]).then_inc(C.qG, 16)

    P.gpsimd.wait_ge(C.qG, C.qG.val)
    P.gpsimd.collective_compute(
        "AllToAll", ALU.bypass, replica_groups=[list(range(NCORES))],
        ins=[aux["a2a_in"].ap().opt()], outs=[aux["a2a_out"].ap().opt()],
    ).then_inc(C.cc, 1)
    P.gpsimd.wait_ge(C.cc, C.cc.val)
    YTR = bft(A_YTR, 16384)
    for src in range(8):
        for sb in range(8):
            P.gpsimd.wait_ge(C.cT, C.cT.val)
            P.gpsimd.dma_start(out=SLAB[:, 0:256], in_=aux["a2a_out"].ap()[src, sb]).then_inc(C.qR[0], 16)
            P.tensor.wait_ge(C.qR[0], C.qR[0].val)
            P.tensor.wait_ge(C.cV, C.cV.val)
            for par in range(2):
                sel = SLAB[:, 0:256].rearrange("p (j u two) -> p two (j u)", j=16, u=8)[:, par, :]
                P.tensor.matmul(C.PS[par][:, 0:128], sel, IBF, start=True, stop=True)
            P.mark_last('tensor', C.cT)
            P.vector.wait_ge(C.cT, C.cT.val)
            for par in range(2):
                col = 2 * (8 * src + sb) + par
                P.vector.tensor_copy(YTR[:, 128 * col:128 * (col + 1)], C.PS[par][:, 0:128])
    SCR = f32t(A_SCR, 1024)
    X1T = f32t(A_X1T, 1024)
    for F in range(16):
        for rb in range(8):
            if rb == 0:
                P.sync.wait_ge(C.cV, C.cV.val)
            else:
                P.sync.wait_ge(C.cT, C.cT.val)
            P.sync.dma_start(out=f32t(A_STG, 128),
                             in_=aux["xrows"][128 * rb:128 * (rb + 1), 128 * F:128 * (F + 1)]).then_inc(C.qE[1], 16)
            P.tensor.wait_ge(C.qE[1], C.qE[1].val)
            if rb == 0:
                P.tensor.wait_ge(C.cV, C.cV.val)
            pb_ = C.PS[4 + rb // 4]
            P.tensor.matmul(pb_[:, 128 * (rb % 4):128 * (rb % 4 + 1)], f32t(A_STG, 128), IF32, start=True, stop=True)
            P.mark_last('tensor', C.cT)
        P.vector.wait_ge(C.cT, C.cT.val)
        P.vector.wait_ge(C.qD[0], C.qD[0].val)
        ysel = YTR[:, :].rearrange("p (q a g) -> p q a g", q=128, a=8, g=16)[:, :, :, F]
        ysel = ysel.rearrange("p (s2 par) a -> p par s2 a", par=2)
        P.vector.tensor_copy(SCR.rearrange("p (par s a) -> p par s a", par=2, s=64), ysel)
        P.vector.tensor_copy(X1T[:, 0:512], C.PS[4][:, :])
        P.vector.tensor_copy(X1T[:, 512:1024], C.PS[5][:, :])
        P.vector.tensor_tensor(X1T, X1T, SCR, ALU.add)
        P.gpsimd.wait_ge(C.cV, C.cV.val)
        P.gpsimd.dma_start(out=x1_dram[F], in_=X1T).then_inc(C.qD[0], 16)


def build_full():
    kb = KB()
    C = Common(kb)
    din = {
        "lua": kb.din("lua", [D, 16]), "lda": kb.din("lda", [FF, 16]),
        "b_up": kb.din("b_up", [128, FC]), "b_down": kb.din("b_down", [128, KC]),
    }
    lub = kb.din("lub", [16, FF])
    ldb = kb.din("ldb", [16, D])
    wq_u = kb.din("wupT_q", [D, FF], I32)
    ws_u = kb.din("wup_sc", [D, FF])
    wq_d = kb.din("wdnT_q", [FF, D], I32)
    ws_d = kb.din("wdn_sc", [FF, D])
    out_rows = kb.dout("out_rows", [T, D])
    auxa = declare_attna_aux(kb)
    auxb = declare_attnb_aux(kb)
    x1 = kb.dscratch("x1s", [KC, 128, T]).ap()
    x2 = kb.dscratch("x2s", [KC, 128, T]).ap()
    x3 = kb.dscratch("x3s", [KC, 128, T]).ap()
    hm = kb.dscratch("hms", [KC, 128, T]).ap()
    x4 = kb.dscratch("x4s", [KC, 128, T]).ap()
    wup_bf = kb.dscratch("wup_bf", [FC, 128, KC, 128], BF16).ap()
    wdn_bf = kb.dscratch("wdn_bf", [4 * KC, 128, KC, 128], BF16).ap()
    P = kb.prog

    C.preload(din)
    predeq_chunk = make_predequant(kb, C, (wq_u, ws_u), (wq_d, ws_d), wup_bf, wdn_bf)
    emit_attn_a(kb, C, auxa, x1, chunk_cb=predeq_chunk)
    P.gpsimd.wait_ge(C.qC[0], C.qC[0].val)
    P.gpsimd.wait_ge(C.qC[1], C.qC[1].val)
    emit_ffn(kb, C, x1, x2, dequant=False,
             wup_src=[wup_bf[m] for m in range(FC)],
             wdn_src=[wdn_bf[st] for st in range(4 * KC)],
             lub_dram=lub, ldb_dram=ldb,
             halo_out=auxb["halo_in"].ap())
    emit_attn_b(kb, C, x2, x3, hm, auxb)
    emit_ffn(kb, C, x3, x4, dequant=False,
             wup_src=[wup_bf[m] for m in range(FC)],
             wdn_src=[wdn_bf[st] for st in range(4 * KC)],
             lub_dram=lub, ldb_dram=ldb)
    IF32 = C.AR0[:, A_IF32:A_IF32 + 128]
    STG = C.AR0[:, A_STG:A_STG + 128]
    P.sync.wait_ge(C.qD[0], C.qD[0].val)
    P.sync.wait_ge(C.qD[1], C.qD[1].val)
    P.sync.wait_ge(C.cT, C.cT.val)
    P.sync.dma_start(out=IF32, in_=auxa["I128"]).then_inc(C.qS, 16)
    P.tensor.wait_ge(C.qS, C.qS.val)
    prev_out = 0
    for rb in range(8):
        for F in range(16):
            P.sync.wait_ge(C.cT, C.cT.val)
            P.sync.dma_start(out=STG, in_=x4[F][:, 128 * rb:128 * (rb + 1)]).then_inc(C.qE[0], 16)
            P.tensor.wait_ge(C.qE[0], C.qE[0].val)
            if F % 4 == 0:
                P.tensor.wait_ge(C.cV, C.cV.val)
            pb_ = C.PS[4 + (F // 4) % 2]
            P.tensor.matmul(pb_[:, 128 * (F % 4):128 * (F % 4 + 1)], STG, IF32, start=True, stop=True)
            P.mark_last('tensor', C.cT)
            if F % 4 == 3:
                P.vector.wait_ge(C.cT, C.cT.val)
                if F == 3:
                    P.vector.wait_ge(C.qD[1], prev_out)
                P.vector.tensor_copy(C.XT2[:, 512 * (F // 4):512 * (F // 4 + 1)], pb_[:, :])
                if F == 15:
                    P.gpsimd.wait_ge(C.cV, C.cV.val)
                    P.gpsimd.dma_start(out=out_rows[128 * rb:128 * (rb + 1), :], in_=C.XT2[:, :]).then_inc(C.qD[1], 16)
                    prev_out = C.qD[1].val
    P.gpsimd.wait_ge(C.qD[1], C.qD[1].val)
    P.gpsimd.wait_ge(C.qD[0], C.qD[0].val)
    P.gpsimd.wait_ge(C.qG, C.qG.val)
    P.sync.wait_ge(C.qS, C.qS.val)
    return kb.finalize()


K_BUILD_CACHE = {}


def full_in_maps(inp):
    maps = prep_inputs(inp)
    ac = prep_attna_consts(inp)
    for i, m in enumerate(maps):
        m.update(ac)
        pp, cmrow = prep_attnb_pack(inp, i)
        m["parpack"] = pp
        m["cmrow"] = cmrow
        for nm in ["Wq", "Wk", "Wv", "Wig", "Wog", "Wo", "Wg", "Wgate", "Wval", "Wout"]:
            m["b_" + nm] = inp["L0_attn_b_" + nm]
        m["lub"] = inp["L0_ffn_w_up_lora_b"]
        m["ldb"] = inp["L0_ffn_w_down_lora_b"]
    return maps


def kernel(**inputs):
    maps = full_in_maps(inputs)
    nc = build_full()
    res = run_bass_kernel_spmd(nc, maps, core_ids=list(range(NCORES)))
    out = np.concatenate([res.results[i]["out_rows"] for i in range(NCORES)], axis=0)
    return out.reshape(1, 8192, 2048)


# version 11
# speedup vs baseline: 1.3958x; 1.0031x over previous
import contextlib
import numpy as np
import ml_dtypes
import concourse.bass as bass
import concourse.mybir as mybir
from concourse.bass_utils import run_bass_kernel_spmd

F32 = mybir.dt.float32
BF16 = mybir.dt.bfloat16
I32 = mybir.dt.int32
ALU = mybir.AluOpType
AF = mybir.ActivationFunctionType
AXX = mybir.AxisListType.X

NCORES = 8
T = 1024
D = 2048
FF = 8192
NH = 16
HD = 128
KC = D // 128
FC = FF // 128
EPS = 1e-5


class Call:
    __slots__ = ("name", "args", "kwargs", "chain")

    def __init__(self, name, args, kwargs):
        self.name, self.args, self.kwargs, self.chain = name, args, kwargs, []

    def then_inc(self, sem, k=1):
        sem.val += k
        self.chain.append((sem, k))
        return self


_NOCHAIN = {"wait_ge", "dma_start", "dma_start_transpose", "nop", "collective_compute"}


class EngRec:

    def __init__(self, prog, key):
        self._prog, self._key = prog, key

    def __getattr__(self, name):
        def call(*a, **kw):
            chain = self._prog.chains.get(self._key)
            if chain is not None and name not in _NOCHAIN:
                if chain.val > 0:
                    self._prog.streams[self._key].append(Call("wait_ge", (chain, chain.val), {}))
                c = Call(name, a, kw)
                self._prog.streams[self._key].append(c)
                c.then_inc(chain, 1)
                return c
            c = Call(name, a, kw)
            self._prog.streams[self._key].append(c)
            return c
        return call


class Prog:
    ENGS = ("tensor", "vector", "scalar", "gpsimd", "sync")

    def __init__(self):
        self.streams = {e: [] for e in self.ENGS}
        self.chains = {}
        for e in self.ENGS:
            setattr(self, e, EngRec(self, e))

    def set_chains(self, sems):
        self.chains.update(sems)

    def mark_last(self, key, sem, k=1):
        self.streams[key][-1].then_inc(sem, k)

    def emit(self, block):
        def mk(key):
            def f(eng):
                for c in self.streams[key]:
                    args = [a.h if isinstance(a, Sem) else a for a in c.args]
                    r = getattr(eng, c.name)(*args, **c.kwargs)
                    for (sem, k) in c.chain:
                        r = r.then_inc(sem.h, k)
            return f
        block.tensor(mk("tensor"))
        block.vector(mk("vector"))
        block.scalar(mk("scalar"))
        block.gpsimd(mk("gpsimd"))
        block.sync(mk("sync"))


class Sem:

    def __init__(self, name):
        self.name, self.val, self.h = name, 0, None

    def inc(self, k=1):
        self.val += k
        return self.val


def prep_inputs(inp):
    x = np.ascontiguousarray(inp["L0_attn_a_x"].reshape(8192, 2048))
    wupT_q = np.ascontiguousarray(inp["L0_ffn_w_up_q"].T)
    wdnT_q = np.ascontiguousarray(inp["L0_ffn_w_down_q"].T)
    wup_sc = np.ascontiguousarray(np.repeat(inp["L0_ffn_w_up_scale"], 64, axis=1).T)
    wdn_sc = np.ascontiguousarray(np.repeat(inp["L0_ffn_w_down_scale"], 64, axis=1).T)
    I128 = np.eye(128, dtype=np.float32)
    base = {
        "I128": I128,
        "wupT_q": wupT_q, "wdnT_q": wdnT_q, "wup_sc": wup_sc, "wdn_sc": wdn_sc,
        "b_up": np.ascontiguousarray(inp["L0_ffn_b_up"].reshape(FC, 128).T),
        "b_down": np.ascontiguousarray(inp["L0_ffn_b_down"].reshape(KC, 128).T),
        "lua": inp["L0_ffn_w_up_lora_a"],
        "lub": inp["L0_ffn_w_up_lora_b"],
        "lda": inp["L0_ffn_w_down_lora_a"],
        "ldb": inp["L0_ffn_w_down_lora_b"],
    }
    maps = []
    for i in range(NCORES):
        m = dict(base)
        m["xrows"] = np.ascontiguousarray(x[T * i:T * (i + 1)])
        maps.append(m)
    return maps


class KB:

    def __init__(self):
        self.nc = bass.Bass("TRN2", target_bir_lowering=False)
        self.stack = contextlib.ExitStack()
        self.prog = Prog()
        self.sems = {}
        self._psum_used = []

    def sem(self, name):
        s = Sem(name)
        self.sems[name] = s
        return s

    def din(self, name, shape, dtype=F32):
        return self.nc.dram_tensor(name, list(shape), dtype, kind="ExternalInput").ap()

    def dout(self, name, shape, dtype=F32):
        return self.nc.dram_tensor(name, list(shape), dtype, kind="ExternalOutput").ap()

    def dscratch(self, name, shape, dtype=F32):
        return self.nc.dram_tensor(name, list(shape), dtype)

    def sb(self, name, shape, dtype=F32):
        return self.stack.enter_context(self.nc.sbuf_tensor(name, list(shape), dtype))

    def ps(self, name, shape, dtype=F32):
        return self.stack.enter_context(self.nc.psum_tensor(name, list(shape), dtype))

    def finalize(self):
        from concourse.aot_env import allow_ici_all_to_all
        with contextlib.ExitStack() as es:
            es.enter_context(allow_ici_all_to_all())
            es.enter_context(self.nc.allow_low_precision(
                reason="bf16 working tiles by design; matmul/reduce accumulate fp32 internally"))
            for s in self.sems.values():
                s.h = es.enter_context(self.nc.semaphore(s.name))
            block = es.enter_context(self.nc.Block())
            self.prog.emit(block)
        self.stack.close()
        return self.nc


def view(t, dtype, shape):
    ap = t[:, :] if not isinstance(t, bass.AP) else t
    ap = ap.bitcast(dtype)
    if ap.ndim == 3:
        ap = ap.rearrange("p a b -> p (a b)")
    if len(shape) == 2:
        return ap.rearrange("p (a b) -> p a b", a=shape[0], b=shape[1])
    if len(shape) == 1:
        return ap
    raise ValueError(shape)


class Common:

    def __init__(self, kb: KB):
        self.kb = kb
        self.PS = [kb.ps(f"PS{i}", [128, 512], F32) for i in range(8)]
        self.AR0 = kb.sb("AR0", [128, 32768], F32)
        ar0b = self.AR0[:, :].bitcast(BF16)
        self.XH = ar0b.rearrange("p (a b) -> p a b", a=FC, b=T)
        self.AB0 = [ar0b[:, 16384 * i:16384 * (i + 1)].rearrange("p (a b) -> p a b", a=KC, b=T) for i in range(4)]
        self.XT2 = kb.sb("XT2", [128, 2048], F32)
        self.XT = [self.XT2[:, 0:1024], self.XT2[:, 1024:2048]]
        self.XB = kb.sb("XB", [128, KC, 1028], BF16)
        self.SA = self.XT2[0:1, 0:1024]
        self.SB = self.XT2[0:1, 1024:2048]
        self.MB = kb.sb("MB", [128, 1024], BF16)
        self.RB = kb.sb("RB", [128, 1024], BF16)
        self.SQ = self.MB
        self.WQF = [kb.sb(f"WQF{i}", [128, 4, 128], F32) for i in range(2)]
        self.SCF = [kb.sb(f"SCF{i}", [128, 4, 128], F32) for i in range(2)]
        self.WUB = [kb.sb(f"WUB{i}", [128, KC, 128], BF16) for i in range(2)]
        self.STB = kb.sb("STB", [1, 2048], BF16)
        self.SMALL = kb.sb("SMALL", [128, 1024], F32)
        self.LUA = kb.sb("LUA", [128, KC, 16], BF16)
        self.LUBd = [kb.sb(f"LUBd{i}", [16, 128], BF16) for i in range(2)]
        self.LDA = kb.sb("LDA", [128, FC, 16], BF16)
        self.LDBd = [kb.sb(f"LDBd{i}", [16, 128], BF16) for i in range(2)]
        self.LT1 = kb.sb("LT1", [16, 1024], BF16)
        self.LT2 = kb.sb("LT2", [16, 1024], BF16)
        self.SM0 = self.LT1[0:1, :]
        self.SR0 = self.LT2[0:1, :]

        self.BUP = kb.sb("BUP", [128, FC], F32)
        self.BDN = kb.sb("BDN", [128, KC], F32)
        self.ones_col_bf = kb.sb("ones_col_bf", [128, 1], BF16)
        self.ones_row_bf = kb.sb("ones_row_bf", [1, 128], BF16)
        self.invd_col_bf = kb.sb("invd_col_bf", [128, 1], BF16)
        self.sqrthd_row_bf = kb.sb("sqrthd_row_bf", [1, 128], BF16)
        self.eps_l2 = kb.sb("eps_l2", [1, 1], F32)
        self.eps_ln = kb.sb("eps_ln", [1, 1], F32)
        self.m75 = kb.sb("m75", [128, 1], F32)
        self.wqf_guard = [None, None]
        self.qS = kb.sem("qS")
        self.qL = kb.sem("qL")
        self.qW = [kb.sem("qW0"), kb.sem("qW1")]
        self.qP = [kb.sem("qP0"), kb.sem("qP1")]
        self.qB = [kb.sem("qB0"), kb.sem("qB1")]
        self.qE = [kb.sem("qE0"), kb.sem("qE1")]
        self.qC = [kb.sem("qC0"), kb.sem("qC1")]
        self.qD = [kb.sem("qD0"), kb.sem("qD1")]
        self.qR = [kb.sem("qR0"), kb.sem("qR1")]
        self.qG = kb.sem("qG")
        self.qH = kb.sem("qH")
        self.cT = kb.sem("cT")
        self.cc = kb.sem("cc")
        self.cV = kb.sem("cV")
        self.cA = kb.sem("cA")
        self.cP = kb.sem("cP")
        kb.prog.set_chains({"vector": self.cV, "scalar": self.cA, "gpsimd": self.cP})
        self.ones_col_f32 = kb.sb("ones_col_f32", [128, 1], F32)
        self.ones_row_f32 = kb.sb("ones_row_f32", [1, 128], F32)
        self.WG = kb.sb("WG", [128, KC, 16], BF16)

    def xb(self, k, n0=0, n1=T):
        return self.XB[:, k, 4 + n0:4 + n1]

    def preload(self, din):
        P, C = self.kb.prog, self
        P.vector.memset(C.ones_col_bf[:, :], 1.0)
        P.vector.memset(C.ones_row_bf[:, :], 1.0)
        P.vector.memset(C.invd_col_bf[:, :], 1.0 / 128.0)
        P.vector.memset(C.sqrthd_row_bf[:, :], float(np.sqrt(128.0)))
        P.vector.memset(C.eps_l2[:, :], 1e-12)
        P.vector.memset(C.eps_ln[:, :], float(EPS * HD))
        P.vector.memset(C.m75[:, :], -7.5)
        P.vector.memset(C.ones_col_f32[:, :], 1.0)
        P.vector.memset(C.ones_row_f32[:, :], 1.0)
        P.gpsimd.dma_start(out=C.LUA[:, :, :], in_=din["lua"].rearrange("(kc p) r -> p kc r", p=128)).then_inc(C.qG, 16)
        P.gpsimd.dma_start(out=C.LDA[:, :, :], in_=din["lda"].rearrange("(h p) r -> p h r", p=128)).then_inc(C.qG, 16)
        P.sync.dma_start(out=C.BUP[:, :], in_=din["b_up"]).then_inc(C.qS, 16)
        P.sync.dma_start(out=C.BDN[:, :], in_=din["b_down"]).then_inc(C.qS, 16)


def emit_ln(kb, C, src_dram, *, fused_w=None, fused_b=None, fused_rho=False,
            out_bf=None, keep_rho=None):
    P = kb.prog
    out_bf = out_bf if out_bf is not None else C.XB

    def OBF(k, n0=0, n1=T):
        if out_bf is C.XB:
            return out_bf[:, k, 4 + n0:4 + n1]
        return out_bf[:, k, n0:n1]
    P.gpsimd.wait_ge(C.cT, C.cT.val)
    P.gpsimd.wait_ge(C.cV, C.cV.val)
    P.gpsimd.wait_ge(C.qG, C.qG.val)
    P.gpsimd.wait_ge(C.qD[0], C.qD[0].val)
    P.gpsimd.wait_ge(C.qD[1], C.qD[1].val)
    P.sync.wait_ge(C.qG, C.qG.val)
    P.sync.wait_ge(C.cT, C.cT.val)
    P.sync.wait_ge(C.qC[0], C.qC[0].val)
    P.sync.wait_ge(C.qC[1], C.qC[1].val)
    P.tensor.wait_ge(C.cV, C.cV.val)
    for k in range(KC):
        P.gpsimd.dma_start(out=OBF(k), in_=src_dram[k]).then_inc(C.qL, 16)
    P.scalar.wait_ge(C.qL, C.qL.val)
    for k in range(KC):
        if k > 0:
            P.scalar.wait_ge(C.cT, C.cT.val)
        P.scalar.activation(C.SQ[:, :], OBF(k), AF.Square)
        P.tensor.wait_ge(C.cA, C.cA.val)
        P.tensor.matmul(C.PS[0][0:1, :], C.ones_col_bf[:, :], OBF(k, 0, 512),
                        start=(k == 0), stop=(k == KC - 1))
        P.tensor.matmul(C.PS[1][0:1, :], C.ones_col_bf[:, :], OBF(k, 512, 1024),
                        start=(k == 0), stop=(k == KC - 1))
        P.tensor.matmul(C.PS[2][0:1, :], C.ones_col_bf[:, :], C.SQ[:, 0:512],
                        start=(k == 0), stop=(k == KC - 1))
        P.tensor.matmul(C.PS[3][0:1, :], C.ones_col_bf[:, :], C.SQ[:, 512:1024],
                        start=(k == 0), stop=(k == KC - 1)).then_inc(C.cT, 1)
    P.vector.wait_ge(C.cT, C.cT.val)
    P.vector.tensor_scalar(C.SA[:, 0:512], C.PS[0][0:1, :], 1.0 / D, None, ALU.mult)
    P.vector.tensor_scalar(C.SA[:, 512:1024], C.PS[1][0:1, :], 1.0 / D, None, ALU.mult)
    P.vector.tensor_copy(C.STB[0:1, 0:1024], C.SA)
    P.vector.tensor_scalar(C.SB[:, 0:512], C.PS[2][0:1, :], 1.0 / D, None, ALU.mult)
    P.vector.tensor_scalar(C.SB[:, 512:1024], C.PS[3][0:1, :], 1.0 / D, None, ALU.mult)
    P.vector.tensor_tensor(C.STB[0:1, 1024:2048], C.SA, C.SA, ALU.mult)
    P.vector.tensor_tensor(C.SB, C.SB, C.STB[0:1, 1024:2048], ALU.subtract)
    P.vector.tensor_scalar(C.SA, C.SB, EPS, None, ALU.add)
    P.vector.reciprocal(C.SA, C.SA)
    if fused_rho or keep_rho is not None:
        P.vector.tensor_tensor(C.SB, C.SB, C.SA, ALU.mult)
        P.vector.tensor_scalar(C.SB, C.SB, EPS, None, ALU.add)
        P.scalar.wait_ge(C.cV, C.cV.val)
        P.scalar.activation(C.SB, C.SB, AF.Sqrt)
        P.scalar.activation(C.SA, C.SA, AF.Sqrt)
        P.vector.wait_ge(C.cA, C.cA.val)
        P.vector.reciprocal(C.SB, C.SB)
        if keep_rho is not None:
            P.vector.tensor_copy(C.SM0, C.STB[0:1, 0:1024])
            P.vector.tensor_copy(C.SR0, C.SA)
        if fused_rho:
            P.vector.tensor_tensor(C.SA, C.SA, C.SB, ALU.mult)
    else:
        P.vector.wait_ge(C.cV, C.cV.val)
        P.scalar.wait_ge(C.cV, C.cV.val)
        P.scalar.activation(C.SA, C.SA, AF.Sqrt)
        P.vector.wait_ge(C.cA, C.cA.val)
    P.vector.tensor_copy(C.STB[0:1, 1024:2048], C.SA)
    P.tensor.wait_ge(C.cV, C.cV.val)
    for n in range(2):
        P.tensor.matmul(C.PS[0 + n][:, :], C.ones_row_bf[:, :], C.STB[0:1, 512 * n:512 * (n + 1)], start=True, stop=True)
        P.tensor.matmul(C.PS[2 + n][:, :], C.ones_row_bf[:, :], C.STB[0:1, 1024 + 512 * n:1024 + 512 * (n + 1)], start=True, stop=True)
    P.mark_last('tensor', C.cT)
    P.vector.wait_ge(C.cT, C.cT.val)
    P.vector.tensor_copy(C.MB[:, 0:512], C.PS[0][:, :])
    P.vector.tensor_copy(C.MB[:, 512:1024], C.PS[1][:, :])
    P.vector.tensor_copy(C.RB[:, 0:512], C.PS[2][:, :])
    P.vector.tensor_copy(C.RB[:, 512:1024], C.PS[3][:, :])
    for k in range(KC):
        P.vector.tensor_tensor(OBF(k), OBF(k), C.MB[:, :], ALU.subtract)
        P.vector.tensor_tensor(OBF(k), OBF(k), C.RB[:, :], ALU.mult)
        if fused_w is not None:
            P.vector.tensor_scalar(OBF(k), OBF(k), fused_w[:, k:k + 1], fused_b[:, k:k + 1], ALU.mult, ALU.add)
    P.vector.nop()


def _stream_lhsT(kb, C, par, src, cache, mcols, rows0, dequant, mm_wait=None):
    P = kb.prog
    if not dequant:
        for w in (mm_wait or []):
            P.sync.wait_ge(*w)
        P.sync.dma_start(out=C.WUB[par][:, :, :], in_=src).then_inc(C.qP[par], 16)
        return [(C.qP[par], C.qP[par].val)]
    qsrc, ssrc = src
    done = []
    for q in range(4):
        b = q % 2
        r = rows0 + 512 * q
        g = C.wqf_guard[b]
        if g is not None:
            P.gpsimd.wait_ge(*g)
        if q == 0:
            for w in (mm_wait or []):
                P.gpsimd.wait_ge(*w)
        P.gpsimd.dma_start(out=C.WQF[b][:, :, :],
                           in_=qsrc[r:r + 512, mcols].rearrange("(kc p) n -> p kc n", p=128)).then_inc(C.qW[b], 16)
        P.gpsimd.dma_start(out=C.SCF[b][:, :, :],
                           in_=ssrc[r:r + 512, mcols].rearrange("(kc p) n -> p kc n", p=128)).then_inc(C.qW[b], 16)
        eng, cm = (P.vector, C.cV) if b == 0 else (P.gpsimd, C.cP)
        eng.wait_ge(C.qW[b], C.qW[b].val)
        for w in (mm_wait or []):
            eng.wait_ge(*w)
        if b == 0:
            eng.scalar_tensor_tensor(
                C.WUB[par][:, 4 * q:4 * (q + 1), :].rearrange("p a b -> p (a b)"),
                C.WQF[b][:, :, :].rearrange("p a b -> p (a b)"), -7.5,
                C.SCF[b][:, :, :].rearrange("p a b -> p (a b)"),
                ALU.add, ALU.mult)
        else:
            eng.tensor_tensor(C.WQF[b][:, :, :].rearrange("p a b -> p (a b)"),
                              C.WQF[b][:, :, :].rearrange("p a b -> p (a b)"),
                              C.m75[:, 0:1].to_broadcast([128, 512]), ALU.add)
            eng.tensor_tensor(
                C.WUB[par][:, 4 * q:4 * (q + 1), :].rearrange("p a b -> p (a b)"),
                C.WQF[b][:, :, :].rearrange("p a b -> p (a b)"),
                C.SCF[b][:, :, :].rearrange("p a b -> p (a b)"),
                ALU.mult)
        C.wqf_guard[b] = (cm, cm.val)
        done.append((cm, cm.val))
    if cache is not None:
        for (sm, vv) in done[-2:]:
            P.gpsimd.wait_ge(sm, vv)
        P.gpsimd.dma_start(out=cache, in_=C.WUB[par][:, :, :]).then_inc(C.qC[par], 16)
    return done


def make_predequant(kb, C, wup_src, wdn_src, wup_cache, wdn_cache, n_chunks=8):
    P = kb.prog
    blocks = []
    for m in range(FC):
        blocks.append((wup_src, wup_cache[m], slice(128 * m, 128 * (m + 1)), 0))
    for d in range(KC):
        for r4 in range(4):
            blocks.append((wdn_src, wdn_cache[4 * d + r4], slice(128 * d, 128 * (d + 1)), 2048 * r4))
    quarters = []
    for bi, (src, cache, mcols, rows0) in enumerate(blocks):
        for q in range(4):
            quarters.append((bi, q, src, cache, mcols, rows0 + 512 * q))
    NQ = len(quarters)
    ldq = {}
    lds = {}
    tt_done = {}
    wr = {}
    st = {"started": False}

    def issue_quant(i):
        bi, q, (qsrc, ssrc), cache, mcols, r = quarters[i]
        b = i % 2
        if i >= 2:
            P.gpsimd.wait_ge(C.cP, tt_done[i - 2])
        P.gpsimd.dma_start(out=C.WQF[b][:, :, :],
                           in_=qsrc[r:r + 512, mcols].rearrange("(kc p) n -> p kc n", p=128)).then_inc(C.qW[b], 16)
        ldq[i] = C.qW[b].val

    def issue_scale(i):
        bi, q, (qsrc, ssrc), cache, mcols, r = quarters[i]
        b = i % 2
        if i >= 2:
            P.sync.wait_ge(C.cP, tt_done[i - 2])
        P.sync.dma_start(out=C.SCF[b][:, :, :],
                         in_=ssrc[r:r + 512, mcols].rearrange("(kc p) n -> p kc n", p=128)).then_inc(C.qP[b], 16)
        lds[i] = C.qP[b].val

    per = NQ // n_chunks

    def emit_chunk():
        if not st["started"]:
            st["started"] = True
            P.gpsimd.wait_ge(C.cV, C.cV.val)
            issue_quant(0)
            issue_quant(1)
            issue_scale(0)
            issue_scale(1)
            st["lo"] = 0
        lo = st["lo"]
        hi = min(lo + per, NQ)
        st["lo"] = hi
        for i in range(lo, hi):
            bi, q, src, cache, mcols, r = quarters[i]
            b = i % 2
            par = bi % 2
            P.gpsimd.wait_ge(C.qW[b], ldq[i])
            P.gpsimd.wait_ge(C.qP[b], lds[i])
            if q == 0 and bi >= 2:
                P.gpsimd.wait_ge(C.qC[par], wr[bi - 2])
            P.gpsimd.tensor_tensor(C.WQF[b][:, :, :].rearrange("p a b -> p (a b)"),
                                   C.WQF[b][:, :, :].rearrange("p a b -> p (a b)"),
                                   C.m75[:, 0:1].to_broadcast([128, 512]), ALU.add)
            P.gpsimd.tensor_tensor(C.WUB[par][:, 4 * q:4 * (q + 1), :].rearrange("p a b -> p (a b)"),
                                   C.WQF[b][:, :, :].rearrange("p a b -> p (a b)"),
                                   C.SCF[b][:, :, :].rearrange("p a b -> p (a b)"), ALU.mult)
            tt_done[i] = C.cP.val
            if i + 2 < NQ:
                issue_quant(i + 2)
                issue_scale(i + 2)
            if q == 3:
                P.gpsimd.wait_ge(C.cP, C.cP.val)
                P.gpsimd.dma_start(out=cache, in_=C.WUB[par][:, :, :]).then_inc(C.qC[par], 16)
                wr[bi] = C.qC[par].val

    return emit_chunk


def emit_ffn(kb, C, src_dram, dst_dram, *, dequant, wup_src, wdn_src, lub_dram, ldb_dram,
             wup_cache=None, wdn_cache=None, halo_out=None):
    P = kb.prog
    emit_ln(kb, C, src_dram)
    P.tensor.wait_ge(C.cV, C.cV.val)
    for n in range(2):
        if n == 1:
            P.tensor.wait_ge(C.cV, C.cV.val)
        for k in range(KC):
            P.tensor.matmul(C.PS[4][0:16, :], C.LUA[:, k, :], C.xb(k, 512 * n, 512 * (n + 1)),
                            start=(k == 0), stop=(k == KC - 1))
        P.mark_last('tensor', C.cT)
        P.vector.wait_ge(C.cT, C.cT.val)
        P.vector.tensor_copy(C.LT1[:, 512 * n:512 * (n + 1)], C.PS[4][0:16, :])
    lt1_ready = C.cV.val

    mm_done = {}
    evac_done = {}
    cache_done = {}
    for m in range(FC):
        par = m % 2
        mm_wait = []
        if m >= 2:
            mm_wait.append((C.cT, mm_done[m - 2]))
            if m - 2 in cache_done:
                mm_wait.append((C.qC[par], cache_done[m - 2]))
        if dequant:
            src = wup_src
        else:
            src = wup_src[m]
        rdy = _stream_lhsT(kb, C, par, src, wup_cache[m] if wup_cache is not None else None,
                           slice(128 * m, 128 * (m + 1)), 0, dequant, mm_wait)
        if wup_cache is not None:
            cache_done[m] = C.qC[par].val
        P.gpsimd.wait_ge(C.cT, mm_done[m - 2] if m >= 2 else 0)
        P.gpsimd.dma_start(out=C.LUBd[par][:, :], in_=lub_dram[0:16, 128 * m:128 * (m + 1)]).then_inc(C.qB[par], 16)
        pu, pu2 = C.PS[4 + 2 * par], C.PS[5 + 2 * par]
        for (sm, vv) in rdy:
            P.tensor.wait_ge(sm, vv)
        P.tensor.wait_ge(C.qB[par], C.qB[par].val)
        if m == 0:
            P.tensor.wait_ge(C.cV, lt1_ready)
        if m >= 2:
            P.tensor.wait_ge(C.cA, evac_done[m - 2])
        for k in range(KC):
            P.tensor.matmul(pu[:, :], C.WUB[par][:, k, :], C.xb(k, 0, 512), start=(k == 0), stop=False)
            P.tensor.matmul(pu2[:, :], C.WUB[par][:, k, :], C.xb(k, 512, 1024), start=(k == 0), stop=False)
        P.tensor.matmul(pu[:, :], C.LUBd[par][:, :], C.LT1[:, 0:512], start=False, stop=True)
        P.tensor.matmul(pu2[:, :], C.LUBd[par][:, :], C.LT1[:, 512:1024], start=False, stop=True).then_inc(C.cT, 1)
        mm_done[m] = C.cT.val
        P.scalar.wait_ge(C.cT, C.cT.val)
        if m == 0:
            P.scalar.wait_ge(C.qS, C.qS.val)
        P.scalar.activation(C.XH[:, m, 0:512], pu[:, :], AF.Relu, bias=C.BUP[:, m:m + 1], scale=1.0)
        P.scalar.activation(C.XH[:, m, 512:1024], pu2[:, :], AF.Relu, bias=C.BUP[:, m:m + 1], scale=1.0)
        evac_done[m] = C.cA.val
    up_done = C.cA.val

    P.tensor.wait_ge(C.cA, up_done)
    for n in range(2):
        if n == 1:
            P.tensor.wait_ge(C.cV, C.cV.val)
        for h in range(FC):
            P.tensor.matmul(C.PS[0][0:16, :], C.LDA[:, h, :], C.XH[:, h, 512 * n:512 * (n + 1)],
                            start=(h == 0), stop=(h == FC - 1))
        P.mark_last('tensor', C.cT)
        P.vector.wait_ge(C.cT, C.cT.val)
        P.vector.tensor_copy(C.LT2[:, 512 * n:512 * (n + 1)], C.PS[0][0:16, :])
    lt2_ready = C.cV.val

    mmd_done = {}
    ep_done = {}
    out_done = {}
    cached_done = {}
    for d in range(KC):
        pd, pd2 = C.PS[4 + 2 * (d % 2)], C.PS[5 + 2 * (d % 2)]
        P.gpsimd.wait_ge(C.cT, mmd_done[4 * (d - 2) + 3] if d >= 2 else C.cT.val)
        P.gpsimd.dma_start(out=C.LDBd[d % 2][:, :], in_=ldb_dram[0:16, 128 * d:128 * (d + 1)]).then_inc(C.qB[d % 2], 16)
        for r4 in range(4):
            par = r4 % 2
            step = 4 * d + r4
            mm_wait = []
            if step >= 2:
                mm_wait.append((C.cT, mmd_done[step - 2]))
                if step - 2 in cached_done:
                    mm_wait.append((C.qC[par], cached_done[step - 2]))
            else:
                mm_wait.append((C.cT, C.cT.val))
                mm_wait.append((C.qC[par], C.qC[par].val))
            if dequant:
                src = wdn_src
            else:
                src = wdn_src[step]
            rdy = _stream_lhsT(kb, C, par, src, wdn_cache[step] if wdn_cache is not None else None,
                               slice(128 * d, 128 * (d + 1)), 2048 * r4, dequant, mm_wait)
            if wdn_cache is not None:
                cached_done[step] = C.qC[par].val
            for (sm, vv) in rdy:
                P.tensor.wait_ge(sm, vv)
            if step == 0:
                P.tensor.wait_ge(C.cV, lt2_ready)
            if r4 == 0 and d >= 2:
                P.tensor.wait_ge(C.cV, ep_done[d - 2])
            for kk in range(KC):
                h = KC * r4 + kk
                P.tensor.matmul(pd[:, :], C.WUB[par][:, kk, :], C.XH[:, h, 0:512], start=(step % 4 == 0 and kk == 0), stop=False)
                P.tensor.matmul(pd2[:, :], C.WUB[par][:, kk, :], C.XH[:, h, 512:1024], start=(step % 4 == 0 and kk == 0), stop=False)
            if r4 == 3:
                P.tensor.wait_ge(C.qB[d % 2], C.qB[d % 2].val)
                P.tensor.matmul(pd[:, :], C.LDBd[d % 2][:, :], C.LT2[:, 0:512], start=False, stop=True)
                P.tensor.matmul(pd2[:, :], C.LDBd[d % 2][:, :], C.LT2[:, 512:1024], start=False, stop=True)
            P.mark_last('tensor', C.cT)
            mmd_done[step] = C.cT.val
        buf = C.XT[d % 2]
        if d == 0:
            P.sync.wait_ge(C.cV, C.cV.val)
            P.sync.wait_ge(C.cA, C.cA.val)
        if d >= 2:
            P.sync.wait_ge(C.cV, ep_done[d - 2])
            P.sync.wait_ge(C.qD[d % 2], out_done[d - 2])
        P.sync.dma_start(out=buf, in_=src_dram[d]).then_inc(C.qE[d % 2], 16)
        P.vector.wait_ge(C.qE[d % 2], C.qE[d % 2].val)
        P.vector.wait_ge(C.cT, C.cT.val)
        if d == 0:
            P.vector.wait_ge(C.qS, C.qS.val)
        P.vector.scalar_tensor_tensor(buf[:, 0:512], pd[:, :], C.BDN[:, d:d + 1], buf[:, 0:512], ALU.add, ALU.add)
        P.vector.scalar_tensor_tensor(buf[:, 512:1024], pd2[:, :], C.BDN[:, d:d + 1], buf[:, 512:1024], ALU.add, ALU.add)
        ep_done[d] = C.cV.val
        P.gpsimd.wait_ge(C.cV, C.cV.val)
        P.gpsimd.dma_start(out=dst_dram[d], in_=buf).then_inc(C.qD[d % 2], 16)
        if halo_out is not None:
            P.gpsimd.dma_start(out=halo_out[:, d, :], in_=buf[:, 1021:1024]).then_inc(C.qD[d % 2], 16)
        out_done[d] = C.qD[d % 2].val


PP = 512
PP_CONVW = PP + 0
PP_CONVB = PP + 64
PP_VNW = PP + 80
PP_VNB = PP + 81
PP_MNW = PP + 82
PP_MNB = PP + 83
PP_GNW = PP + 84
PP_GNB = PP + 100
PP_BIG = PP + 116
PP_BOG = PP + 132
PP_LN1W = PP + 148
PP_LN1B = PP + 164
PP_LN2W = PP + 180
PP_LN2B = PP + 196
PP_BG = PP + 212
PP_CMS = PP + 213
PP_HM8 = PP + 221
PPN = 229
SM_SD = 741
SM_TMP = 757
SM_CARRY = 773
SM_HALO = 789
SM_HBT = 837


def emit_attn_b(kb, C, src_dram, dst_dram, hm_dram, aux):
    P = kb.prog
    SM = C.SMALL
    xcb, MEM, QB, OUTB = C.AB0
    T0 = C.XT2[:, 1024:1536].bitcast(BF16)
    T1 = C.XT2[:, 1536:2048].bitcast(BF16)
    S0 = SM[:, 0:512].bitcast(BF16)
    GAMS = C.XT2[0:16, 0:1024]

    def wslab(segs, wsrc, col, sem):
        for (dst, r0, r1) in segs:
            P.gpsimd.dma_start(
                out=dst,
                in_=wsrc[r0:r1, 128 * col:128 * (col + 1)].rearrange("(kc p) n -> p kc n", p=128),
            ).then_inc(sem, 16)

    WUB0q = [(C.WUB[0][:, 0:16, :], 0, 2048)]
    WUB1q = [(C.WUB[1][:, 0:16, :], 0, 2048)]
    WQFv = [view(C.WQF[0][:, :, :], BF16, (8, 128)), view(C.WQF[1][:, :, :], BF16, (8, 128))]
    SCFv = [view(C.SCF[0][:, :, :], BF16, (8, 128)), view(C.SCF[1][:, :, :], BF16, (8, 128))]
    WQFvq = [(WQFv[0][:, 0:8, :], 0, 1024), (WQFv[1][:, 0:8, :], 1024, 2048)]
    SCFvq = [(SCFv[0][:, 0:8, :], 0, 1024), (SCFv[1][:, 0:8, :], 1024, 2048)]

    def vtile(kc):
        return WQFv[kc // 8][:, kc % 8, :]

    def itile(kc):
        return SCFv[kc // 8][:, kc % 8, :]

    P.gpsimd.dma_start(out=SM[:, PP:PP + PPN], in_=aux["parpack"]).then_inc(C.qG, 16)
    P.gpsimd.dma_start(out=C.WG[:, :, :], in_=aux["Wg"].rearrange("(kc p) n -> p kc n", p=128)).then_inc(C.qG, 16)

    emit_ln(kb, C, src_dram, fused_w=SM[:, PP_LN1W:PP_LN1W + 16], fused_b=SM[:, PP_LN1B:PP_LN1B + 16],
            fused_rho=True, keep_rho=True)
    ln_done = C.cV.val

    P.gpsimd.wait_ge(C.qD[0], C.qD[0].val)
    P.gpsimd.wait_ge(C.qD[1], C.qD[1].val)
    P.gpsimd.collective_compute(
        "AllGather", ALU.bypass, replica_groups=[list(range(NCORES))],
        ins=[aux["halo_in"].ap().opt()], outs=[aux["halo_all"].ap().opt()],
    ).then_inc(C.cc, 1)
    P.gpsimd.wait_ge(C.cc, C.cc.val)
    HLD = C.XT2[:, 1024:1408]
    P.gpsimd.wait_ge(C.cV, C.cV.val)
    P.gpsimd.wait_ge(C.cA, C.cA.val)
    P.gpsimd.wait_ge(C.cT, C.cT.val)
    P.gpsimd.dma_start(out=HLD.rearrange("p (j k t) -> p j k t", j=8, k=16, t=3),
                       in_=aux["halo_all"].ap().rearrange("j p k t -> p j k t")).then_inc(C.qG, 16)
    P.vector.wait_ge(C.qG, C.qG.val)
    P.vector.memset(SM[:, SM_HALO:SM_HALO + 48], 0.0)
    for j in range(NCORES):
        P.vector.scalar_tensor_tensor(SM[:, SM_HALO:SM_HALO + 48], HLD[:, 48 * j:48 * (j + 1)],
                                      SM[:, PP_HM8 + j:PP_HM8 + j + 1], SM[:, SM_HALO:SM_HALO + 48],
                                      ALU.mult, ALU.add)
    P.tensor.wait_ge(C.cV, C.cV.val)
    P.tensor.matmul(C.PS[0][0:1, 0:48], C.ones_col_f32[:, :], SM[:, SM_HALO:SM_HALO + 48], start=True, stop=True)
    P.mark_last('tensor', C.cT)
    P.vector.wait_ge(C.cT, C.cT.val)
    SQH = C.XT2[:, 1408:1456]
    P.vector.tensor_tensor(SQH, SM[:, SM_HALO:SM_HALO + 48], SM[:, SM_HALO:SM_HALO + 48], ALU.mult)
    P.tensor.wait_ge(C.cV, C.cV.val)
    P.tensor.matmul(C.PS[1][0:1, 0:48], C.ones_col_f32[:, :], SQH, start=True, stop=True)
    P.mark_last('tensor', C.cT)
    P.vector.wait_ge(C.cT, C.cT.val)
    HS = C.XT2[0:1, 1456:1459]
    HQ = C.XT2[0:1, 1459:1462]
    P.vector.tensor_reduce(HS, C.PS[0][0:1, 0:48].rearrange("p (k t) -> p t k", k=16), axis=AXX, op=ALU.add)
    P.vector.tensor_reduce(HQ, C.PS[1][0:1, 0:48].rearrange("p (k t) -> p t k", k=16), axis=AXX, op=ALU.add)
    HM_ = C.XT2[0:1, 1462:1465]
    HR = C.XT2[0:1, 1465:1468]
    P.vector.tensor_scalar(HM_, HS, 1.0 / D, None, ALU.mult)
    P.vector.tensor_scalar(HQ, HQ, 1.0 / D, None, ALU.mult)
    P.vector.tensor_tensor(HS, HM_, HM_, ALU.mult)
    P.vector.tensor_tensor(HQ, HQ, HS, ALU.subtract)
    P.vector.tensor_scalar(HS, HQ, EPS, None, ALU.add)
    P.vector.reciprocal(HS, HS)
    P.vector.tensor_tensor(HQ, HQ, HS, ALU.mult)
    P.vector.tensor_scalar(HQ, HQ, EPS, None, ALU.add)
    P.scalar.wait_ge(C.cV, C.cV.val)
    P.scalar.activation(HQ, HQ, AF.Sqrt)
    P.scalar.activation(HR, HS, AF.Sqrt)
    P.vector.wait_ge(C.cA, C.cA.val)
    P.vector.reciprocal(HQ, HQ)
    P.vector.tensor_tensor(HR, HR, HQ, ALU.mult)
    P.tensor.wait_ge(C.cV, C.cV.val)
    P.tensor.matmul(C.PS[2][:, 0:3], C.ones_row_f32[:, :], HM_, start=True, stop=True)
    P.tensor.matmul(C.PS[3][:, 0:3], C.ones_row_f32[:, :], HR, start=True, stop=True)
    P.mark_last('tensor', C.cT)
    P.vector.wait_ge(C.cT, C.cT.val)
    HBTb = SM[:, SM_HBT:SM_HBT + 24].bitcast(BF16)
    for k in range(KC):
        hk = SM[:, SM_HALO + 3 * k:SM_HALO + 3 * (k + 1)]
        P.vector.tensor_tensor(hk, hk, C.PS[2][:, 0:3], ALU.subtract)
        P.vector.tensor_tensor(hk, hk, C.PS[3][:, 0:3], ALU.mult)
        P.vector.tensor_scalar(HBTb[:, 3 * k:3 * (k + 1)], hk, SM[:, PP_LN1W + k:PP_LN1W + k + 1],
                               SM[:, PP_LN1B + k:PP_LN1B + k + 1], ALU.mult, ALU.add)
    for k in range(KC):
        P.vector.tensor_copy(C.XB[:, k, 1:4], HBTb[:, 3 * k:3 * (k + 1)])
    P.vector.memset(C.XB[:, 0, 0:1], 0.0)

    for k in range(KC):
        acc = T0
        if k > 0:
            P.vector.wait_ge(C.cA, C.cA.val)
        P.vector.tensor_scalar(acc[:, :], C.XB[:, k, 1:1025], SM[:, PP_CONVW + 4 * k:PP_CONVW + 4 * k + 1],
                               None, ALU.mult)
        for j in (1, 2):
            P.vector.scalar_tensor_tensor(acc[:, :], C.XB[:, k, 1 + j:1025 + j],
                                          SM[:, PP_CONVW + 4 * k + j:PP_CONVW + 4 * k + j + 1], acc[:, :],
                                          ALU.mult, ALU.add)
        P.vector.scalar_tensor_tensor(acc[:, :], C.xb(k), SM[:, PP_CONVW + 4 * k + 3:PP_CONVW + 4 * k + 4],
                                      acc[:, :], ALU.mult, ALU.add)
        P.vector.tensor_scalar(acc[:, :], acc[:, :], SM[:, PP_CONVB + k:PP_CONVB + k + 1], None, ALU.add)
        P.scalar.wait_ge(C.cV, C.cV.val)
        P.scalar.activation(xcb[:, k, :], acc[:, :], AF.Silu)
    conv_done = C.cA.val

    P.tensor.wait_ge(C.cA, conv_done)
    P.tensor.wait_ge(C.qG, C.qG.val)
    for n in range(2):
        for k in range(KC):
            P.tensor.matmul(C.PS[4 + n][0:16, :], C.WG[:, k, :], xcb[:, k, 512 * n:512 * (n + 1)],
                            start=(k == 0), stop=(k == KC - 1))
    P.mark_last('tensor', C.cT)
    P.scalar.wait_ge(C.cT, C.cT.val)
    P.scalar.activation(GAMS[:, 0:512], C.PS[4][0:16, :], AF.Sigmoid, bias=SM[0:16, PP_BG:PP_BG + 1], scale=1.0)
    P.scalar.activation(GAMS[:, 512:1024], C.PS[5][0:16, :], AF.Sigmoid, bias=SM[0:16, PP_BG:PP_BG + 1], scale=1.0)
    P.gpsimd.wait_ge(C.cA, C.cA.val)
    P.gpsimd.dma_start(out=aux["gam"].ap(), in_=GAMS).then_inc(C.qG, 16)
    ONES16 = C.ones_col_f32[0:16, 0:1].to_broadcast([16, 512])
    P.vector.wait_ge(C.cA, C.cA.val)
    P.vector.tensor_tensor_scan(C.PS[6][0:16, :], GAMS[:, 0:512], ONES16, 1.0, ALU.mult, ALU.mult)
    P.vector.tensor_tensor_scan(C.PS[7][0:16, :], GAMS[:, 512:1024], ONES16, C.PS[6][0:16, 511:512], ALU.mult, ALU.mult)
    P.vector.wait_ge(C.qG, C.qG.val)
    P.vector.wait_ge(C.cP, C.cP.val)
    P.vector.tensor_copy(GAMS[:, 0:512], C.PS[6][0:16, :])
    P.vector.tensor_copy(GAMS[:, 512:1024], C.PS[7][0:16, :])
    P.gpsimd.wait_ge(C.cV, C.cV.val)
    P.gpsimd.dma_start(out=aux["gamc"].ap(), in_=GAMS).then_inc(C.qG, 16)
    P.gpsimd.dma_start(out=aux["abounce"].ap(), in_=GAMS[:, 1023:1024]).then_inc(C.qG, 16)

    def phead_norm(ps_a, ps_b, raw_tile, mmv, l2, w_ap, b_ap, out_tile, from_psum=True, war_cv=None,
                   war_stats_cv=None, war_stats_ca=None, stb_off=0, sq_tile=None):
        STBa = C.STB[0:1, stb_off:stb_off + 512]
        STBb = C.STB[0:1, stb_off + 512:stb_off + 1024]
        sq = C.MB if sq_tile is None else sq_tile
        if from_psum:
            P.scalar.wait_ge(C.cT, mmv)
            P.scalar.wait_ge(C.cV, C.cV.val if war_cv is None else war_cv)
            P.scalar.activation(raw_tile[:, 0:512], ps_a[:, :], AF.Identity)
            P.scalar.activation(raw_tile[:, 512:1024], ps_b[:, :], AF.Identity)
        if not l2:
            P.tensor.wait_ge(C.cA, C.cA.val if war_stats_ca is None else war_stats_ca)
            P.tensor.wait_ge(C.cV, C.cV.val if war_stats_cv is None else war_stats_cv)
            P.tensor.matmul(ps_a[0:1, :], C.invd_col_bf[:, :], raw_tile[:, 0:512], start=True, stop=True)
            P.tensor.matmul(ps_b[0:1, :], C.invd_col_bf[:, :], raw_tile[:, 512:1024], start=True, stop=True)
            P.mark_last('tensor', C.cT)
            P.scalar.wait_ge(C.cT, C.cT.val)
            P.scalar.activation(STBa, ps_a[0:1, :], AF.Identity)
            P.scalar.activation(STBb, ps_b[0:1, :], AF.Identity)
            P.tensor.wait_ge(C.cA, C.cA.val)
            P.tensor.matmul(ps_a[:, :], C.ones_row_bf[:, :], STBa, start=True, stop=True)
            P.tensor.matmul(ps_b[:, :], C.ones_row_bf[:, :], STBb, start=True, stop=True)
            P.mark_last('tensor', C.cT)
            P.vector.wait_ge(C.cT, C.cT.val)
            P.vector.tensor_tensor(raw_tile[:, 0:512], raw_tile[:, 0:512], ps_a[:, :], ALU.subtract)
            P.vector.tensor_tensor(raw_tile[:, 512:1024], raw_tile[:, 512:1024], ps_b[:, :], ALU.subtract)
        P.scalar.wait_ge(C.cV, C.cV.val)
        P.scalar.wait_ge(C.cT, C.cT.val)
        P.scalar.activation(sq[:, :], raw_tile[:, :], AF.Square)
        if l2:
            P.tensor.wait_ge(C.cV, C.cV.val if war_stats_cv is None else war_stats_cv)
        P.tensor.wait_ge(C.cA, C.cA.val)
        P.tensor.matmul(ps_a[0:1, :], C.ones_col_bf[:, :], sq[:, 0:512], start=True, stop=True)
        P.tensor.matmul(ps_b[0:1, :], C.ones_col_bf[:, :], sq[:, 512:1024], start=True, stop=True)
        P.mark_last('tensor', C.cT)
        epsap = C.eps_l2[0:1, 0:1] if l2 else C.eps_ln[0:1, 0:1]
        P.scalar.wait_ge(C.cT, C.cT.val)
        P.scalar.activation(ps_a[0:1, :], ps_a[0:1, :], AF.Ln, bias=epsap, scale=1.0)
        P.scalar.activation(ps_b[0:1, :], ps_b[0:1, :], AF.Ln, bias=epsap, scale=1.0)
        P.scalar.activation(STBa, ps_a[0:1, :], AF.Exp, bias=0.0, scale=-0.5)
        P.scalar.activation(STBb, ps_b[0:1, :], AF.Exp, bias=0.0, scale=-0.5)
        row = C.ones_row_bf if l2 else C.sqrthd_row_bf
        P.tensor.wait_ge(C.cA, C.cA.val)
        P.tensor.matmul(ps_a[:, :], row[:, :], STBa, start=True, stop=True)
        P.tensor.matmul(ps_b[:, :], row[:, :], STBb, start=True, stop=True)
        P.mark_last('tensor', C.cT)
        P.vector.wait_ge(C.cT, C.cT.val)
        P.vector.tensor_tensor(raw_tile[:, 0:512], raw_tile[:, 0:512], ps_a[:, :], ALU.mult)
        P.vector.tensor_tensor(raw_tile[:, 512:1024], raw_tile[:, 512:1024], ps_b[:, :], ALU.mult)
        if w_ap is not None:
            P.vector.tensor_scalar(out_tile[:, :], raw_tile[:, :], w_ap, b_ap, ALU.mult, ALU.add)
        elif out_tile is not raw_tile:
            P.vector.tensor_copy(out_tile[:, :], raw_tile[:, :])

    q_mm = {}
    k_mm = {}
    v_mm = {}
    i_mm = {}
    qn_done = {}
    kn_done = {}
    vn_done = {}
    ig_ev = {}
    gb_ev = {}
    u_done = {}
    scan_done = {}
    GBT = C.XT2[:, 0:1024]
    for h in range(NH):
        if h == 0:
            P.gpsimd.wait_ge(C.cT, C.cT.val)
            P.gpsimd.wait_ge(C.qC[0], C.qC[0].val)
            P.gpsimd.wait_ge(C.qC[1], C.qC[1].val)
        P.gpsimd.wait_ge(C.cT, q_mm.get(h - 1, 0))
        wslab(WUB0q, aux["Wq"], h, C.qW[0])
        ld_q = C.qW[0].val
        P.gpsimd.wait_ge(C.cT, k_mm.get(h - 1, 0))
        wslab(WUB1q, aux["Wk"], h, C.qW[1])
        ld_k = C.qW[1].val
        P.gpsimd.wait_ge(C.cT, v_mm.get(h - 1, 0))
        wslab(WQFvq, aux["Wv"], h, C.qB[0])
        ld_v = C.qB[0].val
        P.gpsimd.wait_ge(C.cT, i_mm.get(h - 1, 0))
        wslab(SCFvq, aux["Wig"], h, C.qB[1])
        ld_i = C.qB[1].val
        P.tensor.wait_ge(C.qB[1], ld_i)
        if h == 0:
            P.tensor.wait_ge(C.cV, C.cV.val)
        else:
            P.tensor.wait_ge(C.cA, gb_ev[h - 1])
        for k in range(KC):
            P.tensor.matmul(C.PS[6][:, :], itile(k), C.xb(k, 0, 512), start=(k == 0), stop=(k == KC - 1))
            P.tensor.matmul(C.PS[7][:, :], itile(k), C.xb(k, 512, 1024), start=(k == 0), stop=(k == KC - 1))
        P.mark_last('tensor', C.cT)
        i_mm[h] = C.cT.val
        P.tensor.wait_ge(C.qW[0], ld_q)
        if h > 0:
            P.tensor.wait_ge(C.cV, qn_done[h - 1])
        for k in range(KC):
            P.tensor.matmul(C.PS[0][:, :], C.WUB[0][:, k, :], C.xb(k, 0, 512), start=(k == 0), stop=(k == KC - 1))
            P.tensor.matmul(C.PS[1][:, :], C.WUB[0][:, k, :], C.xb(k, 512, 1024), start=(k == 0), stop=(k == KC - 1))
        P.mark_last('tensor', C.cT)
        q_mm[h] = C.cT.val
        P.tensor.wait_ge(C.qW[1], ld_k)
        if h > 0:
            P.tensor.wait_ge(C.cV, kn_done[h - 1])
        for k in range(KC):
            P.tensor.matmul(C.PS[2][:, :], C.WUB[1][:, k, :], C.xb(k, 0, 512), start=(k == 0), stop=(k == KC - 1))
            P.tensor.matmul(C.PS[3][:, :], C.WUB[1][:, k, :], C.xb(k, 512, 1024), start=(k == 0), stop=(k == KC - 1))
        P.mark_last('tensor', C.cT)
        k_mm[h] = C.cT.val
        P.tensor.wait_ge(C.qB[0], ld_v)
        if h > 0:
            P.tensor.wait_ge(C.cV, vn_done[h - 1])
        for k in range(KC):
            P.tensor.matmul(C.PS[4][:, :], vtile(k), C.xb(k, 0, 512), start=(k == 0), stop=(k == KC - 1))
            P.tensor.matmul(C.PS[5][:, :], vtile(k), C.xb(k, 512, 1024), start=(k == 0), stop=(k == KC - 1))
        P.mark_last('tensor', C.cT)
        v_mm[h] = C.cT.val

        P.scalar.wait_ge(C.cT, i_mm[h])
        if h > 0:
            P.scalar.wait_ge(C.cV, u_done[h - 1])
        P.scalar.activation(S0[:, 0:512], C.PS[6][:, :], AF.Sigmoid, bias=SM[:, PP_BIG + h:PP_BIG + h + 1], scale=1.0)
        P.scalar.activation(S0[:, 512:1024], C.PS[7][:, :], AF.Sigmoid, bias=SM[:, PP_BIG + h:PP_BIG + h + 1], scale=1.0)
        ig_ev[h] = C.cA.val
        P.gpsimd.wait_ge(C.cA, ig_ev[h])
        P.gpsimd.wait_ge(C.cT, C.cT.val)
        P.gpsimd.wait_ge(C.qG, C.qG.val)
        P.gpsimd.wait_ge(C.cV, C.cV.val)
        P.gpsimd.dma_start(out=C.XT2[0:1, 0:1024], in_=aux["gam"].ap()[h:h + 1, :]).then_inc(C.qR[0], 16)
        P.tensor.wait_ge(C.qR[0], C.qR[0].val)
        P.tensor.matmul(C.PS[6][:, :], C.ones_row_f32[:, :], C.XT2[0:1, 0:512], start=True, stop=True)
        P.tensor.matmul(C.PS[7][:, :], C.ones_row_f32[:, :], C.XT2[0:1, 512:1024], start=True, stop=True)
        P.mark_last('tensor', C.cT)
        P.scalar.wait_ge(C.cT, C.cT.val)
        P.scalar.activation(GBT[:, 0:512], C.PS[6][:, :], AF.Identity)
        P.scalar.activation(GBT[:, 512:1024], C.PS[7][:, :], AF.Identity)
        gb_ev[h] = C.cA.val

        phead_norm(C.PS[0], C.PS[1], QB[:, h, :], q_mm[h], True, None, None, QB[:, h, :], war_cv=0)
        qn_done[h] = C.cV.val
        phead_norm(C.PS[2], C.PS[3], T0, k_mm[h], True, None, None, T0,
                   war_cv=u_done.get(h - 1, 0))
        kn_done[h] = C.cV.val
        phead_norm(C.PS[4], C.PS[5], T1, v_mm[h], False,
                   SM[:, PP_VNW:PP_VNW + 1], SM[:, PP_VNB:PP_VNB + 1], T1,
                   war_cv=scan_done.get(h - 1, 0))
        vn_done[h] = C.cV.val
        P.vector.wait_ge(C.cA, ig_ev[h])
        P.vector.tensor_tensor(T1[:, :], T1[:, :], T0[:, :], ALU.mult)
        P.vector.tensor_tensor(T1[:, :], T1[:, :], S0[:, :], ALU.mult)
        u_done[h] = C.cV.val
        P.vector.wait_ge(C.cA, gb_ev[h])
        P.vector.tensor_tensor_scan(MEM[:, h, 0:512], GBT[:, 0:512], T1[:, 0:512], 0.0, ALU.mult, ALU.add)
        P.vector.tensor_tensor_scan(MEM[:, h, 512:1024], GBT[:, 512:1024], T1[:, 512:1024],
                                    MEM[:, h, 511:512], ALU.mult, ALU.add)
        scan_done[h] = C.cV.val

    P.vector.tensor_copy(S0[:, 0:16], MEM[:, :, 1023:1024].rearrange("p a b -> p (a b)"))
    P.gpsimd.wait_ge(C.cV, C.cV.val)
    P.gpsimd.dma_start(out=aux["sbounce"].ap(), in_=S0[:, 0:16]).then_inc(C.qG, 16)
    P.gpsimd.wait_ge(C.qG, C.qG.val)
    P.gpsimd.collective_compute(
        "AllGather", ALU.bypass, replica_groups=[list(range(NCORES))],
        ins=[aux["sbounce"].ap().opt()], outs=[aux["sa2_all"].ap().opt()],
    ).then_inc(C.cc, 1)
    P.gpsimd.collective_compute(
        "AllGather", ALU.bypass, replica_groups=[list(range(NCORES))],
        ins=[aux["abounce"].ap().opt()], outs=[aux["aa_all"].ap().opt()],
    ).then_inc(C.cc, 1)
    P.gpsimd.wait_ge(C.cc, C.cc.val)
    P.gpsimd.dma_start(out=C.XT2[0:1, 0:128], in_=aux["aa_all"].ap().rearrange("a b -> (a b)").unsqueeze(0)).then_inc(C.qG, 16)
    P.gpsimd.dma_start(out=C.XT2[0:1, 128:256], in_=aux["cmrow"][0:1, :]).then_inc(C.qG, 16)
    P.gpsimd.dma_start(out=C.XT2[0:1, 256:384], in_=aux["cmrow"][1:2, :]).then_inc(C.qG, 16)
    P.vector.wait_ge(C.qG, C.qG.val)
    P.vector.tensor_tensor(C.XT2[0:1, 0:128], C.XT2[0:1, 0:128], C.XT2[0:1, 128:256], ALU.mult)
    P.vector.tensor_tensor(C.XT2[0:1, 0:128], C.XT2[0:1, 0:128], C.XT2[0:1, 256:384], ALU.add)
    P.vector.tensor_copy(C.STB[0:1, 0:128], C.XT2[0:1, 0:128])
    P.tensor.wait_ge(C.cV, C.cV.val)
    P.tensor.matmul(C.PS[0][:, 0:128], C.ones_row_bf[:, :], C.STB[0:1, 0:128], start=True, stop=True)
    P.mark_last('tensor', C.cT)
    P.vector.wait_ge(C.cT, C.cT.val)
    P.vector.memset(SM[:, SM_CARRY:SM_CARRY + 16], 0.0)
    for j in range(NCORES):
        P.gpsimd.wait_ge(C.cV, C.cV.val)
        P.gpsimd.dma_start(out=SM[:, SM_SD:SM_SD + 16], in_=aux["sa2_all"].ap()[128 * j:128 * (j + 1), :]).then_inc(C.qR[1], 16)
        P.vector.wait_ge(C.qR[1], C.qR[1].val)
        P.vector.tensor_scalar(SM[:, SM_SD:SM_SD + 16], SM[:, SM_SD:SM_SD + 16],
                               SM[:, PP_CMS + j:PP_CMS + j + 1], None, ALU.mult)
        P.vector.tensor_tensor(SM[:, SM_CARRY:SM_CARRY + 16], SM[:, SM_CARRY:SM_CARRY + 16],
                               C.PS[0][:, 16 * j:16 * (j + 1)], ALU.mult)
        P.vector.tensor_tensor(SM[:, SM_CARRY:SM_CARRY + 16], SM[:, SM_CARRY:SM_CARRY + 16],
                               SM[:, SM_SD:SM_SD + 16], ALU.add)

    og_mm = {}
    og_sig = {}
    outb_done = {}
    fix_done = {}
    cv0 = C.cV.val
    ca0 = C.cA.val
    pair_cv = {0: cv0, 1: cv0}
    pair_ca = {0: ca0, 1: ca0}
    for h in range(NH):
        p = h % 2
        sa, sb = (C.PS[0], C.PS[1]) if p == 0 else (C.PS[4], C.PS[5])
        P.gpsimd.wait_ge(C.cT, C.cT.val)
        P.gpsimd.wait_ge(C.qG, C.qG.val)
        P.gpsimd.dma_start(out=C.XT2[0:1, 0:1024], in_=aux["gamc"].ap()[h:h + 1, :]).then_inc(C.qR[0], 16)
        P.tensor.wait_ge(C.qR[0], C.qR[0].val)
        P.tensor.wait_ge(C.cV, fix_done.get(h - 1, cv0))
        P.tensor.matmul(C.PS[6][:, :], C.ones_row_f32[:, :], C.XT2[0:1, 0:512], start=True, stop=True)
        P.tensor.matmul(C.PS[7][:, :], C.ones_row_f32[:, :], C.XT2[0:1, 512:1024], start=True, stop=True)
        P.mark_last('tensor', C.cT)
        gamc_mm = C.cT.val
        P.gpsimd.wait_ge(C.cT, og_mm.get(h - 1, 0))
        wslab(WUB0q, aux["Wog"], h, C.qW[0])
        P.tensor.wait_ge(C.qW[0], C.qW[0].val)
        P.tensor.wait_ge(C.cA, og_sig.get(h - 1, ca0))
        if h == 0:
            P.tensor.wait_ge(C.cV, cv0)
        for k in range(KC):
            P.tensor.matmul(C.PS[2][:, :], C.WUB[0][:, k, :], xcb[:, k, 0:512], start=(k == 0), stop=(k == KC - 1))
            P.tensor.matmul(C.PS[3][:, :], C.WUB[0][:, k, :], xcb[:, k, 512:1024], start=(k == 0), stop=(k == KC - 1))
        P.mark_last('tensor', C.cT)
        og_mm[h] = C.cT.val
        memv = MEM[:, h, :]
        P.vector.wait_ge(C.cT, gamc_mm)
        P.vector.scalar_tensor_tensor(memv[:, 0:512], C.PS[6][:, :], SM[:, SM_CARRY + h:SM_CARRY + h + 1],
                                      memv[:, 0:512], ALU.mult, ALU.add)
        P.vector.scalar_tensor_tensor(memv[:, 512:1024], C.PS[7][:, :], SM[:, SM_CARRY + h:SM_CARRY + h + 1],
                                      memv[:, 512:1024], ALU.mult, ALU.add)
        fix_done[h] = C.cV.val
        phead_norm(sa, sb, memv, 0, False,
                   SM[:, PP_MNW:PP_MNW + 1], SM[:, PP_MNB:PP_MNB + 1], T0, from_psum=False,
                   war_stats_cv=max(fix_done[h], pair_cv[p]), war_stats_ca=pair_ca[p], stb_off=1024 * p)
        P.vector.tensor_tensor(T0[:, :], T0[:, :], QB[:, h, :], ALU.mult)
        phead_norm(sa, sb, T0, 0, False,
                   SM[:, PP_GNW + h:PP_GNW + h + 1], SM[:, PP_GNB + h:PP_GNB + h + 1], T0,
                   from_psum=False, stb_off=1024 * p)
        pair_cv[p] = C.cV.val
        pair_ca[p] = C.cA.val
        P.scalar.wait_ge(C.cT, og_mm[h])
        P.scalar.wait_ge(C.cV, outb_done.get(h - 1, cv0))
        P.scalar.activation(S0[:, 0:512], C.PS[2][:, :], AF.Sigmoid, bias=SM[:, PP_BOG + h:PP_BOG + h + 1], scale=1.0)
        P.scalar.activation(S0[:, 512:1024], C.PS[3][:, :], AF.Sigmoid, bias=SM[:, PP_BOG + h:PP_BOG + h + 1], scale=1.0)
        og_sig[h] = C.cA.val
        P.vector.wait_ge(C.cA, og_sig[h])
        P.vector.tensor_tensor(OUTB[:, h, :], T0[:, :], S0[:, :], ALU.mult)
        outb_done[h] = C.cV.val
    p2_done = C.cV.val

    M0B = C.AR0[:, 0:1024]
    R0B = C.AR0[:, 1024:2048]
    Q0B = C.AR0[:, 2048:3072]
    P.tensor.wait_ge(C.cV, p2_done)
    P.tensor.matmul(C.PS[4][:, :], C.ones_row_bf[:, :], C.SM0[:, 0:512], start=True, stop=True)
    P.tensor.matmul(C.PS[5][:, :], C.ones_row_bf[:, :], C.SM0[:, 512:1024], start=True, stop=True)
    P.tensor.matmul(C.PS[6][:, :], C.ones_row_bf[:, :], C.SR0[:, 0:512], start=True, stop=True)
    P.tensor.matmul(C.PS[7][:, :], C.ones_row_bf[:, :], C.SR0[:, 512:1024], start=True, stop=True)
    P.mark_last('tensor', C.cT)
    P.vector.wait_ge(C.cT, C.cT.val)
    P.vector.tensor_copy(M0B[:, 0:512], C.PS[4][:, :])
    P.vector.tensor_copy(M0B[:, 512:1024], C.PS[5][:, :])
    P.vector.tensor_copy(R0B[:, 0:512], C.PS[6][:, :])
    P.vector.tensor_copy(R0B[:, 512:1024], C.PS[7][:, :])
    P.vector.tensor_tensor(Q0B, M0B, R0B, ALU.mult)
    wo_mm = {}
    ep = {}
    outw = {}
    for m in range(KC):
        par = m % 2
        P.gpsimd.wait_ge(C.cT, wo_mm.get(m - 2, C.cT.val if m < 2 else 0))
        wslab(WUB0q if par == 0 else WUB1q, aux["Wo"], m, C.qW[par])
        pd, pd2 = C.PS[0 + 2 * par], C.PS[1 + 2 * par]
        P.tensor.wait_ge(C.qW[par], C.qW[par].val)
        if m < 2:
            P.tensor.wait_ge(C.cV, C.cV.val)
        else:
            P.tensor.wait_ge(C.cV, ep.get(m - 2, 0))
        for k in range(NH):
            P.tensor.matmul(pd[:, :], (C.WUB[0] if par == 0 else C.WUB[1])[:, k, :], OUTB[:, k, 0:512],
                            start=(k == 0), stop=(k == NH - 1))
            P.tensor.matmul(pd2[:, :], (C.WUB[0] if par == 0 else C.WUB[1])[:, k, :], OUTB[:, k, 512:1024],
                            start=(k == 0), stop=(k == NH - 1))
        P.mark_last('tensor', C.cT)
        wo_mm[m] = C.cT.val
        buf = C.XT[par]
        if m >= 2:
            P.sync.wait_ge(C.cV, ep[m - 2])
            P.sync.wait_ge(C.qD[par], outw[m - 2])
        else:
            P.sync.wait_ge(C.cV, C.cV.val)
        P.sync.dma_start(out=buf, in_=src_dram[m]).then_inc(C.qE[par], 16)
        P.vector.wait_ge(C.qE[par], C.qE[par].val)
        P.vector.wait_ge(C.cT, C.cT.val)
        P.vector.tensor_tensor(buf, buf, R0B, ALU.mult)
        P.vector.tensor_tensor(buf, buf, Q0B, ALU.subtract)
        P.vector.tensor_tensor(buf[:, 0:512], buf[:, 0:512], pd[:, :], ALU.add)
        P.vector.tensor_tensor(buf[:, 512:1024], buf[:, 512:1024], pd2[:, :], ALU.add)
        ep[m] = C.cV.val
        P.gpsimd.wait_ge(C.cV, C.cV.val)
        P.gpsimd.dma_start(out=hm_dram[m], in_=buf).then_inc(C.qD[par], 16)
        outw[m] = C.qD[par].val

    emit_ln(kb, C, hm_dram, fused_w=SM[:, PP_LN2W:PP_LN2W + 16], fused_b=SM[:, PP_LN2B:PP_LN2B + 16])
    g_mm = {}
    ev2 = {}
    WQS = [(WQFv[0][:, 0:8, :], 0, 1024), (WQFv[1][:, 0:8, :], 1024, 2048)]
    SCS = [(SCFv[0][:, 0:8, :], 0, 1024), (SCFv[1][:, 0:8, :], 1024, 2048)]

    def gtile(m, k):
        return C.WUB[0][:, k, :] if m % 2 == 0 else WQFv[k // 8][:, k % 8, :]

    def vtile2(m, k):
        return C.WUB[1][:, k, :] if m % 2 == 0 else SCFv[k // 8][:, k % 8, :]

    for m in range(FC):
        par = m % 2
        sg, sv = (C.qW[0], C.qW[1]) if par == 0 else (C.qB[0], C.qB[1])
        P.gpsimd.wait_ge(C.cT, g_mm.get(m - 2, C.cT.val if m < 2 else 0))
        wslab(WUB0q if par == 0 else WQS, aux["Wgate"], m, sg)
        wslab(WUB1q if par == 0 else SCS, aux["Wval"], m, sv)
        pg, pg2 = C.PS[0 + 2 * par], C.PS[1 + 2 * par]
        pv, pv2 = C.PS[4 + 2 * par], C.PS[5 + 2 * par]
        P.tensor.wait_ge(sg, sg.val)
        P.tensor.wait_ge(sv, sv.val)
        if m < 2:
            P.tensor.wait_ge(C.cV, C.cV.val)
        else:
            P.tensor.wait_ge(C.cV, ev2[m - 2])
        for k in range(KC):
            P.tensor.matmul(pg[:, :], gtile(m, k), C.xb(k, 0, 512), start=(k == 0), stop=(k == KC - 1))
            P.tensor.matmul(pg2[:, :], gtile(m, k), C.xb(k, 512, 1024), start=(k == 0), stop=(k == KC - 1))
            P.tensor.matmul(pv[:, :], vtile2(m, k), C.xb(k, 0, 512), start=(k == 0), stop=(k == KC - 1))
            P.tensor.matmul(pv2[:, :], vtile2(m, k), C.xb(k, 512, 1024), start=(k == 0), stop=(k == KC - 1))
        P.mark_last('tensor', C.cT)
        g_mm[m] = C.cT.val
        P.scalar.wait_ge(C.cT, C.cT.val)
        if m > 0:
            P.scalar.wait_ge(C.cV, ev2[m - 1])
        P.scalar.activation(S0[:, 0:512], pg[:, :], AF.Silu)
        P.scalar.activation(S0[:, 512:1024], pg2[:, :], AF.Silu)
        P.vector.wait_ge(C.cA, C.cA.val)
        P.vector.tensor_tensor(C.XH[:, m, 0:512], S0[:, 0:512], pv[:, :], ALU.mult)
        P.vector.tensor_tensor(C.XH[:, m, 512:1024], S0[:, 512:1024], pv2[:, :], ALU.mult)
        ev2[m] = C.cV.val
    mlp_up_done = C.cV.val

    wom = {}
    epo = {}
    outd = {}
    for d in range(KC):
        pd, pd2 = C.PS[0 + 2 * (d % 2)], C.PS[1 + 2 * (d % 2)]
        for r4 in range(4):
            par = r4 % 2
            step = 4 * d + r4
            P.gpsimd.wait_ge(C.cT, wom.get(step - 2, C.cT.val if step < 2 else 0))
            src_ap = aux["Wout"][2048 * r4:2048 * (r4 + 1), 128 * d:128 * (d + 1)]
            P.gpsimd.dma_start(
                out=C.WUB[par][:, 0:16, :],
                in_=src_ap.rearrange("(kc p) n -> p kc n", p=128),
            ).then_inc(C.qW[par], 16)
            P.tensor.wait_ge(C.qW[par], C.qW[par].val)
            if step < 2:
                P.tensor.wait_ge(C.cV, mlp_up_done)
            if r4 == 0 and d >= 2:
                P.tensor.wait_ge(C.cV, epo[d - 2])
            for kk in range(KC):
                h = KC * r4 + kk
                st_ = (r4 == 0 and kk == 0)
                sp_ = (r4 == 3 and kk == KC - 1)
                P.tensor.matmul(pd[:, :], C.WUB[par][:, kk, :], C.XH[:, h, 0:512], start=st_, stop=sp_)
                P.tensor.matmul(pd2[:, :], C.WUB[par][:, kk, :], C.XH[:, h, 512:1024], start=st_, stop=sp_)
            P.mark_last('tensor', C.cT)
            wom[step] = C.cT.val
        if d >= 1:
            P.sync.wait_ge(C.cV, epo[d - 1])
            P.sync.wait_ge(C.qD[0], outd[d - 1])
        else:
            P.sync.wait_ge(C.cV, C.cV.val)
        P.sync.dma_start(out=C.XT[0], in_=src_dram[d]).then_inc(C.qE[0], 16)
        P.sync.dma_start(out=C.XT[1], in_=hm_dram[d]).then_inc(C.qE[1], 16)
        P.vector.wait_ge(C.qE[0], C.qE[0].val)
        P.vector.wait_ge(C.qE[1], C.qE[1].val)
        P.vector.wait_ge(C.cT, C.cT.val)
        P.vector.tensor_tensor(C.XT[0], C.XT[0], C.XT[1], ALU.add)
        P.vector.tensor_tensor(C.XT[0][:, 0:512], C.XT[0][:, 0:512], pd[:, :], ALU.add)
        P.vector.tensor_tensor(C.XT[0][:, 512:1024], C.XT[0][:, 512:1024], pd2[:, :], ALU.add)
        epo[d] = C.cV.val
        P.gpsimd.wait_ge(C.cV, C.cV.val)
        P.gpsimd.dma_start(out=dst_dram[d], in_=C.XT[0]).then_inc(C.qD[0], 16)
        outd[d] = C.qD[0].val


def prep_attnb_pack(inp, core):
    pp = np.zeros((128, PPN), dtype=np.float32)
    def col(c):
        return c - PP
    cw = inp["L0_attn_b_conv_w"].reshape(D, 4)
    for k in range(KC):
        for j in range(4):
            pp[:, col(PP_CONVW) + 4 * k + j] = cw[128 * k:128 * (k + 1), j]
        pp[:, col(PP_CONVB) + k] = inp["L0_attn_b_conv_b"][128 * k:128 * (k + 1)]
        pp[:, col(PP_LN1W) + k] = inp["L0_attn_b_ln1w"][128 * k:128 * (k + 1)]
        pp[:, col(PP_LN1B) + k] = inp["L0_attn_b_ln1b"][128 * k:128 * (k + 1)]
        pp[:, col(PP_LN2W) + k] = inp["L0_attn_b_ln2w"][128 * k:128 * (k + 1)]
        pp[:, col(PP_LN2B) + k] = inp["L0_attn_b_ln2b"][128 * k:128 * (k + 1)]
    pp[:, col(PP_VNW)] = inp["L0_attn_b_vnw"]
    pp[:, col(PP_VNB)] = inp["L0_attn_b_vnb"]
    pp[:, col(PP_MNW)] = inp["L0_attn_b_mnw"]
    pp[:, col(PP_MNB)] = inp["L0_attn_b_mnb"]
    for h in range(NH):
        pp[:, col(PP_GNW) + h] = inp["L0_attn_b_gnw"][128 * h:128 * (h + 1)]
        pp[:, col(PP_GNB) + h] = inp["L0_attn_b_gnb"][128 * h:128 * (h + 1)]
        pp[:, col(PP_BIG) + h] = inp["L0_attn_b_big"][128 * h:128 * (h + 1)]
        pp[:, col(PP_BOG) + h] = inp["L0_attn_b_bog"][128 * h:128 * (h + 1)]
    pp[0:16, col(PP_BG)] = inp["L0_attn_b_bg"]
    for j in range(NCORES):
        pp[:, col(PP_CMS) + j] = 1.0 if j < core else 0.0
        pp[:, col(PP_HM8) + j] = 1.0 if j == core - 1 else 0.0
    cmrow = np.zeros((2, 128), dtype=np.float32)
    for j in range(NCORES):
        for h in range(NH):
            cmrow[0, 16 * j + h] = 1.0 if j < core else 0.0
    cmrow[1] = 1.0 - cmrow[0]
    return np.ascontiguousarray(pp), cmrow


def declare_attnb_aux(kb, with_inputs=True):
    aux = {}
    for nm, shp in [("Wq", [D, D]), ("Wk", [D, D]), ("Wv", [D, D]), ("Wig", [D, D]),
                    ("Wog", [D, D]), ("Wo", [D, D]), ("Wg", [D, NH]),
                    ("Wgate", [D, FF]), ("Wval", [D, FF]), ("Wout", [FF, D])]:
        aux[nm] = kb.din("b_" + nm, shp)
    aux["parpack"] = kb.din("parpack", [128, PPN])
    aux["cmrow"] = kb.din("cmrow", [2, 128])
    aux["gam"] = kb.dscratch("gam", [NH, T])
    aux["gamc"] = kb.dscratch("gamc", [NH, T])
    aux["halo_in"] = kb.dscratch("halo_in", [128, KC, 3])
    aux["halo_all"] = kb.dscratch("halo_all", [NCORES, 128, KC, 3])
    aux["sbounce"] = kb.dscratch("sbounce", [128, NH])
    aux["sa2_all"] = kb.dscratch("sa2_all", [NCORES * 128, NH])
    aux["abounce"] = kb.dscratch("abounce", [NH, 1])
    aux["aa_all"] = kb.dscratch("aa_all", [NCORES * NH, 1])
    return aux


A_XNB, A_TT, A_AOUT, A_COUT = 0, 1024, 2048, 3072
A_DOUT, A_D2, A_EOUT, A_E2 = 4096, 5120, 6144, 7168
A_CQK, A_CV = 8192, 9216
A_O1B = 10240
A_O2T1, A_O2T2, A_LEP2, A_Y = 12288, 13312, 14336, 15360
A_PR1, A_S1F, A_P1B = 16384, 17408, 17920
A_S2F, A_S2B, A_P2T, A_RST = 18432, 18944, 19200, 19456
A_IBF, A_BD, A_BDP = 19472, 19536, 19920
A_W0B, A_W1, A_ALNW, A_ALNB, A_PB = 20176, 20272, 20275, 20403, 20531
A_M2, A_SLAB, A_YC, A_ST16, A_SQ16 = 20532, 21044, 22068, 22580, 22624
A_IF32 = 22884
A_PR1F, A_O1F, A_L2F = 23040, 25088, 27136
A_B0, A_B1 = 29184, 29248
A_SQF = 2048
A_YTR, A_SCR, A_X1T, A_STG = 0, 8192, 9216, 10240


def prep_attna_consts(inp):
    qw = inp["L0_attn_a_qkv_w"]
    pw = inp["L0_attn_a_proj_w"]
    cw0 = inp["L0_attn_a_conv_w0"]
    cw1 = inp["L0_attn_a_conv_w1"]
    def bd(colmap, umatch):
        m = np.zeros((128, 128), dtype=np.float32)
        for u in range(8):
            for c in range(16):
                for oc, (uu, col) in colmap(u):
                    pass
        return m
    bdA = np.zeros((128, 128), np.float32)
    bdC = np.zeros((128, 128), np.float32)
    bdD = np.zeros((128, 128), np.float32)
    bdD2 = np.zeros((128, 128), np.float32)
    bdE = np.zeros((128, 128), np.float32)
    bdE2 = np.zeros((128, 128), np.float32)
    for u in range(8):
        for c in range(16):
            r = 16 * u + c
            for oc in range(8):
                bdA[r, 16 * u + oc] = qw[c, 0 + oc]
                bdA[r, 16 * u + 8 + oc] = qw[c, 16 + oc]
            for oc in range(16):
                bdC[r, 16 * u + oc] = qw[c, 32 + oc]
            if u < 4:
                for oc in range(8):
                    bdD[r, 32 * u + oc] = qw[c, 8 + oc]
                    bdE[r, 32 * u + oc] = qw[c, 24 + oc]
            else:
                for oc in range(8):
                    bdD2[r, 32 * (u - 4) + oc] = qw[c, 8 + oc]
                    bdE2[r, 32 * (u - 4) + oc] = qw[c, 24 + oc]
    bdP1 = np.zeros((128, 128), np.float32)
    bdPL = np.zeros((128, 128), np.float32)
    for u in range(8):
        for c in range(8):
            for oc in range(16):
                bdP1[16 * u + c, 16 * u + oc] = pw[c, oc]
                bdPL[16 * u + 8 + c, 16 * u + oc] = pw[8 + c, oc]
    bdPG = np.zeros((128, 128), np.float32)
    bdPG2 = np.zeros((128, 128), np.float32)
    for r in range(4):
        for c in range(8):
            for oc in range(16):
                bdPG[32 * r + c, 16 * r + oc] = pw[8 + c, oc]
                bdPG2[32 * r + c, 16 * (r + 4) + oc] = pw[8 + c, oc]
    W0 = np.zeros((128, 192), np.float32)
    for du in range(3):
        for i in range(8):
            for c in range(8):
                W0[:, 64 * du + 8 * i + c] = cw0[c, 0, 1, du]
    W1 = np.zeros((128, 3), np.float32)
    for dt in range(3):
        for u in range(8):
            for c in range(8):
                W1[16 * u + 8 + c, dt] = cw1[c, 0, dt, 1]
    m2 = np.zeros((32, 32), np.float32)
    for a in range(32):
        for b in range(32):
            if a % 4 != b % 4:
                m2[a, b] = -1e9
    M2 = np.tile(m2, (4, 16))
    alnw = np.zeros((128, 128), np.float32)
    alnb = np.zeros((128, 128), np.float32)
    for u in range(8):
        for c in range(16):
            alnw[:, 16 * u + c] = inp["L0_attn_a_ln_w"][c]
            alnb[:, 16 * u + c] = inp["L0_attn_a_ln_b"][c]
    pb = np.zeros((128, 1), np.float32)
    for u in range(8):
        for c in range(16):
            pb[16 * u + c, 0] = inp["L0_attn_a_proj_b"][c]
    b0 = np.zeros((128, 64), np.float32)
    for i in range(8):
        for c in range(8):
            b0[:, 8 * i + c] = inp["L0_attn_a_conv_b0"][c]
    b1 = np.zeros((128, 1), np.float32)
    for u in range(8):
        for c in range(8):
            b1[16 * u + 8 + c, 0] = inp["L0_attn_a_conv_b1"][c]
    bds = np.stack([bdA, bdC, bdD, bdD2, bdE, bdE2], 0)
    bdp = np.stack([bdP1, bdPG, bdPG2, bdPL], 0)
    return {"a_bds": bds, "a_bdp": bdp, "a_w0": W0, "a_w1": W1, "a_m2": M2,
            "a_alnw": alnw, "a_alnb": alnb, "a_pb": pb, "a_b0": b0, "a_b1": b1}


def declare_attna_aux(kb):
    aux = {}
    aux["a_bds"] = kb.din("a_bds", [6, 128, 128])
    aux["a_bdp"] = kb.din("a_bdp", [4, 128, 128])
    aux["a_w0"] = kb.din("a_w0", [128, 192])
    aux["a_w1"] = kb.din("a_w1", [128, 3])
    aux["a_m2"] = kb.din("a_m2", [128, 512])
    aux["a_alnw"] = kb.din("a_alnw", [128, 128])
    aux["a_alnb"] = kb.din("a_alnb", [128, 128])
    aux["a_pb"] = kb.din("a_pb", [128, 1])
    aux["a_b0"] = kb.din("a_b0", [128, 64])
    aux["a_b1"] = kb.din("a_b1", [128, 1])
    aux["I128"] = kb.din("I128", [128, 128])
    aux["xrows"] = kb.din("xrows", [T, D])
    aux["a2a_in"] = kb.dscratch("a2a_in", [8, 8, 128, 256], BF16)
    aux["a2a_out"] = kb.dscratch("a2a_out", [8, 8, 128, 256], BF16)
    return aux


def emit_attn_a(kb, C, aux, x1_dram, chunk_cb=None):
    P = kb.prog
    A = C.AR0
    Ab = A[:, :].bitcast(BF16)

    def bft(c0, n):
        return Ab[:, 2 * c0:2 * c0 + n]

    def f32t(c0, n):
        return A[:, c0:c0 + n]

    XNB = bft(A_XNB, 2048)
    TT_ = bft(A_TT, 2048)
    AOUT = bft(A_AOUT, 2048)
    COUT = bft(A_COUT, 2048)
    DOUT, D2OUT = bft(A_DOUT, 2048), bft(A_D2, 2048)
    EOUT, E2OUT = bft(A_EOUT, 2048), bft(A_E2, 2048)
    CQK, CVt = bft(A_CQK, 2048), bft(A_CV, 2048)
    O1B = bft(A_O1B, 2048)
    O2T1, O2T2 = bft(A_O2T1, 2048), bft(A_O2T2, 2048)
    LEP2, Yt = bft(A_LEP2, 2048), bft(A_Y, 2048)
    PR1 = bft(A_PR1, 2048)
    S1F = f32t(A_S1F, 512)
    P1B = bft(A_P1B, 1024)
    S2F = f32t(A_S2F, 512)
    S2B = bft(A_S2B, 1024)
    P2T = bft(A_P2T, 512)
    RST = f32t(A_RST, 16)
    IBF = bft(A_IBF, 128)
    BD = [bft(A_BD + 64 * i, 128) for i in range(6)]
    BDP = [bft(A_BDP + 64 * i, 128) for i in range(4)]
    W0B = bft(A_W0B, 192)
    W1T = f32t(A_W1, 3)
    ALNW, ALNB = f32t(A_ALNW, 128), f32t(A_ALNB, 128)
    PBT = f32t(A_PB, 1)
    M2T = f32t(A_M2, 512)
    SLAB = bft(A_SLAB, 2048)
    YC = f32t(A_YC, 512)
    ST32 = f32t(A_ST16, 32)
    SQ16 = bft(A_SQ16, 512)
    IF32 = f32t(A_IF32, 128)
    SQF = f32t(A_SQF, 2048)
    PR1F = f32t(A_PR1F, 2048)
    O1F = f32t(A_O1F, 2048)
    L2F = f32t(A_L2F, 2048)
    B0T = f32t(A_B0, 64)
    B1T = f32t(A_B1, 1)

    for i in range(6):
        P.gpsimd.dma_start(out=BD[i], in_=aux["a_bds"][i]).then_inc(C.qG, 16)
    for i in range(4):
        P.gpsimd.dma_start(out=BDP[i], in_=aux["a_bdp"][i]).then_inc(C.qG, 16)
    P.gpsimd.dma_start(out=IBF, in_=aux["I128"]).then_inc(C.qG, 16)
    P.gpsimd.dma_start(out=W0B, in_=aux["a_w0"]).then_inc(C.qG, 16)
    P.sync.dma_start(out=W1T, in_=aux["a_w1"]).then_inc(C.qS, 16)
    P.sync.dma_start(out=M2T, in_=aux["a_m2"]).then_inc(C.qS, 16)
    P.sync.dma_start(out=ALNW, in_=aux["a_alnw"]).then_inc(C.qS, 16)
    P.sync.dma_start(out=ALNB, in_=aux["a_alnb"]).then_inc(C.qS, 16)
    P.sync.dma_start(out=PBT, in_=aux["a_pb"]).then_inc(C.qS, 16)
    P.sync.dma_start(out=IF32, in_=aux["I128"]).then_inc(C.qS, 16)
    P.sync.dma_start(out=B0T, in_=aux["a_b0"]).then_inc(C.qS, 16)
    P.sync.dma_start(out=B1T, in_=aux["a_b1"]).then_inc(C.qS, 16)
    pre_g, pre_s = C.qG.val, C.qS.val

    for sb in range(8):
        P.sync.wait_ge(C.cV, xt2_free if sb > 0 else C.cV.val)
        P.sync.wait_ge(C.cT, C.cT.val)
        if sb > 0:
            P.sync.wait_ge(C.cA, a_xnb)
        P.sync.dma_start(out=C.XT2[:, :], in_=aux["xrows"][128 * sb:128 * (sb + 1), :]).then_inc(C.qE[0], 16)
        P.vector.wait_ge(C.qE[0], C.qE[0].val)
        if sb == 0:
            P.vector.wait_ge(C.qS, pre_s)
            P.vector.wait_ge(C.qG, pre_g)
        P.vector.tensor_reduce(RST[:, 0:1], C.XT2[:, :], axis=AXX, op=ALU.add)
        P.vector.tensor_scalar(RST[:, 0:1], RST[:, 0:1], 1.0 / D, None, ALU.mult)
        P.vector.tensor_scalar(C.XT2[:, :], C.XT2[:, :], RST[:, 0:1], None, ALU.subtract)
        P.vector.tensor_tensor(SQF, C.XT2[:, :], C.XT2[:, :], ALU.mult)
        P.vector.tensor_reduce(RST[:, 1:2], SQF, axis=AXX, op=ALU.add)
        P.vector.tensor_scalar(RST[:, 1:2], RST[:, 1:2], 1.0 / D, None, ALU.mult)
        P.vector.tensor_scalar(RST[:, 1:2], RST[:, 1:2], EPS, None, ALU.add)
        P.scalar.wait_ge(C.cV, C.cV.val)
        P.scalar.activation(RST[:, 1:2], RST[:, 1:2], AF.Sqrt)
        P.vector.wait_ge(C.cA, C.cA.val)
        P.vector.reciprocal(RST[:, 1:2], RST[:, 1:2])
        P.vector.tensor_scalar(C.XT2[:, :], C.XT2[:, :], RST[:, 1:2], None, ALU.mult)
        ln_ok = C.cV.val
        P.scalar.wait_ge(C.cV, ln_ok)
        P.scalar.activation(XNB, C.XT2[:, :], AF.Identity)
        a_xnb = C.cA.val
        P.vector.memset(O1F[:, :], 0.0)
        P.vector.memset(O1B[:, :], 0.0)
        if chunk_cb is not None:
            chunk_cb()

        tb_ev = []
        for g in range(4):
            P.tensor.wait_ge(C.cV, ln_ok)
            if g == 0:
                P.tensor.wait_ge(C.cA, a_xnb)
            if sb > 0:
                P.tensor.wait_ge(C.cA, prev_cp)
            for q in range(4):
                j = 4 * g + q
                P.tensor.matmul(C.PS[g % 4][:, 128 * q:128 * (q + 1)], XNB[:, 128 * j:128 * (j + 1)], IBF,
                                start=True, stop=True)
            P.mark_last('tensor', C.cT)
            P.scalar.wait_ge(C.cT, C.cT.val)
            P.scalar.activation(TT_[:, 512 * g:512 * (g + 1)], C.PS[g % 4][:, :], AF.Identity)
            tb_ev.append(C.cA.val)

        outs = [AOUT, COUT, DOUT, D2OUT, EOUT, E2OUT]
        bd_ev = []
        for ci in range(6):
            for g in range(4):
                i = 4 * ci + g
                P.tensor.wait_ge(C.cA, tb_ev[g] if i < 4 else bd_ev[i - 4])
                for q in range(4):
                    j = 4 * g + q
                    P.tensor.matmul(C.PS[4 + g % 4][:, 128 * q:128 * (q + 1)], BD[ci],
                                    TT_[:, 128 * j:128 * (j + 1)], start=True, stop=True)
                P.mark_last('tensor', C.cT)
                P.scalar.wait_ge(C.cT, C.cT.val)
                P.scalar.activation(outs[ci][:, 512 * g:512 * (g + 1)], C.PS[4 + g % 4][:, :], AF.Identity)
                bd_ev.append(C.cA.val)

        cp_ev = []
        for si, (srcT, dstT) in enumerate(((AOUT, CQK), (COUT, CVt))):
            for g in range(4):
                i = 4 * si + g
                P.tensor.wait_ge(C.cA, bd_ev[4 * si + g] if i < 4 else cp_ev[i - 4])
                for q in range(4):
                    j = 4 * g + q
                    P.tensor.matmul(C.PS[g % 4][:, 128 * q:128 * (q + 1)], srcT[:, 128 * j:128 * (j + 1)], IBF,
                                    start=True, stop=True)
                P.mark_last('tensor', C.cT)
                P.scalar.wait_ge(C.cT, C.cT.val)
                P.scalar.activation(dstT[:, 512 * g:512 * (g + 1)], C.PS[g % 4][:, :], AF.Identity)
                cp_ev.append(C.cA.val)
        prev_cp = cp_ev[-1]


        cv_ap = CVt[:, :].rearrange("p (j u c) -> p j u c", j=16, u=8, c=16)
        for uh in range(2):
            Dsrc = DOUT if uh == 0 else D2OUT
            Esrc = EOUT if uh == 0 else E2OUT
            P.tensor.wait_ge(C.cV, C.cV.val)
            P.tensor.wait_ge(C.cA, C.cA.val)
            for j in range(16):
                for r in range(4):
                    for band in range(4):
                        P.tensor.matmul(
                            C.PS[4 + r][32 * band:32 * band + 32, 32 * j:32 * j + 32],
                            Dsrc[32 * r:32 * r + 8, 128 * j + 32 * band:128 * j + 32 * band + 32],
                            Esrc[32 * r:32 * r + 8, 128 * j + 32 * band:128 * j + 32 * band + 32],
                            start=True, stop=True, tile_position=(32 * r, 32 * band),
                        )
            P.mark_last('tensor', C.cT)
            gram_done = C.cT.val
            O2 = O2T1 if uh == 0 else O2T2
            for r in range(4):
                P.vector.wait_ge(C.cT, gram_done)
                P.vector.tensor_tensor(S2F[:, :], C.PS[4 + r][:, :], M2T, ALU.add)
                P.vector.tensor_reduce(RST[:, 0:16], S2F[:, :].rearrange("p (j b) -> p j b", j=16),
                                       axis=AXX, op=ALU.max)
                P.vector.tensor_tensor(S2F[:, :].rearrange("p (j b) -> p j b", j=16),
                                       S2F[:, :].rearrange("p (j b) -> p j b", j=16),
                                       RST[:, 0:16].unsqueeze(2).to_broadcast([128, 16, 32]), ALU.subtract)
                P.scalar.wait_ge(C.cV, C.cV.val)
                P.scalar.activation(S2B[:, 0:512], S2F[:, :], AF.Exp, bias=0.0, scale=float(1.0 / np.sqrt(8.0)))
                P.vector.wait_ge(C.cA, C.cA.val)
                P.vector.tensor_reduce(RST[:, 0:16], S2B[:, 0:512].rearrange("p (j b) -> p j b", j=16),
                                       axis=AXX, op=ALU.add)
                P.vector.reciprocal(RST[:, 0:16], RST[:, 0:16])
                P.vector.tensor_copy(SQ16[:, 0:16], RST[:, 0:16])
                P.vector.tensor_tensor(S2B[:, 0:512].rearrange("p (j b) -> p j b", j=16),
                                       S2B[:, 0:512].rearrange("p (j b) -> p j b", j=16),
                                       SQ16[:, 0:16].unsqueeze(2).to_broadcast([128, 16, 32]), ALU.mult)
                P.vector.wait_ge(C.cT, C.cT.val)
                P.vector.transpose(P2T[:, :], S2B[:, 0:512])
                P.tensor.wait_ge(C.cV, C.cV.val)
                for j in range(16):
                    for band in range(4):
                        P.tensor.matmul(
                            C.PS[band][32 * r:32 * r + 8, 32 * j:32 * j + 32],
                            cv_ap[32 * band:32 * band + 32, j, 4 * uh + r, 8:16],
                            P2T[32 * band:32 * band + 32, 32 * j:32 * j + 32],
                            start=True, stop=True, tile_position=(32 * band, 32 * r),
                        )
                P.mark_last('tensor', C.cT)
            P.vector.wait_ge(C.cT, C.cT.val)
            for band in range(4):
                P.vector.tensor_copy(
                    O2[:, :].rearrange("p (j rl) -> p j rl", j=16)[:, :, 32 * band:32 * band + 32],
                    C.PS[band][:, :].rearrange("p (j b) -> p j b", j=16))
        P.vector.wait_ge(C.cA, cp_ev[-1])
        for g in range(4):
            q_ap = CQK[:, 512 * g:512 * (g + 1)].rearrange("p (j u c) -> p j u c", j=4, u=8, c=16)
            qq = q_ap[:, :, :, 0:8]
            kk_ = q_ap[:, :, :, 8:16]
            for jj in range(4):
                P.vector.tensor_tensor(PR1F[:, 512 * jj:512 * (jj + 1)].rearrange("p (a b c) -> p a b c", a=8, b=8, c=8),
                                       qq[:, jj].unsqueeze(2).to_broadcast([128, 8, 8, 8]),
                                       kk_[:, jj].unsqueeze(1).to_broadcast([128, 8, 8, 8]), ALU.mult)
            P.vector.tensor_reduce(S1F[:, 0:256],
                                   PR1F[:, 0:2048].rearrange("p (x c) -> p x c", c=8), axis=AXX, op=ALU.add)
            P.vector.tensor_reduce(ST32[:, 0:32], S1F[:, 0:256].rearrange("p (x b) -> p x b", b=8),
                                   axis=AXX, op=ALU.max)
            P.vector.tensor_tensor(S1F[:, 0:256].rearrange("p (x b) -> p x b", b=8),
                                   S1F[:, 0:256].rearrange("p (x b) -> p x b", b=8),
                                   ST32[:, 0:32].unsqueeze(2).to_broadcast([128, 32, 8]), ALU.subtract)
            P.scalar.wait_ge(C.cV, C.cV.val)
            P.scalar.activation(P1B[:, 256 * g:256 * (g + 1)], S1F[:, 0:256], AF.Exp,
                                bias=0.0, scale=float(1.0 / np.sqrt(8.0)))
            P.vector.wait_ge(C.cA, C.cA.val)
            P.vector.tensor_reduce(ST32[:, 0:32], P1B[:, 256 * g:256 * (g + 1)].rearrange("p (x b) -> p x b", b=8),
                                   axis=AXX, op=ALU.add)
            P.vector.reciprocal(ST32[:, 0:32], ST32[:, 0:32])
            P.vector.tensor_copy(SQ16[:, 0:32], ST32[:, 0:32])
            P.vector.tensor_tensor(P1B[:, 256 * g:256 * (g + 1)].rearrange("p (x b) -> p x b", b=8),
                                   P1B[:, 256 * g:256 * (g + 1)].rearrange("p (x b) -> p x b", b=8),
                                   SQ16[:, 0:32].unsqueeze(2).to_broadcast([128, 32, 8]), ALU.mult)
            p_ap = P1B[:, 256 * g:256 * (g + 1)].rearrange("p (j i b) -> p j i b", j=4, i=8, b=8)
            v1 = CVt[:, 512 * g:512 * (g + 1)].rearrange("p (j u c) -> p j u c", j=4, u=8, c=16)[:, :, :, 0:8]
            for jj in range(4):
                P.vector.tensor_tensor(PR1F[:, 512 * jj:512 * (jj + 1)].rearrange("p (a c b) -> p a c b", a=8, c=8, b=8),
                                       p_ap[:, jj].unsqueeze(2).to_broadcast([128, 8, 8, 8]),
                                       v1[:, jj].rearrange("p b c -> p c b").unsqueeze(1).to_broadcast([128, 8, 8, 8]),
                                       ALU.mult)
            o_ap = O1F[:, 512 * g:512 * (g + 1)].rearrange("p (j u c) -> p j u c", j=4, u=8, c=16)
            P.vector.tensor_reduce(o_ap[:, :, :, 0:8], PR1F[:, 0:2048].rearrange("p (x b) -> p x b", b=8),
                                   axis=AXX, op=ALU.add)
        cv_ap = CVt[:, :].rearrange("p (j u c) -> p j u c", j=16, u=8, c=16)
        o_all = O1F[:, :].rearrange("p (j u c) -> p j u c", j=16, u=8, c=16)
        for du, (o_lo, o_hi, v_lo) in ((0, (1, 8, 0)), (1, (0, 8, 0)), (2, (0, 7, 1))):
            n_u = o_hi - o_lo
            w0_ap = W0B[:, 64 * du:64 * du + 64].rearrange("p (u c) -> p u c", u=8, c=8)[:, o_lo:o_hi, :]
            tmp = PR1F[:, 0:16 * n_u * 8].rearrange("p (j u c) -> p j u c", j=16, u=n_u, c=8)
            P.vector.tensor_tensor(tmp, cv_ap[:, :, v_lo:v_lo + n_u, 0:8],
                                   w0_ap.unsqueeze(1).to_broadcast([128, 16, n_u, 8]), ALU.mult)
            P.vector.tensor_tensor(o_all[:, :, o_lo:o_hi, 0:8], o_all[:, :, o_lo:o_hi, 0:8], tmp, ALU.add)
        P.vector.tensor_tensor(o_all[:, :, :, 0:8], o_all[:, :, :, 0:8],
                               B0T.rearrange("p (u c) -> p u c", u=8)[:, :, :].unsqueeze(1).to_broadcast([128, 16, 8, 8]),
                               ALU.add)
        P.vector.tensor_copy(O1B[:, :], O1F[:, :])
        b1_done = C.cV.val
        P.vector.wait_ge(C.cA, cp_ev[-1])
        P.vector.memset(L2F[:, :], 0.0)
        c2_ap = COUT[:, :].rearrange("p (j band t r4) -> p j band t r4", j=16, band=4, t=8, r4=4)
        l_ap = L2F[:, :].rearrange("p (j band t r4) -> p j band t r4", j=16, band=4, t=8, r4=4)
        for dt, (t_lo, t_hi, s_lo) in ((0, (1, 8, 0)), (1, (0, 8, 0)), (2, (0, 7, 1))):
            n_t = t_hi - t_lo
            for band in range(4):
                tmp = PR1F[:, 0:16 * n_t * 4].rearrange("p (j t r4) -> p j t r4", j=16, t=n_t, r4=4)
                P.vector.tensor_scalar(tmp, c2_ap[:, :, band, s_lo:s_lo + n_t, :], W1T[:, dt:dt + 1], None, ALU.mult)
                P.vector.tensor_tensor(l_ap[:, :, band, t_lo:t_hi, :], l_ap[:, :, band, t_lo:t_hi, :], tmp, ALU.add)
        P.vector.tensor_scalar(L2F[:, :], L2F[:, :], B1T[:, 0:1], None, ALU.add)
        P.vector.tensor_copy(LEP2[:, :], L2F[:, :])

        for g in range(4):
            P.tensor.wait_ge(C.cV, C.cV.val)
            for q in range(4):
                j = 4 * g + q
                P.tensor.matmul(C.PS[q][:, 0:128], O1B[:, 128 * j:128 * (j + 1)], IBF, start=True, stop=True)
            P.mark_last('tensor', C.cT)
            P.vector.wait_ge(C.cT, C.cT.val)
            for q in range(4):
                P.vector.tensor_copy(PR1[:, 128 * q:128 * (q + 1)], C.PS[q][:, 0:128])
            P.tensor.wait_ge(C.cV, C.cV.val)
            for q in range(4):
                j = 4 * g + q
                P.tensor.matmul(C.PS[4 + q][:, 0:128], BDP[0], PR1[:, 128 * q:128 * (q + 1)], start=True, stop=False)
                P.tensor.matmul(C.PS[4 + q][:, 0:128], BDP[1], O2T1[:, 128 * j:128 * (j + 1)], start=False, stop=False)
                P.tensor.matmul(C.PS[4 + q][:, 0:128], BDP[2], O2T2[:, 128 * j:128 * (j + 1)], start=False, stop=False)
                P.tensor.matmul(C.PS[4 + q][:, 0:128], BDP[3], LEP2[:, 128 * j:128 * (j + 1)], start=False, stop=True)
            P.mark_last('tensor', C.cT)
            P.vector.wait_ge(C.cT, C.cT.val)
            for q in range(4):
                P.vector.tensor_scalar(PR1[:, 128 * q:128 * (q + 1)], C.PS[4 + q][:, 0:128], PBT[:, 0:1], None, ALU.add)
            P.tensor.wait_ge(C.cV, C.cV.val)
            for q in range(4):
                P.tensor.matmul(C.PS[q][:, 0:128], PR1[:, 128 * q:128 * (q + 1)], IBF, start=True, stop=True)
            P.mark_last('tensor', C.cT)
            P.vector.wait_ge(C.cT, C.cT.val)
            for q in range(4):
                P.vector.tensor_copy(YC[:, 128 * q:128 * (q + 1)], C.PS[q][:, 0:128])
            yap = YC[:, :].rearrange("p (x c) -> p x c", c=16)
            P.vector.tensor_reduce(ST32[:, 0:32], yap, axis=AXX, op=ALU.add)
            P.vector.tensor_scalar(ST32[:, 0:32], ST32[:, 0:32], 1.0 / 16, None, ALU.mult)
            P.vector.tensor_tensor(yap, yap, ST32[:, 0:32].unsqueeze(2).to_broadcast([128, 32, 16]), ALU.subtract)
            P.scalar.wait_ge(C.cV, C.cV.val)
            P.scalar.activation(SQ16[:, 0:512], YC[:, :], AF.Square)
            P.vector.wait_ge(C.cA, C.cA.val)
            P.vector.tensor_reduce(ST32[:, 0:32], SQ16[:, 0:512].rearrange("p (x c) -> p x c", c=16),
                                   axis=AXX, op=ALU.add)
            P.vector.tensor_scalar(ST32[:, 0:32], ST32[:, 0:32], 1.0 / 16, None, ALU.mult)
            P.vector.tensor_scalar(ST32[:, 0:32], ST32[:, 0:32], EPS, None, ALU.add)
            P.scalar.wait_ge(C.cV, C.cV.val)
            P.scalar.activation(ST32[:, 0:32], ST32[:, 0:32], AF.Sqrt)
            P.vector.wait_ge(C.cA, C.cA.val)
            P.vector.reciprocal(ST32[:, 0:32], ST32[:, 0:32])
            P.vector.tensor_tensor(yap, yap, ST32[:, 0:32].unsqueeze(2).to_broadcast([128, 32, 16]), ALU.mult)
            ycv = YC[:, :].rearrange("p (a c) -> p a c", a=4)
            P.vector.tensor_tensor(ycv, ycv, ALNW.unsqueeze(1).to_broadcast([128, 4, 128]), ALU.mult)
            P.vector.tensor_tensor(ycv, ycv, ALNB.unsqueeze(1).to_broadcast([128, 4, 128]), ALU.add)
            P.vector.tensor_tensor(YC[:, :], YC[:, :], C.XT2[:, 512 * g:512 * (g + 1)], ALU.add)
            P.vector.tensor_copy(Yt[:, 512 * g:512 * (g + 1)], YC[:, :])
        xt2_free = C.cV.val

        y_ap = Yt[:, :].rearrange("p (j u c) -> p j u c", j=16, u=8, c=16)
        P.vector.wait_ge(C.qG, C.qG.val)
        for dd in range(8):
            P.vector.tensor_copy(
                SLAB[:, 256 * dd:256 * (dd + 1)].rearrange("p (j u two) -> p j u two", j=16, u=8, two=2),
                y_ap[:, :, :, 2 * dd:2 * dd + 2])
        P.gpsimd.wait_ge(C.cV, C.cV.val)
        for dd in range(8):
            P.gpsimd.dma_start(out=aux["a2a_in"].ap()[dd, sb], in_=SLAB[:, 256 * dd:256 * (dd + 1)]).then_inc(C.qG, 16)

    P.gpsimd.wait_ge(C.qG, C.qG.val)
    P.gpsimd.collective_compute(
        "AllToAll", ALU.bypass, replica_groups=[list(range(NCORES))],
        ins=[aux["a2a_in"].ap().opt()], outs=[aux["a2a_out"].ap().opt()],
    ).then_inc(C.cc, 1)
    P.gpsimd.wait_ge(C.cc, C.cc.val)
    YTR = bft(A_YTR, 16384)
    for src in range(8):
        for sb in range(8):
            P.gpsimd.wait_ge(C.cT, C.cT.val)
            P.gpsimd.dma_start(out=SLAB[:, 0:256], in_=aux["a2a_out"].ap()[src, sb]).then_inc(C.qR[0], 16)
            P.tensor.wait_ge(C.qR[0], C.qR[0].val)
            P.tensor.wait_ge(C.cV, C.cV.val)
            for par in range(2):
                sel = SLAB[:, 0:256].rearrange("p (j u two) -> p two (j u)", j=16, u=8)[:, par, :]
                P.tensor.matmul(C.PS[par][:, 0:128], sel, IBF, start=True, stop=True)
            P.mark_last('tensor', C.cT)
            P.vector.wait_ge(C.cT, C.cT.val)
            for par in range(2):
                col = 2 * (8 * src + sb) + par
                P.vector.tensor_copy(YTR[:, 128 * col:128 * (col + 1)], C.PS[par][:, 0:128])
    SCR = f32t(A_SCR, 1024)
    X1T = f32t(A_X1T, 1024)
    for F in range(16):
        for rb in range(8):
            if rb == 0:
                P.sync.wait_ge(C.cV, C.cV.val)
            else:
                P.sync.wait_ge(C.cT, C.cT.val)
            P.sync.dma_start(out=f32t(A_STG, 128),
                             in_=aux["xrows"][128 * rb:128 * (rb + 1), 128 * F:128 * (F + 1)]).then_inc(C.qE[1], 16)
            P.tensor.wait_ge(C.qE[1], C.qE[1].val)
            if rb == 0:
                P.tensor.wait_ge(C.cV, C.cV.val)
            pb_ = C.PS[4 + rb // 4]
            P.tensor.matmul(pb_[:, 128 * (rb % 4):128 * (rb % 4 + 1)], f32t(A_STG, 128), IF32, start=True, stop=True)
            P.mark_last('tensor', C.cT)
        P.vector.wait_ge(C.cT, C.cT.val)
        P.vector.wait_ge(C.qD[0], C.qD[0].val)
        ysel = YTR[:, :].rearrange("p (q a g) -> p q a g", q=128, a=8, g=16)[:, :, :, F]
        ysel = ysel.rearrange("p (s2 par) a -> p par s2 a", par=2)
        P.vector.tensor_copy(SCR.rearrange("p (par s a) -> p par s a", par=2, s=64), ysel)
        P.vector.tensor_copy(X1T[:, 0:512], C.PS[4][:, :])
        P.vector.tensor_copy(X1T[:, 512:1024], C.PS[5][:, :])
        P.vector.tensor_tensor(X1T, X1T, SCR, ALU.add)
        P.gpsimd.wait_ge(C.cV, C.cV.val)
        P.gpsimd.dma_start(out=x1_dram[F], in_=X1T).then_inc(C.qD[0], 16)


def build_full():
    kb = KB()
    C = Common(kb)
    din = {
        "lua": kb.din("lua", [D, 16]), "lda": kb.din("lda", [FF, 16]),
        "b_up": kb.din("b_up", [128, FC]), "b_down": kb.din("b_down", [128, KC]),
    }
    lub = kb.din("lub", [16, FF])
    ldb = kb.din("ldb", [16, D])
    wq_u = kb.din("wupT_q", [D, FF], I32)
    ws_u = kb.din("wup_sc", [D, FF])
    wq_d = kb.din("wdnT_q", [FF, D], I32)
    ws_d = kb.din("wdn_sc", [FF, D])
    out_rows = kb.dout("out_rows", [T, D])
    auxa = declare_attna_aux(kb)
    auxb = declare_attnb_aux(kb)
    x1 = kb.dscratch("x1s", [KC, 128, T]).ap()
    x2 = kb.dscratch("x2s", [KC, 128, T]).ap()
    x3 = kb.dscratch("x3s", [KC, 128, T]).ap()
    hm = kb.dscratch("hms", [KC, 128, T]).ap()
    x4 = kb.dscratch("x4s", [KC, 128, T]).ap()
    wup_bf = kb.dscratch("wup_bf", [FC, 128, KC, 128], BF16).ap()
    wdn_bf = kb.dscratch("wdn_bf", [4 * KC, 128, KC, 128], BF16).ap()
    P = kb.prog

    C.preload(din)
    predeq_chunk = make_predequant(kb, C, (wq_u, ws_u), (wq_d, ws_d), wup_bf, wdn_bf)
    emit_attn_a(kb, C, auxa, x1, chunk_cb=predeq_chunk)
    P.gpsimd.wait_ge(C.qC[0], C.qC[0].val)
    P.gpsimd.wait_ge(C.qC[1], C.qC[1].val)
    emit_ffn(kb, C, x1, x2, dequant=False,
             wup_src=[wup_bf[m] for m in range(FC)],
             wdn_src=[wdn_bf[st] for st in range(4 * KC)],
             lub_dram=lub, ldb_dram=ldb,
             halo_out=auxb["halo_in"].ap())
    emit_attn_b(kb, C, x2, x3, hm, auxb)
    emit_ffn(kb, C, x3, x4, dequant=False,
             wup_src=[wup_bf[m] for m in range(FC)],
             wdn_src=[wdn_bf[st] for st in range(4 * KC)],
             lub_dram=lub, ldb_dram=ldb)
    IF32 = C.AR0[:, A_IF32:A_IF32 + 128]
    STG = C.AR0[:, A_STG:A_STG + 128]
    P.sync.wait_ge(C.qD[0], C.qD[0].val)
    P.sync.wait_ge(C.qD[1], C.qD[1].val)
    P.sync.wait_ge(C.cT, C.cT.val)
    P.sync.dma_start(out=IF32, in_=auxa["I128"]).then_inc(C.qS, 16)
    P.tensor.wait_ge(C.qS, C.qS.val)
    prev_out = 0
    for rb in range(8):
        for F in range(16):
            P.sync.wait_ge(C.cT, C.cT.val)
            P.sync.dma_start(out=STG, in_=x4[F][:, 128 * rb:128 * (rb + 1)]).then_inc(C.qE[0], 16)
            P.tensor.wait_ge(C.qE[0], C.qE[0].val)
            if F % 4 == 0:
                P.tensor.wait_ge(C.cV, C.cV.val)
            pb_ = C.PS[4 + (F // 4) % 2]
            P.tensor.matmul(pb_[:, 128 * (F % 4):128 * (F % 4 + 1)], STG, IF32, start=True, stop=True)
            P.mark_last('tensor', C.cT)
            if F % 4 == 3:
                P.vector.wait_ge(C.cT, C.cT.val)
                if F == 3:
                    P.vector.wait_ge(C.qD[1], prev_out)
                P.vector.tensor_copy(C.XT2[:, 512 * (F // 4):512 * (F // 4 + 1)], pb_[:, :])
                if F == 15:
                    P.gpsimd.wait_ge(C.cV, C.cV.val)
                    P.gpsimd.dma_start(out=out_rows[128 * rb:128 * (rb + 1), :], in_=C.XT2[:, :]).then_inc(C.qD[1], 16)
                    prev_out = C.qD[1].val
    P.gpsimd.wait_ge(C.qD[1], C.qD[1].val)
    P.gpsimd.wait_ge(C.qD[0], C.qD[0].val)
    P.gpsimd.wait_ge(C.qG, C.qG.val)
    P.sync.wait_ge(C.qS, C.qS.val)
    return kb.finalize()


K_BUILD_CACHE = {}


def full_in_maps(inp):
    maps = prep_inputs(inp)
    ac = prep_attna_consts(inp)
    for i, m in enumerate(maps):
        m.update(ac)
        pp, cmrow = prep_attnb_pack(inp, i)
        m["parpack"] = pp
        m["cmrow"] = cmrow
        for nm in ["Wq", "Wk", "Wv", "Wig", "Wog", "Wo", "Wg", "Wgate", "Wval", "Wout"]:
            m["b_" + nm] = inp["L0_attn_b_" + nm]
        m["lub"] = inp["L0_ffn_w_up_lora_b"]
        m["ldb"] = inp["L0_ffn_w_down_lora_b"]
    return maps


def kernel(**inputs):
    maps = full_in_maps(inputs)
    nc = build_full()
    res = run_bass_kernel_spmd(nc, maps, core_ids=list(range(NCORES)))
    out = np.concatenate([res.results[i]["out_rows"] for i in range(NCORES)], axis=0)
    return out.reshape(1, 8192, 2048)
